# version 03
# baseline (speedup 1.0000x reference)
import contextlib
import math
import numpy as np
import ml_dtypes
import concourse.bass as bass
import concourse.mybir as mybir
from concourse.bass_utils import run_bass_kernel_spmd

F32 = mybir.dt.float32
BF16 = mybir.dt.bfloat16
AF = mybir.ActivationFunctionType
ALU = mybir.AluOpType
AX = mybir.AxisListType.X

MAGIC = float(np.float32(1.5 * 2 ** 23))
PI = float(np.float32(math.pi))
TWO_PI = 6.283185307179586
INV_2PI = float(np.float32(1.0 / TWO_PI))
CW1 = float(np.float32(6.28125))
_r1 = TWO_PI - 6.28125
CW2 = float(np.float32(_r1))
CW3 = float(np.float32(_r1 - float(np.float32(_r1))))
BIGNEG = -1e30

D_MODEL = 2048
N_HEADS = 64
D_INNER = 4096
D_IN_PROJ = 8384
D_FFN = 8192
EPS_A = 1e-6
EPS_F = 1e-5
EPS_LN = 1e-5

TOK = 1024
NT = 8
NTH = 9
TOKH = NTH * 128

NPH = 99


class Prog:

    def __init__(self, nc):
        self.nc = nc
        self.engines = {
            "pe": nc.tensor, "act": nc.scalar, "dve": nc.vector,
            "pool": nc.gpsimd, "sp": nc.sync,
        }
        self.ops = []
        self.deps = []
        self.cur_fence = set()
        self.last_on_eng = {}

    def fence(self):
        self.cur_fence = set(self.last_on_eng.values())

    class Buf:
        __slots__ = ("name", "writer", "readers", "psum")
        def __init__(self, name, psum=False):
            self.name = name
            self.writer = None
            self.readers = set()
            self.psum = psum

    def buf(self, name, psum=False):
        return Prog.Buf(name, psum)

    serialize_all = False
    serial_now = False

    def op(self, eng, emit_fn, reads=(), writes=(), dma=False):
        idx = len(self.ops)
        dep = set()
        if (self.serialize_all or self.serial_now) and idx > 0:
            dep.add(idx - 1)
        for bb in reads:
            if bb.writer is not None:
                dep.add(bb.writer)
            if bb.psum:
                for r in bb.readers:
                    if self.ops[r][0] != eng:
                        dep.add(r)
        for bb in writes:
            if bb.writer is not None:
                dep.add(bb.writer)
            dep.update(bb.readers)
        dep.update(self.cur_fence)
        dep.discard(idx)
        self.ops.append((eng, emit_fn, 16 if dma else 1))
        self.deps.append(dep)
        self.last_on_eng[eng] = idx
        for bb in reads:
            bb.readers.add(idx)
        for bb in writes:
            bb.writer = idx
            bb.readers = set()
        return idx

    truncate = None

    def finalize(self):
        nc = self.nc
        n = len(self.ops)
        if self.truncate is not None:
            n = min(n, self.truncate)
        op_eng = [o[0] for o in self.ops]
        needs_inc = [o[2] == 16 for o in self.ops]
        waits = [[] for _ in range(n)]
        import os
        pe_pe = os.environ.get("K_PEPE", "0") == "1"
        for i in range(n):
            e = op_eng[i]
            for j in self.deps[i]:
                je = op_eng[j]
                if je == e and e == "pe" and not pe_pe:
                    continue
                needs_inc[j] = True
                waits[i].append(j)
        sem_val = [0] * n
        cnt = {e: 0 for e in self.engines}
        for j in range(n):
            if needs_inc[j]:
                cnt[op_eng[j]] += self.ops[j][2]
            sem_val[j] = cnt[op_eng[j]]
        self.final_counts = dict(cnt)
        sems = {e: nc.semaphore(f"psem_{e}").__enter__() for e in self.engines}
        seen = {}
        for i in range(n):
            e, emit_fn, inc = self.ops[i]
            byeng = {}
            for j in waits[i]:
                je = op_eng[j]
                v = sem_val[j]
                if v > byeng.get(je, 0):
                    byeng[je] = v
            for je in sorted(byeng):
                v = byeng[je]
                if seen.get((e, je), 0) >= v:
                    continue
                self.engines[e].wait_ge(sems[je], v)
                seen[(e, je)] = v
            inst = emit_fn(self.engines[e])
            if needs_inc[i]:
                inst.then_inc(sems[e], inc)
        for e in self.engines:
            if cnt[e] > 0:
                self.engines[e].wait_ge(sems[e], cnt[e])


def build_program(n_phases=NPH, taps=(), truncate=None, serialize_all=False,
                  ser_phases=()):
    taps = set(taps)
    ser_phases = set(ser_phases)
    nc = bass.Bass("TRN2", target_bir_lowering=False)
    p = Prog(nc)
    p.serialize_all = serialize_all

    def din(name, shape, dtype=F32):
        return nc.dram_tensor(name, shape, dtype, kind="ExternalInput").ap()

    x_hbm = din("x", [TOKH, D_MODEL])
    w_in = din("w_in", [D_MODEL, D_IN_PROJ])
    w_out = din("w_out", [D_INNER, D_MODEL])
    wg = din("wg", [D_FFN, D_MODEL])
    wu = din("wu", [D_FFN, D_MODEL])
    wd = din("wd", [D_MODEL, D_FFN])
    dt_bias_r = din("dt_bias_r", [1, N_HEADS])
    a_log_c = din("a_log_c", [N_HEADS, 1])
    d_rep_r = din("d_rep_r", [1, D_INNER])
    bb_ev = din("bb_ev", [1, 2048])
    bb_od = din("bb_od", [1, 2048])
    cb_ev = din("cb_ev", [1, 2048])
    cb_od = din("cb_od", [1, 2048])
    normw_r = din("normw_r", [1, D_INNER])
    r32f = din("r32f", [N_HEADS, 2048])
    rn_ev = din("rn_ev", [N_HEADS, 2048], BF16)
    rn_od = din("rn_od", [N_HEADS, 2048], BF16)
    idf = din("idf", [128, 128])
    idb = din("idb", [128, 128], BF16)
    onesf = din("onesf", [1, 128])
    actc = din("actc", [128, 8])
    bnegp = din("bnegp", [128, 128])

    y_hbm = nc.dram_tensor("y", [TOK, D_MODEL], F32, kind="ExternalOutput").ap()

    xqT_dram = nc.dram_tensor("xqT_scr", [16, 128, TOK], BF16, kind="Internal").ap()
    xs_dram = nc.dram_tensor("xs_scr", [NTH, 128, D_INNER], BF16, kind="Internal").ap()
    gu_dram = nc.dram_tensor("gu_scr", [NT, 128, D_FFN], F32, kind="Internal").ap()
    wqdT_dram = nc.dram_tensor("wqdT_scr", [64, 128, D_MODEL], BF16, kind="Internal").ap()

    tap_outs = {}
    def tap(name, shape, dtype=F32):
        if name in taps:
            tap_outs[name] = nc.dram_tensor("tap_" + name, shape, dtype,
                                            kind="ExternalOutput").ap()
            return tap_outs[name]
        return None

    A = nc.alloc_sbuf_tensor

    s_pool = A("s_pool", [128, NT * 4096], BF16)
    s_raw = A("s_raw", [128, NTH * 192], F32)
    s_dt = A("s_dt", [128, NTH * 64], F32)
    s_slam = A("s_slam", [128, NTH * 64], F32)
    s_sdtT = A("s_sdtT", [64, NTH * 128], F32)
    s_slamT = A("s_slamT", [64, NTH * 128], F32)
    s_sall = A("s_sall", [128, 28], F32)
    s_rmax = A("s_rmax", [128, 12], F32)
    s_carry = A("s_carry", [64, 4], F32)
    s_idf = A("s_idf", [128, 128], F32)
    s_idb = A("s_idb", [128, 128], BF16)
    s_ones = A("s_ones", [1, 128], F32)
    s_onesc = A("s_onesc", [128, 1], F32)
    s_lam = A("s_lam", [64, 1], F32)
    s_ws = A("s_ws", [128, 8], F32)
    s_actc = A("s_actc", [128, 8], F32)

    b = {}
    def B(name):
        if name not in b:
            b[name] = p.buf(name, psum=name.startswith("ps"))
        return b[name]

    def dma(dst, src, reads=(), writes=(), q="sp"):
        return p.op(q, lambda e, d=dst, s=src: e.dma_start(out=d, in_=s),
                    reads=reads, writes=writes, dma=True)

    def mm(ps, lhsT, rhs, start, stop, reads=(), writes=()):
        return p.op("pe", lambda e, o=ps, l=lhsT, r=rhs, st=start, sp=stop:
                    e.matmul(o, l, r, start=st, stop=sp), reads=reads, writes=writes)

    def tr(ps, in_, ident, reads=(), writes=()):
        return p.op("pe", lambda e, o=ps, i=in_, d=ident: e.transpose(o, i, d),
                    reads=reads, writes=writes)

    cb = B("consts")
    lamb = B("lam")
    carb = B("carry")
    sallb = B("sall")
    rmaxb = B("rmax")
    rawb = B("raw")
    dtb_ = B("dt")
    slamb = B("slam")
    sdtTb = B("sdtT")
    slamTb = B("slamT")
    wsb2 = B("ws")
    xqTdb = B("xqT_dram")
    xsdb = B("xs_dram")
    gudb = B("gu_dram")
    wqdb = B("wqd_dram")
    poolb = B("pool")
    outb = B("out")

    dma(s_idf[:], idf, writes=[cb])
    dma(s_idb[:], idb, writes=[cb])
    dma(s_ones[:], onesf, writes=[cb])
    dma(s_actc[:], actc, writes=[cb])
    p.op("dve", lambda e: e.memset(s_onesc[:], 1.0), writes=[cb])
    p.op("dve", lambda e: e.memset(s_carry[:], 0.0), writes=[carb])
    p.op("dve", lambda e: e.memset(s_rmax[:], 0.0), writes=[rmaxb])

    with nc.sbuf_tensor("s_alog", [64, 1], F32) as s_alog:
        dma(s_alog[:], a_log_c, writes=[lamb])
        p.op("act", lambda e: e.activation(s_lam[:], s_alog[:], AF.Exp),
             reads=[lamb], writes=[lamb])
    p.fence()

    x_t = x_hbm.rearrange("(n p) f -> n p f", p=128)
    y_t = y_hbm.rearrange("(n p) f -> n p f", p=128)

    p.serial_now = "AB" in ser_phases
    with contextlib.ExitStack() as _esAB:
        _e = _esAB.enter_context
        s_xnT = _e(nc.sbuf_tensor("s_xnT", [128, NTH * 2048], BF16))
        s_wsb = [_e(nc.sbuf_tensor(f"s_wsb{i}", [128, 2048], F32)) for i in range(2)]
        s_wsjunk = _e(nc.sbuf_tensor("s_wsjunk", [128, 2048], F32))
        s_psums = _e(nc.sbuf_tensor("s_psums", [128, 4], F32))
        s_scal = _e(nc.sbuf_tensor("s_scal", [1, 8], F32))
        xnTb = B("xnT")
        wsbb = [B("wsb0"), B("wsb1")]
        wsjb = B("wsjunk")
        psmb = B("psums")
        scalb = B("scal")

        p.op("dve", lambda e: e.memset(s_psums[:], 0.0), writes=[psmb])

        def scale_gen():
            wg_v = wg.rearrange("(n p) k -> n p k", p=128)
            wu_v = wu.rearrange("(n p) k -> n p k", p=128)
            wd_v = wd.rearrange("(n p) k -> n p k", p=128)
            views = [
                (0, lambda i: wg_v[i], 64),
                (1, lambda i: wu_v[i], 64),
                (2, lambda i: wd_v[i // 4][:, (i % 4) * 2048:(i % 4 + 1) * 2048], 64),
            ]
            for col, view, ntl in views:
                for i in range(ntl):
                    par = i % 2
                    dma(s_wsb[par][:], view(i), reads=[wsbb[par]], writes=[wsbb[par]],
                        q="pool")
                    p.op("act", lambda e, pr=par, c=col: e.activation(
                        s_wsjunk[:], s_wsb[pr][:], AF.Abs,
                        accum_out=s_psums[:, 3:4]),
                        reads=[wsbb[par]], writes=[wsjb, psmb])
                    p.op("dve", lambda e, c=col: e.tensor_add(
                        s_psums[:, c:c + 1], s_psums[:, c:c + 1], s_psums[:, 3:4]),
                        reads=[psmb], writes=[psmb])
                    yield

        def scale_fin():
            with (nc.psum_tensor("ps_sc", [1, 8], F32) as ps_sc,
                  nc.psum_tensor("ps_ws", [128, 8], F32) as ps_ws):
                psc = B("ps_sc"); pws = B("ps_ws")
                mm(ps_sc[0:1, 0:3], s_onesc[:], s_psums[:, 0:3], True, True,
                   reads=[psmb, cb], writes=[psc])
                p.op("dve", lambda e: e.tensor_copy(s_scal[0:1, 0:3], ps_sc[0:1, 0:3]),
                     reads=[psc], writes=[scalb])
                nume = float(D_FFN * D_MODEL)
                p.op("dve", lambda e: e.tensor_scalar(
                    s_scal[0:1, 4:7], s_scal[0:1, 0:3], 1.0 / nume, EPS_F,
                    ALU.mult, ALU.add), reads=[scalb], writes=[scalb])
                p.op("dve", lambda e: e.reciprocal(s_scal[0:1, 0:3], s_scal[0:1, 4:7]),
                     reads=[scalb], writes=[scalb])
                mm(ps_ws[:, 0:8], s_ones[:], s_scal[0:1, 0:8], True, True,
                   reads=[scalb, cb], writes=[pws])
                p.op("dve", lambda e: e.tensor_copy(s_ws[:], ps_ws[:, 0:8]),
                     reads=[pws], writes=[wsb2])
            t_ws = tap("wscal", [128, 8])
            if t_ws is not None:
                dma(t_ws, s_ws[:], reads=[wsb2])

        sg = scale_gen()

        def pump(k):
            for _ in range(k):
                if next(sg, "END") == "END":
                    return

        t_xn = tap("xn", [TOK, D_MODEL])
        t_xq = tap("xq", [TOK, D_MODEL])
        t_raw = tap("raw", [TOK, 192])
        t_dt = tap("dt", [TOK, 64])
        t_sdt = tap("sdt", [64, TOK])
        t_slam = tap("slam", [64, TOK])

        with contextlib.ExitStack() as _esA:
            _e = _esA.enter_context
            s_wraw = _e(nc.sbuf_tensor("s_wraw", [128, 16 * 192], F32))
            s_dtb_bc = _e(nc.sbuf_tensor("s_dtb_bc", [128, 64], F32))
            s_xt = _e(nc.sbuf_tensor("s_xt", [128, 2048], F32))
            s_xn = _e(nc.sbuf_tensor("s_xn", [128, 2048], F32))
            s_xnTf = _e(nc.sbuf_tensor("s_xnTf", [128, 2048], F32))
            s_xq = _e(nc.sbuf_tensor("s_xq", [128, 2048], BF16))
            s_sq = _e(nc.sbuf_tensor("s_sq", [128, 2048], F32))
            s_c1 = _e(nc.sbuf_tensor("s_c1", [128, 8], F32))
            s_dtT = _e(nc.sbuf_tensor("s_dtT", [64, 128], F32))
            s_ldtT = _e(nc.sbuf_tensor("s_ldtT", [64, 128], F32))
            s_stg = _e(nc.sbuf_tensor("s_stg", [128, 512], BF16))
            psA0 = _e(nc.psum_tensor("psA0", [128, 512], F32))
            psA1 = _e(nc.psum_tensor("psA1", [128, 512], F32))
            psA2 = _e(nc.psum_tensor("psA2", [128, 512], BF16))
            psA3 = _e(nc.psum_tensor("psA3", [128, 192], F32))
            psA4 = _e(nc.psum_tensor("psA4", [64, 128], F32))
            psA5 = _e(nc.psum_tensor("psA5", [128, 64], F32))
            wrb = B("wraw"); dtbb = B("dtb_bc")
            xtb = B("xt"); xnb = B("xn"); xnTfb = B("xnTf"); xqb = B("xq")
            sqb = B("sq"); c1b = B("c1"); dtTb = B("dtT"); ldtTb = B("ldtT")
            stgb = B("stg")
            pA0 = B("psA0"); pA1 = B("psA1"); pA2 = B("psA2")
            pA3 = B("psA3"); pA4 = B("psA4"); pA5 = B("psA5")

            dma(s_wraw[:].rearrange("p (kc o) -> p kc o", kc=16),
                w_in[:, 8192:8384].rearrange("(kc p) o -> p kc o", p=128),
                writes=[wrb])
            with nc.sbuf_tensor("s_row64", [1, 64], F32) as s_row64:
                r64b = B("row64")
                dma(s_row64[:], dt_bias_r, writes=[r64b])
                mm(psA5[:], s_ones[:], s_row64[0:1, 0:64], True, True,
                   reads=[r64b, cb], writes=[pA5])
                p.op("dve", lambda e: e.tensor_copy(s_dtb_bc[:], psA5[:]),
                     reads=[pA5], writes=[dtbb])

            for j in range(NTH):
                jr = j - 1
                dma(s_xt[:], x_t[j], reads=[xtb], writes=[xtb])
                p.op("dve", lambda e: e.reduce_sum(s_c1[:, 0:1], s_xt[:], axis=AX),
                     reads=[xtb], writes=[c1b])
                p.op("act", lambda e: e.activation(s_sq[:], s_xt[:], AF.Square,
                                                   accum_out=s_c1[:, 1:2]),
                     reads=[xtb], writes=[sqb, c1b])
                p.op("dve", lambda e: e.tensor_scalar(s_c1[:, 2:3], s_c1[:, 0:1], -1.0 / 2048.0, None, ALU.mult),
                     reads=[c1b], writes=[c1b])
                p.op("dve", lambda e: e.tensor_mul(s_c1[:, 3:4], s_c1[:, 2:3], s_c1[:, 2:3]),
                     reads=[c1b], writes=[c1b])
                p.op("dve", lambda e: e.tensor_scalar(s_c1[:, 4:5], s_c1[:, 1:2], 1.0 / 2048.0, None, ALU.mult),
                     reads=[c1b], writes=[c1b])
                p.op("dve", lambda e: e.tensor_sub(s_c1[:, 4:5], s_c1[:, 4:5], s_c1[:, 3:4]),
                     reads=[c1b], writes=[c1b])
                p.op("act", lambda e: e.activation(s_c1[:, 5:6], s_c1[:, 4:5], AF.Ln, bias=s_actc[:, 0:1]),
                     reads=[c1b, cb], writes=[c1b])
                p.op("act", lambda e: e.activation(s_c1[:, 5:6], s_c1[:, 5:6], AF.Exp, scale=-0.5),
                     reads=[c1b], writes=[c1b])
                p.op("dve", lambda e: e.tensor_scalar(s_xn[:], s_xt[:], s_c1[:, 2:3], s_c1[:, 5:6], ALU.add, ALU.mult),
                     reads=[xtb, c1b], writes=[xnb])
                if t_xn is not None and j > 0:
                    dma(t_xn.rearrange("(n p) f -> n p f", p=128)[jr], s_xn[:], reads=[xnb])
                if j > 0:
                    p.op("dve", lambda e: e.reduce_max(s_c1[:, 6:7], s_xn[:], axis=AX, apply_absolute_value=True),
                         reads=[xnb], writes=[c1b])
                    p.op("dve", lambda e: e.tensor_scalar(s_c1[:, 6:7], s_c1[:, 6:7], 1.0 / 127.0, EPS_F, ALU.mult, ALU.add),
                         reads=[c1b], writes=[c1b])
                    p.op("dve", lambda e, jj=jr: e.tensor_copy(s_sall[:, jj:jj + 1], s_c1[:, 6:7]),
                         reads=[c1b], writes=[sallb])
                    p.op("dve", lambda e: e.reciprocal(s_c1[:, 7:8], s_c1[:, 6:7]),
                         reads=[c1b], writes=[c1b])
                    p.op("dve", lambda e: e.tensor_scalar(s_sq[:], s_xn[:], s_c1[:, 7:8], MAGIC, ALU.mult, ALU.add),
                         reads=[xnb, c1b], writes=[sqb])
                    p.op("dve", lambda e: e.tensor_scalar(s_xq[:], s_sq[:], MAGIC, None, ALU.subtract),
                         reads=[sqb], writes=[xqb])
                    if t_xq is not None:
                        p.op("dve", lambda e: e.tensor_copy(s_sq[:], s_xq[:]), reads=[xqb], writes=[sqb])
                        dma(t_xq.rearrange("(n p) f -> n p f", p=128)[jr], s_sq[:], reads=[sqb])
                for q in range(4):
                    ps_h, pb_h = (psA0, pA0) if q % 2 == 0 else (psA1, pA1)
                    for c in range(4):
                        kc = q * 4 + c
                        tr(ps_h[:, c * 128:(c + 1) * 128], s_xn[:, kc * 128:(kc + 1) * 128],
                           s_idf[:], reads=[xnb, cb], writes=[pb_h])
                    base = q * 512
                    p.op("act", lambda e, b0=base, ph=ps_h: e.activation(
                        s_xnTf[:, b0:b0 + 512], ph[:, 0:512], AF.Identity),
                        reads=[pb_h], writes=[xnTfb])
                    p.op("dve", lambda e, b0=base, jj=j, ph=ps_h: e.tensor_copy(
                        s_xnT[:, jj * 2048 + b0: jj * 2048 + b0 + 512], ph[:, 0:512]),
                        reads=[pb_h], writes=[xnTb])
                if j > 0:
                    for q in range(4):
                        for c in range(4):
                            kc = q * 4 + c
                            tr(psA2[:, c * 128:(c + 1) * 128], s_xq[:, kc * 128:(kc + 1) * 128],
                               s_idb[:], reads=[xqb, cb], writes=[pA2])
                        p.op("dve", lambda e: e.tensor_copy(s_stg[:], psA2[:, 0:512]),
                             reads=[pA2], writes=[stgb])
                        for c in range(4):
                            kc = q * 4 + c
                            dma(xqT_dram[kc, :, jr * 128:(jr + 1) * 128], s_stg[:, c * 128:(c + 1) * 128],
                                reads=[stgb], writes=[xqTdb])
                for kc in range(16):
                    mm(psA3[:], s_xnTf[:, kc * 128:(kc + 1) * 128],
                       s_wraw[:, kc * 192:(kc + 1) * 192],
                       kc == 0, kc == 15, reads=[xnTfb, wrb], writes=[pA3])
                p.op("dve", lambda e, jj=j: e.tensor_copy(s_raw[:, jj * 192:(jj + 1) * 192], psA3[:]),
                     reads=[pA3], writes=[rawb])
                if t_raw is not None and j > 0:
                    dma(t_raw.rearrange("(n p) f -> n p f", p=128)[jr],
                        s_raw[:, j * 192:(j + 1) * 192], reads=[rawb])
                o0 = j * 192 + 128
                p.op("dve", lambda e, o=o0: e.tensor_add(s_sq[:, 0:64], s_raw[:, o:o + 64], s_dtb_bc[:]),
                     reads=[rawb, dtbb], writes=[sqb])
                p.op("dve", lambda e: e.tensor_scalar(s_sq[:, 64:128], s_sq[:, 0:64], 0.0, None, ALU.max),
                     reads=[sqb], writes=[sqb])
                p.op("act", lambda e: e.activation(s_sq[:, 128:192], s_sq[:, 0:64], AF.Abs),
                     reads=[sqb], writes=[sqb])
                p.op("act", lambda e: e.activation(s_sq[:, 128:192], s_sq[:, 128:192], AF.Exp, scale=-1.0),
                     reads=[sqb], writes=[sqb])
                p.op("act", lambda e: e.activation(s_sq[:, 128:192], s_sq[:, 128:192], AF.Ln, bias=s_actc[:, 1:2]),
                     reads=[sqb, cb], writes=[sqb])
                p.op("dve", lambda e, jj=j: e.tensor_add(s_dt[:, jj * 64:(jj + 1) * 64],
                                                         s_sq[:, 64:128], s_sq[:, 128:192]),
                     reads=[sqb], writes=[dtb_])
                if t_dt is not None and j > 0:
                    dma(t_dt.rearrange("(n p) f -> n p f", p=128)[jr],
                        s_dt[:, j * 64:(j + 1) * 64], reads=[dtb_])
                tr(psA4[:], s_dt[:, j * 64:(j + 1) * 64], s_idf[:], reads=[dtb_, cb], writes=[pA4])
                p.op("dve", lambda e: e.tensor_copy(s_dtT[:], psA4[:]), reads=[pA4], writes=[dtTb])
                p.op("dve", lambda e: e.tensor_scalar(s_ldtT[:], s_dtT[:], s_lam[:], None, ALU.mult),
                     reads=[dtTb, lamb], writes=[ldtTb])
                p.op("dve", lambda e, jj=j: e.tensor_tensor_scan(
                    s_sdtT[:, jj * 128:(jj + 1) * 128], s_dtT[:], s_dtT[:], s_carry[:, 0:1],
                    ALU.add, ALU.bypass), reads=[dtTb, carb], writes=[sdtTb])
                p.op("dve", lambda e, jj=j: e.tensor_tensor_scan(
                    s_slamT[:, jj * 128:(jj + 1) * 128], s_ldtT[:], s_ldtT[:], s_carry[:, 1:2],
                    ALU.add, ALU.bypass), reads=[ldtTb, carb], writes=[slamTb])
                p.op("dve", lambda e, jj=j: e.tensor_copy(
                    s_carry[:, 0:1], s_sdtT[:, jj * 128 + 127: jj * 128 + 128]),
                    reads=[sdtTb], writes=[carb])
                p.op("dve", lambda e, jj=j: e.tensor_copy(
                    s_carry[:, 1:2], s_slamT[:, jj * 128 + 127: jj * 128 + 128]),
                    reads=[slamTb], writes=[carb])
                tr(psA5[:], s_slamT[:, j * 128:(j + 1) * 128], s_idf[0:64, 0:64],
                   reads=[slamTb, cb], writes=[pA5])
                p.op("dve", lambda e, jj=j: e.tensor_copy(s_slam[:, jj * 64:(jj + 1) * 64], psA5[:]),
                     reads=[pA5], writes=[slamb])
                pump(8)
            if t_sdt is not None:
                dma(t_sdt, s_sdtT[:], reads=[sdtTb])
            if t_slam is not None:
                dma(t_slam, s_slamT[:], reads=[slamTb])

        p.fence()
        if n_phases >= 2:
            t_z = tap("z", [TOK, D_INNER])
            t_xs = tap("xs", [TOK, D_INNER])
            with contextlib.ExitStack() as _esB:
                _e = _esB.enter_context
                s_wstg = _e(nc.sbuf_tensor("s_wstg", [128, 512], F32))
                s_wblk = _e(nc.sbuf_tensor("s_wblk", [128, 16 * 512], BF16))
                s_bstg = _e(nc.sbuf_tensor("s_bstg", [128, 512], BF16))
                s_bstgf = _e(nc.sbuf_tensor("s_bstgf", [128, 512], F32))
                psB = _e(nc.psum_tensor("psB", [128, 4096], F32))
                wstgb = B("wstg")
                wblkb = B("wblk")
                bstgb = B("bstg")
                bstgfb = B("bstgf")
                pB = [B(f"psB{j}") for j in range(8)]
                w_in_t = w_in.rearrange("(kc p) o -> kc p o", p=128)
                for g in range(16):
                    is_z = g < 8
                    for kc in range(16):
                        dma(s_wstg[:], w_in_t[kc, :, g * 512:(g + 1) * 512],
                            reads=[wstgb], writes=[wstgb])
                        p.op("pool", lambda e, k0=kc: e.tensor_copy(
                            s_wblk[:, k0 * 512:(k0 + 1) * 512], s_wstg[:]),
                            reads=[wstgb], writes=[wblkb])
                    for kc in range(16):
                        for jr in range(8):
                            jt = jr + 1
                            mm(psB[:, jr * 512:(jr + 1) * 512],
                               s_xnT[:, (jt * 16 + kc) * 128:(jt * 16 + kc + 1) * 128],
                               s_wblk[:, kc * 512:(kc + 1) * 512], kc == 0, kc == 15,
                               reads=[xnTb, wblkb], writes=[pB[jr]])
                    for jr in range(8):
                        pslice = psB[:, jr * 512:(jr + 1) * 512]
                        if is_z:
                            p.op("act", lambda e, jj=jr, gg=g, ps=pslice: e.activation(
                                s_pool[:, jj * 4096 + gg * 512: jj * 4096 + (gg + 1) * 512],
                                ps, AF.Silu), reads=[pB[jr]], writes=[poolb])
                            if t_z is not None:
                                p.op("dve", lambda e, ps=pslice: e.tensor_copy(s_bstgf[:], ps),
                                     reads=[pB[jr]], writes=[bstgfb])
                                dma(t_z.rearrange("(n p) f -> n p f", p=128)[jr, :, g * 512:(g + 1) * 512],
                                    s_bstgf[:], reads=[bstgfb])
                        else:
                            gg = g - 8
                            p.op("dve", lambda e, ps=pslice: e.tensor_copy(s_bstg[:], ps),
                                 reads=[pB[jr]], writes=[bstgb])
                            dma(xs_dram[jr + 1, :, gg * 512:(gg + 1) * 512], s_bstg[:],
                                reads=[bstgb], writes=[xsdb])
                            if t_xs is not None:
                                p.op("dve", lambda e, ps=pslice: e.tensor_copy(s_bstgf[:], ps),
                                     reads=[pB[jr]], writes=[bstgfb])
                                dma(t_xs.rearrange("(n p) f -> n p f", p=128)[jr, :, gg * 512:(gg + 1) * 512],
                                    s_bstgf[:], reads=[bstgfb])
                    if not is_z:
                        gg = g - 8
                        for kc in range(16):
                            mm(psB[:, 0:512],
                               s_xnT[:, (0 * 16 + kc) * 128:(0 * 16 + kc + 1) * 128],
                               s_wblk[:, kc * 512:(kc + 1) * 512], kc == 0, kc == 15,
                               reads=[xnTb, wblkb], writes=[pB[0]])
                        p.op("dve", lambda e: e.tensor_copy(s_bstg[:], psB[:, 0:512]),
                             reads=[pB[0]], writes=[bstgb])
                        dma(xs_dram[0, :, gg * 512:(gg + 1) * 512], s_bstg[:],
                            reads=[bstgb], writes=[xsdb])
                    pump(10)
        pump(10 ** 9)
        scale_fin()

    p.fence()

    p.serial_now = "C" in ser_phases
    if n_phases >= 3:
        t_rb = tap("rbv", [TOK, 64], BF16)
        t_rbt = tap("rbtv", [64, TOK], BF16)
        t_i1 = tap("i1v", [TOK, 2048], BF16)
        t_i2 = tap("i2v", [TOK, 2048], BF16)
        t_sc = tap("sinv", [TOK, 2048], BF16)
        t_cc2 = tap("cosv", [TOK, 2048], BF16)
        t_bh = tap("bh", [TOK, D_INNER], BF16)
        t_ch = tap("ch", [TOK, D_INNER], BF16)
        t_y = tap("y_scan", [TOK, D_INNER], BF16)
        t_yn = tap("yn", [TOK, D_INNER])
        with contextlib.ExitStack() as _esC:
            _e = _esC.enter_context
            s_drep = _e(nc.sbuf_tensor("s_drep", [128, D_INNER], BF16))
            s_normw = _e(nc.sbuf_tensor("s_normw", [128, D_INNER], BF16))
            s_bbev = _e(nc.sbuf_tensor("s_bbev", [128, 2048], BF16))
            s_bbod = _e(nc.sbuf_tensor("s_bbod", [128, 2048], BF16))
            s_cbev = _e(nc.sbuf_tensor("s_cbev", [128, 2048], BF16))
            s_cbod = _e(nc.sbuf_tensor("s_cbod", [128, 2048], BF16))
            s_r32f = _e(nc.sbuf_tensor("s_r32f", [64, 2048], F32))
            s_rnev = _e(nc.sbuf_tensor("s_rnev", [64, 2048], BF16))
            s_rnod = _e(nc.sbuf_tensor("s_rnod", [64, 2048], BF16))
            s_bneg = _e(nc.sbuf_tensor("s_bneg", [128, 128], F32))
            s_xsj = _e(nc.sbuf_tensor("s_xsj", [128, 2048], BF16))
            s_flat = _e(nc.sbuf_tensor("s_flat", [1, 512], F32))
            s_sin = _e(nc.sbuf_tensor("s_sin", [128, 1024], BF16))
            s_cos = _e(nc.sbuf_tensor("s_cos", [128, 1024], BF16))
            s_phr = _e(nc.sbuf_tensor("s_phr", [128, 512], F32))
            s_k = _e(nc.sbuf_tensor("s_k", [128, 512], F32))
            s_ph2 = _e(nc.sbuf_tensor("s_ph2", [128, 512], F32))
            s_rbB = _e(nc.sbuf_tensor("s_rbB", [128, 64], BF16))
            s_rbC = _e(nc.sbuf_tensor("s_rbC", [128, 64], BF16))
            s_rmsBT = _e(nc.sbuf_tensor("s_rmsBT", [64, 128], BF16))
            s_rmsCT = _e(nc.sbuf_tensor("s_rmsCT", [64, 128], BF16))
            s_i1 = _e(nc.sbuf_tensor("s_i1", [128, 1024], BF16))
            s_i2 = _e(nc.sbuf_tensor("s_i2", [128, 1024], BF16))
            s_t1 = _e(nc.sbuf_tensor("s_t1", [128, 1024], BF16))
            s_t2 = _e(nc.sbuf_tensor("s_t2", [128, 1024], BF16))
            s_Bh = _e(nc.sbuf_tensor("s_Bh", [128, 2048], BF16))
            s_Ch = _e(nc.sbuf_tensor("s_Ch", [128, 2048], BF16))
            s_smB = _e(nc.sbuf_tensor("s_smB", [128, 2048], BF16))
            s_smC = _e(nc.sbuf_tensor("s_smC", [128, 2048], BF16))
            s_state = _e(nc.sbuf_tensor("s_state", [128, 2048], BF16))
            s_e1 = _e(nc.sbuf_tensor("s_e1", [128, 256], F32))
            s_W = _e(nc.sbuf_tensor("s_W", [128, 256], BF16))
            s_WG = _e(nc.sbuf_tensor("s_WG", [128, 256], BF16))
            s_U = _e(nc.sbuf_tensor("s_U", [128, 128], BF16))
            s_Bdec = _e(nc.sbuf_tensor("s_Bdec", [128, 128], BF16))
            s_tyi = _e(nc.sbuf_tensor("s_tyi", [128, 512], F32))
            s_xsd = _e(nc.sbuf_tensor("s_xsd", [128, 512], BF16))
            s_A8 = _e(nc.sbuf_tensor("s_A8", [128, 64], F32))
            s_Adec = _e(nc.sbuf_tensor("s_Adec", [128, 64], F32))
            s_rowL = _e(nc.sbuf_tensor("s_rowL", [1, 64], F32))
            s_rowP = _e(nc.sbuf_tensor("s_rowP", [1, 64], F32))
            s_Drow = _e(nc.sbuf_tensor("s_Drow", [1, 64], F32))
            s_Dmat = _e(nc.sbuf_tensor("s_Dmat", [128, 64], F32))
            s_ybuf = _e(nc.sbuf_tensor("s_ybuf", [128, D_INNER], BF16))
            s_gbuf = _e(nc.sbuf_tensor("s_gbuf", [128, 1024], F32))
            s_ynb = _e(nc.sbuf_tensor("s_ynb", [128, D_INNER], BF16))
            psC_a = _e(nc.psum_tensor("psC_a", [128, 512], F32))
            psC_b = _e(nc.psum_tensor("psC_b", [128, 512], F32))
            psC_t0 = _e(nc.psum_tensor("psC_t0", [128, 1024], BF16))
            psC_t1 = _e(nc.psum_tensor("psC_t1", [128, 1024], BF16))
            psC_g0 = _e(nc.psum_tensor("psC_g0", [128, 512], F32))
            psC_g1 = _e(nc.psum_tensor("psC_g1", [128, 512], F32))
            psC_y0 = _e(nc.psum_tensor("psC_y0", [128, 512], F32))
            psC_y1 = _e(nc.psum_tensor("psC_y1", [128, 512], F32))

            Cb = {}
            for nm in ("drep normw bbev bbod cbev cbod r32f rnev rnod bneg xsj flat "
                       "sin cos phr ph2 kbuf rbB rbC rmsBT rmsCT i1 i2 t1 t2 Bh Ch "
                       "smB smC e1 W WG tyi xsd A8 Adec rowL rowP Drow Dmat ybuf "
                       "gbuf ynb cc").split():
                Cb[nm] = B("c_" + nm)
            pCa = B("psC_a"); pCb = B("psC_b")
            pCt = [B("psC_t0"), B("psC_t1")]
            pCg = [B("psC_g0"), B("psC_g1")]
            pCy = [B("psC_y0"), B("psC_y1")]
            stb = [B(f"c_state{i}") for i in range(8)]
            Ub = [B(f"c_U{i}") for i in range(4)]
            Bdb = [B(f"c_Bd{i}") for i in range(4)]
            Wb = [B(f"c_W{i}") for i in range(4)]
            WGb = [B(f"c_WG{i}") for i in range(4)]
            e1b = [B(f"c_e1{i}") for i in range(4)]

            dma(s_bneg[:], bnegp, writes=[Cb["bneg"]])
            dma(s_r32f[:], r32f, writes=[Cb["r32f"]])
            dma(s_rnev[:], rn_ev, writes=[Cb["rnev"]])
            dma(s_rnod[:], rn_od, writes=[Cb["rnod"]])
            crb = Cb["flat"]
            def bcast(dst, src_row, width, wbuf):
                for g0 in range(0, width, 512):
                    w = min(512, width - g0)
                    ps = psC_a if (g0 // 512) % 2 == 0 else psC_b
                    pbx = pCa if (g0 // 512) % 2 == 0 else pCb
                    dma(s_flat[0:1, 0:w], src_row[0:1, g0:g0 + w],
                        reads=[crb], writes=[crb])
                    mm(ps[:, 0:w], s_ones[:], s_flat[0:1, 0:w], True, True,
                       reads=[crb, cb], writes=[pbx])
                    p.op("dve", lambda e, d=dst, g=g0, ww=w, pp=ps: e.tensor_copy(
                        d[:, g:g + ww], pp[:, 0:ww]), reads=[pbx], writes=[wbuf])
            bcast(s_drep, d_rep_r, 4096, Cb["drep"])
            bcast(s_normw, normw_r, 4096, Cb["normw"])
            bcast(s_bbev, bb_ev, 2048, Cb["bbev"])
            bcast(s_bbod, bb_od, 2048, Cb["bbod"])
            bcast(s_cbev, cb_ev, 2048, Cb["cbev"])
            bcast(s_cbod, cb_od, 2048, Cb["cbod"])
            p.op("dve", lambda e: e.memset(s_state[:], 0.0), writes=stb)
            p.op("dve", lambda e: e.memset(s_rowP[:], 0.0), writes=[Cb["rowP"]])

            def view32(t, blk):
                return t[:].rearrange("p (h g) -> p h g", g=64)[:, :, blk * 32:(blk + 1) * 32]

            for j in range(NTH):
                halo = (j == 0)
                p.serial_now = ("C" in ser_phases) or ("Cpro" in ser_phases)
                dma(s_rowL[:], s_slamT[:, j * 128 + 127: j * 128 + 128],
                    reads=[slamTb, Cb["rowL"]], writes=[Cb["rowL"]])
                if not halo:
                    mm(psC_a[:, 0:64], s_ones[:], s_rowP[0:1, 0:64], True, True,
                       reads=[Cb["rowP"], cb], writes=[pCa])
                    p.op("dve", lambda e, jj=j: e.tensor_sub(
                        s_A8[:], psC_a[:, 0:64], s_slam[:, jj * 64:(jj + 1) * 64]),
                        reads=[slamb, pCa], writes=[Cb["A8"]])
                    p.op("act", lambda e: e.activation(s_A8[:], s_A8[:], AF.Exp),
                         reads=[Cb["A8"]], writes=[Cb["A8"]])
                    p.op("dve", lambda e: e.tensor_scalar(s_A8[:], s_A8[:], 0.125, None, ALU.mult),
                         reads=[Cb["A8"]], writes=[Cb["A8"]])
                p.op("dve", lambda e: e.tensor_sub(s_Drow[:], s_rowP[:], s_rowL[:]),
                     reads=[Cb["rowP"], Cb["rowL"]], writes=[Cb["Drow"]])
                p.op("act", lambda e: e.activation(s_Drow[:], s_Drow[:], AF.Exp),
                     reads=[Cb["Drow"]], writes=[Cb["Drow"]])
                p.op("dve", lambda e: e.tensor_copy(s_rowP[:], s_rowL[:]),
                     reads=[Cb["rowL"]], writes=[Cb["rowP"]])
                mm(psC_a[:, 0:64], s_ones[:], s_Drow[0:1, 0:64], True, True,
                   reads=[Cb["Drow"], cb], writes=[pCa])
                p.op("dve", lambda e: e.tensor_copy(s_Dmat[:], psC_a[:, 0:64]),
                     reads=[pCa], writes=[Cb["Dmat"]])
                mm(psC_b[:, 0:64], s_ones[:], s_rowL[0:1, 0:64], True, True,
                   reads=[Cb["rowL"], cb], writes=[pCb])
                p.op("dve", lambda e, jj=j: e.tensor_sub(
                    s_Adec[:], s_slam[:, jj * 64:(jj + 1) * 64], psC_b[:, 0:64]),
                    reads=[slamb, pCb], writes=[Cb["Adec"]])
                p.op("act", lambda e: e.activation(s_Adec[:], s_Adec[:], AF.Exp),
                     reads=[Cb["Adec"]], writes=[Cb["Adec"]])
                for (src_off, rb_t, rb_b, rT_t, rT_b) in (
                        ((0, s_rbB, Cb["rbB"], s_rmsBT, Cb["rmsBT"]),
                         (64, s_rbC, Cb["rbC"], s_rmsCT, Cb["rmsCT"]))
                        if not halo else
                        ((0, s_rbB, Cb["rbB"], s_rmsBT, Cb["rmsBT"]),)):
                    off = j * 192 + src_off
                    p.op("act", lambda e, o=off: e.activation(
                        s_gbuf[:, 0:64], s_raw[:, o:o + 64], AF.Square,
                        accum_out=s_sall[:, 20:21]),
                        reads=[rawb], writes=[Cb["gbuf"], Cb["cc"]])
                    p.op("act", lambda e: e.activation(s_sall[:, 21:22], s_sall[:, 20:21], AF.Ln,
                                                       scale=1.0 / 64.0, bias=s_actc[:, 2:3]),
                         reads=[Cb["cc"], cb], writes=[Cb["cc"]])
                    p.op("act", lambda e: e.activation(s_sall[:, 21:22], s_sall[:, 21:22], AF.Exp, scale=-0.5),
                         reads=[Cb["cc"]], writes=[Cb["cc"]])
                    p.op("dve", lambda e, o=off, rt=rb_t: e.tensor_scalar(
                        rt[:], s_raw[:, o:o + 64], s_sall[:, 21:22], None, ALU.mult),
                        reads=[rawb, Cb["cc"]], writes=[rb_b])
                    tr(psC_t0[0:64, 0:128], rb_t[:], s_idb[:], reads=[rb_b, cb], writes=[pCt[0]])
                    p.op("dve", lambda e, rt=rT_t: e.tensor_copy(rt[:], psC_t0[0:64, 0:128]),
                         reads=[pCt[0]], writes=[rT_b])
                    if t_rb is not None and t_rbt is not None and rb_t is s_rbB and not halo:
                        dma(t_rb.rearrange("(n p) f -> n p f", p=128)[j - 1], rb_t[:], reads=[rb_b])
                        dma(t_rbt[:, (j - 1) * 128:j * 128], rT_t[:], reads=[rT_b])

                for h2 in range(2):
                    p.serial_now = ("C" in ser_phases) or ("Cbc" in ser_phases)
                    hb = h2 * 1024
                    dma(s_xsj[:], xs_dram[j][:, h2 * 2048:(h2 + 1) * 2048],
                        reads=[Cb["xsj"], xsdb], writes=[Cb["xsj"]])
                    tensors_bc = (
                        ((s_rmsBT, Cb["rmsBT"], s_rnev, s_rnod, s_bbev, Cb["bbev"],
                          s_bbod, Cb["bbod"], s_Bh, Cb["Bh"], s_smB, Cb["smB"], t_bh),
                         (s_rmsCT, Cb["rmsCT"], s_rnev, s_rnod, s_cbev, Cb["cbev"],
                          s_cbod, Cb["cbod"], s_Ch, Cb["Ch"], s_smC, Cb["smC"], t_ch))
                        if not halo else
                        ((s_rmsBT, Cb["rmsBT"], s_rnev, s_rnod, s_bbev, Cb["bbev"],
                          s_bbod, Cb["bbod"], s_Bh, Cb["Bh"], s_smB, Cb["smB"], None),))
                    for g in range(2):
                        ps, pbx = (psC_a, pCa) if g == 0 else (psC_b, pCb)
                        mm(ps[:], s_sdtT[:, j * 128:(j + 1) * 128],
                           s_r32f[:, hb + g * 512: hb + (g + 1) * 512], True, True,
                           reads=[sdtTb, Cb["r32f"]], writes=[pbx])
                        p.op("dve", lambda e, pp=ps: e.tensor_scalar(
                            s_k[:], pp[:], INV_2PI, MAGIC, ALU.mult, ALU.add),
                            reads=[pbx], writes=[Cb["kbuf"]])
                        p.op("dve", lambda e: e.tensor_scalar(s_k[:], s_k[:], MAGIC, None, ALU.subtract),
                             reads=[Cb["kbuf"]], writes=[Cb["kbuf"]])
                        p.op("dve", lambda e, pp=ps: e.scalar_tensor_tensor(
                            s_phr[:], s_k[:], -CW1, pp[:], ALU.mult, ALU.add),
                            reads=[pbx, Cb["kbuf"]], writes=[Cb["phr"]])
                        p.op("dve", lambda e: e.scalar_tensor_tensor(
                            s_ph2[:], s_k[:], -CW2, s_phr[:], ALU.mult, ALU.add),
                            reads=[Cb["phr"], Cb["kbuf"]], writes=[Cb["ph2"]])
                        p.op("dve", lambda e: e.scalar_tensor_tensor(
                            s_phr[:], s_k[:], -CW3, s_ph2[:], ALU.mult, ALU.add),
                            reads=[Cb["ph2"], Cb["kbuf"]], writes=[Cb["phr"]])
                        p.op("act", lambda e, gg=g: e.activation(
                            s_sin[:, gg * 512:(gg + 1) * 512], s_phr[:], AF.Sin),
                            reads=[Cb["phr"]], writes=[Cb["sin"]])
                        p.op("act", lambda e: e.activation(s_ph2[:], s_phr[:], AF.Abs),
                             reads=[Cb["phr"]], writes=[Cb["ph2"]])
                        p.op("act", lambda e, gg=g: e.activation(
                            s_cos[:, gg * 512:(gg + 1) * 512], s_ph2[:], AF.Sin,
                            scale=-1.0, bias=s_actc[:, 3:4]),
                            reads=[Cb["ph2"], cb], writes=[Cb["cos"]])
                    if t_sc is not None and not halo:
                        dma(t_sc.rearrange("(n p) f -> n p f", p=128)[j - 1, :, h2 * 1024:(h2 + 1) * 1024],
                            s_sin[:], reads=[Cb["sin"]])
                    if t_cc2 is not None and not halo:
                        dma(t_cc2.rearrange("(n p) f -> n p f", p=128)[j - 1, :, h2 * 1024:(h2 + 1) * 1024],
                            s_cos[:], reads=[Cb["cos"]])
                    for (rT_t, rT_b, pev, pod, bev, bevb, bod, bodb, out_t, out_b,
                         sm_t, sm_b, tp) in tensors_bc:
                        for (pat, pat_b, bias_t, bias_b, dst, dst_b) in (
                                (pev, Cb["rnev"], bev, bevb, s_i1, Cb["i1"]),
                                (pod, Cb["rnod"], bod, bodb, s_i2, Cb["i2"])):
                            for g in range(2):
                                ps, pbx = (psC_a, pCa) if g == 0 else (psC_b, pCb)
                                mm(ps[:], rT_t[:], pat[:, hb + g * 512: hb + (g + 1) * 512],
                                   True, True, reads=[rT_b, pat_b], writes=[pbx])
                                p.op("dve", lambda e, pp=ps, gg=g, bt=bias_t, dd=dst, hb_=hb:
                                     e.tensor_add(dd[:, gg * 512:(gg + 1) * 512], pp[:],
                                                  bt[:, hb_ + gg * 512: hb_ + (gg + 1) * 512]),
                                     reads=[pbx, bias_b], writes=[dst_b])
                        i1v = s_i1[:].rearrange("p (h f) -> p h f", f=32)
                        i2v = s_i2[:].rearrange("p (h f) -> p h f", f=32)
                        sv = s_sin[:].rearrange("p (h f) -> p h f", f=32)
                        cv = s_cos[:].rearrange("p (h f) -> p h f", f=32)
                        t1v = s_t1[:].rearrange("p (h f) -> p h f", f=32)
                        t2v = s_t2[:].rearrange("p (h f) -> p h f", f=32)
                        o1 = view32(out_t, 0)
                        o2 = view32(out_t, 1)
                        p.op("dve", lambda e, o=t1v, a=i1v, c=cv: e.tensor_mul(o, a, c),
                             reads=[Cb["i1"], Cb["cos"]], writes=[Cb["t1"]])
                        p.op("dve", lambda e, o=t2v, a=i2v, c=sv: e.tensor_mul(o, a, c),
                             reads=[Cb["i2"], Cb["sin"]], writes=[Cb["t2"]])
                        p.op("dve", lambda e, o=o1, a=t1v, c=t2v: e.tensor_sub(o, a, c),
                             reads=[Cb["t1"], Cb["t2"]], writes=[out_b])
                        p.op("dve", lambda e, o=t1v, a=i1v, c=sv: e.tensor_mul(o, a, c),
                             reads=[Cb["i1"], Cb["sin"]], writes=[Cb["t1"]])
                        p.op("dve", lambda e, o=t2v, a=i2v, c=cv: e.tensor_mul(o, a, c),
                             reads=[Cb["i2"], Cb["cos"]], writes=[Cb["t2"]])
                        p.op("dve", lambda e, o=o2, a=t1v, c=t2v: e.tensor_add(o, a, c),
                             reads=[Cb["t1"], Cb["t2"]], writes=[out_b])
                        if tp is not None and not halo:
                            dma(tp.rearrange("(n p) f -> n p f", p=128)[j - 1, :, h2 * 2048:(h2 + 1) * 2048],
                                out_t[:], reads=[out_b])
                        if t_i1 is not None and t_i2 is not None and out_t is s_Bh and not halo:
                            dma(t_i1.rearrange("(n p) f -> n p f", p=128)[j - 1, :, h2 * 1024:(h2 + 1) * 1024],
                                s_i1[:], reads=[Cb["i1"]])
                            dma(t_i2.rearrange("(n p) f -> n p f", p=128)[j - 1, :, h2 * 1024:(h2 + 1) * 1024],
                                s_i2[:], reads=[Cb["i2"]])
                        for half16 in (() if halo else range(2)):
                            pst, pstb = (psC_t0, pCt[0]) if half16 == 0 else (psC_t1, pCt[1])
                            for g2 in range(8):
                                gg2 = half16 * 8 + g2
                                tr(pst[:, g2 * 128:(g2 + 1) * 128],
                                   out_t[:, gg2 * 128:(gg2 + 1) * 128], s_idb[:],
                                   reads=[out_b, cb], writes=[pstb])
                            p.op("dve", lambda e, st=sm_t, hh=half16, pp=pst: e.tensor_copy(
                                st[:, hh * 1024:(hh + 1) * 1024], pp[:]),
                                reads=[pstb], writes=[sm_b])
                    p.serial_now = ("C" in ser_phases) or ("Cscan" in ser_phases)
                    for bb2 in range(16):
                        psm, psmb_ = (psC_a, pCa) if bb2 % 2 == 0 else (psC_b, pCb)
                        psg, psgb = (psC_g0, pCg[0]) if bb2 % 2 == 0 else (psC_g1, pCg[1])
                        hh0 = bb2 * 2
                        if hh0 % 4 == 0 and not halo:
                            h0g = h2 * 32 + hh0
                            dma(s_flat[:], s_slamT[h0g:h0g + 4, j * 128:(j + 1) * 128],
                                reads=[slamTb, Cb["flat"]], writes=[Cb["flat"]])
                        for q in range(2):
                            hh = hh0 + q
                            h = h2 * 32 + hh
                            hg = j * 64 + h
                            row0 = q * 64
                            g2c = bb2 * 128
                            bank = h // 8
                            psy, psyb = (psC_y0, pCy[0]) if bank % 2 == 0 else (psC_y1, pCy[1])
                            sl = (h % 8) * 64
                            scol = (h // 2) * 64
                            if halo:
                                p.op("pool", lambda e, qq=q, hl=hh, c=hg: e.tensor_scalar(
                                    s_U[:, qq * 64:(qq + 1) * 64],
                                    s_xsj[:, hl * 64:(hl + 1) * 64],
                                    s_dt[:, c:c + 1], None, ALU.mult),
                                    reads=[Cb["xsj"], dtb_], writes=[Ub[q]])
                                p.op("pool", lambda e, qq=q, hh_=hh, c=h: e.tensor_scalar(
                                    s_Bdec[:, qq * 64:(qq + 1) * 64],
                                    s_Bh[:, hh_ * 64:(hh_ + 1) * 64],
                                    s_Adec[:, c:c + 1], None, ALU.mult),
                                    reads=[Cb["Bh"], Cb["Adec"]], writes=[Bdb[q]])
                                mm(psm[0:64, 256 + q * 64: 256 + (q + 1) * 64],
                                   s_Bdec[:, q * 64:(q + 1) * 64],
                                   s_U[:, q * 64:(q + 1) * 64], True, True,
                                   reads=[Bdb[q], Ub[q]], writes=[psmb_])
                                p.op("pool", lambda e, r0=row0, sc=scol, hh_=h: e.tensor_scalar(
                                    s_state[r0:r0 + 64, sc:sc + 64],
                                    s_state[r0:r0 + 64, sc:sc + 64],
                                    s_Dmat[r0:r0 + 64, hh_:hh_ + 1], None, ALU.mult),
                                    reads=[stb[h // 8], Cb["Dmat"]], writes=[stb[h // 8]])
                                p.op("dve", lambda e, r0=row0, sc=scol, qq=q, pp=psm: e.tensor_add(
                                    s_state[r0:r0 + 64, sc:sc + 64],
                                    s_state[r0:r0 + 64, sc:sc + 64],
                                    pp[0:64, 256 + qq * 64: 256 + (qq + 1) * 64]),
                                    reads=[stb[h // 8], psmb_], writes=[stb[h // 8]])
                                continue
                            mm(psm[:, q * 128:(q + 1) * 128], s_ones[:],
                               s_flat[0:1, (hh % 4) * 128:(hh % 4 + 1) * 128], True, True,
                               reads=[Cb["flat"], cb], writes=[psmb_])
                            p.op("dve", lambda e, qq=q, pp=psm: e.scalar_tensor_tensor(
                                s_e1[:, qq * 128:(qq + 1) * 128], pp[:, qq * 128:(qq + 1) * 128],
                                -1.0, s_bneg[:], ALU.mult, ALU.add),
                                reads=[psmb_, Cb["bneg"]], writes=[e1b[q]])
                            p.op("act", lambda e, qq=q, c=hg: e.activation(
                                s_W[:, qq * 128:(qq + 1) * 128], s_e1[:, qq * 128:(qq + 1) * 128],
                                AF.Exp, bias=s_slam[:, c:c + 1]),
                                reads=[e1b[q], slamb], writes=[Wb[q]])
                            mm(psg[:, q * 128:(q + 1) * 128],
                               s_smB[row0:row0 + 64, g2c:g2c + 128],
                               s_smC[row0:row0 + 64, g2c:g2c + 128], True, True,
                               reads=[Cb["smB"], Cb["smC"]], writes=[psgb])
                            p.op("dve", lambda e, qq=q, pp=psg: e.tensor_mul(
                                s_WG[:, qq * 128:(qq + 1) * 128],
                                s_W[:, qq * 128:(qq + 1) * 128],
                                pp[:, qq * 128:(qq + 1) * 128]),
                                reads=[Wb[q], psgb], writes=[WGb[q]])
                            p.op("pool", lambda e, qq=q, hl=hh, c=hg: e.tensor_scalar(
                                s_U[:, qq * 64:(qq + 1) * 64],
                                s_xsj[:, hl * 64:(hl + 1) * 64],
                                s_dt[:, c:c + 1], None, ALU.mult),
                                reads=[Cb["xsj"], dtb_], writes=[Ub[q]])
                            p.op("pool", lambda e, qq=q, hh_=hh, c=h: e.tensor_scalar(
                                s_Bdec[:, qq * 64:(qq + 1) * 64],
                                s_Bh[:, hh_ * 64:(hh_ + 1) * 64],
                                s_Adec[:, c:c + 1], None, ALU.mult),
                                reads=[Cb["Bh"], Cb["Adec"]], writes=[Bdb[q]])
                            mm(psy[:, sl:sl + 64], s_WG[:, q * 128:(q + 1) * 128],
                               s_U[:, q * 64:(q + 1) * 64], True, True,
                               reads=[WGb[q], Ub[q]], writes=[psyb])
                            mm(psg[:, 256 + q * 64: 256 + (q + 1) * 64],
                               s_smC[row0:row0 + 64, g2c:g2c + 128],
                               s_state[row0:row0 + 64, scol:scol + 64], True, True,
                               reads=[Cb["smC"], stb[h // 8]], writes=[psgb])
                            p.op("pool", lambda e, hh_=h, hl=hh: e.tensor_mul(
                                s_xsd[:, (hh_ % 8) * 64:((hh_ % 8) + 1) * 64],
                                s_xsj[:, hl * 64:(hl + 1) * 64],
                                s_drep[:, hh_ * 64:(hh_ + 1) * 64]),
                                reads=[Cb["xsj"], Cb["drep"]], writes=[Cb["xsd"]])
                            p.op("dve", lambda e, qq=q, hh_=h, pp=psg: e.scalar_tensor_tensor(
                                s_tyi[:, (hh_ % 8) * 64:((hh_ % 8) + 1) * 64],
                                pp[:, 256 + qq * 64: 256 + (qq + 1) * 64],
                                s_A8[:, hh_:hh_ + 1],
                                s_xsd[:, (hh_ % 8) * 64:((hh_ % 8) + 1) * 64],
                                ALU.mult, ALU.add),
                                reads=[psgb, Cb["A8"], Cb["xsd"]], writes=[Cb["tyi"]])
                            mm(psm[0:64, 256 + q * 64: 256 + (q + 1) * 64],
                               s_Bdec[:, q * 64:(q + 1) * 64],
                               s_U[:, q * 64:(q + 1) * 64], True, True,
                               reads=[Bdb[q], Ub[q]], writes=[psmb_])
                            p.op("pool", lambda e, r0=row0, sc=scol, hh_=h: e.tensor_scalar(
                                s_state[r0:r0 + 64, sc:sc + 64],
                                s_state[r0:r0 + 64, sc:sc + 64],
                                s_Dmat[r0:r0 + 64, hh_:hh_ + 1], None, ALU.mult),
                                reads=[stb[h // 8], Cb["Dmat"]], writes=[stb[h // 8]])
                            p.op("dve", lambda e, r0=row0, sc=scol, qq=q, pp=psm: e.tensor_add(
                                s_state[r0:r0 + 64, sc:sc + 64],
                                s_state[r0:r0 + 64, sc:sc + 64],
                                pp[0:64, 256 + qq * 64: 256 + (qq + 1) * 64]),
                                reads=[stb[h // 8], psmb_], writes=[stb[h // 8]])
                        if (hh0 + 2) % 8 == 0 and not halo:
                            bank = (h2 * 32 + hh0) // 8
                            psy, psyb = (psC_y0, pCy[0]) if bank % 2 == 0 else (psC_y1, pCy[1])
                            p.op("dve", lambda e, bk=bank, pp=psy: e.scalar_tensor_tensor(
                                s_ybuf[:, bk * 512:(bk + 1) * 512], pp[:], 0.125,
                                s_tyi[:], ALU.mult, ALU.add),
                                reads=[psyb, Cb["tyi"]], writes=[Cb["ybuf"]])
                p.serial_now = ("C" in ser_phases) or ("Cgate" in ser_phases)
                if halo:
                    continue
                if t_y is not None:
                    dma(t_y.rearrange("(n p) f -> n p f", p=128)[j - 1], s_ybuf[:],
                        reads=[Cb["ybuf"]])
                for gh in range(4):
                    p.op("dve", lambda e, jj=j - 1, g0=gh: e.tensor_mul(
                        s_gbuf[:], s_ybuf[:, g0 * 1024:(g0 + 1) * 1024],
                        s_pool[:, jj * 4096 + g0 * 1024: jj * 4096 + (g0 + 1) * 1024]),
                        reads=[Cb["ybuf"], poolb], writes=[Cb["gbuf"]])
                    p.op("act", lambda e, g0=gh: e.activation(
                        s_ynb[:, g0 * 1024:(g0 + 1) * 1024], s_gbuf[:], AF.Square,
                        accum_out=s_sall[:, 24 + g0:25 + g0]),
                        reads=[Cb["gbuf"]], writes=[Cb["ynb"], Cb["cc"]])
                p.op("dve", lambda e: e.tensor_add(s_sall[:, 24:25], s_sall[:, 24:25], s_sall[:, 25:26]),
                     reads=[Cb["cc"]], writes=[Cb["cc"]])
                p.op("dve", lambda e: e.tensor_add(s_sall[:, 26:27], s_sall[:, 26:27], s_sall[:, 27:28]),
                     reads=[Cb["cc"]], writes=[Cb["cc"]])
                p.op("dve", lambda e: e.tensor_add(s_sall[:, 22:23], s_sall[:, 24:25], s_sall[:, 26:27]),
                     reads=[Cb["cc"]], writes=[Cb["cc"]])
                p.op("act", lambda e: e.activation(s_sall[:, 23:24], s_sall[:, 22:23], AF.Ln,
                                                   scale=1.0 / float(D_INNER), bias=s_actc[:, 2:3]),
                     reads=[Cb["cc"], cb], writes=[Cb["cc"]])
                p.op("act", lambda e: e.activation(s_sall[:, 23:24], s_sall[:, 23:24], AF.Exp, scale=-0.5),
                     reads=[Cb["cc"]], writes=[Cb["cc"]])
                for gh in range(4):
                    p.op("dve", lambda e, jj=j - 1, g0=gh: e.tensor_mul(
                        s_gbuf[:], s_ybuf[:, g0 * 1024:(g0 + 1) * 1024],
                        s_pool[:, jj * 4096 + g0 * 1024: jj * 4096 + (g0 + 1) * 1024]),
                        reads=[Cb["ybuf"], poolb], writes=[Cb["gbuf"]])
                    p.op("dve", lambda e, g0=gh: e.scalar_tensor_tensor(
                        s_ynb[:, g0 * 1024:(g0 + 1) * 1024], s_gbuf[:], s_sall[:, 23:24],
                        s_normw[:, g0 * 1024:(g0 + 1) * 1024], ALU.mult, ALU.mult),
                        reads=[Cb["gbuf"], Cb["cc"], Cb["normw"]], writes=[Cb["ynb"]])
                if t_yn is not None:
                    for gh in range(4):
                        p.op("dve", lambda e, g0=gh: e.tensor_copy(
                            s_gbuf[:], s_ynb[:, g0 * 1024:(g0 + 1) * 1024]),
                            reads=[Cb["ynb"]], writes=[Cb["gbuf"]])
                        dma(t_yn.rearrange("(n p) f -> n p f", p=128)[j - 1, :, gh * 1024:(gh + 1) * 1024],
                            s_gbuf[:], reads=[Cb["gbuf"]])
                for q in range(4):
                    pst, pstb = (psC_t0, pCt[0]) if q % 2 == 0 else (psC_t1, pCt[1])
                    for c in range(8):
                        kc = q * 8 + c
                        tr(pst[:, c * 128:(c + 1) * 128], s_ynb[:, kc * 128:(kc + 1) * 128],
                           s_idb[:], reads=[Cb["ynb"], cb], writes=[pstb])
                    p.op("dve", lambda e, jj=j - 1, qq=q, pp=pst: e.tensor_copy(
                        s_pool[:, jj * 4096 + qq * 1024: jj * 4096 + (qq + 1) * 1024], pp[:]),
                        reads=[pstb], writes=[poolb])
    p.fence()

    p.serial_now = "E" in ser_phases
    with contextlib.ExitStack() as _esED:
        _e = _esED.enter_context
        s_attn = _e(nc.sbuf_tensor("s_attn", [128, NT * 2048], F32))
        attnb = B("attn")
        if n_phases >= 4:
            t_attn = tap("attn_out", [TOK, D_MODEL])
            with contextlib.ExitStack() as _esE:
                _e2 = _esE.enter_context
                s_web = [_e2(nc.sbuf_tensor(f"s_web{i}", [128, 512], F32)) for i in range(2)]
                s_webb = [_e2(nc.sbuf_tensor(f"s_webb{i}", [128, 512], BF16)) for i in range(2)]
                psE = _e2(nc.psum_tensor("psE", [128, 4096], F32))
                webb = [B("web0"), B("web1")]
                webbb = [B("webb0"), B("webb1")]
                pE = [B(f"psE{j}") for j in range(8)]
                w_out_t = w_out.rearrange("(kc p) o -> kc p o", p=128)
                for og in range(4):
                    for kc in range(32):
                        par = kc % 2
                        dma(s_web[par][:], w_out_t[kc, :, og * 512:(og + 1) * 512],
                            reads=[webb[par]], writes=[webb[par]])
                        p.op("pool", lambda e, pr=par: e.tensor_copy(s_webb[pr][:], s_web[pr][:]),
                             reads=[webb[par]], writes=[webbb[par]])
                        for j in range(NT):
                            mm(psE[:, j * 512:(j + 1) * 512],
                               s_pool[:, j * 4096 + kc * 128: j * 4096 + (kc + 1) * 128],
                               s_webb[par][:], kc == 0, kc == 31,
                               reads=[poolb, webbb[par]], writes=[pE[j]])
                    for j in range(NT):
                        p.op("dve", lambda e, jj=j, o=og: e.tensor_copy(
                            s_attn[:, jj * 2048 + o * 512: jj * 2048 + (o + 1) * 512],
                            psE[:, jj * 512:(jj + 1) * 512]),
                            reads=[pE[j]], writes=[attnb])
            if t_attn is not None:
                for j in range(NT):
                    dma(t_attn.rearrange("(n p) f -> n p f", p=128)[j],
                        s_attn[:, j * 2048:(j + 1) * 2048], reads=[attnb])
        p.fence()

        p.serial_now = "D1" in ser_phases
        if n_phases >= 5:
            t_gu = tap("gu", [TOK, D_FFN])
            t_gate = tap("gatev", [TOK, D_FFN])
            for kc in range(16):
                dma(s_pool[:, kc * 1024:(kc + 1) * 1024], xqT_dram[kc],
                    reads=[xqTdb, poolb], writes=[poolb])
            with contextlib.ExitStack() as _esD1:
                _e2 = _esD1.enter_context
                s_wfb = [_e2(nc.sbuf_tensor("s_wfb0", [128, 2048], F32))]
                s_wfq = _e2(nc.sbuf_tensor("s_wfq", [128, 2048], BF16))
                s_wqT = _e2(nc.sbuf_tensor("s_wqT", [128, 16 * 256], BF16))
                s_gg = _e2(nc.sbuf_tensor("s_gg", [128, 4096], F32))
                s_sxw = _e2(nc.sbuf_tensor("s_sxw", [128, 16], F32))
                s_gu2 = _e2(nc.sbuf_tensor("s_gu2", [128, 512], F32))
                s_gstg = _e2(nc.sbuf_tensor("s_gstg", [128, 512], F32))
                s_wdb = _e2(nc.sbuf_tensor("s_wdb", [128, 2048], F32))
                s_wdq = _e2(nc.sbuf_tensor("s_wdq", [128, 2048], BF16))
                s_wdstg = _e2(nc.sbuf_tensor("s_wdstg", [128, 1024], BF16))
                psD = _e2(nc.psum_tensor("psD", [128, 2048], F32))
                psDt = _e2(nc.psum_tensor("psDt", [128, 1024], BF16))
                psDt2 = _e2(nc.psum_tensor("psDt2", [128, 1024], BF16))
                wfbb = [B("wfb0")]
                wfqb = B("wfq")
                wqTb = B("wqT")
                ggb = B("gg")
                sxwb = B("sxw")
                gstgb = B("gstg")
                wdbb = B("wdb"); wdqb = B("wdq"); wdstgb = B("wdstg")
                pD = [B(f"psD{j}") for j in range(4)]
                pDt = [B("psDt"), B("psDt2")]

                gu2b = B("gu2")
                for j in range(NT):
                    p.op("dve", lambda e, jj=j: e.tensor_mul(
                        s_sxw[:, jj:jj + 1], s_sall[:, jj:jj + 1], s_ws[:, 4:5]),
                        reads=[sallb, wsb2], writes=[sxwb])
                    p.op("dve", lambda e, jj=j: e.tensor_mul(
                        s_sxw[:, 8 + jj:9 + jj], s_sall[:, jj:jj + 1], s_ws[:, 5:6]),
                        reads=[sallb, wsb2], writes=[sxwb])

                def wd_prepass():
                    wd_v3 = wd.rearrange("(n p) k -> n p k", p=128)
                    for blk in range(64):
                        r, quar = blk // 4, blk % 4
                        dma(s_wdb[:], wd_v3[r][:, quar * 2048:(quar + 1) * 2048],
                            reads=[wdbb], writes=[wdbb], q="pool")
                        p.op("dve", lambda e: e.tensor_scalar(
                            s_wdb[:], s_wdb[:], s_ws[:, 2:3], MAGIC, ALU.mult, ALU.add),
                            reads=[wdbb, wsb2], writes=[wdbb])
                        p.op("dve", lambda e: e.tensor_scalar(
                            s_wdb[:], s_wdb[:], MAGIC, 1.0, ALU.subtract, ALU.min),
                            reads=[wdbb], writes=[wdbb])
                        p.op("dve", lambda e: e.tensor_scalar(
                            s_wdq[:], s_wdb[:], -1.0, None, ALU.max),
                            reads=[wdbb], writes=[wdqb])
                        for q in range(2):
                            pst, pstb = (psDt, pDt[0]) if q % 2 == 0 else (psDt2, pDt[1])
                            for c in range(8):
                                kcl = q * 8 + c
                                tr(pst[:, c * 128:(c + 1) * 128],
                                   s_wdq[:, kcl * 128:(kcl + 1) * 128], s_idb[:],
                                   reads=[wdqb, cb], writes=[pstb])
                            p.op("dve", lambda e, pp=pst: e.tensor_copy(s_wdstg[:], pp[:]),
                                 reads=[pstb], writes=[wdstgb])
                            for c in range(8):
                                kcl = q * 8 + c
                                kc2 = quar * 16 + kcl
                                dma(wqdT_dram[kc2, :, r * 128:(r + 1) * 128],
                                    s_wdstg[:, c * 128:(c + 1) * 128],
                                    reads=[wdstgb], writes=[wqdb], q="pool")
                        yield

                wdgen = wd_prepass()

                for g in range(16):
                    for mi, (wm, wscol) in enumerate(((wg, 4), (wu, 5))):
                        wm_t = wm.rearrange("(n p) k -> n p k", p=128)
                        for ho in range(2):
                            for c in range(2):
                                par = 0
                                dma(s_wfb[par][:], wm_t[g * 4 + ho * 2 + c],
                                    reads=[wfbb[par]], writes=[wfbb[par]])
                                wcol = 0 if wscol == 4 else 1
                                p.op("dve", lambda e, pr=par, wc=wcol: e.tensor_scalar(
                                    s_wfb[pr][:], s_wfb[pr][:], s_ws[:, wc:wc + 1], MAGIC,
                                    ALU.mult, ALU.add), reads=[wfbb[par], wsb2], writes=[wfbb[par]])
                                p.op("dve", lambda e, pr=par: e.tensor_scalar(
                                    s_wfb[pr][:], s_wfb[pr][:], MAGIC, 1.0, ALU.subtract, ALU.min),
                                    reads=[wfbb[par]], writes=[wfbb[par]])
                                p.op("dve", lambda e, pr=par: e.tensor_scalar(
                                    s_wfq[:], s_wfb[pr][:], -1.0, None, ALU.max),
                                    reads=[wfbb[par]], writes=[wfqb])
                                for q in range(2):
                                    pst, pstb = (psDt, pDt[0]) if q % 2 == 0 else (psDt2, pDt[1])
                                    for cc2 in range(8):
                                        kcl = q * 8 + cc2
                                        tr(pst[:, cc2 * 128:(cc2 + 1) * 128],
                                           s_wfq[:, kcl * 128:(kcl + 1) * 128], s_idb[:],
                                           reads=[wfqb, cb], writes=[pstb])
                                    p.op("dve", lambda e, qq=q, ccx=c, pp=pst: e.tensor_copy(
                                        s_wqT[:].rearrange("p (kc o) -> p kc o", o=256)
                                        [:, qq * 8:(qq + 1) * 8, ccx * 128:(ccx + 1) * 128],
                                        pp[:].rearrange("p (kc t) -> p kc t", t=128)),
                                        reads=[pstb], writes=[wqTb])
                            for tg in range(2):
                                for kc in range(16):
                                    for ji in range(4):
                                        j = tg * 4 + ji
                                        mm(psD[:, ji * 512 + ho * 256: ji * 512 + ho * 256 + 256],
                                           s_pool[:, kc * 1024 + j * 128: kc * 1024 + j * 128 + 128],
                                           s_wqT[:, kc * 256:(kc + 1) * 256], kc == 0, kc == 15,
                                           reads=[poolb, wqTb], writes=[pD[ji]])
                                for ji in range(4):
                                    j = tg * 4 + ji
                                    psl = psD[:, ji * 512 + ho * 256: ji * 512 + ho * 256 + 256]
                                    c0 = g * 512 + ho * 256
                                    if mi == 0:
                                        p.op("act", lambda e, jj=j, ps=psl, h0=ho: e.activation(
                                            s_gg[:, jj * 512 + h0 * 256: jj * 512 + h0 * 256 + 256],
                                            ps, AF.Silu, scale=s_sxw[:, jj:jj + 1]),
                                            reads=[pD[ji], sxwb], writes=[ggb])
                                        if t_gate is not None:
                                            dma(t_gate.rearrange("(n p) f -> n p f", p=128)[j, :, c0:c0 + 256],
                                                s_gg[:, j * 512 + ho * 256: j * 512 + ho * 256 + 256],
                                                reads=[ggb])
                                    else:
                                        p.op("dve", lambda e, jj=j, ps=psl: e.tensor_scalar(
                                            s_gstg[:, 0:256], ps,
                                            s_sxw[:, 8 + jj:9 + jj], None, ALU.mult),
                                            reads=[pD[ji], sxwb], writes=[gstgb])
                                        p.op("dve", lambda e, jj=j, h0=ho: e.tensor_mul(
                                            s_gu2[:, 0:256],
                                            s_gg[:, jj * 512 + h0 * 256: jj * 512 + h0 * 256 + 256],
                                            s_gstg[:, 0:256]),
                                            reads=[ggb, gstgb], writes=[gu2b])
                                        p.op("dve", lambda e: e.reduce_max(
                                            s_rmax[:, 8:9], s_gu2[:, 0:256], axis=AX,
                                            apply_absolute_value=True),
                                            reads=[gu2b], writes=[rmaxb])
                                        p.op("dve", lambda e, jj=j: e.tensor_max(
                                            s_rmax[:, jj:jj + 1], s_rmax[:, jj:jj + 1],
                                            s_rmax[:, 8:9]),
                                            reads=[rmaxb], writes=[rmaxb])
                                        dma(gu_dram[j, :, c0:c0 + 256], s_gu2[:, 0:256],
                                            reads=[gu2b], writes=[gudb])
                                        if t_gu is not None:
                                            dma(t_gu.rearrange("(n p) f -> n p f", p=128)[j, :, c0:c0 + 256],
                                                s_gu2[:, 0:256], reads=[gu2b])
                    for _ in range(4):
                        next(wdgen, None)
                for _ in wdgen:
                    pass
        p.fence()

        p.serial_now = "D2" in ser_phases
        if n_phases >= 6:
            with contextlib.ExitStack() as _esD2:
                _e2 = _esD2.enter_context
                s_gub = _e2(nc.sbuf_tensor("s_gub", [128, 4096], F32))
                s_xq2 = _e2(nc.sbuf_tensor("s_xq2", [128, 4096], BF16))
                s_wdT = [_e2(nc.sbuf_tensor(f"s_wdT{i}", [128, 512], BF16)) for i in range(2)]
                s_dcol = _e2(nc.sbuf_tensor("s_dcol", [128, 8], F32))
                s_xrl = _e2(nc.sbuf_tensor("s_xrl", [128, 512], F32))
                s_otmp = _e2(nc.sbuf_tensor("s_otmp", [128, 512], F32))
                psD2 = _e2(nc.psum_tensor("psD2", [128, 2048], F32))
                psDq = _e2(nc.psum_tensor("psDq", [128, 1024], BF16))
                psDq2 = _e2(nc.psum_tensor("psDq2", [128, 1024], BF16))
                gubb = B("gub"); xq2b = B("xq2")
                wdTb = [B("wdT0"), B("wdT1")]
                dcolb = B("dcol"); xrlb = B("xrl"); otmpb = B("otmp")
                pD2 = [B(f"psD2{j}") for j in range(4)]
                pDq = [B("psDq"), B("psDq2")]

                for half in range(2):
                    js = [half * 4 + i for i in range(4)]
                    for j in js:
                        p.op("dve", lambda e, jj=j: e.tensor_scalar(
                            s_sall[:, 8 + jj: 9 + jj], s_rmax[:, jj:jj + 1],
                            1.0 / 127.0, EPS_F, ALU.mult, ALU.add),
                            reads=[rmaxb], writes=[sallb])
                        p.op("dve", lambda e, jj=j: e.reciprocal(
                            s_sall[:, 16:17], s_sall[:, 8 + jj: 9 + jj]),
                            reads=[sallb], writes=[sallb])
                        for gq in range(2):
                            dma(s_gub[:], gu_dram[j, :, gq * 4096:(gq + 1) * 4096],
                                reads=[gudb, gubb], writes=[gubb])
                            p.op("dve", lambda e: e.tensor_scalar(
                                s_gub[:], s_gub[:], s_sall[:, 16:17], MAGIC, ALU.mult, ALU.add),
                                reads=[gubb, sallb], writes=[gubb])
                            p.op("dve", lambda e: e.tensor_scalar(
                                s_xq2[:], s_gub[:], MAGIC, None, ALU.subtract),
                                reads=[gubb], writes=[xq2b])
                            for q in range(4):
                                pst, pstb = (psDq, pDq[0]) if q % 2 == 0 else (psDq2, pDq[1])
                                for c in range(8):
                                    kcl = q * 8 + c
                                    tr(pst[:, c * 128:(c + 1) * 128],
                                       s_xq2[:, kcl * 128:(kcl + 1) * 128], s_idb[:],
                                       reads=[xq2b, cb], writes=[pstb])
                                p.op("dve", lambda e, jj=j, gg=gq, qq=q, pp=pst: e.tensor_copy(
                                    s_pool[:, (jj % 4) * 8192 + gg * 4096 + qq * 1024:
                                            (jj % 4) * 8192 + gg * 4096 + (qq + 1) * 1024], pp[:]),
                                    reads=[pstb], writes=[poolb])
                    for og in range(4):
                        for kc2 in range(64):
                            par = kc2 % 2
                            dma(s_wdT[par][:], wqdT_dram[kc2, :, og * 512:(og + 1) * 512],
                                reads=[wdTb[par], wqdb], writes=[wdTb[par]])
                            for ji, j in enumerate(js):
                                mm(psD2[:, ji * 512:(ji + 1) * 512],
                                   s_pool[:, (j % 4) * 8192 + kc2 * 128: (j % 4) * 8192 + kc2 * 128 + 128],
                                   s_wdT[par][:], kc2 == 0, kc2 == 63,
                                   reads=[poolb, wdTb[par]], writes=[pD2[ji]])
                        for ji, j in enumerate(js):
                            p.op("dve", lambda e, jj=j: e.tensor_mul(
                                s_dcol[:, jj % 4:jj % 4 + 1], s_sall[:, 8 + jj: 9 + jj],
                                s_ws[:, 6:7]), reads=[sallb, wsb2], writes=[dcolb])
                            dma(s_xrl[:], x_t[j + 1][:, og * 512:(og + 1) * 512],
                                reads=[xrlb], writes=[xrlb])
                            p.op("dve", lambda e, jj=j, ji_=ji, o=og: e.scalar_tensor_tensor(
                                s_otmp[:], psD2[:, ji_ * 512:(ji_ + 1) * 512],
                                s_dcol[:, jj % 4:jj % 4 + 1],
                                s_attn[:, jj * 2048 + o * 512: jj * 2048 + (o + 1) * 512],
                                ALU.mult, ALU.add),
                                reads=[pD2[ji], dcolb, attnb], writes=[otmpb])
                            p.op("dve", lambda e: e.tensor_add(s_otmp[:], s_otmp[:], s_xrl[:]),
                                 reads=[otmpb, xrlb], writes=[otmpb])
                            dma(y_t[j][:, og * 512:(og + 1) * 512], s_otmp[:],
                                reads=[otmpb], writes=[outb])

    if n_phases < 6:
        with nc.sbuf_tensor("s_dummy", [128, 2048], F32) as s_dummy:
            p.op("dve", lambda e: e.memset(s_dummy[:], 0.0), writes=[B("dummy")])
            for j in range(NT):
                dma(y_t[j], s_dummy[:], reads=[B("dummy")], writes=[outb])

    p.truncate = truncate
    p.finalize()
    return nc, tap_outs



def _consts(inputs):
    rope_freqs = np.asarray(inputs["L0_attn_rope_freqs"], np.float32)
    B_bias = np.asarray(inputs["L0_attn_B_bias"], np.float32)
    C_bias = np.asarray(inputs["L0_attn_C_bias"], np.float32)
    D_param = np.asarray(inputs["L0_attn_D_param"], np.float32)
    norm_w = np.asarray(inputs["L0_attn_norm_w"], np.float32)
    dt_bias = np.asarray(inputs["L0_attn_dt_bias"], np.float32)
    A_log = np.asarray(inputs["L0_attn_A_log"], np.float32)

    r32f = np.zeros((64, 2048), np.float32)
    for h in range(64):
        r32f[h, h * 32:(h + 1) * 32] = rope_freqs[h]
    rn_ev = np.zeros((64, 2048), ml_dtypes.bfloat16)
    rn_od = np.zeros((64, 2048), ml_dtypes.bfloat16)
    for h in range(64):
        for f in range(32):
            rn_ev[2 * f, h * 32 + f] = 1
            rn_od[2 * f + 1, h * 32 + f] = 1
    bneg = np.where(np.arange(128)[:, None] <= np.arange(128)[None, :], 0.0,
                    BIGNEG).astype(np.float32)
    return {
        "dt_bias_r": dt_bias.reshape(1, 64),
        "a_log_c": A_log.reshape(64, 1),
        "d_rep_r": np.repeat(D_param, 64).reshape(1, 4096),
        "bb_ev": np.ascontiguousarray(B_bias[:, 0::2]).reshape(1, 2048),
        "bb_od": np.ascontiguousarray(B_bias[:, 1::2]).reshape(1, 2048),
        "cb_ev": np.ascontiguousarray(C_bias[:, 0::2]).reshape(1, 2048),
        "cb_od": np.ascontiguousarray(C_bias[:, 1::2]).reshape(1, 2048),
        "normw_r": norm_w.reshape(1, 4096),
        "r32f": r32f,
        "rn_ev": rn_ev,
        "rn_od": rn_od,
        "idf": np.eye(128, dtype=np.float32),
        "idb": np.eye(128, dtype=ml_dtypes.bfloat16),
        "onesf": np.ones((1, 128), np.float32),
        "actc": np.broadcast_to(np.array([[1e-5, 1.0, 1e-6, 0.5 * math.pi, 0, 0, 0, 0]],
                                         np.float32), (128, 8)).copy(),
        "bnegp": bneg,
    }


def make_in_maps(inputs):
    x = np.asarray(inputs["L0_attn_x"], np.float32).reshape(8192, 2048)
    w_in = np.ascontiguousarray(np.asarray(inputs["L0_attn_W_in"], np.float32))
    w_out = np.ascontiguousarray(np.asarray(inputs["L0_attn_W_out"], np.float32))
    wgm = np.ascontiguousarray(np.asarray(inputs["L0_ffn_w_gate"], np.float32))
    wum = np.ascontiguousarray(np.asarray(inputs["L0_ffn_w_up"], np.float32))
    wdm = np.ascontiguousarray(np.asarray(inputs["L0_ffn_w_down"], np.float32))
    consts = _consts(inputs)
    in_maps = []
    zrow = np.zeros((128, 2048), np.float32)
    for k in range(8):
        m = dict(consts)
        halo = zrow if k == 0 else x[k * 1024 - 128: k * 1024]
        m["x"] = np.ascontiguousarray(np.concatenate([halo, x[k * 1024:(k + 1) * 1024]], axis=0))
        m["w_in"] = w_in
        m["w_out"] = w_out
        m["wg"] = wgm
        m["wu"] = wum
        m["wd"] = wdm
        in_maps.append(m)
    return in_maps


def kernel(**inputs):
    nc, _ = build_program(serialize_all=True)
    in_maps = make_in_maps(inputs)
    res = run_bass_kernel_spmd(nc, in_maps, core_ids=list(range(8)))
    out = np.concatenate([np.asarray(r["y"]) for r in res.results], axis=0)
    return out.reshape(1, 8192, 2048).astype(np.float32)


# version 05
# speedup vs baseline: 1.6363x; 1.6363x over previous
import contextlib
import math
import numpy as np
import ml_dtypes
import concourse.bass as bass
import concourse.mybir as mybir
from concourse.bass_utils import run_bass_kernel_spmd

F32 = mybir.dt.float32
BF16 = mybir.dt.bfloat16
AF = mybir.ActivationFunctionType
ALU = mybir.AluOpType
AX = mybir.AxisListType.X

MAGIC = float(np.float32(1.5 * 2 ** 23))
PI = float(np.float32(math.pi))
TWO_PI = 6.283185307179586
INV_2PI = float(np.float32(1.0 / TWO_PI))
CW1 = float(np.float32(6.28125))
_r1 = TWO_PI - 6.28125
CW2 = float(np.float32(_r1))
CW3 = float(np.float32(_r1 - float(np.float32(_r1))))
BIGNEG = -1e30

D_MODEL = 2048
N_HEADS = 64
D_INNER = 4096
D_IN_PROJ = 8384
D_FFN = 8192
EPS_A = 1e-6
EPS_F = 1e-5
EPS_LN = 1e-5

TOK = 1024
NT = 8
NTH = 9
TOKH = NTH * 128

NPH = 99


class Prog:

    def __init__(self, nc):
        self.nc = nc
        self.engines = {
            "pe": nc.tensor, "act": nc.scalar, "dve": nc.vector,
            "pool": nc.gpsimd, "sp": nc.sync,
        }
        self.ops = []
        self.deps = []
        self.dma_key = []
        self.cur_fence = set()
        self.last_on_eng = {}

    def fence(self):
        self.cur_fence = set(self.last_on_eng.values())

    class Buf:
        __slots__ = ("name", "writer", "readers", "psum")
        def __init__(self, name, psum=False):
            self.name = name
            self.writer = None
            self.readers = set()
            self.psum = psum

    def buf(self, name, psum=False):
        return Prog.Buf(name, psum)

    serialize_all = False
    serial_now = False

    def op(self, eng, emit_fn, reads=(), writes=(), dma=False):
        idx = len(self.ops)
        dep = set()
        if (self.serialize_all or self.serial_now) and idx > 0:
            dep.add(idx - 1)
        for bb in reads:
            if bb.writer is not None:
                dep.add(bb.writer)
            if bb.psum:
                for r in bb.readers:
                    if self.ops[r][0] != eng:
                        dep.add(r)
        for bb in writes:
            if bb.writer is not None:
                dep.add(bb.writer)
            dep.update(bb.readers)
        dep.update(self.cur_fence)
        dep.discard(idx)
        self.ops.append((eng, emit_fn, 16 if dma else 1))
        self.dma_key.append((writes[0].name if writes else "tapout") if dma else None)
        self.deps.append(dep)
        self.last_on_eng[eng] = idx
        for bb in reads:
            bb.readers.add(idx)
        for bb in writes:
            bb.writer = idx
            bb.readers = set()
        return idx

    truncate = None

    def finalize(self):
        nc = self.nc
        n = len(self.ops)
        if self.truncate is not None:
            n = min(n, self.truncate)
        op_eng = [o[0] for o in self.ops]
        needs_inc = [o[2] == 16 for o in self.ops]
        waits = [[] for _ in range(n)]
        import os
        pe_pe = os.environ.get("K_PEPE", "0") == "1"
        for i in range(n):
            e = op_eng[i]
            for j in self.deps[i]:
                je = op_eng[j]
                if je == e and e == "pe" and not pe_pe:
                    continue
                needs_inc[j] = True
                waits[i].append(j)
        op_key = [None] * n
        for j in range(n):
            if self.ops[j][2] == 16:
                op_key[j] = ("d", self.dma_key[j] or f"anon{j}")
            else:
                op_key[j] = ("e", op_eng[j])
        sem_val = [0] * n
        cnt = {}
        for j in range(n):
            k = op_key[j]
            if needs_inc[j]:
                cnt[k] = cnt.get(k, 0) + self.ops[j][2]
            sem_val[j] = cnt.get(k, 0)
        self.final_counts = dict(cnt)
        sems = {}
        for j in range(n):
            k = op_key[j]
            if k not in sems:
                sems[k] = nc.semaphore(f"ps_{k[0]}_{k[1]}"[:28]).__enter__()
        seen = {}
        for i in range(n):
            e, emit_fn, inc = self.ops[i]
            byk = {}
            for j in waits[i]:
                k = op_key[j]
                v = sem_val[j]
                if v > byk.get(k, 0):
                    byk[k] = v
            for k in sorted(byk):
                v = byk[k]
                if seen.get((e, k), 0) >= v:
                    continue
                self.engines[e].wait_ge(sems[k], v)
                seen[(e, k)] = v
            inst = emit_fn(self.engines[e])
            if needs_inc[i]:
                inst.then_inc(sems[op_key[i]], inc)
        for k, v in cnt.items():
            if v > 0:
                self.engines["sp" if k[0] == "d" else k[1]].wait_ge(sems[k], v)


def build_program(n_phases=NPH, taps=(), truncate=None, serialize_all=False,
                  ser_phases=()):
    taps = set(taps)
    ser_phases = set(ser_phases)
    nc = bass.Bass("TRN2", target_bir_lowering=False)
    p = Prog(nc)
    p.serialize_all = serialize_all

    def din(name, shape, dtype=F32):
        return nc.dram_tensor(name, shape, dtype, kind="ExternalInput").ap()

    x_hbm = din("x", [TOKH, D_MODEL])
    w_in = din("w_in", [D_MODEL, D_IN_PROJ])
    w_out = din("w_out", [D_INNER, D_MODEL])
    wg = din("wg", [D_FFN, D_MODEL])
    wu = din("wu", [D_FFN, D_MODEL])
    wd = din("wd", [D_MODEL, D_FFN])
    dt_bias_r = din("dt_bias_r", [1, N_HEADS])
    a_log_c = din("a_log_c", [N_HEADS, 1])
    d_rep_r = din("d_rep_r", [1, D_INNER])
    bb_ev = din("bb_ev", [1, 2048])
    bb_od = din("bb_od", [1, 2048])
    cb_ev = din("cb_ev", [1, 2048])
    cb_od = din("cb_od", [1, 2048])
    normw_r = din("normw_r", [1, D_INNER])
    r32f = din("r32f", [N_HEADS, 2048])
    rn_ev = din("rn_ev", [N_HEADS, 2048], BF16)
    rn_od = din("rn_od", [N_HEADS, 2048], BF16)
    idf = din("idf", [128, 128])
    idb = din("idb", [128, 128], BF16)
    onesf = din("onesf", [1, 128])
    actc = din("actc", [128, 8])
    bnegp = din("bnegp", [128, 128])

    y_hbm = nc.dram_tensor("y", [TOK, D_MODEL], F32, kind="ExternalOutput").ap()

    xqT_dram = nc.dram_tensor("xqT_scr", [16, 128, TOK], BF16, kind="Internal").ap()
    xs_dram = nc.dram_tensor("xs_scr", [NTH, 128, D_INNER], BF16, kind="Internal").ap()
    gu_dram = nc.dram_tensor("gu_scr", [NT, 128, D_FFN], F32, kind="Internal").ap()
    wqdT_dram = nc.dram_tensor("wqdT_scr", [64, 128, D_MODEL], BF16, kind="Internal").ap()

    tap_outs = {}
    def tap(name, shape, dtype=F32):
        if name in taps:
            tap_outs[name] = nc.dram_tensor("tap_" + name, shape, dtype,
                                            kind="ExternalOutput").ap()
            return tap_outs[name]
        return None

    A = nc.alloc_sbuf_tensor

    s_pool = A("s_pool", [128, NT * 4096], BF16)
    s_raw = A("s_raw", [128, NTH * 192], F32)
    s_dt = A("s_dt", [128, NTH * 64], F32)
    s_slam = A("s_slam", [128, NTH * 64], F32)
    s_sdtT = A("s_sdtT", [64, NTH * 128], F32)
    s_slamT = A("s_slamT", [64, NTH * 128], F32)
    s_sall = A("s_sall", [128, 28], F32)
    s_rmax = A("s_rmax", [128, 12], F32)
    s_carry = A("s_carry", [64, 4], F32)
    s_idf = A("s_idf", [128, 128], F32)
    s_idb = A("s_idb", [128, 128], BF16)
    s_ones = A("s_ones", [1, 128], F32)
    s_onesc = A("s_onesc", [128, 1], F32)
    s_lam = A("s_lam", [64, 1], F32)
    s_ws = A("s_ws", [128, 8], F32)
    s_actc = A("s_actc", [128, 8], F32)

    b = {}
    def B(name):
        if name not in b:
            b[name] = p.buf(name, psum=name.startswith("ps"))
        return b[name]

    def dma(dst, src, reads=(), writes=(), q="sp"):
        return p.op(q, lambda e, d=dst, s=src: e.dma_start(out=d, in_=s),
                    reads=reads, writes=writes, dma=True)

    def mm(ps, lhsT, rhs, start, stop, reads=(), writes=()):
        return p.op("pe", lambda e, o=ps, l=lhsT, r=rhs, st=start, sp=stop:
                    e.matmul(o, l, r, start=st, stop=sp), reads=reads, writes=writes)

    def tr(ps, in_, ident, reads=(), writes=()):
        return p.op("pe", lambda e, o=ps, i=in_, d=ident: e.transpose(o, i, d),
                    reads=reads, writes=writes)

    cb = B("consts")
    lamb = B("lam")
    carb = B("carry")
    sallb = B("sall")
    rmaxb = B("rmax")
    rawb = B("raw")
    dtb_ = B("dt")
    slamb = B("slam")
    sdtTb = B("sdtT")
    slamTb = B("slamT")
    wsb2 = B("ws")
    xqTdb = B("xqT_dram")
    xsdb = B("xs_dram")
    gudb = B("gu_dram")
    wqdb = B("wqd_dram")
    poolb = B("pool")
    outb = B("out")

    dma(s_idf[:], idf, writes=[cb])
    dma(s_idb[:], idb, writes=[cb])
    dma(s_ones[:], onesf, writes=[cb])
    dma(s_actc[:], actc, writes=[cb])
    p.op("dve", lambda e: e.memset(s_onesc[:], 1.0), writes=[cb])
    p.op("dve", lambda e: e.memset(s_carry[:], 0.0), writes=[carb])
    p.op("dve", lambda e: e.memset(s_rmax[:], 0.0), writes=[rmaxb])

    with nc.sbuf_tensor("s_alog", [64, 1], F32) as s_alog:
        dma(s_alog[:], a_log_c, writes=[lamb])
        p.op("act", lambda e: e.activation(s_lam[:], s_alog[:], AF.Exp),
             reads=[lamb], writes=[lamb])
    p.fence()

    x_t = x_hbm.rearrange("(n p) f -> n p f", p=128)
    y_t = y_hbm.rearrange("(n p) f -> n p f", p=128)

    p.serial_now = "AB" in ser_phases
    with contextlib.ExitStack() as _esAB:
        _e = _esAB.enter_context
        s_xnT = _e(nc.sbuf_tensor("s_xnT", [128, NTH * 2048], BF16))
        s_wsb = [_e(nc.sbuf_tensor(f"s_wsb{i}", [128, 2048], F32)) for i in range(2)]
        s_wsjunk = _e(nc.sbuf_tensor("s_wsjunk", [128, 2048], F32))
        s_psums = _e(nc.sbuf_tensor("s_psums", [128, 4], F32))
        s_scal = _e(nc.sbuf_tensor("s_scal", [1, 8], F32))
        xnTb = B("xnT")
        wsbb = [B("wsb0"), B("wsb1")]
        wsjb = B("wsjunk")
        psmb = B("psums")
        scalb = B("scal")

        p.op("dve", lambda e: e.memset(s_psums[:], 0.0), writes=[psmb])

        def scale_gen():
            wg_v = wg.rearrange("(n p) k -> n p k", p=128)
            wu_v = wu.rearrange("(n p) k -> n p k", p=128)
            wd_v = wd.rearrange("(n p) k -> n p k", p=128)
            views = [
                (0, lambda i: wg_v[i], 64),
                (1, lambda i: wu_v[i], 64),
                (2, lambda i: wd_v[i // 4][:, (i % 4) * 2048:(i % 4 + 1) * 2048], 64),
            ]
            for col, view, ntl in views:
                for i in range(ntl):
                    par = i % 2
                    dma(s_wsb[par][:], view(i), reads=[wsbb[par]], writes=[wsbb[par]],
                        q="pool")
                    p.op("act", lambda e, pr=par, c=col: e.activation(
                        s_wsjunk[:], s_wsb[pr][:], AF.Abs,
                        accum_out=s_psums[:, 3:4]),
                        reads=[wsbb[par]], writes=[wsjb, psmb])
                    p.op("dve", lambda e, c=col: e.tensor_add(
                        s_psums[:, c:c + 1], s_psums[:, c:c + 1], s_psums[:, 3:4]),
                        reads=[psmb], writes=[psmb])
                    yield

        def scale_fin():
            with (nc.psum_tensor("ps_sc", [1, 8], F32) as ps_sc,
                  nc.psum_tensor("ps_ws", [128, 8], F32) as ps_ws):
                psc = B("ps_sc"); pws = B("ps_ws")
                mm(ps_sc[0:1, 0:3], s_onesc[:], s_psums[:, 0:3], True, True,
                   reads=[psmb, cb], writes=[psc])
                p.op("dve", lambda e: e.tensor_copy(s_scal[0:1, 0:3], ps_sc[0:1, 0:3]),
                     reads=[psc], writes=[scalb])
                nume = float(D_FFN * D_MODEL)
                p.op("dve", lambda e: e.tensor_scalar(
                    s_scal[0:1, 4:7], s_scal[0:1, 0:3], 1.0 / nume, EPS_F,
                    ALU.mult, ALU.add), reads=[scalb], writes=[scalb])
                p.op("dve", lambda e: e.reciprocal(s_scal[0:1, 0:3], s_scal[0:1, 4:7]),
                     reads=[scalb], writes=[scalb])
                mm(ps_ws[:, 0:8], s_ones[:], s_scal[0:1, 0:8], True, True,
                   reads=[scalb, cb], writes=[pws])
                p.op("dve", lambda e: e.tensor_copy(s_ws[:], ps_ws[:, 0:8]),
                     reads=[pws], writes=[wsb2])
            t_ws = tap("wscal", [128, 8])
            if t_ws is not None:
                dma(t_ws, s_ws[:], reads=[wsb2])

        sg = scale_gen()

        def pump(k):
            for _ in range(k):
                if next(sg, "END") == "END":
                    return

        t_xn = tap("xn", [TOK, D_MODEL])
        t_xq = tap("xq", [TOK, D_MODEL])
        t_raw = tap("raw", [TOK, 192])
        t_dt = tap("dt", [TOK, 64])
        t_sdt = tap("sdt", [64, TOK])
        t_slam = tap("slam", [64, TOK])

        with contextlib.ExitStack() as _esA:
            _e = _esA.enter_context
            s_wraw = _e(nc.sbuf_tensor("s_wraw", [128, 16 * 192], F32))
            s_dtb_bc = _e(nc.sbuf_tensor("s_dtb_bc", [128, 64], F32))
            s_xt = _e(nc.sbuf_tensor("s_xt", [128, 2048], F32))
            s_xn = _e(nc.sbuf_tensor("s_xn", [128, 2048], F32))
            s_xnTf = _e(nc.sbuf_tensor("s_xnTf", [128, 2048], F32))
            s_xq = _e(nc.sbuf_tensor("s_xq", [128, 2048], BF16))
            s_sq = _e(nc.sbuf_tensor("s_sq", [128, 2048], F32))
            s_c1 = _e(nc.sbuf_tensor("s_c1", [128, 8], F32))
            s_dtT = _e(nc.sbuf_tensor("s_dtT", [64, 128], F32))
            s_ldtT = _e(nc.sbuf_tensor("s_ldtT", [64, 128], F32))
            s_stg = _e(nc.sbuf_tensor("s_stg", [128, 512], BF16))
            psA0 = _e(nc.psum_tensor("psA0", [128, 512], F32))
            psA1 = _e(nc.psum_tensor("psA1", [128, 512], F32))
            psA2 = _e(nc.psum_tensor("psA2", [128, 512], BF16))
            psA3 = _e(nc.psum_tensor("psA3", [128, 192], F32))
            psA4 = _e(nc.psum_tensor("psA4", [64, 128], F32))
            psA5 = _e(nc.psum_tensor("psA5", [128, 64], F32))
            wrb = B("wraw"); dtbb = B("dtb_bc")
            xtb = B("xt"); xnb = B("xn"); xnTfb = B("xnTf"); xqb = B("xq")
            sqb = B("sq"); c1b = B("c1"); dtTb = B("dtT"); ldtTb = B("ldtT")
            stgb = B("stg")
            pA0 = B("psA0"); pA1 = B("psA1"); pA2 = B("psA2")
            pA3 = B("psA3"); pA4 = B("psA4"); pA5 = B("psA5")

            dma(s_wraw[:].rearrange("p (kc o) -> p kc o", kc=16),
                w_in[:, 8192:8384].rearrange("(kc p) o -> p kc o", p=128),
                writes=[wrb])
            with nc.sbuf_tensor("s_row64", [1, 64], F32) as s_row64:
                r64b = B("row64")
                dma(s_row64[:], dt_bias_r, writes=[r64b])
                mm(psA5[:], s_ones[:], s_row64[0:1, 0:64], True, True,
                   reads=[r64b, cb], writes=[pA5])
                p.op("dve", lambda e: e.tensor_copy(s_dtb_bc[:], psA5[:]),
                     reads=[pA5], writes=[dtbb])

            for j in range(NTH):
                jr = j - 1
                dma(s_xt[:], x_t[j], reads=[xtb], writes=[xtb])
                p.op("dve", lambda e: e.reduce_sum(s_c1[:, 0:1], s_xt[:], axis=AX),
                     reads=[xtb], writes=[c1b])
                p.op("act", lambda e: e.activation(s_sq[:], s_xt[:], AF.Square,
                                                   accum_out=s_c1[:, 1:2]),
                     reads=[xtb], writes=[sqb, c1b])
                p.op("dve", lambda e: e.tensor_scalar(s_c1[:, 2:3], s_c1[:, 0:1], -1.0 / 2048.0, None, ALU.mult),
                     reads=[c1b], writes=[c1b])
                p.op("dve", lambda e: e.tensor_mul(s_c1[:, 3:4], s_c1[:, 2:3], s_c1[:, 2:3]),
                     reads=[c1b], writes=[c1b])
                p.op("dve", lambda e: e.tensor_scalar(s_c1[:, 4:5], s_c1[:, 1:2], 1.0 / 2048.0, None, ALU.mult),
                     reads=[c1b], writes=[c1b])
                p.op("dve", lambda e: e.tensor_sub(s_c1[:, 4:5], s_c1[:, 4:5], s_c1[:, 3:4]),
                     reads=[c1b], writes=[c1b])
                p.op("act", lambda e: e.activation(s_c1[:, 5:6], s_c1[:, 4:5], AF.Ln, bias=s_actc[:, 0:1]),
                     reads=[c1b, cb], writes=[c1b])
                p.op("act", lambda e: e.activation(s_c1[:, 5:6], s_c1[:, 5:6], AF.Exp, scale=-0.5),
                     reads=[c1b], writes=[c1b])
                p.op("dve", lambda e: e.tensor_scalar(s_xn[:], s_xt[:], s_c1[:, 2:3], s_c1[:, 5:6], ALU.add, ALU.mult),
                     reads=[xtb, c1b], writes=[xnb])
                if t_xn is not None and j > 0:
                    dma(t_xn.rearrange("(n p) f -> n p f", p=128)[jr], s_xn[:], reads=[xnb])
                if j > 0:
                    p.op("dve", lambda e: e.reduce_max(s_c1[:, 6:7], s_xn[:], axis=AX, apply_absolute_value=True),
                         reads=[xnb], writes=[c1b])
                    p.op("dve", lambda e: e.tensor_scalar(s_c1[:, 6:7], s_c1[:, 6:7], 1.0 / 127.0, EPS_F, ALU.mult, ALU.add),
                         reads=[c1b], writes=[c1b])
                    p.op("dve", lambda e, jj=jr: e.tensor_copy(s_sall[:, jj:jj + 1], s_c1[:, 6:7]),
                         reads=[c1b], writes=[sallb])
                    p.op("dve", lambda e: e.reciprocal(s_c1[:, 7:8], s_c1[:, 6:7]),
                         reads=[c1b], writes=[c1b])
                    p.op("dve", lambda e: e.tensor_scalar(s_sq[:], s_xn[:], s_c1[:, 7:8], MAGIC, ALU.mult, ALU.add),
                         reads=[xnb, c1b], writes=[sqb])
                    p.op("dve", lambda e: e.tensor_scalar(s_xq[:], s_sq[:], MAGIC, None, ALU.subtract),
                         reads=[sqb], writes=[xqb])
                    if t_xq is not None:
                        p.op("dve", lambda e: e.tensor_copy(s_sq[:], s_xq[:]), reads=[xqb], writes=[sqb])
                        dma(t_xq.rearrange("(n p) f -> n p f", p=128)[jr], s_sq[:], reads=[sqb])
                for q in range(4):
                    ps_h, pb_h = (psA0, pA0) if q % 2 == 0 else (psA1, pA1)
                    for c in range(4):
                        kc = q * 4 + c
                        tr(ps_h[:, c * 128:(c + 1) * 128], s_xn[:, kc * 128:(kc + 1) * 128],
                           s_idf[:], reads=[xnb, cb], writes=[pb_h])
                    base = q * 512
                    p.op("act", lambda e, b0=base, ph=ps_h: e.activation(
                        s_xnTf[:, b0:b0 + 512], ph[:, 0:512], AF.Identity),
                        reads=[pb_h], writes=[xnTfb])
                    p.op("dve", lambda e, b0=base, jj=j, ph=ps_h: e.tensor_copy(
                        s_xnT[:, jj * 2048 + b0: jj * 2048 + b0 + 512], ph[:, 0:512]),
                        reads=[pb_h], writes=[xnTb])
                if j > 0:
                    for q in range(4):
                        for c in range(4):
                            kc = q * 4 + c
                            tr(psA2[:, c * 128:(c + 1) * 128], s_xq[:, kc * 128:(kc + 1) * 128],
                               s_idb[:], reads=[xqb, cb], writes=[pA2])
                        p.op("dve", lambda e: e.tensor_copy(s_stg[:], psA2[:, 0:512]),
                             reads=[pA2], writes=[stgb])
                        for c in range(4):
                            kc = q * 4 + c
                            dma(xqT_dram[kc, :, jr * 128:(jr + 1) * 128], s_stg[:, c * 128:(c + 1) * 128],
                                reads=[stgb], writes=[xqTdb])
                for kc in range(16):
                    mm(psA3[:], s_xnTf[:, kc * 128:(kc + 1) * 128],
                       s_wraw[:, kc * 192:(kc + 1) * 192],
                       kc == 0, kc == 15, reads=[xnTfb, wrb], writes=[pA3])
                p.op("dve", lambda e, jj=j: e.tensor_copy(s_raw[:, jj * 192:(jj + 1) * 192], psA3[:]),
                     reads=[pA3], writes=[rawb])
                if t_raw is not None and j > 0:
                    dma(t_raw.rearrange("(n p) f -> n p f", p=128)[jr],
                        s_raw[:, j * 192:(j + 1) * 192], reads=[rawb])
                o0 = j * 192 + 128
                p.op("dve", lambda e, o=o0: e.tensor_add(s_sq[:, 0:64], s_raw[:, o:o + 64], s_dtb_bc[:]),
                     reads=[rawb, dtbb], writes=[sqb])
                p.op("dve", lambda e: e.tensor_scalar(s_sq[:, 64:128], s_sq[:, 0:64], 0.0, None, ALU.max),
                     reads=[sqb], writes=[sqb])
                p.op("act", lambda e: e.activation(s_sq[:, 128:192], s_sq[:, 0:64], AF.Abs),
                     reads=[sqb], writes=[sqb])
                p.op("act", lambda e: e.activation(s_sq[:, 128:192], s_sq[:, 128:192], AF.Exp, scale=-1.0),
                     reads=[sqb], writes=[sqb])
                p.op("act", lambda e: e.activation(s_sq[:, 128:192], s_sq[:, 128:192], AF.Ln, bias=s_actc[:, 1:2]),
                     reads=[sqb, cb], writes=[sqb])
                p.op("dve", lambda e, jj=j: e.tensor_add(s_dt[:, jj * 64:(jj + 1) * 64],
                                                         s_sq[:, 64:128], s_sq[:, 128:192]),
                     reads=[sqb], writes=[dtb_])
                if t_dt is not None and j > 0:
                    dma(t_dt.rearrange("(n p) f -> n p f", p=128)[jr],
                        s_dt[:, j * 64:(j + 1) * 64], reads=[dtb_])
                tr(psA4[:], s_dt[:, j * 64:(j + 1) * 64], s_idf[:], reads=[dtb_, cb], writes=[pA4])
                p.op("dve", lambda e: e.tensor_copy(s_dtT[:], psA4[:]), reads=[pA4], writes=[dtTb])
                p.op("dve", lambda e: e.tensor_scalar(s_ldtT[:], s_dtT[:], s_lam[:], None, ALU.mult),
                     reads=[dtTb, lamb], writes=[ldtTb])
                p.op("dve", lambda e, jj=j: e.tensor_tensor_scan(
                    s_sdtT[:, jj * 128:(jj + 1) * 128], s_dtT[:], s_dtT[:], s_carry[:, 0:1],
                    ALU.add, ALU.bypass), reads=[dtTb, carb], writes=[sdtTb])
                p.op("dve", lambda e, jj=j: e.tensor_tensor_scan(
                    s_slamT[:, jj * 128:(jj + 1) * 128], s_ldtT[:], s_ldtT[:], s_carry[:, 1:2],
                    ALU.add, ALU.bypass), reads=[ldtTb, carb], writes=[slamTb])
                p.op("dve", lambda e, jj=j: e.tensor_copy(
                    s_carry[:, 0:1], s_sdtT[:, jj * 128 + 127: jj * 128 + 128]),
                    reads=[sdtTb], writes=[carb])
                p.op("dve", lambda e, jj=j: e.tensor_copy(
                    s_carry[:, 1:2], s_slamT[:, jj * 128 + 127: jj * 128 + 128]),
                    reads=[slamTb], writes=[carb])
                tr(psA5[:], s_slamT[:, j * 128:(j + 1) * 128], s_idf[0:64, 0:64],
                   reads=[slamTb, cb], writes=[pA5])
                p.op("dve", lambda e, jj=j: e.tensor_copy(s_slam[:, jj * 64:(jj + 1) * 64], psA5[:]),
                     reads=[pA5], writes=[slamb])
                pump(8)
            if t_sdt is not None:
                dma(t_sdt, s_sdtT[:], reads=[sdtTb])
            if t_slam is not None:
                dma(t_slam, s_slamT[:], reads=[slamTb])

        p.fence()
        if n_phases >= 2:
            t_z = tap("z", [TOK, D_INNER])
            t_xs = tap("xs", [TOK, D_INNER])
            with contextlib.ExitStack() as _esB:
                _e = _esB.enter_context
                s_wstg = _e(nc.sbuf_tensor("s_wstg", [128, 512], F32))
                s_wblk = _e(nc.sbuf_tensor("s_wblk", [128, 16 * 512], BF16))
                s_bstg = _e(nc.sbuf_tensor("s_bstg", [128, 512], BF16))
                s_bstgf = _e(nc.sbuf_tensor("s_bstgf", [128, 512], F32))
                psB = _e(nc.psum_tensor("psB", [128, 4096], F32))
                wstgb = B("wstg")
                wblkb = B("wblk")
                bstgb = B("bstg")
                bstgfb = B("bstgf")
                pB = [B(f"psB{j}") for j in range(8)]
                w_in_t = w_in.rearrange("(kc p) o -> kc p o", p=128)
                for g in range(16):
                    is_z = g < 8
                    for kc in range(16):
                        dma(s_wstg[:], w_in_t[kc, :, g * 512:(g + 1) * 512],
                            reads=[wstgb], writes=[wstgb])
                        p.op("pool", lambda e, k0=kc: e.tensor_copy(
                            s_wblk[:, k0 * 512:(k0 + 1) * 512], s_wstg[:]),
                            reads=[wstgb], writes=[wblkb])
                    for kc in range(16):
                        for jr in range(8):
                            jt = jr + 1
                            mm(psB[:, jr * 512:(jr + 1) * 512],
                               s_xnT[:, (jt * 16 + kc) * 128:(jt * 16 + kc + 1) * 128],
                               s_wblk[:, kc * 512:(kc + 1) * 512], kc == 0, kc == 15,
                               reads=[xnTb, wblkb], writes=[pB[jr]])
                    for jr in range(8):
                        pslice = psB[:, jr * 512:(jr + 1) * 512]
                        if is_z:
                            p.op("act", lambda e, jj=jr, gg=g, ps=pslice: e.activation(
                                s_pool[:, jj * 4096 + gg * 512: jj * 4096 + (gg + 1) * 512],
                                ps, AF.Silu), reads=[pB[jr]], writes=[poolb])
                            if t_z is not None:
                                p.op("dve", lambda e, ps=pslice: e.tensor_copy(s_bstgf[:], ps),
                                     reads=[pB[jr]], writes=[bstgfb])
                                dma(t_z.rearrange("(n p) f -> n p f", p=128)[jr, :, g * 512:(g + 1) * 512],
                                    s_bstgf[:], reads=[bstgfb])
                        else:
                            gg = g - 8
                            p.op("dve", lambda e, ps=pslice: e.tensor_copy(s_bstg[:], ps),
                                 reads=[pB[jr]], writes=[bstgb])
                            dma(xs_dram[jr + 1, :, gg * 512:(gg + 1) * 512], s_bstg[:],
                                reads=[bstgb], writes=[xsdb])
                            if t_xs is not None:
                                p.op("dve", lambda e, ps=pslice: e.tensor_copy(s_bstgf[:], ps),
                                     reads=[pB[jr]], writes=[bstgfb])
                                dma(t_xs.rearrange("(n p) f -> n p f", p=128)[jr, :, gg * 512:(gg + 1) * 512],
                                    s_bstgf[:], reads=[bstgfb])
                    if not is_z:
                        gg = g - 8
                        for kc in range(16):
                            mm(psB[:, 0:512],
                               s_xnT[:, (0 * 16 + kc) * 128:(0 * 16 + kc + 1) * 128],
                               s_wblk[:, kc * 512:(kc + 1) * 512], kc == 0, kc == 15,
                               reads=[xnTb, wblkb], writes=[pB[0]])
                        p.op("dve", lambda e: e.tensor_copy(s_bstg[:], psB[:, 0:512]),
                             reads=[pB[0]], writes=[bstgb])
                        dma(xs_dram[0, :, gg * 512:(gg + 1) * 512], s_bstg[:],
                            reads=[bstgb], writes=[xsdb])
                    pump(10)
        pump(10 ** 9)
        scale_fin()

    p.fence()

    p.serial_now = "C" in ser_phases
    if n_phases >= 3:
        t_rb = tap("rbv", [TOK, 64], BF16)
        t_rbt = tap("rbtv", [64, TOK], BF16)
        t_i1 = tap("i1v", [TOK, 2048], BF16)
        t_i2 = tap("i2v", [TOK, 2048], BF16)
        t_sc = tap("sinv", [TOK, 2048], BF16)
        t_cc2 = tap("cosv", [TOK, 2048], BF16)
        t_bh = tap("bh", [TOK, D_INNER], BF16)
        t_ch = tap("ch", [TOK, D_INNER], BF16)
        t_y = tap("y_scan", [TOK, D_INNER], BF16)
        t_yn = tap("yn", [TOK, D_INNER])
        with contextlib.ExitStack() as _esC:
            _e = _esC.enter_context
            s_drep = _e(nc.sbuf_tensor("s_drep", [128, D_INNER], BF16))
            s_normw = _e(nc.sbuf_tensor("s_normw", [128, D_INNER], BF16))
            s_bbev = _e(nc.sbuf_tensor("s_bbev", [128, 2048], BF16))
            s_bbod = _e(nc.sbuf_tensor("s_bbod", [128, 2048], BF16))
            s_cbev = _e(nc.sbuf_tensor("s_cbev", [128, 2048], BF16))
            s_cbod = _e(nc.sbuf_tensor("s_cbod", [128, 2048], BF16))
            s_r32f = _e(nc.sbuf_tensor("s_r32f", [64, 2048], F32))
            s_rnev = _e(nc.sbuf_tensor("s_rnev", [64, 2048], BF16))
            s_rnod = _e(nc.sbuf_tensor("s_rnod", [64, 2048], BF16))
            s_bneg = _e(nc.sbuf_tensor("s_bneg", [128, 128], F32))
            s_xsj = _e(nc.sbuf_tensor("s_xsj", [128, 2048], BF16))
            s_flat = _e(nc.sbuf_tensor("s_flat", [1, 512], F32))
            s_sin = _e(nc.sbuf_tensor("s_sin", [128, 1024], BF16))
            s_cos = _e(nc.sbuf_tensor("s_cos", [128, 1024], BF16))
            s_phr = _e(nc.sbuf_tensor("s_phr", [128, 512], F32))
            s_k = _e(nc.sbuf_tensor("s_k", [128, 512], F32))
            s_ph2 = _e(nc.sbuf_tensor("s_ph2", [128, 512], F32))
            s_rbB = _e(nc.sbuf_tensor("s_rbB", [128, 64], BF16))
            s_rbC = _e(nc.sbuf_tensor("s_rbC", [128, 64], BF16))
            s_rmsBT = _e(nc.sbuf_tensor("s_rmsBT", [64, 128], BF16))
            s_rmsCT = _e(nc.sbuf_tensor("s_rmsCT", [64, 128], BF16))
            s_i1 = _e(nc.sbuf_tensor("s_i1", [128, 1024], BF16))
            s_i2 = _e(nc.sbuf_tensor("s_i2", [128, 1024], BF16))
            s_t1 = _e(nc.sbuf_tensor("s_t1", [128, 1024], BF16))
            s_t2 = _e(nc.sbuf_tensor("s_t2", [128, 1024], BF16))
            s_Bh = _e(nc.sbuf_tensor("s_Bh", [128, 2048], BF16))
            s_Ch = _e(nc.sbuf_tensor("s_Ch", [128, 2048], BF16))
            s_smB = _e(nc.sbuf_tensor("s_smB", [128, 2048], BF16))
            s_smC = _e(nc.sbuf_tensor("s_smC", [128, 2048], BF16))
            s_state = _e(nc.sbuf_tensor("s_state", [128, 2048], BF16))
            s_e1 = _e(nc.sbuf_tensor("s_e1", [128, 256], F32))
            s_W = _e(nc.sbuf_tensor("s_W", [128, 256], BF16))
            s_WG = _e(nc.sbuf_tensor("s_WG", [128, 256], BF16))
            s_U = _e(nc.sbuf_tensor("s_U", [128, 128], BF16))
            s_Bdec = _e(nc.sbuf_tensor("s_Bdec", [128, 128], BF16))
            s_tyi = _e(nc.sbuf_tensor("s_tyi", [128, 512], F32))
            s_xsd = _e(nc.sbuf_tensor("s_xsd", [128, 512], BF16))
            s_A8 = _e(nc.sbuf_tensor("s_A8", [128, 64], F32))
            s_Adec = _e(nc.sbuf_tensor("s_Adec", [128, 64], F32))
            s_rowL = _e(nc.sbuf_tensor("s_rowL", [1, 64], F32))
            s_rowP = _e(nc.sbuf_tensor("s_rowP", [1, 64], F32))
            s_Drow = _e(nc.sbuf_tensor("s_Drow", [1, 64], F32))
            s_Dmat = _e(nc.sbuf_tensor("s_Dmat", [128, 64], F32))
            s_ybuf = _e(nc.sbuf_tensor("s_ybuf", [128, D_INNER], BF16))
            s_gbuf = _e(nc.sbuf_tensor("s_gbuf", [128, 1024], F32))
            s_ynb = _e(nc.sbuf_tensor("s_ynb", [128, D_INNER], BF16))
            psC_a = _e(nc.psum_tensor("psC_a", [128, 512], F32))
            psC_b = _e(nc.psum_tensor("psC_b", [128, 512], F32))
            psC_t0 = _e(nc.psum_tensor("psC_t0", [128, 1024], BF16))
            psC_t1 = _e(nc.psum_tensor("psC_t1", [128, 1024], BF16))
            psC_g0 = _e(nc.psum_tensor("psC_g0", [128, 512], F32))
            psC_g1 = _e(nc.psum_tensor("psC_g1", [128, 512], F32))
            psC_y0 = _e(nc.psum_tensor("psC_y0", [128, 512], F32))
            psC_y1 = _e(nc.psum_tensor("psC_y1", [128, 512], F32))

            Cb = {}
            for nm in ("drep normw bbev bbod cbev cbod r32f rnev rnod bneg xsj flat "
                       "sin cos phr ph2 kbuf rbB rbC rmsBT rmsCT i1 i2 t1 t2 Bh Ch "
                       "smB smC e1 W WG tyi xsd A8 Adec rowL rowP Drow Dmat ybuf "
                       "gbuf ynb cc").split():
                Cb[nm] = B("c_" + nm)
            pCa = B("psC_a"); pCb = B("psC_b")
            pCt = [B("psC_t0"), B("psC_t1")]
            pCg = [B("psC_g0"), B("psC_g1")]
            pCy = [B("psC_y0"), B("psC_y1")]
            stb = [B(f"c_state{i}") for i in range(8)]
            Ub = [B(f"c_U{i}") for i in range(4)]
            Bdb = [B(f"c_Bd{i}") for i in range(4)]
            Wb = [B(f"c_W{i}") for i in range(4)]
            WGb = [B(f"c_WG{i}") for i in range(4)]
            e1b = [B(f"c_e1{i}") for i in range(4)]

            dma(s_bneg[:], bnegp, writes=[Cb["bneg"]])
            dma(s_r32f[:], r32f, writes=[Cb["r32f"]])
            dma(s_rnev[:], rn_ev, writes=[Cb["rnev"]])
            dma(s_rnod[:], rn_od, writes=[Cb["rnod"]])
            crb = Cb["flat"]
            def bcast(dst, src_row, width, wbuf):
                for g0 in range(0, width, 512):
                    w = min(512, width - g0)
                    ps = psC_a if (g0 // 512) % 2 == 0 else psC_b
                    pbx = pCa if (g0 // 512) % 2 == 0 else pCb
                    dma(s_flat[0:1, 0:w], src_row[0:1, g0:g0 + w],
                        reads=[crb], writes=[crb])
                    mm(ps[:, 0:w], s_ones[:], s_flat[0:1, 0:w], True, True,
                       reads=[crb, cb], writes=[pbx])
                    p.op("dve", lambda e, d=dst, g=g0, ww=w, pp=ps: e.tensor_copy(
                        d[:, g:g + ww], pp[:, 0:ww]), reads=[pbx], writes=[wbuf])
            bcast(s_drep, d_rep_r, 4096, Cb["drep"])
            bcast(s_normw, normw_r, 4096, Cb["normw"])
            bcast(s_bbev, bb_ev, 2048, Cb["bbev"])
            bcast(s_bbod, bb_od, 2048, Cb["bbod"])
            bcast(s_cbev, cb_ev, 2048, Cb["cbev"])
            bcast(s_cbod, cb_od, 2048, Cb["cbod"])
            p.op("dve", lambda e: e.memset(s_state[:], 0.0), writes=stb)
            p.op("dve", lambda e: e.memset(s_rowP[:], 0.0), writes=[Cb["rowP"]])

            def view32(t, blk):
                return t[:].rearrange("p (h g) -> p h g", g=64)[:, :, blk * 32:(blk + 1) * 32]

            for j in range(NTH):
                halo = (j == 0)
                p.serial_now = ("C" in ser_phases) or ("Cpro" in ser_phases)
                dma(s_rowL[:], s_slamT[:, j * 128 + 127: j * 128 + 128],
                    reads=[slamTb, Cb["rowL"]], writes=[Cb["rowL"]])
                if not halo:
                    mm(psC_a[:, 0:64], s_ones[:], s_rowP[0:1, 0:64], True, True,
                       reads=[Cb["rowP"], cb], writes=[pCa])
                    p.op("dve", lambda e, jj=j: e.tensor_sub(
                        s_A8[:], psC_a[:, 0:64], s_slam[:, jj * 64:(jj + 1) * 64]),
                        reads=[slamb, pCa], writes=[Cb["A8"]])
                    p.op("act", lambda e: e.activation(s_A8[:], s_A8[:], AF.Exp),
                         reads=[Cb["A8"]], writes=[Cb["A8"]])
                    p.op("dve", lambda e: e.tensor_scalar(s_A8[:], s_A8[:], 0.125, None, ALU.mult),
                         reads=[Cb["A8"]], writes=[Cb["A8"]])
                p.op("dve", lambda e: e.tensor_sub(s_Drow[:], s_rowP[:], s_rowL[:]),
                     reads=[Cb["rowP"], Cb["rowL"]], writes=[Cb["Drow"]])
                p.op("act", lambda e: e.activation(s_Drow[:], s_Drow[:], AF.Exp),
                     reads=[Cb["Drow"]], writes=[Cb["Drow"]])
                p.op("dve", lambda e: e.tensor_copy(s_rowP[:], s_rowL[:]),
                     reads=[Cb["rowL"]], writes=[Cb["rowP"]])
                mm(psC_a[:, 0:64], s_ones[:], s_Drow[0:1, 0:64], True, True,
                   reads=[Cb["Drow"], cb], writes=[pCa])
                p.op("dve", lambda e: e.tensor_copy(s_Dmat[:], psC_a[:, 0:64]),
                     reads=[pCa], writes=[Cb["Dmat"]])
                mm(psC_b[:, 0:64], s_ones[:], s_rowL[0:1, 0:64], True, True,
                   reads=[Cb["rowL"], cb], writes=[pCb])
                p.op("dve", lambda e, jj=j: e.tensor_sub(
                    s_Adec[:], s_slam[:, jj * 64:(jj + 1) * 64], psC_b[:, 0:64]),
                    reads=[slamb, pCb], writes=[Cb["Adec"]])
                p.op("act", lambda e: e.activation(s_Adec[:], s_Adec[:], AF.Exp),
                     reads=[Cb["Adec"]], writes=[Cb["Adec"]])
                for (src_off, rb_t, rb_b, rT_t, rT_b) in (
                        ((0, s_rbB, Cb["rbB"], s_rmsBT, Cb["rmsBT"]),
                         (64, s_rbC, Cb["rbC"], s_rmsCT, Cb["rmsCT"]))
                        if not halo else
                        ((0, s_rbB, Cb["rbB"], s_rmsBT, Cb["rmsBT"]),)):
                    off = j * 192 + src_off
                    p.op("act", lambda e, o=off: e.activation(
                        s_gbuf[:, 0:64], s_raw[:, o:o + 64], AF.Square,
                        accum_out=s_sall[:, 20:21]),
                        reads=[rawb], writes=[Cb["gbuf"], Cb["cc"]])
                    p.op("act", lambda e: e.activation(s_sall[:, 21:22], s_sall[:, 20:21], AF.Ln,
                                                       scale=1.0 / 64.0, bias=s_actc[:, 2:3]),
                         reads=[Cb["cc"], cb], writes=[Cb["cc"]])
                    p.op("act", lambda e: e.activation(s_sall[:, 21:22], s_sall[:, 21:22], AF.Exp, scale=-0.5),
                         reads=[Cb["cc"]], writes=[Cb["cc"]])
                    p.op("dve", lambda e, o=off, rt=rb_t: e.tensor_scalar(
                        rt[:], s_raw[:, o:o + 64], s_sall[:, 21:22], None, ALU.mult),
                        reads=[rawb, Cb["cc"]], writes=[rb_b])
                    tr(psC_t0[0:64, 0:128], rb_t[:], s_idb[:], reads=[rb_b, cb], writes=[pCt[0]])
                    p.op("dve", lambda e, rt=rT_t: e.tensor_copy(rt[:], psC_t0[0:64, 0:128]),
                         reads=[pCt[0]], writes=[rT_b])
                    if t_rb is not None and t_rbt is not None and rb_t is s_rbB and not halo:
                        dma(t_rb.rearrange("(n p) f -> n p f", p=128)[j - 1], rb_t[:], reads=[rb_b])
                        dma(t_rbt[:, (j - 1) * 128:j * 128], rT_t[:], reads=[rT_b])

                for h2 in range(2):
                    p.serial_now = ("C" in ser_phases) or ("Cbc" in ser_phases)
                    hb = h2 * 1024
                    dma(s_xsj[:], xs_dram[j][:, h2 * 2048:(h2 + 1) * 2048],
                        reads=[Cb["xsj"], xsdb], writes=[Cb["xsj"]])
                    tensors_bc = (
                        ((s_rmsBT, Cb["rmsBT"], s_rnev, s_rnod, s_bbev, Cb["bbev"],
                          s_bbod, Cb["bbod"], s_Bh, Cb["Bh"], s_smB, Cb["smB"], t_bh),
                         (s_rmsCT, Cb["rmsCT"], s_rnev, s_rnod, s_cbev, Cb["cbev"],
                          s_cbod, Cb["cbod"], s_Ch, Cb["Ch"], s_smC, Cb["smC"], t_ch))
                        if not halo else
                        ((s_rmsBT, Cb["rmsBT"], s_rnev, s_rnod, s_bbev, Cb["bbev"],
                          s_bbod, Cb["bbod"], s_Bh, Cb["Bh"], s_smB, Cb["smB"], None),))
                    for g in range(2):
                        ps, pbx = (psC_a, pCa) if g == 0 else (psC_b, pCb)
                        mm(ps[:], s_sdtT[:, j * 128:(j + 1) * 128],
                           s_r32f[:, hb + g * 512: hb + (g + 1) * 512], True, True,
                           reads=[sdtTb, Cb["r32f"]], writes=[pbx])
                        p.op("dve", lambda e, pp=ps: e.tensor_scalar(
                            s_k[:], pp[:], INV_2PI, MAGIC, ALU.mult, ALU.add),
                            reads=[pbx], writes=[Cb["kbuf"]])
                        p.op("dve", lambda e: e.tensor_scalar(s_k[:], s_k[:], MAGIC, None, ALU.subtract),
                             reads=[Cb["kbuf"]], writes=[Cb["kbuf"]])
                        p.op("dve", lambda e, pp=ps: e.scalar_tensor_tensor(
                            s_phr[:], s_k[:], -CW1, pp[:], ALU.mult, ALU.add),
                            reads=[pbx, Cb["kbuf"]], writes=[Cb["phr"]])
                        p.op("dve", lambda e: e.scalar_tensor_tensor(
                            s_ph2[:], s_k[:], -CW2, s_phr[:], ALU.mult, ALU.add),
                            reads=[Cb["phr"], Cb["kbuf"]], writes=[Cb["ph2"]])
                        p.op("dve", lambda e: e.scalar_tensor_tensor(
                            s_phr[:], s_k[:], -CW3, s_ph2[:], ALU.mult, ALU.add),
                            reads=[Cb["ph2"], Cb["kbuf"]], writes=[Cb["phr"]])
                        p.op("act", lambda e, gg=g: e.activation(
                            s_sin[:, gg * 512:(gg + 1) * 512], s_phr[:], AF.Sin),
                            reads=[Cb["phr"]], writes=[Cb["sin"]])
                        p.op("act", lambda e: e.activation(s_ph2[:], s_phr[:], AF.Abs),
                             reads=[Cb["phr"]], writes=[Cb["ph2"]])
                        p.op("act", lambda e, gg=g: e.activation(
                            s_cos[:, gg * 512:(gg + 1) * 512], s_ph2[:], AF.Sin,
                            scale=-1.0, bias=s_actc[:, 3:4]),
                            reads=[Cb["ph2"], cb], writes=[Cb["cos"]])
                    if t_sc is not None and not halo:
                        dma(t_sc.rearrange("(n p) f -> n p f", p=128)[j - 1, :, h2 * 1024:(h2 + 1) * 1024],
                            s_sin[:], reads=[Cb["sin"]])
                    if t_cc2 is not None and not halo:
                        dma(t_cc2.rearrange("(n p) f -> n p f", p=128)[j - 1, :, h2 * 1024:(h2 + 1) * 1024],
                            s_cos[:], reads=[Cb["cos"]])
                    for (rT_t, rT_b, pev, pod, bev, bevb, bod, bodb, out_t, out_b,
                         sm_t, sm_b, tp) in tensors_bc:
                        for (pat, pat_b, bias_t, bias_b, dst, dst_b) in (
                                (pev, Cb["rnev"], bev, bevb, s_i1, Cb["i1"]),
                                (pod, Cb["rnod"], bod, bodb, s_i2, Cb["i2"])):
                            for g in range(2):
                                ps, pbx = (psC_a, pCa) if g == 0 else (psC_b, pCb)
                                mm(ps[:], rT_t[:], pat[:, hb + g * 512: hb + (g + 1) * 512],
                                   True, True, reads=[rT_b, pat_b], writes=[pbx])
                                p.op("dve", lambda e, pp=ps, gg=g, bt=bias_t, dd=dst, hb_=hb:
                                     e.tensor_add(dd[:, gg * 512:(gg + 1) * 512], pp[:],
                                                  bt[:, hb_ + gg * 512: hb_ + (gg + 1) * 512]),
                                     reads=[pbx, bias_b], writes=[dst_b])
                        i1v = s_i1[:].rearrange("p (h f) -> p h f", f=32)
                        i2v = s_i2[:].rearrange("p (h f) -> p h f", f=32)
                        sv = s_sin[:].rearrange("p (h f) -> p h f", f=32)
                        cv = s_cos[:].rearrange("p (h f) -> p h f", f=32)
                        t1v = s_t1[:].rearrange("p (h f) -> p h f", f=32)
                        t2v = s_t2[:].rearrange("p (h f) -> p h f", f=32)
                        o1 = view32(out_t, 0)
                        o2 = view32(out_t, 1)
                        p.op("dve", lambda e, o=t1v, a=i1v, c=cv: e.tensor_mul(o, a, c),
                             reads=[Cb["i1"], Cb["cos"]], writes=[Cb["t1"]])
                        p.op("dve", lambda e, o=t2v, a=i2v, c=sv: e.tensor_mul(o, a, c),
                             reads=[Cb["i2"], Cb["sin"]], writes=[Cb["t2"]])
                        p.op("dve", lambda e, o=o1, a=t1v, c=t2v: e.tensor_sub(o, a, c),
                             reads=[Cb["t1"], Cb["t2"]], writes=[out_b])
                        p.op("dve", lambda e, o=t1v, a=i1v, c=sv: e.tensor_mul(o, a, c),
                             reads=[Cb["i1"], Cb["sin"]], writes=[Cb["t1"]])
                        p.op("dve", lambda e, o=t2v, a=i2v, c=cv: e.tensor_mul(o, a, c),
                             reads=[Cb["i2"], Cb["cos"]], writes=[Cb["t2"]])
                        p.op("dve", lambda e, o=o2, a=t1v, c=t2v: e.tensor_add(o, a, c),
                             reads=[Cb["t1"], Cb["t2"]], writes=[out_b])
                        if tp is not None and not halo:
                            dma(tp.rearrange("(n p) f -> n p f", p=128)[j - 1, :, h2 * 2048:(h2 + 1) * 2048],
                                out_t[:], reads=[out_b])
                        if t_i1 is not None and t_i2 is not None and out_t is s_Bh and not halo:
                            dma(t_i1.rearrange("(n p) f -> n p f", p=128)[j - 1, :, h2 * 1024:(h2 + 1) * 1024],
                                s_i1[:], reads=[Cb["i1"]])
                            dma(t_i2.rearrange("(n p) f -> n p f", p=128)[j - 1, :, h2 * 1024:(h2 + 1) * 1024],
                                s_i2[:], reads=[Cb["i2"]])
                        for half16 in (() if halo else range(2)):
                            pst, pstb = (psC_t0, pCt[0]) if half16 == 0 else (psC_t1, pCt[1])
                            for g2 in range(8):
                                gg2 = half16 * 8 + g2
                                tr(pst[:, g2 * 128:(g2 + 1) * 128],
                                   out_t[:, gg2 * 128:(gg2 + 1) * 128], s_idb[:],
                                   reads=[out_b, cb], writes=[pstb])
                            p.op("dve", lambda e, st=sm_t, hh=half16, pp=pst: e.tensor_copy(
                                st[:, hh * 1024:(hh + 1) * 1024], pp[:]),
                                reads=[pstb], writes=[sm_b])
                    p.serial_now = ("C" in ser_phases) or ("Cscan" in ser_phases)
                    for bb2 in range(16):
                        psm, psmb_ = (psC_a, pCa) if bb2 % 2 == 0 else (psC_b, pCb)
                        psg, psgb = (psC_g0, pCg[0]) if bb2 % 2 == 0 else (psC_g1, pCg[1])
                        hh0 = bb2 * 2
                        if hh0 % 4 == 0 and not halo:
                            h0g = h2 * 32 + hh0
                            dma(s_flat[:], s_slamT[h0g:h0g + 4, j * 128:(j + 1) * 128],
                                reads=[slamTb, Cb["flat"]], writes=[Cb["flat"]])
                        for q in range(2):
                            hh = hh0 + q
                            h = h2 * 32 + hh
                            hg = j * 64 + h
                            row0 = q * 64
                            g2c = bb2 * 128
                            bank = h // 8
                            psy, psyb = (psC_y0, pCy[0]) if bank % 2 == 0 else (psC_y1, pCy[1])
                            sl = (h % 8) * 64
                            scol = (h // 2) * 64
                            if halo:
                                p.op("pool", lambda e, qq=q, hl=hh, c=hg: e.tensor_scalar(
                                    s_U[:, qq * 64:(qq + 1) * 64],
                                    s_xsj[:, hl * 64:(hl + 1) * 64],
                                    s_dt[:, c:c + 1], None, ALU.mult),
                                    reads=[Cb["xsj"], dtb_], writes=[Ub[q]])
                                p.op("pool", lambda e, qq=q, hh_=hh, c=h: e.tensor_scalar(
                                    s_Bdec[:, qq * 64:(qq + 1) * 64],
                                    s_Bh[:, hh_ * 64:(hh_ + 1) * 64],
                                    s_Adec[:, c:c + 1], None, ALU.mult),
                                    reads=[Cb["Bh"], Cb["Adec"]], writes=[Bdb[q]])
                                mm(psm[0:64, 256 + q * 64: 256 + (q + 1) * 64],
                                   s_Bdec[:, q * 64:(q + 1) * 64],
                                   s_U[:, q * 64:(q + 1) * 64], True, True,
                                   reads=[Bdb[q], Ub[q]], writes=[psmb_])
                                p.op("pool", lambda e, r0=row0, sc=scol, hh_=h: e.tensor_scalar(
                                    s_state[r0:r0 + 64, sc:sc + 64],
                                    s_state[r0:r0 + 64, sc:sc + 64],
                                    s_Dmat[r0:r0 + 64, hh_:hh_ + 1], None, ALU.mult),
                                    reads=[stb[h // 8], Cb["Dmat"]], writes=[stb[h // 8]])
                                p.op("dve", lambda e, r0=row0, sc=scol, qq=q, pp=psm: e.tensor_add(
                                    s_state[r0:r0 + 64, sc:sc + 64],
                                    s_state[r0:r0 + 64, sc:sc + 64],
                                    pp[0:64, 256 + qq * 64: 256 + (qq + 1) * 64]),
                                    reads=[stb[h // 8], psmb_], writes=[stb[h // 8]])
                                continue
                            mm(psm[:, q * 128:(q + 1) * 128], s_ones[:],
                               s_flat[0:1, (hh % 4) * 128:(hh % 4 + 1) * 128], True, True,
                               reads=[Cb["flat"], cb], writes=[psmb_])
                            p.op("dve", lambda e, qq=q, pp=psm: e.scalar_tensor_tensor(
                                s_e1[:, qq * 128:(qq + 1) * 128], pp[:, qq * 128:(qq + 1) * 128],
                                -1.0, s_bneg[:], ALU.mult, ALU.add),
                                reads=[psmb_, Cb["bneg"]], writes=[e1b[q]])
                            p.op("act", lambda e, qq=q, c=hg: e.activation(
                                s_W[:, qq * 128:(qq + 1) * 128], s_e1[:, qq * 128:(qq + 1) * 128],
                                AF.Exp, bias=s_slam[:, c:c + 1]),
                                reads=[e1b[q], slamb], writes=[Wb[q]])
                            mm(psg[:, q * 128:(q + 1) * 128],
                               s_smB[row0:row0 + 64, g2c:g2c + 128],
                               s_smC[row0:row0 + 64, g2c:g2c + 128], True, True,
                               reads=[Cb["smB"], Cb["smC"]], writes=[psgb])
                            p.op("dve", lambda e, qq=q, pp=psg: e.tensor_mul(
                                s_WG[:, qq * 128:(qq + 1) * 128],
                                s_W[:, qq * 128:(qq + 1) * 128],
                                pp[:, qq * 128:(qq + 1) * 128]),
                                reads=[Wb[q], psgb], writes=[WGb[q]])
                            p.op("pool", lambda e, qq=q, hl=hh, c=hg: e.tensor_scalar(
                                s_U[:, qq * 64:(qq + 1) * 64],
                                s_xsj[:, hl * 64:(hl + 1) * 64],
                                s_dt[:, c:c + 1], None, ALU.mult),
                                reads=[Cb["xsj"], dtb_], writes=[Ub[q]])
                            p.op("pool", lambda e, qq=q, hh_=hh, c=h: e.tensor_scalar(
                                s_Bdec[:, qq * 64:(qq + 1) * 64],
                                s_Bh[:, hh_ * 64:(hh_ + 1) * 64],
                                s_Adec[:, c:c + 1], None, ALU.mult),
                                reads=[Cb["Bh"], Cb["Adec"]], writes=[Bdb[q]])
                            mm(psy[:, sl:sl + 64], s_WG[:, q * 128:(q + 1) * 128],
                               s_U[:, q * 64:(q + 1) * 64], True, True,
                               reads=[WGb[q], Ub[q]], writes=[psyb])
                            mm(psg[:, 256 + q * 64: 256 + (q + 1) * 64],
                               s_smC[row0:row0 + 64, g2c:g2c + 128],
                               s_state[row0:row0 + 64, scol:scol + 64], True, True,
                               reads=[Cb["smC"], stb[h // 8]], writes=[psgb])
                            p.op("pool", lambda e, hh_=h, hl=hh: e.tensor_mul(
                                s_xsd[:, (hh_ % 8) * 64:((hh_ % 8) + 1) * 64],
                                s_xsj[:, hl * 64:(hl + 1) * 64],
                                s_drep[:, hh_ * 64:(hh_ + 1) * 64]),
                                reads=[Cb["xsj"], Cb["drep"]], writes=[Cb["xsd"]])
                            p.op("dve", lambda e, qq=q, hh_=h, pp=psg: e.scalar_tensor_tensor(
                                s_tyi[:, (hh_ % 8) * 64:((hh_ % 8) + 1) * 64],
                                pp[:, 256 + qq * 64: 256 + (qq + 1) * 64],
                                s_A8[:, hh_:hh_ + 1],
                                s_xsd[:, (hh_ % 8) * 64:((hh_ % 8) + 1) * 64],
                                ALU.mult, ALU.add),
                                reads=[psgb, Cb["A8"], Cb["xsd"]], writes=[Cb["tyi"]])
                            mm(psm[0:64, 256 + q * 64: 256 + (q + 1) * 64],
                               s_Bdec[:, q * 64:(q + 1) * 64],
                               s_U[:, q * 64:(q + 1) * 64], True, True,
                               reads=[Bdb[q], Ub[q]], writes=[psmb_])
                            p.op("pool", lambda e, r0=row0, sc=scol, hh_=h: e.tensor_scalar(
                                s_state[r0:r0 + 64, sc:sc + 64],
                                s_state[r0:r0 + 64, sc:sc + 64],
                                s_Dmat[r0:r0 + 64, hh_:hh_ + 1], None, ALU.mult),
                                reads=[stb[h // 8], Cb["Dmat"]], writes=[stb[h // 8]])
                            p.op("dve", lambda e, r0=row0, sc=scol, qq=q, pp=psm: e.tensor_add(
                                s_state[r0:r0 + 64, sc:sc + 64],
                                s_state[r0:r0 + 64, sc:sc + 64],
                                pp[0:64, 256 + qq * 64: 256 + (qq + 1) * 64]),
                                reads=[stb[h // 8], psmb_], writes=[stb[h // 8]])
                        if (hh0 + 2) % 8 == 0 and not halo:
                            bank = (h2 * 32 + hh0) // 8
                            psy, psyb = (psC_y0, pCy[0]) if bank % 2 == 0 else (psC_y1, pCy[1])
                            p.op("dve", lambda e, bk=bank, pp=psy: e.scalar_tensor_tensor(
                                s_ybuf[:, bk * 512:(bk + 1) * 512], pp[:], 0.125,
                                s_tyi[:], ALU.mult, ALU.add),
                                reads=[psyb, Cb["tyi"]], writes=[Cb["ybuf"]])
                p.serial_now = ("C" in ser_phases) or ("Cgate" in ser_phases)
                if halo:
                    continue
                if t_y is not None:
                    dma(t_y.rearrange("(n p) f -> n p f", p=128)[j - 1], s_ybuf[:],
                        reads=[Cb["ybuf"]])
                for gh in range(4):
                    p.op("dve", lambda e, jj=j - 1, g0=gh: e.tensor_mul(
                        s_gbuf[:], s_ybuf[:, g0 * 1024:(g0 + 1) * 1024],
                        s_pool[:, jj * 4096 + g0 * 1024: jj * 4096 + (g0 + 1) * 1024]),
                        reads=[Cb["ybuf"], poolb], writes=[Cb["gbuf"]])
                    p.op("act", lambda e, g0=gh: e.activation(
                        s_ynb[:, g0 * 1024:(g0 + 1) * 1024], s_gbuf[:], AF.Square,
                        accum_out=s_sall[:, 24 + g0:25 + g0]),
                        reads=[Cb["gbuf"]], writes=[Cb["ynb"], Cb["cc"]])
                p.op("dve", lambda e: e.tensor_add(s_sall[:, 24:25], s_sall[:, 24:25], s_sall[:, 25:26]),
                     reads=[Cb["cc"]], writes=[Cb["cc"]])
                p.op("dve", lambda e: e.tensor_add(s_sall[:, 26:27], s_sall[:, 26:27], s_sall[:, 27:28]),
                     reads=[Cb["cc"]], writes=[Cb["cc"]])
                p.op("dve", lambda e: e.tensor_add(s_sall[:, 22:23], s_sall[:, 24:25], s_sall[:, 26:27]),
                     reads=[Cb["cc"]], writes=[Cb["cc"]])
                p.op("act", lambda e: e.activation(s_sall[:, 23:24], s_sall[:, 22:23], AF.Ln,
                                                   scale=1.0 / float(D_INNER), bias=s_actc[:, 2:3]),
                     reads=[Cb["cc"], cb], writes=[Cb["cc"]])
                p.op("act", lambda e: e.activation(s_sall[:, 23:24], s_sall[:, 23:24], AF.Exp, scale=-0.5),
                     reads=[Cb["cc"]], writes=[Cb["cc"]])
                for gh in range(4):
                    p.op("dve", lambda e, jj=j - 1, g0=gh: e.tensor_mul(
                        s_gbuf[:], s_ybuf[:, g0 * 1024:(g0 + 1) * 1024],
                        s_pool[:, jj * 4096 + g0 * 1024: jj * 4096 + (g0 + 1) * 1024]),
                        reads=[Cb["ybuf"], poolb], writes=[Cb["gbuf"]])
                    p.op("dve", lambda e, g0=gh: e.scalar_tensor_tensor(
                        s_ynb[:, g0 * 1024:(g0 + 1) * 1024], s_gbuf[:], s_sall[:, 23:24],
                        s_normw[:, g0 * 1024:(g0 + 1) * 1024], ALU.mult, ALU.mult),
                        reads=[Cb["gbuf"], Cb["cc"], Cb["normw"]], writes=[Cb["ynb"]])
                if t_yn is not None:
                    for gh in range(4):
                        p.op("dve", lambda e, g0=gh: e.tensor_copy(
                            s_gbuf[:], s_ynb[:, g0 * 1024:(g0 + 1) * 1024]),
                            reads=[Cb["ynb"]], writes=[Cb["gbuf"]])
                        dma(t_yn.rearrange("(n p) f -> n p f", p=128)[j - 1, :, gh * 1024:(gh + 1) * 1024],
                            s_gbuf[:], reads=[Cb["gbuf"]])
                for q in range(4):
                    pst, pstb = (psC_t0, pCt[0]) if q % 2 == 0 else (psC_t1, pCt[1])
                    for c in range(8):
                        kc = q * 8 + c
                        tr(pst[:, c * 128:(c + 1) * 128], s_ynb[:, kc * 128:(kc + 1) * 128],
                           s_idb[:], reads=[Cb["ynb"], cb], writes=[pstb])
                    p.op("dve", lambda e, jj=j - 1, qq=q, pp=pst: e.tensor_copy(
                        s_pool[:, jj * 4096 + qq * 1024: jj * 4096 + (qq + 1) * 1024], pp[:]),
                        reads=[pstb], writes=[poolb])
    p.fence()

    p.serial_now = "E" in ser_phases
    with contextlib.ExitStack() as _esED:
        _e = _esED.enter_context
        s_attn = _e(nc.sbuf_tensor("s_attn", [128, NT * 2048], F32))
        attnb = B("attn")
        if n_phases >= 4:
            t_attn = tap("attn_out", [TOK, D_MODEL])
            with contextlib.ExitStack() as _esE:
                _e2 = _esE.enter_context
                s_web = [_e2(nc.sbuf_tensor(f"s_web{i}", [128, 512], F32)) for i in range(2)]
                s_webb = [_e2(nc.sbuf_tensor(f"s_webb{i}", [128, 512], BF16)) for i in range(2)]
                psE = _e2(nc.psum_tensor("psE", [128, 4096], F32))
                webb = [B("web0"), B("web1")]
                webbb = [B("webb0"), B("webb1")]
                pE = [B(f"psE{j}") for j in range(8)]
                w_out_t = w_out.rearrange("(kc p) o -> kc p o", p=128)
                for og in range(4):
                    for kc in range(32):
                        par = kc % 2
                        dma(s_web[par][:], w_out_t[kc, :, og * 512:(og + 1) * 512],
                            reads=[webb[par]], writes=[webb[par]])
                        p.op("pool", lambda e, pr=par: e.tensor_copy(s_webb[pr][:], s_web[pr][:]),
                             reads=[webb[par]], writes=[webbb[par]])
                        for j in range(NT):
                            mm(psE[:, j * 512:(j + 1) * 512],
                               s_pool[:, j * 4096 + kc * 128: j * 4096 + (kc + 1) * 128],
                               s_webb[par][:], kc == 0, kc == 31,
                               reads=[poolb, webbb[par]], writes=[pE[j]])
                    for j in range(NT):
                        p.op("dve", lambda e, jj=j, o=og: e.tensor_copy(
                            s_attn[:, jj * 2048 + o * 512: jj * 2048 + (o + 1) * 512],
                            psE[:, jj * 512:(jj + 1) * 512]),
                            reads=[pE[j]], writes=[attnb])
            if t_attn is not None:
                for j in range(NT):
                    dma(t_attn.rearrange("(n p) f -> n p f", p=128)[j],
                        s_attn[:, j * 2048:(j + 1) * 2048], reads=[attnb])
        p.fence()

        p.serial_now = "D1" in ser_phases
        if n_phases >= 5:
            t_gu = tap("gu", [TOK, D_FFN])
            t_gate = tap("gatev", [TOK, D_FFN])
            for kc in range(16):
                dma(s_pool[:, kc * 1024:(kc + 1) * 1024], xqT_dram[kc],
                    reads=[xqTdb, poolb], writes=[poolb])
            with contextlib.ExitStack() as _esD1:
                _e2 = _esD1.enter_context
                s_wfb = [_e2(nc.sbuf_tensor("s_wfb0", [128, 2048], F32))]
                s_wfq = _e2(nc.sbuf_tensor("s_wfq", [128, 2048], BF16))
                s_wqT = _e2(nc.sbuf_tensor("s_wqT", [128, 16 * 256], BF16))
                s_gg = _e2(nc.sbuf_tensor("s_gg", [128, 4096], F32))
                s_sxw = _e2(nc.sbuf_tensor("s_sxw", [128, 16], F32))
                s_gu2 = _e2(nc.sbuf_tensor("s_gu2", [128, 512], F32))
                s_gstg = _e2(nc.sbuf_tensor("s_gstg", [128, 512], F32))
                s_wdb = _e2(nc.sbuf_tensor("s_wdb", [128, 2048], F32))
                s_wdq = _e2(nc.sbuf_tensor("s_wdq", [128, 2048], BF16))
                s_wdstg = _e2(nc.sbuf_tensor("s_wdstg", [128, 1024], BF16))
                psD = _e2(nc.psum_tensor("psD", [128, 2048], F32))
                psDt = _e2(nc.psum_tensor("psDt", [128, 1024], BF16))
                psDt2 = _e2(nc.psum_tensor("psDt2", [128, 1024], BF16))
                wfbb = [B("wfb0")]
                wfqb = B("wfq")
                wqTb = B("wqT")
                ggb = B("gg")
                sxwb = B("sxw")
                gstgb = B("gstg")
                wdbb = B("wdb"); wdqb = B("wdq"); wdstgb = B("wdstg")
                pD = [B(f"psD{j}") for j in range(4)]
                pDt = [B("psDt"), B("psDt2")]

                gu2b = B("gu2")
                for j in range(NT):
                    p.op("dve", lambda e, jj=j: e.tensor_mul(
                        s_sxw[:, jj:jj + 1], s_sall[:, jj:jj + 1], s_ws[:, 4:5]),
                        reads=[sallb, wsb2], writes=[sxwb])
                    p.op("dve", lambda e, jj=j: e.tensor_mul(
                        s_sxw[:, 8 + jj:9 + jj], s_sall[:, jj:jj + 1], s_ws[:, 5:6]),
                        reads=[sallb, wsb2], writes=[sxwb])

                def wd_prepass():
                    wd_v3 = wd.rearrange("(n p) k -> n p k", p=128)
                    for blk in range(64):
                        r, quar = blk // 4, blk % 4
                        dma(s_wdb[:], wd_v3[r][:, quar * 2048:(quar + 1) * 2048],
                            reads=[wdbb], writes=[wdbb], q="pool")
                        p.op("dve", lambda e: e.tensor_scalar(
                            s_wdb[:], s_wdb[:], s_ws[:, 2:3], MAGIC, ALU.mult, ALU.add),
                            reads=[wdbb, wsb2], writes=[wdbb])
                        p.op("dve", lambda e: e.tensor_scalar(
                            s_wdb[:], s_wdb[:], MAGIC, 1.0, ALU.subtract, ALU.min),
                            reads=[wdbb], writes=[wdbb])
                        p.op("dve", lambda e: e.tensor_scalar(
                            s_wdq[:], s_wdb[:], -1.0, None, ALU.max),
                            reads=[wdbb], writes=[wdqb])
                        for q in range(2):
                            pst, pstb = (psDt, pDt[0]) if q % 2 == 0 else (psDt2, pDt[1])
                            for c in range(8):
                                kcl = q * 8 + c
                                tr(pst[:, c * 128:(c + 1) * 128],
                                   s_wdq[:, kcl * 128:(kcl + 1) * 128], s_idb[:],
                                   reads=[wdqb, cb], writes=[pstb])
                            p.op("dve", lambda e, pp=pst: e.tensor_copy(s_wdstg[:], pp[:]),
                                 reads=[pstb], writes=[wdstgb])
                            for c in range(8):
                                kcl = q * 8 + c
                                kc2 = quar * 16 + kcl
                                dma(wqdT_dram[kc2, :, r * 128:(r + 1) * 128],
                                    s_wdstg[:, c * 128:(c + 1) * 128],
                                    reads=[wdstgb], writes=[wqdb], q="pool")
                        yield

                wdgen = wd_prepass()

                for g in range(16):
                    for mi, (wm, wscol) in enumerate(((wg, 4), (wu, 5))):
                        wm_t = wm.rearrange("(n p) k -> n p k", p=128)
                        for ho in range(2):
                            for c in range(2):
                                par = 0
                                dma(s_wfb[par][:], wm_t[g * 4 + ho * 2 + c],
                                    reads=[wfbb[par]], writes=[wfbb[par]])
                                wcol = 0 if wscol == 4 else 1
                                p.op("dve", lambda e, pr=par, wc=wcol: e.tensor_scalar(
                                    s_wfb[pr][:], s_wfb[pr][:], s_ws[:, wc:wc + 1], MAGIC,
                                    ALU.mult, ALU.add), reads=[wfbb[par], wsb2], writes=[wfbb[par]])
                                p.op("dve", lambda e, pr=par: e.tensor_scalar(
                                    s_wfb[pr][:], s_wfb[pr][:], MAGIC, 1.0, ALU.subtract, ALU.min),
                                    reads=[wfbb[par]], writes=[wfbb[par]])
                                p.op("dve", lambda e, pr=par: e.tensor_scalar(
                                    s_wfq[:], s_wfb[pr][:], -1.0, None, ALU.max),
                                    reads=[wfbb[par]], writes=[wfqb])
                                for q in range(2):
                                    pst, pstb = (psDt, pDt[0]) if q % 2 == 0 else (psDt2, pDt[1])
                                    for cc2 in range(8):
                                        kcl = q * 8 + cc2
                                        tr(pst[:, cc2 * 128:(cc2 + 1) * 128],
                                           s_wfq[:, kcl * 128:(kcl + 1) * 128], s_idb[:],
                                           reads=[wfqb, cb], writes=[pstb])
                                    p.op("dve", lambda e, qq=q, ccx=c, pp=pst: e.tensor_copy(
                                        s_wqT[:].rearrange("p (kc o) -> p kc o", o=256)
                                        [:, qq * 8:(qq + 1) * 8, ccx * 128:(ccx + 1) * 128],
                                        pp[:].rearrange("p (kc t) -> p kc t", t=128)),
                                        reads=[pstb], writes=[wqTb])
                            for tg in range(2):
                                for kc in range(16):
                                    for ji in range(4):
                                        j = tg * 4 + ji
                                        mm(psD[:, ji * 512 + ho * 256: ji * 512 + ho * 256 + 256],
                                           s_pool[:, kc * 1024 + j * 128: kc * 1024 + j * 128 + 128],
                                           s_wqT[:, kc * 256:(kc + 1) * 256], kc == 0, kc == 15,
                                           reads=[poolb, wqTb], writes=[pD[ji]])
                                for ji in range(4):
                                    j = tg * 4 + ji
                                    psl = psD[:, ji * 512 + ho * 256: ji * 512 + ho * 256 + 256]
                                    c0 = g * 512 + ho * 256
                                    if mi == 0:
                                        p.op("act", lambda e, jj=j, ps=psl, h0=ho: e.activation(
                                            s_gg[:, jj * 512 + h0 * 256: jj * 512 + h0 * 256 + 256],
                                            ps, AF.Silu, scale=s_sxw[:, jj:jj + 1]),
                                            reads=[pD[ji], sxwb], writes=[ggb])
                                        if t_gate is not None:
                                            dma(t_gate.rearrange("(n p) f -> n p f", p=128)[j, :, c0:c0 + 256],
                                                s_gg[:, j * 512 + ho * 256: j * 512 + ho * 256 + 256],
                                                reads=[ggb])
                                    else:
                                        p.op("dve", lambda e, jj=j, ps=psl: e.tensor_scalar(
                                            s_gstg[:, 0:256], ps,
                                            s_sxw[:, 8 + jj:9 + jj], None, ALU.mult),
                                            reads=[pD[ji], sxwb], writes=[gstgb])
                                        p.op("dve", lambda e, jj=j, h0=ho: e.tensor_mul(
                                            s_gu2[:, 0:256],
                                            s_gg[:, jj * 512 + h0 * 256: jj * 512 + h0 * 256 + 256],
                                            s_gstg[:, 0:256]),
                                            reads=[ggb, gstgb], writes=[gu2b])
                                        p.op("dve", lambda e: e.reduce_max(
                                            s_rmax[:, 8:9], s_gu2[:, 0:256], axis=AX,
                                            apply_absolute_value=True),
                                            reads=[gu2b], writes=[rmaxb])
                                        p.op("dve", lambda e, jj=j: e.tensor_max(
                                            s_rmax[:, jj:jj + 1], s_rmax[:, jj:jj + 1],
                                            s_rmax[:, 8:9]),
                                            reads=[rmaxb], writes=[rmaxb])
                                        dma(gu_dram[j, :, c0:c0 + 256], s_gu2[:, 0:256],
                                            reads=[gu2b], writes=[gudb])
                                        if t_gu is not None:
                                            dma(t_gu.rearrange("(n p) f -> n p f", p=128)[j, :, c0:c0 + 256],
                                                s_gu2[:, 0:256], reads=[gu2b])
                    for _ in range(4):
                        next(wdgen, None)
                for _ in wdgen:
                    pass
        p.fence()

        p.serial_now = "D2" in ser_phases
        if n_phases >= 6:
            with contextlib.ExitStack() as _esD2:
                _e2 = _esD2.enter_context
                s_gub = _e2(nc.sbuf_tensor("s_gub", [128, 4096], F32))
                s_xq2 = _e2(nc.sbuf_tensor("s_xq2", [128, 4096], BF16))
                s_wdT = [_e2(nc.sbuf_tensor(f"s_wdT{i}", [128, 512], BF16)) for i in range(2)]
                s_dcol = _e2(nc.sbuf_tensor("s_dcol", [128, 8], F32))
                s_xrl = _e2(nc.sbuf_tensor("s_xrl", [128, 512], F32))
                s_otmp = _e2(nc.sbuf_tensor("s_otmp", [128, 512], F32))
                psD2 = _e2(nc.psum_tensor("psD2", [128, 2048], F32))
                psDq = _e2(nc.psum_tensor("psDq", [128, 1024], BF16))
                psDq2 = _e2(nc.psum_tensor("psDq2", [128, 1024], BF16))
                gubb = B("gub"); xq2b = B("xq2")
                wdTb = [B("wdT0"), B("wdT1")]
                dcolb = B("dcol"); xrlb = B("xrl"); otmpb = B("otmp")
                pD2 = [B(f"psD2{j}") for j in range(4)]
                pDq = [B("psDq"), B("psDq2")]

                for half in range(2):
                    js = [half * 4 + i for i in range(4)]
                    for j in js:
                        p.op("dve", lambda e, jj=j: e.tensor_scalar(
                            s_sall[:, 8 + jj: 9 + jj], s_rmax[:, jj:jj + 1],
                            1.0 / 127.0, EPS_F, ALU.mult, ALU.add),
                            reads=[rmaxb], writes=[sallb])
                        p.op("dve", lambda e, jj=j: e.reciprocal(
                            s_sall[:, 16:17], s_sall[:, 8 + jj: 9 + jj]),
                            reads=[sallb], writes=[sallb])
                        for gq in range(2):
                            dma(s_gub[:], gu_dram[j, :, gq * 4096:(gq + 1) * 4096],
                                reads=[gudb, gubb], writes=[gubb])
                            p.op("dve", lambda e: e.tensor_scalar(
                                s_gub[:], s_gub[:], s_sall[:, 16:17], MAGIC, ALU.mult, ALU.add),
                                reads=[gubb, sallb], writes=[gubb])
                            p.op("dve", lambda e: e.tensor_scalar(
                                s_xq2[:], s_gub[:], MAGIC, None, ALU.subtract),
                                reads=[gubb], writes=[xq2b])
                            for q in range(4):
                                pst, pstb = (psDq, pDq[0]) if q % 2 == 0 else (psDq2, pDq[1])
                                for c in range(8):
                                    kcl = q * 8 + c
                                    tr(pst[:, c * 128:(c + 1) * 128],
                                       s_xq2[:, kcl * 128:(kcl + 1) * 128], s_idb[:],
                                       reads=[xq2b, cb], writes=[pstb])
                                p.op("dve", lambda e, jj=j, gg=gq, qq=q, pp=pst: e.tensor_copy(
                                    s_pool[:, (jj % 4) * 8192 + gg * 4096 + qq * 1024:
                                            (jj % 4) * 8192 + gg * 4096 + (qq + 1) * 1024], pp[:]),
                                    reads=[pstb], writes=[poolb])
                    for og in range(4):
                        for kc2 in range(64):
                            par = kc2 % 2
                            dma(s_wdT[par][:], wqdT_dram[kc2, :, og * 512:(og + 1) * 512],
                                reads=[wdTb[par], wqdb], writes=[wdTb[par]])
                            for ji, j in enumerate(js):
                                mm(psD2[:, ji * 512:(ji + 1) * 512],
                                   s_pool[:, (j % 4) * 8192 + kc2 * 128: (j % 4) * 8192 + kc2 * 128 + 128],
                                   s_wdT[par][:], kc2 == 0, kc2 == 63,
                                   reads=[poolb, wdTb[par]], writes=[pD2[ji]])
                        for ji, j in enumerate(js):
                            p.op("dve", lambda e, jj=j: e.tensor_mul(
                                s_dcol[:, jj % 4:jj % 4 + 1], s_sall[:, 8 + jj: 9 + jj],
                                s_ws[:, 6:7]), reads=[sallb, wsb2], writes=[dcolb])
                            dma(s_xrl[:], x_t[j + 1][:, og * 512:(og + 1) * 512],
                                reads=[xrlb], writes=[xrlb])
                            p.op("dve", lambda e, jj=j, ji_=ji, o=og: e.scalar_tensor_tensor(
                                s_otmp[:], psD2[:, ji_ * 512:(ji_ + 1) * 512],
                                s_dcol[:, jj % 4:jj % 4 + 1],
                                s_attn[:, jj * 2048 + o * 512: jj * 2048 + (o + 1) * 512],
                                ALU.mult, ALU.add),
                                reads=[pD2[ji], dcolb, attnb], writes=[otmpb])
                            p.op("dve", lambda e: e.tensor_add(s_otmp[:], s_otmp[:], s_xrl[:]),
                                 reads=[otmpb, xrlb], writes=[otmpb])
                            dma(y_t[j][:, og * 512:(og + 1) * 512], s_otmp[:],
                                reads=[otmpb], writes=[outb])

    if n_phases < 6:
        with nc.sbuf_tensor("s_dummy", [128, 2048], F32) as s_dummy:
            p.op("dve", lambda e: e.memset(s_dummy[:], 0.0), writes=[B("dummy")])
            for j in range(NT):
                dma(y_t[j], s_dummy[:], reads=[B("dummy")], writes=[outb])

    p.truncate = truncate
    p.finalize()
    return nc, tap_outs



def _consts(inputs):
    rope_freqs = np.asarray(inputs["L0_attn_rope_freqs"], np.float32)
    B_bias = np.asarray(inputs["L0_attn_B_bias"], np.float32)
    C_bias = np.asarray(inputs["L0_attn_C_bias"], np.float32)
    D_param = np.asarray(inputs["L0_attn_D_param"], np.float32)
    norm_w = np.asarray(inputs["L0_attn_norm_w"], np.float32)
    dt_bias = np.asarray(inputs["L0_attn_dt_bias"], np.float32)
    A_log = np.asarray(inputs["L0_attn_A_log"], np.float32)

    r32f = np.zeros((64, 2048), np.float32)
    for h in range(64):
        r32f[h, h * 32:(h + 1) * 32] = rope_freqs[h]
    rn_ev = np.zeros((64, 2048), ml_dtypes.bfloat16)
    rn_od = np.zeros((64, 2048), ml_dtypes.bfloat16)
    for h in range(64):
        for f in range(32):
            rn_ev[2 * f, h * 32 + f] = 1
            rn_od[2 * f + 1, h * 32 + f] = 1
    bneg = np.where(np.arange(128)[:, None] <= np.arange(128)[None, :], 0.0,
                    BIGNEG).astype(np.float32)
    return {
        "dt_bias_r": dt_bias.reshape(1, 64),
        "a_log_c": A_log.reshape(64, 1),
        "d_rep_r": np.repeat(D_param, 64).reshape(1, 4096),
        "bb_ev": np.ascontiguousarray(B_bias[:, 0::2]).reshape(1, 2048),
        "bb_od": np.ascontiguousarray(B_bias[:, 1::2]).reshape(1, 2048),
        "cb_ev": np.ascontiguousarray(C_bias[:, 0::2]).reshape(1, 2048),
        "cb_od": np.ascontiguousarray(C_bias[:, 1::2]).reshape(1, 2048),
        "normw_r": norm_w.reshape(1, 4096),
        "r32f": r32f,
        "rn_ev": rn_ev,
        "rn_od": rn_od,
        "idf": np.eye(128, dtype=np.float32),
        "idb": np.eye(128, dtype=ml_dtypes.bfloat16),
        "onesf": np.ones((1, 128), np.float32),
        "actc": np.broadcast_to(np.array([[1e-5, 1.0, 1e-6, 0.5 * math.pi, 0, 0, 0, 0]],
                                         np.float32), (128, 8)).copy(),
        "bnegp": bneg,
    }


def make_in_maps(inputs):
    x = np.asarray(inputs["L0_attn_x"], np.float32).reshape(8192, 2048)
    w_in = np.ascontiguousarray(np.asarray(inputs["L0_attn_W_in"], np.float32))
    w_out = np.ascontiguousarray(np.asarray(inputs["L0_attn_W_out"], np.float32))
    wgm = np.ascontiguousarray(np.asarray(inputs["L0_ffn_w_gate"], np.float32))
    wum = np.ascontiguousarray(np.asarray(inputs["L0_ffn_w_up"], np.float32))
    wdm = np.ascontiguousarray(np.asarray(inputs["L0_ffn_w_down"], np.float32))
    consts = _consts(inputs)
    in_maps = []
    zrow = np.zeros((128, 2048), np.float32)
    for k in range(8):
        m = dict(consts)
        halo = zrow if k == 0 else x[k * 1024 - 128: k * 1024]
        m["x"] = np.ascontiguousarray(np.concatenate([halo, x[k * 1024:(k + 1) * 1024]], axis=0))
        m["w_in"] = w_in
        m["w_out"] = w_out
        m["wg"] = wgm
        m["wu"] = wum
        m["wd"] = wdm
        in_maps.append(m)
    return in_maps


def kernel(**inputs):
    nc, _ = build_program()
    in_maps = make_in_maps(inputs)
    res = run_bass_kernel_spmd(nc, in_maps, core_ids=list(range(8)))
    out = np.concatenate([np.asarray(r["y"]) for r in res.results], axis=0)
    return out.reshape(1, 8192, 2048).astype(np.float32)


# version 06
# speedup vs baseline: 1.7000x; 1.0389x over previous
import contextlib
import math
import numpy as np
import ml_dtypes
import concourse.bass as bass
import concourse.mybir as mybir
from concourse.bass_utils import run_bass_kernel_spmd

F32 = mybir.dt.float32
BF16 = mybir.dt.bfloat16
AF = mybir.ActivationFunctionType
ALU = mybir.AluOpType
AX = mybir.AxisListType.X

MAGIC = float(np.float32(1.5 * 2 ** 23))
PI = float(np.float32(math.pi))
TWO_PI = 6.283185307179586
INV_2PI = float(np.float32(1.0 / TWO_PI))
CW1 = float(np.float32(6.28125))
_r1 = TWO_PI - 6.28125
CW2 = float(np.float32(_r1))
CW3 = float(np.float32(_r1 - float(np.float32(_r1))))
BIGNEG = -1e30

D_MODEL = 2048
N_HEADS = 64
D_INNER = 4096
D_IN_PROJ = 8384
D_FFN = 8192
EPS_A = 1e-6
EPS_F = 1e-5
EPS_LN = 1e-5

TOK = 1024
NT = 8
NTH = 9
TOKH = NTH * 128

NPH = 99


class Prog:

    def __init__(self, nc):
        self.nc = nc
        self.engines = {
            "pe": nc.tensor, "act": nc.scalar, "dve": nc.vector,
            "pool": nc.gpsimd, "sp": nc.sync,
        }
        self.ops = []
        self.deps = []
        self.dma_key = []
        self.cur_fence = set()
        self.last_on_eng = {}

    def fence(self):
        self.cur_fence = set(self.last_on_eng.values())

    class Buf:
        __slots__ = ("name", "writer", "readers", "psum")
        def __init__(self, name, psum=False):
            self.name = name
            self.writer = None
            self.readers = set()
            self.psum = psum

    def buf(self, name, psum=False):
        return Prog.Buf(name, psum)

    serialize_all = False
    serial_now = False

    def op(self, eng, emit_fn, reads=(), writes=(), dma=False):
        idx = len(self.ops)
        dep = set()
        if (self.serialize_all or self.serial_now) and idx > 0:
            dep.add(idx - 1)
        for bb in reads:
            if bb.writer is not None:
                dep.add(bb.writer)
            if bb.psum:
                for r in bb.readers:
                    if self.ops[r][0] != eng:
                        dep.add(r)
        for bb in writes:
            if bb.writer is not None:
                dep.add(bb.writer)
            dep.update(bb.readers)
        dep.update(self.cur_fence)
        dep.discard(idx)
        self.ops.append((eng, emit_fn, 16 if dma else 1))
        self.dma_key.append((writes[0].name if writes else "tapout") if dma else None)
        self.deps.append(dep)
        self.last_on_eng[eng] = idx
        for bb in reads:
            bb.readers.add(idx)
        for bb in writes:
            bb.writer = idx
            bb.readers = set()
        return idx

    truncate = None

    def finalize(self):
        nc = self.nc
        n = len(self.ops)
        if self.truncate is not None:
            n = min(n, self.truncate)
        op_eng = [o[0] for o in self.ops]
        needs_inc = [o[2] == 16 for o in self.ops]
        waits = [[] for _ in range(n)]
        import os
        pe_pe = os.environ.get("K_PEPE", "0") == "1"
        for i in range(n):
            e = op_eng[i]
            for j in self.deps[i]:
                je = op_eng[j]
                if je == e and e == "pe" and not pe_pe:
                    continue
                needs_inc[j] = True
                waits[i].append(j)
        op_key = [None] * n
        for j in range(n):
            if self.ops[j][2] == 16:
                op_key[j] = ("d", self.dma_key[j] or f"anon{j}")
            else:
                op_key[j] = ("e", op_eng[j])
        sem_val = [0] * n
        cnt = {}
        for j in range(n):
            k = op_key[j]
            if needs_inc[j]:
                cnt[k] = cnt.get(k, 0) + self.ops[j][2]
            sem_val[j] = cnt.get(k, 0)
        self.final_counts = dict(cnt)
        sems = {}
        for j in range(n):
            k = op_key[j]
            if k not in sems:
                sems[k] = nc.semaphore(f"ps_{k[0]}_{k[1]}"[:28]).__enter__()
        seen = {}
        for i in range(n):
            e, emit_fn, inc = self.ops[i]
            byk = {}
            for j in waits[i]:
                k = op_key[j]
                v = sem_val[j]
                if v > byk.get(k, 0):
                    byk[k] = v
            for k in sorted(byk):
                v = byk[k]
                if seen.get((e, k), 0) >= v:
                    continue
                self.engines[e].wait_ge(sems[k], v)
                seen[(e, k)] = v
            inst = emit_fn(self.engines[e])
            if needs_inc[i]:
                inst.then_inc(sems[op_key[i]], inc)
        for k, v in cnt.items():
            if v > 0:
                self.engines["sp" if k[0] == "d" else k[1]].wait_ge(sems[k], v)


def build_program(n_phases=NPH, taps=(), truncate=None, serialize_all=False,
                  ser_phases=()):
    taps = set(taps)
    ser_phases = set(ser_phases)
    nc = bass.Bass("TRN2", target_bir_lowering=False)
    p = Prog(nc)
    p.serialize_all = serialize_all

    def din(name, shape, dtype=F32):
        return nc.dram_tensor(name, shape, dtype, kind="ExternalInput").ap()

    x_hbm = din("x", [TOKH, D_MODEL])
    w_in = din("w_in", [D_MODEL, D_IN_PROJ])
    w_out = din("w_out", [D_INNER, D_MODEL])
    wg = din("wg", [D_FFN, D_MODEL])
    wu = din("wu", [D_FFN, D_MODEL])
    wd = din("wd", [D_MODEL, D_FFN])
    dt_bias_r = din("dt_bias_r", [1, N_HEADS])
    a_log_c = din("a_log_c", [N_HEADS, 1])
    d_rep_r = din("d_rep_r", [1, D_INNER])
    bb_ev = din("bb_ev", [1, 2048])
    bb_od = din("bb_od", [1, 2048])
    cb_ev = din("cb_ev", [1, 2048])
    cb_od = din("cb_od", [1, 2048])
    normw_r = din("normw_r", [1, D_INNER])
    r32f = din("r32f", [N_HEADS, 2048])
    rn_ev = din("rn_ev", [N_HEADS, 2048], BF16)
    rn_od = din("rn_od", [N_HEADS, 2048], BF16)
    idf = din("idf", [128, 128])
    idb = din("idb", [128, 128], BF16)
    onesf = din("onesf", [1, 128])
    actc = din("actc", [128, 8])
    bnegp = din("bnegp", [128, 128])

    y_hbm = nc.dram_tensor("y", [TOK, D_MODEL], F32, kind="ExternalOutput").ap()

    xqT_dram = nc.dram_tensor("xqT_scr", [16, 128, TOK], BF16, kind="Internal").ap()
    xs_dram = nc.dram_tensor("xs_scr", [NTH, 128, D_INNER], BF16, kind="Internal").ap()
    gu_dram = nc.dram_tensor("gu_scr", [NT, 128, D_FFN], F32, kind="Internal").ap()
    wqdT_dram = nc.dram_tensor("wqdT_scr", [64, 128, D_MODEL], BF16, kind="Internal").ap()

    tap_outs = {}
    def tap(name, shape, dtype=F32):
        if name in taps:
            tap_outs[name] = nc.dram_tensor("tap_" + name, shape, dtype,
                                            kind="ExternalOutput").ap()
            return tap_outs[name]
        return None

    A = nc.alloc_sbuf_tensor

    s_pool = A("s_pool", [128, NT * 4096], BF16)
    s_raw = A("s_raw", [128, NTH * 192], F32)
    s_dt = A("s_dt", [128, NTH * 64], F32)
    s_slam = A("s_slam", [128, NTH * 64], F32)
    s_sdtT = A("s_sdtT", [64, NTH * 128], F32)
    s_slamT = A("s_slamT", [64, NTH * 128], F32)
    s_sall = A("s_sall", [128, 28], F32)
    s_rmax = A("s_rmax", [128, 12], F32)
    s_carry = A("s_carry", [64, 4], F32)
    s_idf = A("s_idf", [128, 128], F32)
    s_idb = A("s_idb", [128, 128], BF16)
    s_ones = A("s_ones", [1, 128], F32)
    s_onesc = A("s_onesc", [128, 1], F32)
    s_lam = A("s_lam", [64, 1], F32)
    s_ws = A("s_ws", [128, 8], F32)
    s_actc = A("s_actc", [128, 8], F32)

    b = {}
    def B(name):
        if name not in b:
            b[name] = p.buf(name, psum=name.startswith("ps"))
        return b[name]

    def dma(dst, src, reads=(), writes=(), q="sp"):
        return p.op(q, lambda e, d=dst, s=src: e.dma_start(out=d, in_=s),
                    reads=reads, writes=writes, dma=True)

    def mm(ps, lhsT, rhs, start, stop, reads=(), writes=()):
        return p.op("pe", lambda e, o=ps, l=lhsT, r=rhs, st=start, sp=stop:
                    e.matmul(o, l, r, start=st, stop=sp), reads=reads, writes=writes)

    def tr(ps, in_, ident, reads=(), writes=()):
        return p.op("pe", lambda e, o=ps, i=in_, d=ident: e.transpose(o, i, d),
                    reads=reads, writes=writes)

    cb = B("consts")
    lamb = B("lam")
    carb = B("carry")
    sallb = B("sall")
    rmaxb = B("rmax")
    rawb = B("raw")
    dtb_ = B("dt")
    slamb = B("slam")
    sdtTb = B("sdtT")
    slamTb = B("slamT")
    wsb2 = B("ws")
    xqTdb = B("xqT_dram")
    xsdb = B("xs_dram")
    gudb = B("gu_dram")
    wqdb = B("wqd_dram")
    poolb = B("pool")
    outb = B("out")

    dma(s_idf[:], idf, writes=[cb])
    dma(s_idb[:], idb, writes=[cb])
    dma(s_ones[:], onesf, writes=[cb])
    dma(s_actc[:], actc, writes=[cb])
    p.op("dve", lambda e: e.memset(s_onesc[:], 1.0), writes=[cb])
    p.op("dve", lambda e: e.memset(s_carry[:], 0.0), writes=[carb])
    p.op("dve", lambda e: e.memset(s_rmax[:], 0.0), writes=[rmaxb])

    with nc.sbuf_tensor("s_alog", [64, 1], F32) as s_alog:
        dma(s_alog[:], a_log_c, writes=[lamb])
        p.op("act", lambda e: e.activation(s_lam[:], s_alog[:], AF.Exp),
             reads=[lamb], writes=[lamb])
    p.fence()

    x_t = x_hbm.rearrange("(n p) f -> n p f", p=128)
    y_t = y_hbm.rearrange("(n p) f -> n p f", p=128)

    p.serial_now = "AB" in ser_phases
    with contextlib.ExitStack() as _esAB:
        _e = _esAB.enter_context
        s_xnT = _e(nc.sbuf_tensor("s_xnT", [128, NTH * 2048], BF16))
        s_wsb = [_e(nc.sbuf_tensor(f"s_wsb{i}", [128, 2048], F32)) for i in range(2)]
        s_wsjunk = _e(nc.sbuf_tensor("s_wsjunk", [128, 2048], F32))
        s_psums = _e(nc.sbuf_tensor("s_psums", [128, 4], F32))
        s_scal = _e(nc.sbuf_tensor("s_scal", [1, 8], F32))
        xnTb = B("xnT")
        wsbb = [B("wsb0"), B("wsb1")]
        wsjb = B("wsjunk")
        psmb = B("psums")
        scalb = B("scal")

        p.op("dve", lambda e: e.memset(s_psums[:], 0.0), writes=[psmb])

        def scale_gen():
            wg_v = wg.rearrange("(n p) k -> n p k", p=128)
            wu_v = wu.rearrange("(n p) k -> n p k", p=128)
            wd_v = wd.rearrange("(n p) k -> n p k", p=128)
            views = [
                (0, lambda i: wg_v[i], 64),
                (1, lambda i: wu_v[i], 64),
                (2, lambda i: wd_v[i // 4][:, (i % 4) * 2048:(i % 4 + 1) * 2048], 64),
            ]
            for col, view, ntl in views:
                for i in range(ntl):
                    par = i % 2
                    dma(s_wsb[par][:], view(i), reads=[wsbb[par]], writes=[wsbb[par]],
                        q="pool")
                    p.op("act", lambda e, pr=par, c=col: e.activation(
                        s_wsjunk[:], s_wsb[pr][:], AF.Abs,
                        accum_out=s_psums[:, 3:4]),
                        reads=[wsbb[par]], writes=[wsjb, psmb])
                    p.op("dve", lambda e, c=col: e.tensor_add(
                        s_psums[:, c:c + 1], s_psums[:, c:c + 1], s_psums[:, 3:4]),
                        reads=[psmb], writes=[psmb])
                    yield

        def scale_fin():
            with (nc.psum_tensor("ps_sc", [1, 8], F32) as ps_sc,
                  nc.psum_tensor("ps_ws", [128, 8], F32) as ps_ws):
                psc = B("ps_sc"); pws = B("ps_ws")
                mm(ps_sc[0:1, 0:3], s_onesc[:], s_psums[:, 0:3], True, True,
                   reads=[psmb, cb], writes=[psc])
                p.op("dve", lambda e: e.tensor_copy(s_scal[0:1, 0:3], ps_sc[0:1, 0:3]),
                     reads=[psc], writes=[scalb])
                nume = float(D_FFN * D_MODEL)
                p.op("dve", lambda e: e.tensor_scalar(
                    s_scal[0:1, 4:7], s_scal[0:1, 0:3], 1.0 / nume, EPS_F,
                    ALU.mult, ALU.add), reads=[scalb], writes=[scalb])
                p.op("dve", lambda e: e.reciprocal(s_scal[0:1, 0:3], s_scal[0:1, 4:7]),
                     reads=[scalb], writes=[scalb])
                mm(ps_ws[:, 0:8], s_ones[:], s_scal[0:1, 0:8], True, True,
                   reads=[scalb, cb], writes=[pws])
                p.op("dve", lambda e: e.tensor_copy(s_ws[:], ps_ws[:, 0:8]),
                     reads=[pws], writes=[wsb2])
            t_ws = tap("wscal", [128, 8])
            if t_ws is not None:
                dma(t_ws, s_ws[:], reads=[wsb2])

        sg = scale_gen()

        def pump(k):
            for _ in range(k):
                if next(sg, "END") == "END":
                    return

        t_xn = tap("xn", [TOK, D_MODEL])
        t_xq = tap("xq", [TOK, D_MODEL])
        t_raw = tap("raw", [TOK, 192])
        t_dt = tap("dt", [TOK, 64])
        t_sdt = tap("sdt", [64, TOK])
        t_slam = tap("slam", [64, TOK])

        with contextlib.ExitStack() as _esA:
            _e = _esA.enter_context
            s_wraw = _e(nc.sbuf_tensor("s_wraw", [128, 16 * 192], F32))
            s_dtb_bc = _e(nc.sbuf_tensor("s_dtb_bc", [128, 64], F32))
            s_xt = _e(nc.sbuf_tensor("s_xt", [128, 2048], F32))
            s_xn = _e(nc.sbuf_tensor("s_xn", [128, 2048], F32))
            s_xnTf = _e(nc.sbuf_tensor("s_xnTf", [128, 2048], F32))
            s_xq = _e(nc.sbuf_tensor("s_xq", [128, 2048], BF16))
            s_sq = _e(nc.sbuf_tensor("s_sq", [128, 2048], F32))
            s_c1 = _e(nc.sbuf_tensor("s_c1", [128, 8], F32))
            s_dtT = _e(nc.sbuf_tensor("s_dtT", [64, 128], F32))
            s_ldtT = _e(nc.sbuf_tensor("s_ldtT", [64, 128], F32))
            s_stg = _e(nc.sbuf_tensor("s_stg", [128, 512], BF16))
            psA0 = _e(nc.psum_tensor("psA0", [128, 512], F32))
            psA1 = _e(nc.psum_tensor("psA1", [128, 512], F32))
            psA2 = _e(nc.psum_tensor("psA2", [128, 512], BF16))
            psA3 = _e(nc.psum_tensor("psA3", [128, 192], F32))
            psA4 = _e(nc.psum_tensor("psA4", [64, 128], F32))
            psA5 = _e(nc.psum_tensor("psA5", [128, 64], F32))
            wrb = B("wraw"); dtbb = B("dtb_bc")
            xtb = B("xt"); xnb = B("xn"); xnTfb = B("xnTf"); xqb = B("xq")
            sqb = B("sq"); c1b = B("c1"); dtTb = B("dtT"); ldtTb = B("ldtT")
            stgb = B("stg")
            pA0 = B("psA0"); pA1 = B("psA1"); pA2 = B("psA2")
            pA3 = B("psA3"); pA4 = B("psA4"); pA5 = B("psA5")

            dma(s_wraw[:].rearrange("p (kc o) -> p kc o", kc=16),
                w_in[:, 8192:8384].rearrange("(kc p) o -> p kc o", p=128),
                writes=[wrb])
            with nc.sbuf_tensor("s_row64", [1, 64], F32) as s_row64:
                r64b = B("row64")
                dma(s_row64[:], dt_bias_r, writes=[r64b])
                mm(psA5[:], s_ones[:], s_row64[0:1, 0:64], True, True,
                   reads=[r64b, cb], writes=[pA5])
                p.op("dve", lambda e: e.tensor_copy(s_dtb_bc[:], psA5[:]),
                     reads=[pA5], writes=[dtbb])

            for j in range(NTH):
                jr = j - 1
                dma(s_xt[:], x_t[j], reads=[xtb], writes=[xtb])
                p.op("dve", lambda e: e.reduce_sum(s_c1[:, 0:1], s_xt[:], axis=AX),
                     reads=[xtb], writes=[c1b])
                p.op("act", lambda e: e.activation(s_sq[:], s_xt[:], AF.Square,
                                                   accum_out=s_c1[:, 1:2]),
                     reads=[xtb], writes=[sqb, c1b])
                p.op("dve", lambda e: e.tensor_scalar(s_c1[:, 2:3], s_c1[:, 0:1], -1.0 / 2048.0, None, ALU.mult),
                     reads=[c1b], writes=[c1b])
                p.op("dve", lambda e: e.tensor_mul(s_c1[:, 3:4], s_c1[:, 2:3], s_c1[:, 2:3]),
                     reads=[c1b], writes=[c1b])
                p.op("dve", lambda e: e.tensor_scalar(s_c1[:, 4:5], s_c1[:, 1:2], 1.0 / 2048.0, None, ALU.mult),
                     reads=[c1b], writes=[c1b])
                p.op("dve", lambda e: e.tensor_sub(s_c1[:, 4:5], s_c1[:, 4:5], s_c1[:, 3:4]),
                     reads=[c1b], writes=[c1b])
                p.op("act", lambda e: e.activation(s_c1[:, 5:6], s_c1[:, 4:5], AF.Ln, bias=s_actc[:, 0:1]),
                     reads=[c1b, cb], writes=[c1b])
                p.op("act", lambda e: e.activation(s_c1[:, 5:6], s_c1[:, 5:6], AF.Exp, scale=-0.5),
                     reads=[c1b], writes=[c1b])
                p.op("dve", lambda e: e.tensor_scalar(s_xn[:], s_xt[:], s_c1[:, 2:3], s_c1[:, 5:6], ALU.add, ALU.mult),
                     reads=[xtb, c1b], writes=[xnb])
                if t_xn is not None and j > 0:
                    dma(t_xn.rearrange("(n p) f -> n p f", p=128)[jr], s_xn[:], reads=[xnb])
                if j > 0:
                    p.op("dve", lambda e: e.reduce_max(s_c1[:, 6:7], s_xn[:], axis=AX, apply_absolute_value=True),
                         reads=[xnb], writes=[c1b])
                    p.op("dve", lambda e: e.tensor_scalar(s_c1[:, 6:7], s_c1[:, 6:7], 1.0 / 127.0, EPS_F, ALU.mult, ALU.add),
                         reads=[c1b], writes=[c1b])
                    p.op("dve", lambda e, jj=jr: e.tensor_copy(s_sall[:, jj:jj + 1], s_c1[:, 6:7]),
                         reads=[c1b], writes=[sallb])
                    p.op("dve", lambda e: e.reciprocal(s_c1[:, 7:8], s_c1[:, 6:7]),
                         reads=[c1b], writes=[c1b])
                    p.op("dve", lambda e: e.tensor_scalar(s_sq[:], s_xn[:], s_c1[:, 7:8], MAGIC, ALU.mult, ALU.add),
                         reads=[xnb, c1b], writes=[sqb])
                    p.op("dve", lambda e: e.tensor_scalar(s_xq[:], s_sq[:], MAGIC, None, ALU.subtract),
                         reads=[sqb], writes=[xqb])
                    if t_xq is not None:
                        p.op("dve", lambda e: e.tensor_copy(s_sq[:], s_xq[:]), reads=[xqb], writes=[sqb])
                        dma(t_xq.rearrange("(n p) f -> n p f", p=128)[jr], s_sq[:], reads=[sqb])
                for q in range(4):
                    ps_h, pb_h = (psA0, pA0) if q % 2 == 0 else (psA1, pA1)
                    for c in range(4):
                        kc = q * 4 + c
                        tr(ps_h[:, c * 128:(c + 1) * 128], s_xn[:, kc * 128:(kc + 1) * 128],
                           s_idf[:], reads=[xnb, cb], writes=[pb_h])
                    base = q * 512
                    p.op("act", lambda e, b0=base, ph=ps_h: e.activation(
                        s_xnTf[:, b0:b0 + 512], ph[:, 0:512], AF.Identity),
                        reads=[pb_h], writes=[xnTfb])
                    p.op("dve", lambda e, b0=base, jj=j, ph=ps_h: e.tensor_copy(
                        s_xnT[:, jj * 2048 + b0: jj * 2048 + b0 + 512], ph[:, 0:512]),
                        reads=[pb_h], writes=[xnTb])
                if j > 0:
                    for q in range(4):
                        for c in range(4):
                            kc = q * 4 + c
                            tr(psA2[:, c * 128:(c + 1) * 128], s_xq[:, kc * 128:(kc + 1) * 128],
                               s_idb[:], reads=[xqb, cb], writes=[pA2])
                        p.op("dve", lambda e: e.tensor_copy(s_stg[:], psA2[:, 0:512]),
                             reads=[pA2], writes=[stgb])
                        for c in range(4):
                            kc = q * 4 + c
                            dma(xqT_dram[kc, :, jr * 128:(jr + 1) * 128], s_stg[:, c * 128:(c + 1) * 128],
                                reads=[stgb], writes=[xqTdb])
                for kc in range(16):
                    mm(psA3[:], s_xnTf[:, kc * 128:(kc + 1) * 128],
                       s_wraw[:, kc * 192:(kc + 1) * 192],
                       kc == 0, kc == 15, reads=[xnTfb, wrb], writes=[pA3])
                p.op("dve", lambda e, jj=j: e.tensor_copy(s_raw[:, jj * 192:(jj + 1) * 192], psA3[:]),
                     reads=[pA3], writes=[rawb])
                if t_raw is not None and j > 0:
                    dma(t_raw.rearrange("(n p) f -> n p f", p=128)[jr],
                        s_raw[:, j * 192:(j + 1) * 192], reads=[rawb])
                o0 = j * 192 + 128
                p.op("dve", lambda e, o=o0: e.tensor_add(s_sq[:, 0:64], s_raw[:, o:o + 64], s_dtb_bc[:]),
                     reads=[rawb, dtbb], writes=[sqb])
                p.op("dve", lambda e: e.tensor_scalar(s_sq[:, 64:128], s_sq[:, 0:64], 0.0, None, ALU.max),
                     reads=[sqb], writes=[sqb])
                p.op("act", lambda e: e.activation(s_sq[:, 128:192], s_sq[:, 0:64], AF.Abs),
                     reads=[sqb], writes=[sqb])
                p.op("act", lambda e: e.activation(s_sq[:, 128:192], s_sq[:, 128:192], AF.Exp, scale=-1.0),
                     reads=[sqb], writes=[sqb])
                p.op("act", lambda e: e.activation(s_sq[:, 128:192], s_sq[:, 128:192], AF.Ln, bias=s_actc[:, 1:2]),
                     reads=[sqb, cb], writes=[sqb])
                p.op("dve", lambda e, jj=j: e.tensor_add(s_dt[:, jj * 64:(jj + 1) * 64],
                                                         s_sq[:, 64:128], s_sq[:, 128:192]),
                     reads=[sqb], writes=[dtb_])
                if t_dt is not None and j > 0:
                    dma(t_dt.rearrange("(n p) f -> n p f", p=128)[jr],
                        s_dt[:, j * 64:(j + 1) * 64], reads=[dtb_])
                tr(psA4[:], s_dt[:, j * 64:(j + 1) * 64], s_idf[:], reads=[dtb_, cb], writes=[pA4])
                p.op("dve", lambda e: e.tensor_copy(s_dtT[:], psA4[:]), reads=[pA4], writes=[dtTb])
                p.op("dve", lambda e: e.tensor_scalar(s_ldtT[:], s_dtT[:], s_lam[:], None, ALU.mult),
                     reads=[dtTb, lamb], writes=[ldtTb])
                p.op("dve", lambda e, jj=j: e.tensor_tensor_scan(
                    s_sdtT[:, jj * 128:(jj + 1) * 128], s_dtT[:], s_dtT[:], s_carry[:, 0:1],
                    ALU.add, ALU.bypass), reads=[dtTb, carb], writes=[sdtTb])
                p.op("dve", lambda e, jj=j: e.tensor_tensor_scan(
                    s_slamT[:, jj * 128:(jj + 1) * 128], s_ldtT[:], s_ldtT[:], s_carry[:, 1:2],
                    ALU.add, ALU.bypass), reads=[ldtTb, carb], writes=[slamTb])
                p.op("dve", lambda e, jj=j: e.tensor_copy(
                    s_carry[:, 0:1], s_sdtT[:, jj * 128 + 127: jj * 128 + 128]),
                    reads=[sdtTb], writes=[carb])
                p.op("dve", lambda e, jj=j: e.tensor_copy(
                    s_carry[:, 1:2], s_slamT[:, jj * 128 + 127: jj * 128 + 128]),
                    reads=[slamTb], writes=[carb])
                tr(psA5[:], s_slamT[:, j * 128:(j + 1) * 128], s_idf[0:64, 0:64],
                   reads=[slamTb, cb], writes=[pA5])
                p.op("dve", lambda e, jj=j: e.tensor_copy(s_slam[:, jj * 64:(jj + 1) * 64], psA5[:]),
                     reads=[pA5], writes=[slamb])
                pump(8)
            if t_sdt is not None:
                dma(t_sdt, s_sdtT[:], reads=[sdtTb])
            if t_slam is not None:
                dma(t_slam, s_slamT[:], reads=[slamTb])

        p.fence()
        if n_phases >= 2:
            t_z = tap("z", [TOK, D_INNER])
            t_xs = tap("xs", [TOK, D_INNER])
            with contextlib.ExitStack() as _esB:
                _e = _esB.enter_context
                s_wstg = _e(nc.sbuf_tensor("s_wstg", [128, 512], F32))
                s_wblk = _e(nc.sbuf_tensor("s_wblk", [128, 16 * 512], BF16))
                s_bstg = _e(nc.sbuf_tensor("s_bstg", [128, 512], BF16))
                s_bstgf = _e(nc.sbuf_tensor("s_bstgf", [128, 512], F32))
                psB = _e(nc.psum_tensor("psB", [128, 4096], F32))
                wstgb = B("wstg")
                wblkb = B("wblk")
                bstgb = B("bstg")
                bstgfb = B("bstgf")
                pB = [B(f"psB{j}") for j in range(8)]
                w_in_t = w_in.rearrange("(kc p) o -> kc p o", p=128)
                for g in range(16):
                    is_z = g < 8
                    for kc in range(16):
                        dma(s_wstg[:], w_in_t[kc, :, g * 512:(g + 1) * 512],
                            reads=[wstgb], writes=[wstgb])
                        p.op("pool", lambda e, k0=kc: e.tensor_copy(
                            s_wblk[:, k0 * 512:(k0 + 1) * 512], s_wstg[:]),
                            reads=[wstgb], writes=[wblkb])
                    for kc in range(16):
                        for jr in range(8):
                            jt = jr + 1
                            mm(psB[:, jr * 512:(jr + 1) * 512],
                               s_xnT[:, (jt * 16 + kc) * 128:(jt * 16 + kc + 1) * 128],
                               s_wblk[:, kc * 512:(kc + 1) * 512], kc == 0, kc == 15,
                               reads=[xnTb, wblkb], writes=[pB[jr]])
                    for jr in range(8):
                        pslice = psB[:, jr * 512:(jr + 1) * 512]
                        if is_z:
                            p.op("act", lambda e, jj=jr, gg=g, ps=pslice: e.activation(
                                s_pool[:, jj * 4096 + gg * 512: jj * 4096 + (gg + 1) * 512],
                                ps, AF.Silu), reads=[pB[jr]], writes=[poolb])
                            if t_z is not None:
                                p.op("dve", lambda e, ps=pslice: e.tensor_copy(s_bstgf[:], ps),
                                     reads=[pB[jr]], writes=[bstgfb])
                                dma(t_z.rearrange("(n p) f -> n p f", p=128)[jr, :, g * 512:(g + 1) * 512],
                                    s_bstgf[:], reads=[bstgfb])
                        else:
                            gg = g - 8
                            p.op("dve", lambda e, ps=pslice: e.tensor_copy(s_bstg[:], ps),
                                 reads=[pB[jr]], writes=[bstgb])
                            dma(xs_dram[jr + 1, :, gg * 512:(gg + 1) * 512], s_bstg[:],
                                reads=[bstgb], writes=[xsdb])
                            if t_xs is not None:
                                p.op("dve", lambda e, ps=pslice: e.tensor_copy(s_bstgf[:], ps),
                                     reads=[pB[jr]], writes=[bstgfb])
                                dma(t_xs.rearrange("(n p) f -> n p f", p=128)[jr, :, gg * 512:(gg + 1) * 512],
                                    s_bstgf[:], reads=[bstgfb])
                    if not is_z:
                        gg = g - 8
                        for kc in range(16):
                            mm(psB[:, 0:512],
                               s_xnT[:, (0 * 16 + kc) * 128:(0 * 16 + kc + 1) * 128],
                               s_wblk[:, kc * 512:(kc + 1) * 512], kc == 0, kc == 15,
                               reads=[xnTb, wblkb], writes=[pB[0]])
                        p.op("dve", lambda e: e.tensor_copy(s_bstg[:], psB[:, 0:512]),
                             reads=[pB[0]], writes=[bstgb])
                        dma(xs_dram[0, :, gg * 512:(gg + 1) * 512], s_bstg[:],
                            reads=[bstgb], writes=[xsdb])
                    pump(10)
        pump(10 ** 9)
        scale_fin()

    p.fence()

    p.serial_now = "C" in ser_phases
    if n_phases >= 3:
        t_rb = tap("rbv", [TOK, 64], BF16)
        t_rbt = tap("rbtv", [64, TOK], BF16)
        t_i1 = tap("i1v", [TOK, 2048], BF16)
        t_i2 = tap("i2v", [TOK, 2048], BF16)
        t_sc = tap("sinv", [TOK, 2048], BF16)
        t_cc2 = tap("cosv", [TOK, 2048], BF16)
        t_bh = tap("bh", [TOK, D_INNER], BF16)
        t_ch = tap("ch", [TOK, D_INNER], BF16)
        t_y = tap("y_scan", [TOK, D_INNER], BF16)
        t_yn = tap("yn", [TOK, D_INNER])
        with contextlib.ExitStack() as _esC:
            _e = _esC.enter_context
            s_drep = _e(nc.sbuf_tensor("s_drep", [128, D_INNER], BF16))
            s_normw = _e(nc.sbuf_tensor("s_normw", [128, D_INNER], BF16))
            s_bbev = _e(nc.sbuf_tensor("s_bbev", [128, 2048], BF16))
            s_bbod = _e(nc.sbuf_tensor("s_bbod", [128, 2048], BF16))
            s_cbev = _e(nc.sbuf_tensor("s_cbev", [128, 2048], BF16))
            s_cbod = _e(nc.sbuf_tensor("s_cbod", [128, 2048], BF16))
            s_r32f = _e(nc.sbuf_tensor("s_r32f", [64, 2048], F32))
            s_rnev = _e(nc.sbuf_tensor("s_rnev", [64, 2048], BF16))
            s_rnod = _e(nc.sbuf_tensor("s_rnod", [64, 2048], BF16))
            s_bneg = _e(nc.sbuf_tensor("s_bneg", [128, 128], F32))
            s_xsj = _e(nc.sbuf_tensor("s_xsj", [128, 2048], BF16))
            s_flat = _e(nc.sbuf_tensor("s_flat", [1, 512], F32))
            s_sin = _e(nc.sbuf_tensor("s_sin", [128, 1024], BF16))
            s_cos = _e(nc.sbuf_tensor("s_cos", [128, 1024], BF16))
            s_phr = _e(nc.sbuf_tensor("s_phr", [128, 512], F32))
            s_k = _e(nc.sbuf_tensor("s_k", [128, 512], F32))
            s_ph2 = _e(nc.sbuf_tensor("s_ph2", [128, 512], F32))
            s_rbB = _e(nc.sbuf_tensor("s_rbB", [128, 64], BF16))
            s_rbC = _e(nc.sbuf_tensor("s_rbC", [128, 64], BF16))
            s_rmsBT = _e(nc.sbuf_tensor("s_rmsBT", [64, 128], BF16))
            s_rmsCT = _e(nc.sbuf_tensor("s_rmsCT", [64, 128], BF16))
            s_i1 = _e(nc.sbuf_tensor("s_i1", [128, 1024], BF16))
            s_i2 = _e(nc.sbuf_tensor("s_i2", [128, 1024], BF16))
            s_t1 = _e(nc.sbuf_tensor("s_t1", [128, 1024], BF16))
            s_t2 = _e(nc.sbuf_tensor("s_t2", [128, 1024], BF16))
            s_Bh = _e(nc.sbuf_tensor("s_Bh", [128, 2048], BF16))
            s_Ch = _e(nc.sbuf_tensor("s_Ch", [128, 2048], BF16))
            s_smB = _e(nc.sbuf_tensor("s_smB", [128, 2048], BF16))
            s_smC = _e(nc.sbuf_tensor("s_smC", [128, 2048], BF16))
            s_state = _e(nc.sbuf_tensor("s_state", [128, 2048], BF16))
            s_e1 = _e(nc.sbuf_tensor("s_e1", [128, 256], F32))
            s_W = _e(nc.sbuf_tensor("s_W", [128, 256], BF16))
            s_WG = _e(nc.sbuf_tensor("s_WG", [128, 256], BF16))
            s_U = _e(nc.sbuf_tensor("s_U", [128, 128], BF16))
            s_Bdec = _e(nc.sbuf_tensor("s_Bdec", [128, 128], BF16))
            s_tyi = _e(nc.sbuf_tensor("s_tyi", [128, 512], F32))
            s_xsd = _e(nc.sbuf_tensor("s_xsd", [128, 512], BF16))
            s_A8 = _e(nc.sbuf_tensor("s_A8", [128, 64], F32))
            s_Adec = _e(nc.sbuf_tensor("s_Adec", [128, 64], F32))
            s_rowL = _e(nc.sbuf_tensor("s_rowL", [1, 64], F32))
            s_rowP = _e(nc.sbuf_tensor("s_rowP", [1, 64], F32))
            s_Drow = _e(nc.sbuf_tensor("s_Drow", [1, 64], F32))
            s_Dmat = _e(nc.sbuf_tensor("s_Dmat", [128, 64], F32))
            s_ybuf = _e(nc.sbuf_tensor("s_ybuf", [128, D_INNER], BF16))
            s_gbuf = _e(nc.sbuf_tensor("s_gbuf", [128, 1024], F32))
            s_ynb = _e(nc.sbuf_tensor("s_ynb", [128, D_INNER], BF16))
            psC_a = _e(nc.psum_tensor("psC_a", [128, 512], F32))
            psC_b = _e(nc.psum_tensor("psC_b", [128, 512], F32))
            psC_t0 = _e(nc.psum_tensor("psC_t0", [128, 1024], BF16))
            psC_t1 = _e(nc.psum_tensor("psC_t1", [128, 1024], BF16))
            psC_g0 = _e(nc.psum_tensor("psC_g0", [128, 512], F32))
            psC_g1 = _e(nc.psum_tensor("psC_g1", [128, 512], F32))
            psC_y0 = _e(nc.psum_tensor("psC_y0", [128, 512], F32))
            psC_y1 = _e(nc.psum_tensor("psC_y1", [128, 512], F32))

            Cb = {}
            for nm in ("drep normw bbev bbod cbev cbod r32f rnev rnod bneg xsj flat "
                       "sin cos phr ph2 kbuf rbB rbC rmsBT rmsCT i1 i2 t1 t2 Bh Ch "
                       "smB smC e1 W WG tyi xsd A8 Adec rowL rowP Drow Dmat ybuf "
                       "gbuf ynb cc").split():
                Cb[nm] = B("c_" + nm)
            pCa = B("psC_a"); pCb = B("psC_b")
            pCt = [B("psC_t0"), B("psC_t1")]
            pCg = [B("psC_g0"), B("psC_g1")]
            pCy = [B("psC_y0"), B("psC_y1")]
            stb = [B(f"c_state{i}") for i in range(8)]
            Ub = [B(f"c_U{i}") for i in range(4)]
            Bdb = [B(f"c_Bd{i}") for i in range(4)]
            Wb = [B(f"c_W{i}") for i in range(4)]
            WGb = [B(f"c_WG{i}") for i in range(4)]
            e1b = [B(f"c_e1{i}") for i in range(4)]

            dma(s_bneg[:], bnegp, writes=[Cb["bneg"]])
            dma(s_r32f[:], r32f, writes=[Cb["r32f"]])
            dma(s_rnev[:], rn_ev, writes=[Cb["rnev"]])
            dma(s_rnod[:], rn_od, writes=[Cb["rnod"]])
            crb = Cb["flat"]
            def bcast(dst, src_row, width, wbuf):
                for g0 in range(0, width, 512):
                    w = min(512, width - g0)
                    ps = psC_a if (g0 // 512) % 2 == 0 else psC_b
                    pbx = pCa if (g0 // 512) % 2 == 0 else pCb
                    dma(s_flat[0:1, 0:w], src_row[0:1, g0:g0 + w],
                        reads=[crb], writes=[crb])
                    mm(ps[:, 0:w], s_ones[:], s_flat[0:1, 0:w], True, True,
                       reads=[crb, cb], writes=[pbx])
                    p.op("dve", lambda e, d=dst, g=g0, ww=w, pp=ps: e.tensor_copy(
                        d[:, g:g + ww], pp[:, 0:ww]), reads=[pbx], writes=[wbuf])
            bcast(s_drep, d_rep_r, 4096, Cb["drep"])
            bcast(s_normw, normw_r, 4096, Cb["normw"])
            bcast(s_bbev, bb_ev, 2048, Cb["bbev"])
            bcast(s_bbod, bb_od, 2048, Cb["bbod"])
            bcast(s_cbev, cb_ev, 2048, Cb["cbev"])
            bcast(s_cbod, cb_od, 2048, Cb["cbod"])
            p.op("dve", lambda e: e.memset(s_state[:], 0.0), writes=stb)
            p.op("dve", lambda e: e.memset(s_rowP[:], 0.0), writes=[Cb["rowP"]])

            def view32(t, blk):
                return t[:].rearrange("p (h g) -> p h g", g=64)[:, :, blk * 32:(blk + 1) * 32]

            for j in range(NTH):
                halo = (j == 0)
                p.serial_now = ("C" in ser_phases) or ("Cpro" in ser_phases)
                dma(s_rowL[:], s_slamT[:, j * 128 + 127: j * 128 + 128],
                    reads=[slamTb, Cb["rowL"]], writes=[Cb["rowL"]])
                if not halo:
                    mm(psC_a[:, 0:64], s_ones[:], s_rowP[0:1, 0:64], True, True,
                       reads=[Cb["rowP"], cb], writes=[pCa])
                    p.op("dve", lambda e, jj=j: e.tensor_sub(
                        s_A8[:], psC_a[:, 0:64], s_slam[:, jj * 64:(jj + 1) * 64]),
                        reads=[slamb, pCa], writes=[Cb["A8"]])
                    p.op("act", lambda e: e.activation(s_A8[:], s_A8[:], AF.Exp),
                         reads=[Cb["A8"]], writes=[Cb["A8"]])
                    p.op("dve", lambda e: e.tensor_scalar(s_A8[:], s_A8[:], 0.125, None, ALU.mult),
                         reads=[Cb["A8"]], writes=[Cb["A8"]])
                p.op("dve", lambda e: e.tensor_sub(s_Drow[:], s_rowP[:], s_rowL[:]),
                     reads=[Cb["rowP"], Cb["rowL"]], writes=[Cb["Drow"]])
                p.op("act", lambda e: e.activation(s_Drow[:], s_Drow[:], AF.Exp),
                     reads=[Cb["Drow"]], writes=[Cb["Drow"]])
                p.op("dve", lambda e: e.tensor_copy(s_rowP[:], s_rowL[:]),
                     reads=[Cb["rowL"]], writes=[Cb["rowP"]])
                mm(psC_a[:, 0:64], s_ones[:], s_Drow[0:1, 0:64], True, True,
                   reads=[Cb["Drow"], cb], writes=[pCa])
                p.op("dve", lambda e: e.tensor_copy(s_Dmat[:], psC_a[:, 0:64]),
                     reads=[pCa], writes=[Cb["Dmat"]])
                mm(psC_b[:, 0:64], s_ones[:], s_rowL[0:1, 0:64], True, True,
                   reads=[Cb["rowL"], cb], writes=[pCb])
                p.op("dve", lambda e, jj=j: e.tensor_sub(
                    s_Adec[:], s_slam[:, jj * 64:(jj + 1) * 64], psC_b[:, 0:64]),
                    reads=[slamb, pCb], writes=[Cb["Adec"]])
                p.op("act", lambda e: e.activation(s_Adec[:], s_Adec[:], AF.Exp),
                     reads=[Cb["Adec"]], writes=[Cb["Adec"]])
                for (src_off, rb_t, rb_b, rT_t, rT_b) in (
                        ((0, s_rbB, Cb["rbB"], s_rmsBT, Cb["rmsBT"]),
                         (64, s_rbC, Cb["rbC"], s_rmsCT, Cb["rmsCT"]))
                        if not halo else
                        ((0, s_rbB, Cb["rbB"], s_rmsBT, Cb["rmsBT"]),)):
                    off = j * 192 + src_off
                    p.op("act", lambda e, o=off: e.activation(
                        s_gbuf[:, 0:64], s_raw[:, o:o + 64], AF.Square,
                        accum_out=s_sall[:, 20:21]),
                        reads=[rawb], writes=[Cb["gbuf"], Cb["cc"]])
                    p.op("act", lambda e: e.activation(s_sall[:, 21:22], s_sall[:, 20:21], AF.Ln,
                                                       scale=1.0 / 64.0, bias=s_actc[:, 2:3]),
                         reads=[Cb["cc"], cb], writes=[Cb["cc"]])
                    p.op("act", lambda e: e.activation(s_sall[:, 21:22], s_sall[:, 21:22], AF.Exp, scale=-0.5),
                         reads=[Cb["cc"]], writes=[Cb["cc"]])
                    p.op("dve", lambda e, o=off, rt=rb_t: e.tensor_scalar(
                        rt[:], s_raw[:, o:o + 64], s_sall[:, 21:22], None, ALU.mult),
                        reads=[rawb, Cb["cc"]], writes=[rb_b])
                    tr(psC_t0[0:64, 0:128], rb_t[:], s_idb[:], reads=[rb_b, cb], writes=[pCt[0]])
                    p.op("dve", lambda e, rt=rT_t: e.tensor_copy(rt[:], psC_t0[0:64, 0:128]),
                         reads=[pCt[0]], writes=[rT_b])
                    if t_rb is not None and t_rbt is not None and rb_t is s_rbB and not halo:
                        dma(t_rb.rearrange("(n p) f -> n p f", p=128)[j - 1], rb_t[:], reads=[rb_b])
                        dma(t_rbt[:, (j - 1) * 128:j * 128], rT_t[:], reads=[rT_b])

                for h2 in range(2):
                    p.serial_now = ("C" in ser_phases) or ("Cbc" in ser_phases)
                    hb = h2 * 1024
                    dma(s_xsj[:], xs_dram[j][:, h2 * 2048:(h2 + 1) * 2048],
                        reads=[Cb["xsj"], xsdb], writes=[Cb["xsj"]])
                    tensors_bc = (
                        ((s_rmsBT, Cb["rmsBT"], s_rnev, s_rnod, s_bbev, Cb["bbev"],
                          s_bbod, Cb["bbod"], s_Bh, Cb["Bh"], s_smB, Cb["smB"], t_bh),
                         (s_rmsCT, Cb["rmsCT"], s_rnev, s_rnod, s_cbev, Cb["cbev"],
                          s_cbod, Cb["cbod"], s_Ch, Cb["Ch"], s_smC, Cb["smC"], t_ch))
                        if not halo else
                        ((s_rmsBT, Cb["rmsBT"], s_rnev, s_rnod, s_bbev, Cb["bbev"],
                          s_bbod, Cb["bbod"], s_Bh, Cb["Bh"], s_smB, Cb["smB"], None),))
                    for g in range(2):
                        ps, pbx = (psC_a, pCa) if g == 0 else (psC_b, pCb)
                        mm(ps[:], s_sdtT[:, j * 128:(j + 1) * 128],
                           s_r32f[:, hb + g * 512: hb + (g + 1) * 512], True, True,
                           reads=[sdtTb, Cb["r32f"]], writes=[pbx])
                        p.op("dve", lambda e, pp=ps: e.tensor_scalar(
                            s_k[:], pp[:], INV_2PI, MAGIC, ALU.mult, ALU.add),
                            reads=[pbx], writes=[Cb["kbuf"]])
                        p.op("dve", lambda e: e.tensor_scalar(s_k[:], s_k[:], MAGIC, None, ALU.subtract),
                             reads=[Cb["kbuf"]], writes=[Cb["kbuf"]])
                        p.op("dve", lambda e, pp=ps: e.scalar_tensor_tensor(
                            s_phr[:], s_k[:], -CW1, pp[:], ALU.mult, ALU.add),
                            reads=[pbx, Cb["kbuf"]], writes=[Cb["phr"]])
                        p.op("dve", lambda e: e.scalar_tensor_tensor(
                            s_ph2[:], s_k[:], -CW2, s_phr[:], ALU.mult, ALU.add),
                            reads=[Cb["phr"], Cb["kbuf"]], writes=[Cb["ph2"]])
                        p.op("dve", lambda e: e.scalar_tensor_tensor(
                            s_phr[:], s_k[:], -CW3, s_ph2[:], ALU.mult, ALU.add),
                            reads=[Cb["ph2"], Cb["kbuf"]], writes=[Cb["phr"]])
                        p.op("act", lambda e, gg=g: e.activation(
                            s_sin[:, gg * 512:(gg + 1) * 512], s_phr[:], AF.Sin),
                            reads=[Cb["phr"]], writes=[Cb["sin"]])
                        p.op("act", lambda e: e.activation(s_ph2[:], s_phr[:], AF.Abs),
                             reads=[Cb["phr"]], writes=[Cb["ph2"]])
                        p.op("act", lambda e, gg=g: e.activation(
                            s_cos[:, gg * 512:(gg + 1) * 512], s_ph2[:], AF.Sin,
                            scale=-1.0, bias=s_actc[:, 3:4]),
                            reads=[Cb["ph2"], cb], writes=[Cb["cos"]])
                    if t_sc is not None and not halo:
                        dma(t_sc.rearrange("(n p) f -> n p f", p=128)[j - 1, :, h2 * 1024:(h2 + 1) * 1024],
                            s_sin[:], reads=[Cb["sin"]])
                    if t_cc2 is not None and not halo:
                        dma(t_cc2.rearrange("(n p) f -> n p f", p=128)[j - 1, :, h2 * 1024:(h2 + 1) * 1024],
                            s_cos[:], reads=[Cb["cos"]])
                    for (rT_t, rT_b, pev, pod, bev, bevb, bod, bodb, out_t, out_b,
                         sm_t, sm_b, tp) in tensors_bc:
                        for (pat, pat_b, bias_t, bias_b, dst, dst_b) in (
                                (pev, Cb["rnev"], bev, bevb, s_i1, Cb["i1"]),
                                (pod, Cb["rnod"], bod, bodb, s_i2, Cb["i2"])):
                            for g in range(2):
                                ps, pbx = (psC_a, pCa) if g == 0 else (psC_b, pCb)
                                mm(ps[:], rT_t[:], pat[:, hb + g * 512: hb + (g + 1) * 512],
                                   True, True, reads=[rT_b, pat_b], writes=[pbx])
                                p.op("dve", lambda e, pp=ps, gg=g, bt=bias_t, dd=dst, hb_=hb:
                                     e.tensor_add(dd[:, gg * 512:(gg + 1) * 512], pp[:],
                                                  bt[:, hb_ + gg * 512: hb_ + (gg + 1) * 512]),
                                     reads=[pbx, bias_b], writes=[dst_b])
                        i1v = s_i1[:].rearrange("p (h f) -> p h f", f=32)
                        i2v = s_i2[:].rearrange("p (h f) -> p h f", f=32)
                        sv = s_sin[:].rearrange("p (h f) -> p h f", f=32)
                        cv = s_cos[:].rearrange("p (h f) -> p h f", f=32)
                        t1v = s_t1[:].rearrange("p (h f) -> p h f", f=32)
                        t2v = s_t2[:].rearrange("p (h f) -> p h f", f=32)
                        o1 = view32(out_t, 0)
                        o2 = view32(out_t, 1)
                        p.op("dve", lambda e, o=t1v, a=i1v, c=cv: e.tensor_mul(o, a, c),
                             reads=[Cb["i1"], Cb["cos"]], writes=[Cb["t1"]])
                        p.op("dve", lambda e, o=t2v, a=i2v, c=sv: e.tensor_mul(o, a, c),
                             reads=[Cb["i2"], Cb["sin"]], writes=[Cb["t2"]])
                        p.op("dve", lambda e, o=o1, a=t1v, c=t2v: e.tensor_sub(o, a, c),
                             reads=[Cb["t1"], Cb["t2"]], writes=[out_b])
                        p.op("dve", lambda e, o=t1v, a=i1v, c=sv: e.tensor_mul(o, a, c),
                             reads=[Cb["i1"], Cb["sin"]], writes=[Cb["t1"]])
                        p.op("dve", lambda e, o=t2v, a=i2v, c=cv: e.tensor_mul(o, a, c),
                             reads=[Cb["i2"], Cb["cos"]], writes=[Cb["t2"]])
                        p.op("dve", lambda e, o=o2, a=t1v, c=t2v: e.tensor_add(o, a, c),
                             reads=[Cb["t1"], Cb["t2"]], writes=[out_b])
                        if tp is not None and not halo:
                            dma(tp.rearrange("(n p) f -> n p f", p=128)[j - 1, :, h2 * 2048:(h2 + 1) * 2048],
                                out_t[:], reads=[out_b])
                        if t_i1 is not None and t_i2 is not None and out_t is s_Bh and not halo:
                            dma(t_i1.rearrange("(n p) f -> n p f", p=128)[j - 1, :, h2 * 1024:(h2 + 1) * 1024],
                                s_i1[:], reads=[Cb["i1"]])
                            dma(t_i2.rearrange("(n p) f -> n p f", p=128)[j - 1, :, h2 * 1024:(h2 + 1) * 1024],
                                s_i2[:], reads=[Cb["i2"]])
                        for half16 in (() if halo else range(2)):
                            pst, pstb = (psC_t0, pCt[0]) if half16 == 0 else (psC_t1, pCt[1])
                            for g2 in range(8):
                                gg2 = half16 * 8 + g2
                                tr(pst[:, g2 * 128:(g2 + 1) * 128],
                                   out_t[:, gg2 * 128:(gg2 + 1) * 128], s_idb[:],
                                   reads=[out_b, cb], writes=[pstb])
                            p.op("dve", lambda e, st=sm_t, hh=half16, pp=pst: e.tensor_copy(
                                st[:, hh * 1024:(hh + 1) * 1024], pp[:]),
                                reads=[pstb], writes=[sm_b])
                    p.serial_now = ("C" in ser_phases) or ("Cscan" in ser_phases)
                    for bb2 in range(16):
                        psm, psmb_ = (psC_a, pCa) if bb2 % 2 == 0 else (psC_b, pCb)
                        psg, psgb = (psC_g0, pCg[0]) if bb2 % 2 == 0 else (psC_g1, pCg[1])
                        hh0 = bb2 * 2
                        if hh0 % 4 == 0 and not halo:
                            h0g = h2 * 32 + hh0
                            dma(s_flat[:], s_slamT[h0g:h0g + 4, j * 128:(j + 1) * 128],
                                reads=[slamTb, Cb["flat"]], writes=[Cb["flat"]])
                        for q in range(2):
                            hh = hh0 + q
                            h = h2 * 32 + hh
                            hg = j * 64 + h
                            row0 = q * 64
                            g2c = bb2 * 128
                            bank = h // 8
                            psy, psyb = (psC_y0, pCy[0]) if bank % 2 == 0 else (psC_y1, pCy[1])
                            sl = (h % 8) * 64
                            scol = (h // 2) * 64
                            if halo:
                                p.op("dve", lambda e, qq=q, hl=hh, c=hg: e.tensor_scalar(
                                    s_U[:, qq * 64:(qq + 1) * 64],
                                    s_xsj[:, hl * 64:(hl + 1) * 64],
                                    s_dt[:, c:c + 1], None, ALU.mult),
                                    reads=[Cb["xsj"], dtb_], writes=[Ub[q]])
                                p.op("dve", lambda e, qq=q, hh_=hh, c=h: e.tensor_scalar(
                                    s_Bdec[:, qq * 64:(qq + 1) * 64],
                                    s_Bh[:, hh_ * 64:(hh_ + 1) * 64],
                                    s_Adec[:, c:c + 1], None, ALU.mult),
                                    reads=[Cb["Bh"], Cb["Adec"]], writes=[Bdb[q]])
                                mm(psm[0:64, 256 + q * 64: 256 + (q + 1) * 64],
                                   s_Bdec[:, q * 64:(q + 1) * 64],
                                   s_U[:, q * 64:(q + 1) * 64], True, True,
                                   reads=[Bdb[q], Ub[q]], writes=[psmb_])
                                p.op("dve", lambda e, r0=row0, sc=scol, hh_=h: e.tensor_scalar(
                                    s_state[r0:r0 + 64, sc:sc + 64],
                                    s_state[r0:r0 + 64, sc:sc + 64],
                                    s_Dmat[r0:r0 + 64, hh_:hh_ + 1], None, ALU.mult),
                                    reads=[stb[h // 8], Cb["Dmat"]], writes=[stb[h // 8]])
                                p.op("dve", lambda e, r0=row0, sc=scol, qq=q, pp=psm: e.tensor_add(
                                    s_state[r0:r0 + 64, sc:sc + 64],
                                    s_state[r0:r0 + 64, sc:sc + 64],
                                    pp[0:64, 256 + qq * 64: 256 + (qq + 1) * 64]),
                                    reads=[stb[h // 8], psmb_], writes=[stb[h // 8]])
                                continue
                            mm(psm[:, q * 128:(q + 1) * 128], s_ones[:],
                               s_flat[0:1, (hh % 4) * 128:(hh % 4 + 1) * 128], True, True,
                               reads=[Cb["flat"], cb], writes=[psmb_])
                            p.op("dve", lambda e, qq=q, pp=psm: e.scalar_tensor_tensor(
                                s_e1[:, qq * 128:(qq + 1) * 128], pp[:, qq * 128:(qq + 1) * 128],
                                -1.0, s_bneg[:], ALU.mult, ALU.add),
                                reads=[psmb_, Cb["bneg"]], writes=[e1b[q]])
                            p.op("act", lambda e, qq=q, c=hg: e.activation(
                                s_W[:, qq * 128:(qq + 1) * 128], s_e1[:, qq * 128:(qq + 1) * 128],
                                AF.Exp, bias=s_slam[:, c:c + 1]),
                                reads=[e1b[q], slamb], writes=[Wb[q]])
                            mm(psg[:, q * 128:(q + 1) * 128],
                               s_smB[row0:row0 + 64, g2c:g2c + 128],
                               s_smC[row0:row0 + 64, g2c:g2c + 128], True, True,
                               reads=[Cb["smB"], Cb["smC"]], writes=[psgb])
                            p.op("dve", lambda e, qq=q, pp=psg: e.tensor_mul(
                                s_WG[:, qq * 128:(qq + 1) * 128],
                                s_W[:, qq * 128:(qq + 1) * 128],
                                pp[:, qq * 128:(qq + 1) * 128]),
                                reads=[Wb[q], psgb], writes=[WGb[q]])
                            p.op("dve", lambda e, qq=q, hl=hh, c=hg: e.tensor_scalar(
                                s_U[:, qq * 64:(qq + 1) * 64],
                                s_xsj[:, hl * 64:(hl + 1) * 64],
                                s_dt[:, c:c + 1], None, ALU.mult),
                                reads=[Cb["xsj"], dtb_], writes=[Ub[q]])
                            p.op("dve", lambda e, qq=q, hh_=hh, c=h: e.tensor_scalar(
                                s_Bdec[:, qq * 64:(qq + 1) * 64],
                                s_Bh[:, hh_ * 64:(hh_ + 1) * 64],
                                s_Adec[:, c:c + 1], None, ALU.mult),
                                reads=[Cb["Bh"], Cb["Adec"]], writes=[Bdb[q]])
                            mm(psy[:, sl:sl + 64], s_WG[:, q * 128:(q + 1) * 128],
                               s_U[:, q * 64:(q + 1) * 64], True, True,
                               reads=[WGb[q], Ub[q]], writes=[psyb])
                            mm(psg[:, 256 + q * 64: 256 + (q + 1) * 64],
                               s_smC[row0:row0 + 64, g2c:g2c + 128],
                               s_state[row0:row0 + 64, scol:scol + 64], True, True,
                               reads=[Cb["smC"], stb[h // 8]], writes=[psgb])
                            p.op("dve", lambda e, hh_=h, hl=hh: e.tensor_mul(
                                s_xsd[:, (hh_ % 8) * 64:((hh_ % 8) + 1) * 64],
                                s_xsj[:, hl * 64:(hl + 1) * 64],
                                s_drep[:, hh_ * 64:(hh_ + 1) * 64]),
                                reads=[Cb["xsj"], Cb["drep"]], writes=[Cb["xsd"]])
                            p.op("dve", lambda e, qq=q, hh_=h, pp=psg: e.scalar_tensor_tensor(
                                s_tyi[:, (hh_ % 8) * 64:((hh_ % 8) + 1) * 64],
                                pp[:, 256 + qq * 64: 256 + (qq + 1) * 64],
                                s_A8[:, hh_:hh_ + 1],
                                s_xsd[:, (hh_ % 8) * 64:((hh_ % 8) + 1) * 64],
                                ALU.mult, ALU.add),
                                reads=[psgb, Cb["A8"], Cb["xsd"]], writes=[Cb["tyi"]])
                            mm(psm[0:64, 256 + q * 64: 256 + (q + 1) * 64],
                               s_Bdec[:, q * 64:(q + 1) * 64],
                               s_U[:, q * 64:(q + 1) * 64], True, True,
                               reads=[Bdb[q], Ub[q]], writes=[psmb_])
                            p.op("dve", lambda e, r0=row0, sc=scol, hh_=h: e.tensor_scalar(
                                s_state[r0:r0 + 64, sc:sc + 64],
                                s_state[r0:r0 + 64, sc:sc + 64],
                                s_Dmat[r0:r0 + 64, hh_:hh_ + 1], None, ALU.mult),
                                reads=[stb[h // 8], Cb["Dmat"]], writes=[stb[h // 8]])
                            p.op("dve", lambda e, r0=row0, sc=scol, qq=q, pp=psm: e.tensor_add(
                                s_state[r0:r0 + 64, sc:sc + 64],
                                s_state[r0:r0 + 64, sc:sc + 64],
                                pp[0:64, 256 + qq * 64: 256 + (qq + 1) * 64]),
                                reads=[stb[h // 8], psmb_], writes=[stb[h // 8]])
                        if (hh0 + 2) % 8 == 0 and not halo:
                            bank = (h2 * 32 + hh0) // 8
                            psy, psyb = (psC_y0, pCy[0]) if bank % 2 == 0 else (psC_y1, pCy[1])
                            p.op("dve", lambda e, bk=bank, pp=psy: e.scalar_tensor_tensor(
                                s_ybuf[:, bk * 512:(bk + 1) * 512], pp[:], 0.125,
                                s_tyi[:], ALU.mult, ALU.add),
                                reads=[psyb, Cb["tyi"]], writes=[Cb["ybuf"]])
                p.serial_now = ("C" in ser_phases) or ("Cgate" in ser_phases)
                if halo:
                    continue
                if t_y is not None:
                    dma(t_y.rearrange("(n p) f -> n p f", p=128)[j - 1], s_ybuf[:],
                        reads=[Cb["ybuf"]])
                for gh in range(4):
                    p.op("dve", lambda e, jj=j - 1, g0=gh: e.tensor_mul(
                        s_gbuf[:], s_ybuf[:, g0 * 1024:(g0 + 1) * 1024],
                        s_pool[:, jj * 4096 + g0 * 1024: jj * 4096 + (g0 + 1) * 1024]),
                        reads=[Cb["ybuf"], poolb], writes=[Cb["gbuf"]])
                    p.op("act", lambda e, g0=gh: e.activation(
                        s_ynb[:, g0 * 1024:(g0 + 1) * 1024], s_gbuf[:], AF.Square,
                        accum_out=s_sall[:, 24 + g0:25 + g0]),
                        reads=[Cb["gbuf"]], writes=[Cb["ynb"], Cb["cc"]])
                p.op("dve", lambda e: e.tensor_add(s_sall[:, 24:25], s_sall[:, 24:25], s_sall[:, 25:26]),
                     reads=[Cb["cc"]], writes=[Cb["cc"]])
                p.op("dve", lambda e: e.tensor_add(s_sall[:, 26:27], s_sall[:, 26:27], s_sall[:, 27:28]),
                     reads=[Cb["cc"]], writes=[Cb["cc"]])
                p.op("dve", lambda e: e.tensor_add(s_sall[:, 22:23], s_sall[:, 24:25], s_sall[:, 26:27]),
                     reads=[Cb["cc"]], writes=[Cb["cc"]])
                p.op("act", lambda e: e.activation(s_sall[:, 23:24], s_sall[:, 22:23], AF.Ln,
                                                   scale=1.0 / float(D_INNER), bias=s_actc[:, 2:3]),
                     reads=[Cb["cc"], cb], writes=[Cb["cc"]])
                p.op("act", lambda e: e.activation(s_sall[:, 23:24], s_sall[:, 23:24], AF.Exp, scale=-0.5),
                     reads=[Cb["cc"]], writes=[Cb["cc"]])
                for gh in range(4):
                    p.op("dve", lambda e, jj=j - 1, g0=gh: e.tensor_mul(
                        s_gbuf[:], s_ybuf[:, g0 * 1024:(g0 + 1) * 1024],
                        s_pool[:, jj * 4096 + g0 * 1024: jj * 4096 + (g0 + 1) * 1024]),
                        reads=[Cb["ybuf"], poolb], writes=[Cb["gbuf"]])
                    p.op("dve", lambda e, g0=gh: e.scalar_tensor_tensor(
                        s_ynb[:, g0 * 1024:(g0 + 1) * 1024], s_gbuf[:], s_sall[:, 23:24],
                        s_normw[:, g0 * 1024:(g0 + 1) * 1024], ALU.mult, ALU.mult),
                        reads=[Cb["gbuf"], Cb["cc"], Cb["normw"]], writes=[Cb["ynb"]])
                if t_yn is not None:
                    for gh in range(4):
                        p.op("dve", lambda e, g0=gh: e.tensor_copy(
                            s_gbuf[:], s_ynb[:, g0 * 1024:(g0 + 1) * 1024]),
                            reads=[Cb["ynb"]], writes=[Cb["gbuf"]])
                        dma(t_yn.rearrange("(n p) f -> n p f", p=128)[j - 1, :, gh * 1024:(gh + 1) * 1024],
                            s_gbuf[:], reads=[Cb["gbuf"]])
                for q in range(4):
                    pst, pstb = (psC_t0, pCt[0]) if q % 2 == 0 else (psC_t1, pCt[1])
                    for c in range(8):
                        kc = q * 8 + c
                        tr(pst[:, c * 128:(c + 1) * 128], s_ynb[:, kc * 128:(kc + 1) * 128],
                           s_idb[:], reads=[Cb["ynb"], cb], writes=[pstb])
                    p.op("dve", lambda e, jj=j - 1, qq=q, pp=pst: e.tensor_copy(
                        s_pool[:, jj * 4096 + qq * 1024: jj * 4096 + (qq + 1) * 1024], pp[:]),
                        reads=[pstb], writes=[poolb])
    p.fence()

    p.serial_now = "E" in ser_phases
    with contextlib.ExitStack() as _esED:
        _e = _esED.enter_context
        s_attn = _e(nc.sbuf_tensor("s_attn", [128, NT * 2048], F32))
        attnb = B("attn")
        if n_phases >= 4:
            t_attn = tap("attn_out", [TOK, D_MODEL])
            with contextlib.ExitStack() as _esE:
                _e2 = _esE.enter_context
                s_web = [_e2(nc.sbuf_tensor(f"s_web{i}", [128, 512], F32)) for i in range(2)]
                s_webb = [_e2(nc.sbuf_tensor(f"s_webb{i}", [128, 512], BF16)) for i in range(2)]
                psE = _e2(nc.psum_tensor("psE", [128, 4096], F32))
                webb = [B("web0"), B("web1")]
                webbb = [B("webb0"), B("webb1")]
                pE = [B(f"psE{j}") for j in range(8)]
                w_out_t = w_out.rearrange("(kc p) o -> kc p o", p=128)
                for og in range(4):
                    for kc in range(32):
                        par = kc % 2
                        dma(s_web[par][:], w_out_t[kc, :, og * 512:(og + 1) * 512],
                            reads=[webb[par]], writes=[webb[par]])
                        p.op("pool", lambda e, pr=par: e.tensor_copy(s_webb[pr][:], s_web[pr][:]),
                             reads=[webb[par]], writes=[webbb[par]])
                        for j in range(NT):
                            mm(psE[:, j * 512:(j + 1) * 512],
                               s_pool[:, j * 4096 + kc * 128: j * 4096 + (kc + 1) * 128],
                               s_webb[par][:], kc == 0, kc == 31,
                               reads=[poolb, webbb[par]], writes=[pE[j]])
                    for j in range(NT):
                        p.op("dve", lambda e, jj=j, o=og: e.tensor_copy(
                            s_attn[:, jj * 2048 + o * 512: jj * 2048 + (o + 1) * 512],
                            psE[:, jj * 512:(jj + 1) * 512]),
                            reads=[pE[j]], writes=[attnb])
            if t_attn is not None:
                for j in range(NT):
                    dma(t_attn.rearrange("(n p) f -> n p f", p=128)[j],
                        s_attn[:, j * 2048:(j + 1) * 2048], reads=[attnb])
        p.fence()

        p.serial_now = "D1" in ser_phases
        if n_phases >= 5:
            t_gu = tap("gu", [TOK, D_FFN])
            t_gate = tap("gatev", [TOK, D_FFN])
            for kc in range(16):
                dma(s_pool[:, kc * 1024:(kc + 1) * 1024], xqT_dram[kc],
                    reads=[xqTdb, poolb], writes=[poolb])
            with contextlib.ExitStack() as _esD1:
                _e2 = _esD1.enter_context
                s_wfb = [_e2(nc.sbuf_tensor("s_wfb0", [128, 2048], F32))]
                s_wfq = _e2(nc.sbuf_tensor("s_wfq", [128, 2048], BF16))
                s_wqT = _e2(nc.sbuf_tensor("s_wqT", [128, 16 * 256], BF16))
                s_gg = _e2(nc.sbuf_tensor("s_gg", [128, 4096], F32))
                s_sxw = _e2(nc.sbuf_tensor("s_sxw", [128, 16], F32))
                s_gu2 = _e2(nc.sbuf_tensor("s_gu2", [128, 512], F32))
                s_gstg = _e2(nc.sbuf_tensor("s_gstg", [128, 512], F32))
                s_wdb = _e2(nc.sbuf_tensor("s_wdb", [128, 2048], F32))
                s_wdq = _e2(nc.sbuf_tensor("s_wdq", [128, 2048], BF16))
                s_wdstg = _e2(nc.sbuf_tensor("s_wdstg", [128, 1024], BF16))
                psD = _e2(nc.psum_tensor("psD", [128, 2048], F32))
                psDt = _e2(nc.psum_tensor("psDt", [128, 1024], BF16))
                psDt2 = _e2(nc.psum_tensor("psDt2", [128, 1024], BF16))
                wfbb = [B("wfb0")]
                wfqb = B("wfq")
                wqTb = B("wqT")
                ggb = B("gg")
                sxwb = B("sxw")
                gstgb = B("gstg")
                wdbb = B("wdb"); wdqb = B("wdq"); wdstgb = B("wdstg")
                pD = [B(f"psD{j}") for j in range(4)]
                pDt = [B("psDt"), B("psDt2")]

                gu2b = B("gu2")
                for j in range(NT):
                    p.op("dve", lambda e, jj=j: e.tensor_mul(
                        s_sxw[:, jj:jj + 1], s_sall[:, jj:jj + 1], s_ws[:, 4:5]),
                        reads=[sallb, wsb2], writes=[sxwb])
                    p.op("dve", lambda e, jj=j: e.tensor_mul(
                        s_sxw[:, 8 + jj:9 + jj], s_sall[:, jj:jj + 1], s_ws[:, 5:6]),
                        reads=[sallb, wsb2], writes=[sxwb])

                def wd_prepass():
                    wd_v3 = wd.rearrange("(n p) k -> n p k", p=128)
                    for blk in range(64):
                        r, quar = blk // 4, blk % 4
                        dma(s_wdb[:], wd_v3[r][:, quar * 2048:(quar + 1) * 2048],
                            reads=[wdbb], writes=[wdbb], q="pool")
                        p.op("dve", lambda e: e.tensor_scalar(
                            s_wdb[:], s_wdb[:], s_ws[:, 2:3], MAGIC, ALU.mult, ALU.add),
                            reads=[wdbb, wsb2], writes=[wdbb])
                        p.op("dve", lambda e: e.tensor_scalar(
                            s_wdb[:], s_wdb[:], MAGIC, 1.0, ALU.subtract, ALU.min),
                            reads=[wdbb], writes=[wdbb])
                        p.op("dve", lambda e: e.tensor_scalar(
                            s_wdq[:], s_wdb[:], -1.0, None, ALU.max),
                            reads=[wdbb], writes=[wdqb])
                        for q in range(2):
                            pst, pstb = (psDt, pDt[0]) if q % 2 == 0 else (psDt2, pDt[1])
                            for c in range(8):
                                kcl = q * 8 + c
                                tr(pst[:, c * 128:(c + 1) * 128],
                                   s_wdq[:, kcl * 128:(kcl + 1) * 128], s_idb[:],
                                   reads=[wdqb, cb], writes=[pstb])
                            p.op("dve", lambda e, pp=pst: e.tensor_copy(s_wdstg[:], pp[:]),
                                 reads=[pstb], writes=[wdstgb])
                            for c in range(8):
                                kcl = q * 8 + c
                                kc2 = quar * 16 + kcl
                                dma(wqdT_dram[kc2, :, r * 128:(r + 1) * 128],
                                    s_wdstg[:, c * 128:(c + 1) * 128],
                                    reads=[wdstgb], writes=[wqdb], q="pool")
                        yield

                wdgen = wd_prepass()

                for g in range(16):
                    for mi, (wm, wscol) in enumerate(((wg, 4), (wu, 5))):
                        wm_t = wm.rearrange("(n p) k -> n p k", p=128)
                        for ho in range(2):
                            for c in range(2):
                                par = 0
                                dma(s_wfb[par][:], wm_t[g * 4 + ho * 2 + c],
                                    reads=[wfbb[par]], writes=[wfbb[par]])
                                wcol = 0 if wscol == 4 else 1
                                p.op("dve", lambda e, pr=par, wc=wcol: e.tensor_scalar(
                                    s_wfb[pr][:], s_wfb[pr][:], s_ws[:, wc:wc + 1], MAGIC,
                                    ALU.mult, ALU.add), reads=[wfbb[par], wsb2], writes=[wfbb[par]])
                                p.op("dve", lambda e, pr=par: e.tensor_scalar(
                                    s_wfb[pr][:], s_wfb[pr][:], MAGIC, 1.0, ALU.subtract, ALU.min),
                                    reads=[wfbb[par]], writes=[wfbb[par]])
                                p.op("dve", lambda e, pr=par: e.tensor_scalar(
                                    s_wfq[:], s_wfb[pr][:], -1.0, None, ALU.max),
                                    reads=[wfbb[par]], writes=[wfqb])
                                for q in range(2):
                                    pst, pstb = (psDt, pDt[0]) if q % 2 == 0 else (psDt2, pDt[1])
                                    for cc2 in range(8):
                                        kcl = q * 8 + cc2
                                        tr(pst[:, cc2 * 128:(cc2 + 1) * 128],
                                           s_wfq[:, kcl * 128:(kcl + 1) * 128], s_idb[:],
                                           reads=[wfqb, cb], writes=[pstb])
                                    p.op("dve", lambda e, qq=q, ccx=c, pp=pst: e.tensor_copy(
                                        s_wqT[:].rearrange("p (kc o) -> p kc o", o=256)
                                        [:, qq * 8:(qq + 1) * 8, ccx * 128:(ccx + 1) * 128],
                                        pp[:].rearrange("p (kc t) -> p kc t", t=128)),
                                        reads=[pstb], writes=[wqTb])
                            for tg in range(2):
                                for kc in range(16):
                                    for ji in range(4):
                                        j = tg * 4 + ji
                                        mm(psD[:, ji * 512 + ho * 256: ji * 512 + ho * 256 + 256],
                                           s_pool[:, kc * 1024 + j * 128: kc * 1024 + j * 128 + 128],
                                           s_wqT[:, kc * 256:(kc + 1) * 256], kc == 0, kc == 15,
                                           reads=[poolb, wqTb], writes=[pD[ji]])
                                for ji in range(4):
                                    j = tg * 4 + ji
                                    psl = psD[:, ji * 512 + ho * 256: ji * 512 + ho * 256 + 256]
                                    c0 = g * 512 + ho * 256
                                    if mi == 0:
                                        p.op("act", lambda e, jj=j, ps=psl, h0=ho: e.activation(
                                            s_gg[:, jj * 512 + h0 * 256: jj * 512 + h0 * 256 + 256],
                                            ps, AF.Silu, scale=s_sxw[:, jj:jj + 1]),
                                            reads=[pD[ji], sxwb], writes=[ggb])
                                        if t_gate is not None:
                                            dma(t_gate.rearrange("(n p) f -> n p f", p=128)[j, :, c0:c0 + 256],
                                                s_gg[:, j * 512 + ho * 256: j * 512 + ho * 256 + 256],
                                                reads=[ggb])
                                    else:
                                        p.op("dve", lambda e, jj=j, ps=psl: e.tensor_scalar(
                                            s_gstg[:, 0:256], ps,
                                            s_sxw[:, 8 + jj:9 + jj], None, ALU.mult),
                                            reads=[pD[ji], sxwb], writes=[gstgb])
                                        p.op("dve", lambda e, jj=j, h0=ho: e.tensor_mul(
                                            s_gu2[:, 0:256],
                                            s_gg[:, jj * 512 + h0 * 256: jj * 512 + h0 * 256 + 256],
                                            s_gstg[:, 0:256]),
                                            reads=[ggb, gstgb], writes=[gu2b])
                                        p.op("dve", lambda e: e.reduce_max(
                                            s_rmax[:, 8:9], s_gu2[:, 0:256], axis=AX,
                                            apply_absolute_value=True),
                                            reads=[gu2b], writes=[rmaxb])
                                        p.op("dve", lambda e, jj=j: e.tensor_max(
                                            s_rmax[:, jj:jj + 1], s_rmax[:, jj:jj + 1],
                                            s_rmax[:, 8:9]),
                                            reads=[rmaxb], writes=[rmaxb])
                                        dma(gu_dram[j, :, c0:c0 + 256], s_gu2[:, 0:256],
                                            reads=[gu2b], writes=[gudb])
                                        if t_gu is not None:
                                            dma(t_gu.rearrange("(n p) f -> n p f", p=128)[j, :, c0:c0 + 256],
                                                s_gu2[:, 0:256], reads=[gu2b])
                    for _ in range(4):
                        next(wdgen, None)
                for _ in wdgen:
                    pass
        p.fence()

        p.serial_now = "D2" in ser_phases
        if n_phases >= 6:
            with contextlib.ExitStack() as _esD2:
                _e2 = _esD2.enter_context
                s_gub = _e2(nc.sbuf_tensor("s_gub", [128, 4096], F32))
                s_xq2 = _e2(nc.sbuf_tensor("s_xq2", [128, 4096], BF16))
                s_wdT = [_e2(nc.sbuf_tensor(f"s_wdT{i}", [128, 512], BF16)) for i in range(2)]
                s_dcol = _e2(nc.sbuf_tensor("s_dcol", [128, 8], F32))
                s_xrl = _e2(nc.sbuf_tensor("s_xrl", [128, 512], F32))
                s_otmp = _e2(nc.sbuf_tensor("s_otmp", [128, 512], F32))
                psD2 = _e2(nc.psum_tensor("psD2", [128, 2048], F32))
                psDq = _e2(nc.psum_tensor("psDq", [128, 1024], BF16))
                psDq2 = _e2(nc.psum_tensor("psDq2", [128, 1024], BF16))
                gubb = B("gub"); xq2b = B("xq2")
                wdTb = [B("wdT0"), B("wdT1")]
                dcolb = B("dcol"); xrlb = B("xrl"); otmpb = B("otmp")
                pD2 = [B(f"psD2{j}") for j in range(4)]
                pDq = [B("psDq"), B("psDq2")]

                for half in range(2):
                    js = [half * 4 + i for i in range(4)]
                    for j in js:
                        p.op("dve", lambda e, jj=j: e.tensor_scalar(
                            s_sall[:, 8 + jj: 9 + jj], s_rmax[:, jj:jj + 1],
                            1.0 / 127.0, EPS_F, ALU.mult, ALU.add),
                            reads=[rmaxb], writes=[sallb])
                        p.op("dve", lambda e, jj=j: e.reciprocal(
                            s_sall[:, 16:17], s_sall[:, 8 + jj: 9 + jj]),
                            reads=[sallb], writes=[sallb])
                        for gq in range(2):
                            dma(s_gub[:], gu_dram[j, :, gq * 4096:(gq + 1) * 4096],
                                reads=[gudb, gubb], writes=[gubb])
                            p.op("dve", lambda e: e.tensor_scalar(
                                s_gub[:], s_gub[:], s_sall[:, 16:17], MAGIC, ALU.mult, ALU.add),
                                reads=[gubb, sallb], writes=[gubb])
                            p.op("dve", lambda e: e.tensor_scalar(
                                s_xq2[:], s_gub[:], MAGIC, None, ALU.subtract),
                                reads=[gubb], writes=[xq2b])
                            for q in range(4):
                                pst, pstb = (psDq, pDq[0]) if q % 2 == 0 else (psDq2, pDq[1])
                                for c in range(8):
                                    kcl = q * 8 + c
                                    tr(pst[:, c * 128:(c + 1) * 128],
                                       s_xq2[:, kcl * 128:(kcl + 1) * 128], s_idb[:],
                                       reads=[xq2b, cb], writes=[pstb])
                                p.op("dve", lambda e, jj=j, gg=gq, qq=q, pp=pst: e.tensor_copy(
                                    s_pool[:, (jj % 4) * 8192 + gg * 4096 + qq * 1024:
                                            (jj % 4) * 8192 + gg * 4096 + (qq + 1) * 1024], pp[:]),
                                    reads=[pstb], writes=[poolb])
                    for og in range(4):
                        for kc2 in range(64):
                            par = kc2 % 2
                            dma(s_wdT[par][:], wqdT_dram[kc2, :, og * 512:(og + 1) * 512],
                                reads=[wdTb[par], wqdb], writes=[wdTb[par]])
                            for ji, j in enumerate(js):
                                mm(psD2[:, ji * 512:(ji + 1) * 512],
                                   s_pool[:, (j % 4) * 8192 + kc2 * 128: (j % 4) * 8192 + kc2 * 128 + 128],
                                   s_wdT[par][:], kc2 == 0, kc2 == 63,
                                   reads=[poolb, wdTb[par]], writes=[pD2[ji]])
                        for ji, j in enumerate(js):
                            p.op("dve", lambda e, jj=j: e.tensor_mul(
                                s_dcol[:, jj % 4:jj % 4 + 1], s_sall[:, 8 + jj: 9 + jj],
                                s_ws[:, 6:7]), reads=[sallb, wsb2], writes=[dcolb])
                            dma(s_xrl[:], x_t[j + 1][:, og * 512:(og + 1) * 512],
                                reads=[xrlb], writes=[xrlb])
                            p.op("dve", lambda e, jj=j, ji_=ji, o=og: e.scalar_tensor_tensor(
                                s_otmp[:], psD2[:, ji_ * 512:(ji_ + 1) * 512],
                                s_dcol[:, jj % 4:jj % 4 + 1],
                                s_attn[:, jj * 2048 + o * 512: jj * 2048 + (o + 1) * 512],
                                ALU.mult, ALU.add),
                                reads=[pD2[ji], dcolb, attnb], writes=[otmpb])
                            p.op("dve", lambda e: e.tensor_add(s_otmp[:], s_otmp[:], s_xrl[:]),
                                 reads=[otmpb, xrlb], writes=[otmpb])
                            dma(y_t[j][:, og * 512:(og + 1) * 512], s_otmp[:],
                                reads=[otmpb], writes=[outb])

    if n_phases < 6:
        with nc.sbuf_tensor("s_dummy", [128, 2048], F32) as s_dummy:
            p.op("dve", lambda e: e.memset(s_dummy[:], 0.0), writes=[B("dummy")])
            for j in range(NT):
                dma(y_t[j], s_dummy[:], reads=[B("dummy")], writes=[outb])

    p.truncate = truncate
    p.finalize()
    return nc, tap_outs



def _consts(inputs):
    rope_freqs = np.asarray(inputs["L0_attn_rope_freqs"], np.float32)
    B_bias = np.asarray(inputs["L0_attn_B_bias"], np.float32)
    C_bias = np.asarray(inputs["L0_attn_C_bias"], np.float32)
    D_param = np.asarray(inputs["L0_attn_D_param"], np.float32)
    norm_w = np.asarray(inputs["L0_attn_norm_w"], np.float32)
    dt_bias = np.asarray(inputs["L0_attn_dt_bias"], np.float32)
    A_log = np.asarray(inputs["L0_attn_A_log"], np.float32)

    r32f = np.zeros((64, 2048), np.float32)
    for h in range(64):
        r32f[h, h * 32:(h + 1) * 32] = rope_freqs[h]
    rn_ev = np.zeros((64, 2048), ml_dtypes.bfloat16)
    rn_od = np.zeros((64, 2048), ml_dtypes.bfloat16)
    for h in range(64):
        for f in range(32):
            rn_ev[2 * f, h * 32 + f] = 1
            rn_od[2 * f + 1, h * 32 + f] = 1
    bneg = np.where(np.arange(128)[:, None] <= np.arange(128)[None, :], 0.0,
                    BIGNEG).astype(np.float32)
    return {
        "dt_bias_r": dt_bias.reshape(1, 64),
        "a_log_c": A_log.reshape(64, 1),
        "d_rep_r": np.repeat(D_param, 64).reshape(1, 4096),
        "bb_ev": np.ascontiguousarray(B_bias[:, 0::2]).reshape(1, 2048),
        "bb_od": np.ascontiguousarray(B_bias[:, 1::2]).reshape(1, 2048),
        "cb_ev": np.ascontiguousarray(C_bias[:, 0::2]).reshape(1, 2048),
        "cb_od": np.ascontiguousarray(C_bias[:, 1::2]).reshape(1, 2048),
        "normw_r": norm_w.reshape(1, 4096),
        "r32f": r32f,
        "rn_ev": rn_ev,
        "rn_od": rn_od,
        "idf": np.eye(128, dtype=np.float32),
        "idb": np.eye(128, dtype=ml_dtypes.bfloat16),
        "onesf": np.ones((1, 128), np.float32),
        "actc": np.broadcast_to(np.array([[1e-5, 1.0, 1e-6, 0.5 * math.pi, 0, 0, 0, 0]],
                                         np.float32), (128, 8)).copy(),
        "bnegp": bneg,
    }


def make_in_maps(inputs):
    x = np.asarray(inputs["L0_attn_x"], np.float32).reshape(8192, 2048)
    w_in = np.ascontiguousarray(np.asarray(inputs["L0_attn_W_in"], np.float32))
    w_out = np.ascontiguousarray(np.asarray(inputs["L0_attn_W_out"], np.float32))
    wgm = np.ascontiguousarray(np.asarray(inputs["L0_ffn_w_gate"], np.float32))
    wum = np.ascontiguousarray(np.asarray(inputs["L0_ffn_w_up"], np.float32))
    wdm = np.ascontiguousarray(np.asarray(inputs["L0_ffn_w_down"], np.float32))
    consts = _consts(inputs)
    in_maps = []
    zrow = np.zeros((128, 2048), np.float32)
    for k in range(8):
        m = dict(consts)
        halo = zrow if k == 0 else x[k * 1024 - 128: k * 1024]
        m["x"] = np.ascontiguousarray(np.concatenate([halo, x[k * 1024:(k + 1) * 1024]], axis=0))
        m["w_in"] = w_in
        m["w_out"] = w_out
        m["wg"] = wgm
        m["wu"] = wum
        m["wd"] = wdm
        in_maps.append(m)
    return in_maps


def kernel(**inputs):
    nc, _ = build_program()
    in_maps = make_in_maps(inputs)
    res = run_bass_kernel_spmd(nc, in_maps, core_ids=list(range(8)))
    out = np.concatenate([np.asarray(r["y"]) for r in res.results], axis=0)
    return out.reshape(1, 8192, 2048).astype(np.float32)


# version 07 (final)
# speedup vs baseline: 1.7567x; 1.0334x over previous
import contextlib
import math
import numpy as np
import ml_dtypes
import concourse.bass as bass
import concourse.mybir as mybir
from concourse.bass_utils import run_bass_kernel_spmd

F32 = mybir.dt.float32
BF16 = mybir.dt.bfloat16
AF = mybir.ActivationFunctionType
ALU = mybir.AluOpType
AX = mybir.AxisListType.X

MAGIC = float(np.float32(1.5 * 2 ** 23))
PI = float(np.float32(math.pi))
TWO_PI = 6.283185307179586
INV_2PI = float(np.float32(1.0 / TWO_PI))
CW1 = float(np.float32(6.28125))
_r1 = TWO_PI - 6.28125
CW2 = float(np.float32(_r1))
CW3 = float(np.float32(_r1 - float(np.float32(_r1))))
BIGNEG = -1e30

D_MODEL = 2048
N_HEADS = 64
D_INNER = 4096
D_IN_PROJ = 8384
D_FFN = 8192
EPS_A = 1e-6
EPS_F = 1e-5
EPS_LN = 1e-5

TOK = 1024
NT = 8
NTH = 9
TOKH = NTH * 128

NPH = 99


class Prog:

    def __init__(self, nc):
        self.nc = nc
        self.engines = {
            "pe": nc.tensor, "act": nc.scalar, "dve": nc.vector,
            "pool": nc.gpsimd, "sp": nc.sync,
        }
        self.ops = []
        self.deps = []
        self.dma_key = []
        self.cur_fence = set()
        self.last_on_eng = {}

    def fence(self):
        self.cur_fence = set(self.last_on_eng.values())

    class Buf:
        __slots__ = ("name", "writer", "readers", "psum")
        def __init__(self, name, psum=False):
            self.name = name
            self.writer = None
            self.readers = set()
            self.psum = psum

    def buf(self, name, psum=False):
        return Prog.Buf(name, psum)

    serialize_all = False
    serial_now = False

    def op(self, eng, emit_fn, reads=(), writes=(), dma=False):
        idx = len(self.ops)
        dep = set()
        if (self.serialize_all or self.serial_now) and idx > 0:
            dep.add(idx - 1)
        for bb in reads:
            if bb.writer is not None:
                dep.add(bb.writer)
            if bb.psum:
                for r in bb.readers:
                    if self.ops[r][0] != eng:
                        dep.add(r)
        for bb in writes:
            if bb.writer is not None:
                dep.add(bb.writer)
            dep.update(bb.readers)
        dep.update(self.cur_fence)
        dep.discard(idx)
        self.ops.append((eng, emit_fn, 16 if dma else 1))
        self.dma_key.append((writes[0].name if writes else "tapout") if dma else None)
        self.deps.append(dep)
        self.last_on_eng[eng] = idx
        for bb in reads:
            bb.readers.add(idx)
        for bb in writes:
            bb.writer = idx
            bb.readers = set()
        return idx

    truncate = None

    def finalize(self):
        nc = self.nc
        n = len(self.ops)
        if self.truncate is not None:
            n = min(n, self.truncate)
        op_eng = [o[0] for o in self.ops]
        needs_inc = [o[2] == 16 for o in self.ops]
        waits = [[] for _ in range(n)]
        import os
        pe_pe = os.environ.get("K_PEPE", "0") == "1"
        for i in range(n):
            e = op_eng[i]
            for j in self.deps[i]:
                je = op_eng[j]
                if je == e and e == "pe" and not pe_pe:
                    continue
                needs_inc[j] = True
                waits[i].append(j)
        op_key = [None] * n
        for j in range(n):
            if self.ops[j][2] == 16:
                op_key[j] = ("d", self.dma_key[j] or f"anon{j}")
            else:
                op_key[j] = ("e", op_eng[j])
        sem_val = [0] * n
        cnt = {}
        for j in range(n):
            k = op_key[j]
            if needs_inc[j]:
                cnt[k] = cnt.get(k, 0) + self.ops[j][2]
            sem_val[j] = cnt.get(k, 0)
        self.final_counts = dict(cnt)
        sems = {}
        for j in range(n):
            k = op_key[j]
            if k not in sems:
                sems[k] = nc.semaphore(f"ps_{k[0]}_{k[1]}"[:28]).__enter__()
        seen = {}
        for i in range(n):
            e, emit_fn, inc = self.ops[i]
            byk = {}
            for j in waits[i]:
                k = op_key[j]
                v = sem_val[j]
                if v > byk.get(k, 0):
                    byk[k] = v
            for k in sorted(byk):
                v = byk[k]
                if seen.get((e, k), 0) >= v:
                    continue
                self.engines[e].wait_ge(sems[k], v)
                seen[(e, k)] = v
            inst = emit_fn(self.engines[e])
            if needs_inc[i]:
                inst.then_inc(sems[op_key[i]], inc)
        for k, v in cnt.items():
            if v > 0:
                self.engines["sp" if k[0] == "d" else k[1]].wait_ge(sems[k], v)


def build_program(n_phases=NPH, taps=(), truncate=None, serialize_all=False,
                  ser_phases=()):
    taps = set(taps)
    ser_phases = set(ser_phases)
    nc = bass.Bass("TRN2", target_bir_lowering=False)
    p = Prog(nc)
    p.serialize_all = serialize_all

    def din(name, shape, dtype=F32):
        return nc.dram_tensor(name, shape, dtype, kind="ExternalInput").ap()

    x_hbm = din("x", [TOKH, D_MODEL])
    w_in = din("w_in", [D_MODEL, D_IN_PROJ])
    w_out = din("w_out", [D_INNER, D_MODEL])
    wg = din("wg", [D_FFN, D_MODEL])
    wu = din("wu", [D_FFN, D_MODEL])
    wd = din("wd", [D_MODEL, D_FFN])
    dt_bias_r = din("dt_bias_r", [1, N_HEADS])
    a_log_c = din("a_log_c", [N_HEADS, 1])
    d_rep_r = din("d_rep_r", [1, D_INNER])
    bb_ev = din("bb_ev", [1, 2048])
    bb_od = din("bb_od", [1, 2048])
    cb_ev = din("cb_ev", [1, 2048])
    cb_od = din("cb_od", [1, 2048])
    normw_r = din("normw_r", [1, D_INNER])
    r32f = din("r32f", [N_HEADS, 2048])
    rn_ev = din("rn_ev", [N_HEADS, 2048], BF16)
    rn_od = din("rn_od", [N_HEADS, 2048], BF16)
    idf = din("idf", [128, 128])
    idb = din("idb", [128, 128], BF16)
    onesf = din("onesf", [1, 128])
    actc = din("actc", [128, 8])
    bnegp = din("bnegp", [128, 128])

    y_hbm = nc.dram_tensor("y", [TOK, D_MODEL], F32, kind="ExternalOutput").ap()

    xqT_dram = nc.dram_tensor("xqT_scr", [16, 128, TOK], BF16, kind="Internal").ap()
    xs_dram = nc.dram_tensor("xs_scr", [NTH, 128, D_INNER], BF16, kind="Internal").ap()
    gu_dram = nc.dram_tensor("gu_scr", [NT, 128, D_FFN], F32, kind="Internal").ap()
    wqdT_dram = nc.dram_tensor("wqdT_scr", [64, 128, D_MODEL], BF16, kind="Internal").ap()

    tap_outs = {}
    def tap(name, shape, dtype=F32):
        if name in taps:
            tap_outs[name] = nc.dram_tensor("tap_" + name, shape, dtype,
                                            kind="ExternalOutput").ap()
            return tap_outs[name]
        return None

    A = nc.alloc_sbuf_tensor

    s_pool = A("s_pool", [128, NT * 4096], BF16)
    s_raw = A("s_raw", [128, NTH * 192], F32)
    s_dt = A("s_dt", [128, NTH * 64], F32)
    s_slam = A("s_slam", [128, NTH * 64], F32)
    s_sdtT = A("s_sdtT", [64, NTH * 128], F32)
    s_slamT = A("s_slamT", [64, NTH * 128], F32)
    s_sall = A("s_sall", [128, 28], F32)
    s_rmax = A("s_rmax", [128, 12], F32)
    s_carry = A("s_carry", [64, 4], F32)
    s_idf = A("s_idf", [128, 128], F32)
    s_idb = A("s_idb", [128, 128], BF16)
    s_ones = A("s_ones", [1, 128], F32)
    s_onesc = A("s_onesc", [128, 1], F32)
    s_lam = A("s_lam", [64, 1], F32)
    s_ws = A("s_ws", [128, 8], F32)
    s_actc = A("s_actc", [128, 8], F32)

    b = {}
    def B(name):
        if name not in b:
            b[name] = p.buf(name, psum=name.startswith("ps"))
        return b[name]

    def dma(dst, src, reads=(), writes=(), q="sp"):
        return p.op(q, lambda e, d=dst, s=src: e.dma_start(out=d, in_=s),
                    reads=reads, writes=writes, dma=True)

    def mm(ps, lhsT, rhs, start, stop, reads=(), writes=()):
        return p.op("pe", lambda e, o=ps, l=lhsT, r=rhs, st=start, sp=stop:
                    e.matmul(o, l, r, start=st, stop=sp), reads=reads, writes=writes)

    def tr(ps, in_, ident, reads=(), writes=()):
        return p.op("pe", lambda e, o=ps, i=in_, d=ident: e.transpose(o, i, d),
                    reads=reads, writes=writes)

    cb = B("consts")
    lamb = B("lam")
    carb = B("carry")
    sallb = B("sall")
    rmaxb = B("rmax")
    rawb = B("raw")
    dtb_ = B("dt")
    slamb = B("slam")
    sdtTb = B("sdtT")
    slamTb = B("slamT")
    wsb2 = B("ws")
    xqTdb = B("xqT_dram")
    xsdb = B("xs_dram")
    gudb = B("gu_dram")
    wqdb = B("wqd_dram")
    poolb = B("pool")
    outb = B("out")

    dma(s_idf[:], idf, writes=[cb])
    dma(s_idb[:], idb, writes=[cb])
    dma(s_ones[:], onesf, writes=[cb])
    dma(s_actc[:], actc, writes=[cb])
    p.op("dve", lambda e: e.memset(s_onesc[:], 1.0), writes=[cb])
    p.op("dve", lambda e: e.memset(s_carry[:], 0.0), writes=[carb])
    p.op("dve", lambda e: e.memset(s_rmax[:], 0.0), writes=[rmaxb])

    with nc.sbuf_tensor("s_alog", [64, 1], F32) as s_alog:
        dma(s_alog[:], a_log_c, writes=[lamb])
        p.op("act", lambda e: e.activation(s_lam[:], s_alog[:], AF.Exp),
             reads=[lamb], writes=[lamb])
    p.fence()

    x_t = x_hbm.rearrange("(n p) f -> n p f", p=128)
    y_t = y_hbm.rearrange("(n p) f -> n p f", p=128)

    p.serial_now = "AB" in ser_phases
    with contextlib.ExitStack() as _esAB:
        _e = _esAB.enter_context
        s_xnT = _e(nc.sbuf_tensor("s_xnT", [128, NTH * 2048], BF16))
        s_wsb = [_e(nc.sbuf_tensor(f"s_wsb{i}", [128, 2048], F32)) for i in range(2)]
        s_wsjunk = _e(nc.sbuf_tensor("s_wsjunk", [128, 2048], F32))
        s_psums = _e(nc.sbuf_tensor("s_psums", [128, 4], F32))
        s_scal = _e(nc.sbuf_tensor("s_scal", [1, 8], F32))
        xnTb = B("xnT")
        wsbb = [B("wsb0"), B("wsb1")]
        wsjb = B("wsjunk")
        psmb = B("psums")
        scalb = B("scal")

        p.op("dve", lambda e: e.memset(s_psums[:], 0.0), writes=[psmb])

        def scale_gen():
            wg_v = wg.rearrange("(n p) k -> n p k", p=128)
            wu_v = wu.rearrange("(n p) k -> n p k", p=128)
            wd_v = wd.rearrange("(n p) k -> n p k", p=128)
            views = [
                (0, lambda i: wg_v[i], 64),
                (1, lambda i: wu_v[i], 64),
                (2, lambda i: wd_v[i // 4][:, (i % 4) * 2048:(i % 4 + 1) * 2048], 64),
            ]
            for col, view, ntl in views:
                for i in range(ntl):
                    par = i % 2
                    dma(s_wsb[par][:], view(i), reads=[wsbb[par]], writes=[wsbb[par]],
                        q="act")
                    p.op("act", lambda e, pr=par, c=col: e.activation(
                        s_wsjunk[:], s_wsb[pr][:], AF.Abs,
                        accum_out=s_psums[:, 3:4]),
                        reads=[wsbb[par]], writes=[wsjb, psmb])
                    p.op("dve", lambda e, c=col: e.tensor_add(
                        s_psums[:, c:c + 1], s_psums[:, c:c + 1], s_psums[:, 3:4]),
                        reads=[psmb], writes=[psmb])
                    yield

        def scale_fin():
            with (nc.psum_tensor("ps_sc", [1, 8], F32) as ps_sc,
                  nc.psum_tensor("ps_ws", [128, 8], F32) as ps_ws):
                psc = B("ps_sc"); pws = B("ps_ws")
                mm(ps_sc[0:1, 0:3], s_onesc[:], s_psums[:, 0:3], True, True,
                   reads=[psmb, cb], writes=[psc])
                p.op("dve", lambda e: e.tensor_copy(s_scal[0:1, 0:3], ps_sc[0:1, 0:3]),
                     reads=[psc], writes=[scalb])
                nume = float(D_FFN * D_MODEL)
                p.op("dve", lambda e: e.tensor_scalar(
                    s_scal[0:1, 4:7], s_scal[0:1, 0:3], 1.0 / nume, EPS_F,
                    ALU.mult, ALU.add), reads=[scalb], writes=[scalb])
                p.op("dve", lambda e: e.reciprocal(s_scal[0:1, 0:3], s_scal[0:1, 4:7]),
                     reads=[scalb], writes=[scalb])
                mm(ps_ws[:, 0:8], s_ones[:], s_scal[0:1, 0:8], True, True,
                   reads=[scalb, cb], writes=[pws])
                p.op("dve", lambda e: e.tensor_copy(s_ws[:], ps_ws[:, 0:8]),
                     reads=[pws], writes=[wsb2])
            t_ws = tap("wscal", [128, 8])
            if t_ws is not None:
                dma(t_ws, s_ws[:], reads=[wsb2])

        sg = scale_gen()

        def pump(k):
            for _ in range(k):
                if next(sg, "END") == "END":
                    return

        t_xn = tap("xn", [TOK, D_MODEL])
        t_xq = tap("xq", [TOK, D_MODEL])
        t_raw = tap("raw", [TOK, 192])
        t_dt = tap("dt", [TOK, 64])
        t_sdt = tap("sdt", [64, TOK])
        t_slam = tap("slam", [64, TOK])

        with contextlib.ExitStack() as _esA:
            _e = _esA.enter_context
            s_wraw = _e(nc.sbuf_tensor("s_wraw", [128, 16 * 192], F32))
            s_dtb_bc = _e(nc.sbuf_tensor("s_dtb_bc", [128, 64], F32))
            s_xt = _e(nc.sbuf_tensor("s_xt", [128, 2048], F32))
            s_xn = _e(nc.sbuf_tensor("s_xn", [128, 2048], F32))
            s_xnTf = _e(nc.sbuf_tensor("s_xnTf", [128, 2048], F32))
            s_xq = _e(nc.sbuf_tensor("s_xq", [128, 2048], BF16))
            s_sq = _e(nc.sbuf_tensor("s_sq", [128, 2048], F32))
            s_c1 = _e(nc.sbuf_tensor("s_c1", [128, 8], F32))
            s_dtT = _e(nc.sbuf_tensor("s_dtT", [64, 128], F32))
            s_ldtT = _e(nc.sbuf_tensor("s_ldtT", [64, 128], F32))
            s_stg = _e(nc.sbuf_tensor("s_stg", [128, 512], BF16))
            psA0 = _e(nc.psum_tensor("psA0", [128, 512], F32))
            psA1 = _e(nc.psum_tensor("psA1", [128, 512], F32))
            psA2 = _e(nc.psum_tensor("psA2", [128, 512], BF16))
            psA3 = _e(nc.psum_tensor("psA3", [128, 192], F32))
            psA4 = _e(nc.psum_tensor("psA4", [64, 128], F32))
            psA5 = _e(nc.psum_tensor("psA5", [128, 64], F32))
            wrb = B("wraw"); dtbb = B("dtb_bc")
            xtb = B("xt"); xnb = B("xn"); xnTfb = B("xnTf"); xqb = B("xq")
            sqb = B("sq"); c1b = B("c1"); dtTb = B("dtT"); ldtTb = B("ldtT")
            stgb = B("stg")
            pA0 = B("psA0"); pA1 = B("psA1"); pA2 = B("psA2")
            pA3 = B("psA3"); pA4 = B("psA4"); pA5 = B("psA5")

            dma(s_wraw[:].rearrange("p (kc o) -> p kc o", kc=16),
                w_in[:, 8192:8384].rearrange("(kc p) o -> p kc o", p=128),
                writes=[wrb])
            with nc.sbuf_tensor("s_row64", [1, 64], F32) as s_row64:
                r64b = B("row64")
                dma(s_row64[:], dt_bias_r, writes=[r64b])
                mm(psA5[:], s_ones[:], s_row64[0:1, 0:64], True, True,
                   reads=[r64b, cb], writes=[pA5])
                p.op("dve", lambda e: e.tensor_copy(s_dtb_bc[:], psA5[:]),
                     reads=[pA5], writes=[dtbb])

            for j in range(NTH):
                jr = j - 1
                dma(s_xt[:], x_t[j], reads=[xtb], writes=[xtb])
                p.op("dve", lambda e: e.reduce_sum(s_c1[:, 0:1], s_xt[:], axis=AX),
                     reads=[xtb], writes=[c1b])
                p.op("act", lambda e: e.activation(s_sq[:], s_xt[:], AF.Square,
                                                   accum_out=s_c1[:, 1:2]),
                     reads=[xtb], writes=[sqb, c1b])
                p.op("dve", lambda e: e.tensor_scalar(s_c1[:, 2:3], s_c1[:, 0:1], -1.0 / 2048.0, None, ALU.mult),
                     reads=[c1b], writes=[c1b])
                p.op("dve", lambda e: e.tensor_mul(s_c1[:, 3:4], s_c1[:, 2:3], s_c1[:, 2:3]),
                     reads=[c1b], writes=[c1b])
                p.op("dve", lambda e: e.tensor_scalar(s_c1[:, 4:5], s_c1[:, 1:2], 1.0 / 2048.0, None, ALU.mult),
                     reads=[c1b], writes=[c1b])
                p.op("dve", lambda e: e.tensor_sub(s_c1[:, 4:5], s_c1[:, 4:5], s_c1[:, 3:4]),
                     reads=[c1b], writes=[c1b])
                p.op("act", lambda e: e.activation(s_c1[:, 5:6], s_c1[:, 4:5], AF.Ln, bias=s_actc[:, 0:1]),
                     reads=[c1b, cb], writes=[c1b])
                p.op("act", lambda e: e.activation(s_c1[:, 5:6], s_c1[:, 5:6], AF.Exp, scale=-0.5),
                     reads=[c1b], writes=[c1b])
                p.op("dve", lambda e: e.tensor_scalar(s_xn[:], s_xt[:], s_c1[:, 2:3], s_c1[:, 5:6], ALU.add, ALU.mult),
                     reads=[xtb, c1b], writes=[xnb])
                if t_xn is not None and j > 0:
                    dma(t_xn.rearrange("(n p) f -> n p f", p=128)[jr], s_xn[:], reads=[xnb])
                if j > 0:
                    p.op("dve", lambda e: e.reduce_max(s_c1[:, 6:7], s_xn[:], axis=AX, apply_absolute_value=True),
                         reads=[xnb], writes=[c1b])
                    p.op("dve", lambda e: e.tensor_scalar(s_c1[:, 6:7], s_c1[:, 6:7], 1.0 / 127.0, EPS_F, ALU.mult, ALU.add),
                         reads=[c1b], writes=[c1b])
                    p.op("dve", lambda e, jj=jr: e.tensor_copy(s_sall[:, jj:jj + 1], s_c1[:, 6:7]),
                         reads=[c1b], writes=[sallb])
                    p.op("dve", lambda e: e.reciprocal(s_c1[:, 7:8], s_c1[:, 6:7]),
                         reads=[c1b], writes=[c1b])
                    p.op("dve", lambda e: e.tensor_scalar(s_sq[:], s_xn[:], s_c1[:, 7:8], MAGIC, ALU.mult, ALU.add),
                         reads=[xnb, c1b], writes=[sqb])
                    p.op("dve", lambda e: e.tensor_scalar(s_xq[:], s_sq[:], MAGIC, None, ALU.subtract),
                         reads=[sqb], writes=[xqb])
                    if t_xq is not None:
                        p.op("dve", lambda e: e.tensor_copy(s_sq[:], s_xq[:]), reads=[xqb], writes=[sqb])
                        dma(t_xq.rearrange("(n p) f -> n p f", p=128)[jr], s_sq[:], reads=[sqb])
                for q in range(4):
                    ps_h, pb_h = (psA0, pA0) if q % 2 == 0 else (psA1, pA1)
                    for c in range(4):
                        kc = q * 4 + c
                        tr(ps_h[:, c * 128:(c + 1) * 128], s_xn[:, kc * 128:(kc + 1) * 128],
                           s_idf[:], reads=[xnb, cb], writes=[pb_h])
                    base = q * 512
                    p.op("act", lambda e, b0=base, ph=ps_h: e.activation(
                        s_xnTf[:, b0:b0 + 512], ph[:, 0:512], AF.Identity),
                        reads=[pb_h], writes=[xnTfb])
                    p.op("dve", lambda e, b0=base, jj=j, ph=ps_h: e.tensor_copy(
                        s_xnT[:, jj * 2048 + b0: jj * 2048 + b0 + 512], ph[:, 0:512]),
                        reads=[pb_h], writes=[xnTb])
                if j > 0:
                    for q in range(4):
                        for c in range(4):
                            kc = q * 4 + c
                            tr(psA2[:, c * 128:(c + 1) * 128], s_xq[:, kc * 128:(kc + 1) * 128],
                               s_idb[:], reads=[xqb, cb], writes=[pA2])
                        p.op("dve", lambda e: e.tensor_copy(s_stg[:], psA2[:, 0:512]),
                             reads=[pA2], writes=[stgb])
                        for c in range(4):
                            kc = q * 4 + c
                            dma(xqT_dram[kc, :, jr * 128:(jr + 1) * 128], s_stg[:, c * 128:(c + 1) * 128],
                                reads=[stgb], writes=[xqTdb])
                for kc in range(16):
                    mm(psA3[:], s_xnTf[:, kc * 128:(kc + 1) * 128],
                       s_wraw[:, kc * 192:(kc + 1) * 192],
                       kc == 0, kc == 15, reads=[xnTfb, wrb], writes=[pA3])
                p.op("dve", lambda e, jj=j: e.tensor_copy(s_raw[:, jj * 192:(jj + 1) * 192], psA3[:]),
                     reads=[pA3], writes=[rawb])
                if t_raw is not None and j > 0:
                    dma(t_raw.rearrange("(n p) f -> n p f", p=128)[jr],
                        s_raw[:, j * 192:(j + 1) * 192], reads=[rawb])
                o0 = j * 192 + 128
                p.op("dve", lambda e, o=o0: e.tensor_add(s_sq[:, 0:64], s_raw[:, o:o + 64], s_dtb_bc[:]),
                     reads=[rawb, dtbb], writes=[sqb])
                p.op("dve", lambda e: e.tensor_scalar(s_sq[:, 64:128], s_sq[:, 0:64], 0.0, None, ALU.max),
                     reads=[sqb], writes=[sqb])
                p.op("act", lambda e: e.activation(s_sq[:, 128:192], s_sq[:, 0:64], AF.Abs),
                     reads=[sqb], writes=[sqb])
                p.op("act", lambda e: e.activation(s_sq[:, 128:192], s_sq[:, 128:192], AF.Exp, scale=-1.0),
                     reads=[sqb], writes=[sqb])
                p.op("act", lambda e: e.activation(s_sq[:, 128:192], s_sq[:, 128:192], AF.Ln, bias=s_actc[:, 1:2]),
                     reads=[sqb, cb], writes=[sqb])
                p.op("dve", lambda e, jj=j: e.tensor_add(s_dt[:, jj * 64:(jj + 1) * 64],
                                                         s_sq[:, 64:128], s_sq[:, 128:192]),
                     reads=[sqb], writes=[dtb_])
                if t_dt is not None and j > 0:
                    dma(t_dt.rearrange("(n p) f -> n p f", p=128)[jr],
                        s_dt[:, j * 64:(j + 1) * 64], reads=[dtb_])
                tr(psA4[:], s_dt[:, j * 64:(j + 1) * 64], s_idf[:], reads=[dtb_, cb], writes=[pA4])
                p.op("dve", lambda e: e.tensor_copy(s_dtT[:], psA4[:]), reads=[pA4], writes=[dtTb])
                p.op("dve", lambda e: e.tensor_scalar(s_ldtT[:], s_dtT[:], s_lam[:], None, ALU.mult),
                     reads=[dtTb, lamb], writes=[ldtTb])
                p.op("dve", lambda e, jj=j: e.tensor_tensor_scan(
                    s_sdtT[:, jj * 128:(jj + 1) * 128], s_dtT[:], s_dtT[:], s_carry[:, 0:1],
                    ALU.add, ALU.bypass), reads=[dtTb, carb], writes=[sdtTb])
                p.op("dve", lambda e, jj=j: e.tensor_tensor_scan(
                    s_slamT[:, jj * 128:(jj + 1) * 128], s_ldtT[:], s_ldtT[:], s_carry[:, 1:2],
                    ALU.add, ALU.bypass), reads=[ldtTb, carb], writes=[slamTb])
                p.op("dve", lambda e, jj=j: e.tensor_copy(
                    s_carry[:, 0:1], s_sdtT[:, jj * 128 + 127: jj * 128 + 128]),
                    reads=[sdtTb], writes=[carb])
                p.op("dve", lambda e, jj=j: e.tensor_copy(
                    s_carry[:, 1:2], s_slamT[:, jj * 128 + 127: jj * 128 + 128]),
                    reads=[slamTb], writes=[carb])
                tr(psA5[:], s_slamT[:, j * 128:(j + 1) * 128], s_idf[0:64, 0:64],
                   reads=[slamTb, cb], writes=[pA5])
                p.op("dve", lambda e, jj=j: e.tensor_copy(s_slam[:, jj * 64:(jj + 1) * 64], psA5[:]),
                     reads=[pA5], writes=[slamb])
                pump(8)
            if t_sdt is not None:
                dma(t_sdt, s_sdtT[:], reads=[sdtTb])
            if t_slam is not None:
                dma(t_slam, s_slamT[:], reads=[slamTb])

        p.fence()
        if n_phases >= 2:
            t_z = tap("z", [TOK, D_INNER])
            t_xs = tap("xs", [TOK, D_INNER])
            with contextlib.ExitStack() as _esB:
                _e = _esB.enter_context
                s_wstg = _e(nc.sbuf_tensor("s_wstg", [128, 512], F32))
                s_wblk = _e(nc.sbuf_tensor("s_wblk", [128, 16 * 512], BF16))
                s_bstg = _e(nc.sbuf_tensor("s_bstg", [128, 512], BF16))
                s_bstgf = _e(nc.sbuf_tensor("s_bstgf", [128, 512], F32))
                psB = _e(nc.psum_tensor("psB", [128, 4096], F32))
                wstgb = B("wstg")
                wblkb = B("wblk")
                bstgb = B("bstg")
                bstgfb = B("bstgf")
                pB = [B(f"psB{j}") for j in range(8)]
                w_in_t = w_in.rearrange("(kc p) o -> kc p o", p=128)
                for g in range(16):
                    is_z = g < 8
                    for kc in range(16):
                        dma(s_wstg[:], w_in_t[kc, :, g * 512:(g + 1) * 512],
                            reads=[wstgb], writes=[wstgb])
                        p.op("pool", lambda e, k0=kc: e.tensor_copy(
                            s_wblk[:, k0 * 512:(k0 + 1) * 512], s_wstg[:]),
                            reads=[wstgb], writes=[wblkb])
                    for kc in range(16):
                        for jr in range(8):
                            jt = jr + 1
                            mm(psB[:, jr * 512:(jr + 1) * 512],
                               s_xnT[:, (jt * 16 + kc) * 128:(jt * 16 + kc + 1) * 128],
                               s_wblk[:, kc * 512:(kc + 1) * 512], kc == 0, kc == 15,
                               reads=[xnTb, wblkb], writes=[pB[jr]])
                    for jr in range(8):
                        pslice = psB[:, jr * 512:(jr + 1) * 512]
                        if is_z:
                            p.op("act", lambda e, jj=jr, gg=g, ps=pslice: e.activation(
                                s_pool[:, jj * 4096 + gg * 512: jj * 4096 + (gg + 1) * 512],
                                ps, AF.Silu), reads=[pB[jr]], writes=[poolb])
                            if t_z is not None:
                                p.op("dve", lambda e, ps=pslice: e.tensor_copy(s_bstgf[:], ps),
                                     reads=[pB[jr]], writes=[bstgfb])
                                dma(t_z.rearrange("(n p) f -> n p f", p=128)[jr, :, g * 512:(g + 1) * 512],
                                    s_bstgf[:], reads=[bstgfb])
                        else:
                            gg = g - 8
                            p.op("dve", lambda e, ps=pslice: e.tensor_copy(s_bstg[:], ps),
                                 reads=[pB[jr]], writes=[bstgb])
                            dma(xs_dram[jr + 1, :, gg * 512:(gg + 1) * 512], s_bstg[:],
                                reads=[bstgb], writes=[xsdb])
                            if t_xs is not None:
                                p.op("dve", lambda e, ps=pslice: e.tensor_copy(s_bstgf[:], ps),
                                     reads=[pB[jr]], writes=[bstgfb])
                                dma(t_xs.rearrange("(n p) f -> n p f", p=128)[jr, :, gg * 512:(gg + 1) * 512],
                                    s_bstgf[:], reads=[bstgfb])
                    if not is_z:
                        gg = g - 8
                        for kc in range(16):
                            mm(psB[:, 0:512],
                               s_xnT[:, (0 * 16 + kc) * 128:(0 * 16 + kc + 1) * 128],
                               s_wblk[:, kc * 512:(kc + 1) * 512], kc == 0, kc == 15,
                               reads=[xnTb, wblkb], writes=[pB[0]])
                        p.op("dve", lambda e: e.tensor_copy(s_bstg[:], psB[:, 0:512]),
                             reads=[pB[0]], writes=[bstgb])
                        dma(xs_dram[0, :, gg * 512:(gg + 1) * 512], s_bstg[:],
                            reads=[bstgb], writes=[xsdb])
                    pump(10)
        pump(10 ** 9)
        scale_fin()

    p.fence()

    p.serial_now = "C" in ser_phases
    if n_phases >= 3:
        t_rb = tap("rbv", [TOK, 64], BF16)
        t_rbt = tap("rbtv", [64, TOK], BF16)
        t_i1 = tap("i1v", [TOK, 2048], BF16)
        t_i2 = tap("i2v", [TOK, 2048], BF16)
        t_sc = tap("sinv", [TOK, 2048], BF16)
        t_cc2 = tap("cosv", [TOK, 2048], BF16)
        t_bh = tap("bh", [TOK, D_INNER], BF16)
        t_ch = tap("ch", [TOK, D_INNER], BF16)
        t_y = tap("y_scan", [TOK, D_INNER], BF16)
        t_yn = tap("yn", [TOK, D_INNER])
        with contextlib.ExitStack() as _esC:
            _e = _esC.enter_context
            s_drep = _e(nc.sbuf_tensor("s_drep", [128, D_INNER], BF16))
            s_normw = _e(nc.sbuf_tensor("s_normw", [128, D_INNER], BF16))
            s_bbev = _e(nc.sbuf_tensor("s_bbev", [128, 2048], BF16))
            s_bbod = _e(nc.sbuf_tensor("s_bbod", [128, 2048], BF16))
            s_cbev = _e(nc.sbuf_tensor("s_cbev", [128, 2048], BF16))
            s_cbod = _e(nc.sbuf_tensor("s_cbod", [128, 2048], BF16))
            s_r32f = _e(nc.sbuf_tensor("s_r32f", [64, 2048], F32))
            s_rnev = _e(nc.sbuf_tensor("s_rnev", [64, 2048], BF16))
            s_rnod = _e(nc.sbuf_tensor("s_rnod", [64, 2048], BF16))
            s_bneg = _e(nc.sbuf_tensor("s_bneg", [128, 128], F32))
            s_xsj = _e(nc.sbuf_tensor("s_xsj", [128, 2048], BF16))
            s_flat = _e(nc.sbuf_tensor("s_flat", [1, 512], F32))
            s_sin = _e(nc.sbuf_tensor("s_sin", [128, 1024], BF16))
            s_cos = _e(nc.sbuf_tensor("s_cos", [128, 1024], BF16))
            s_phr = _e(nc.sbuf_tensor("s_phr", [128, 512], F32))
            s_k = _e(nc.sbuf_tensor("s_k", [128, 512], F32))
            s_ph2 = _e(nc.sbuf_tensor("s_ph2", [128, 512], F32))
            s_rbB = _e(nc.sbuf_tensor("s_rbB", [128, 64], BF16))
            s_rbC = _e(nc.sbuf_tensor("s_rbC", [128, 64], BF16))
            s_rmsBT = _e(nc.sbuf_tensor("s_rmsBT", [64, 128], BF16))
            s_rmsCT = _e(nc.sbuf_tensor("s_rmsCT", [64, 128], BF16))
            s_i1 = _e(nc.sbuf_tensor("s_i1", [128, 1024], BF16))
            s_i2 = _e(nc.sbuf_tensor("s_i2", [128, 1024], BF16))
            s_t1 = _e(nc.sbuf_tensor("s_t1", [128, 1024], BF16))
            s_t2 = _e(nc.sbuf_tensor("s_t2", [128, 1024], BF16))
            s_Bh = _e(nc.sbuf_tensor("s_Bh", [128, 2048], BF16))
            s_Ch = _e(nc.sbuf_tensor("s_Ch", [128, 2048], BF16))
            s_smB = _e(nc.sbuf_tensor("s_smB", [128, 2048], BF16))
            s_smC = _e(nc.sbuf_tensor("s_smC", [128, 2048], BF16))
            s_state = _e(nc.sbuf_tensor("s_state", [128, 2048], BF16))
            s_e1 = _e(nc.sbuf_tensor("s_e1", [128, 256], F32))
            s_W = _e(nc.sbuf_tensor("s_W", [128, 256], BF16))
            s_WG = _e(nc.sbuf_tensor("s_WG", [128, 256], BF16))
            s_U = _e(nc.sbuf_tensor("s_U", [128, 128], BF16))
            s_Bdec = _e(nc.sbuf_tensor("s_Bdec", [128, 128], BF16))
            s_tyi = _e(nc.sbuf_tensor("s_tyi", [128, 512], F32))
            s_xsd = _e(nc.sbuf_tensor("s_xsd", [128, 512], BF16))
            s_A8 = _e(nc.sbuf_tensor("s_A8", [128, 64], F32))
            s_Adec = _e(nc.sbuf_tensor("s_Adec", [128, 64], F32))
            s_rowL = _e(nc.sbuf_tensor("s_rowL", [1, 64], F32))
            s_rowP = _e(nc.sbuf_tensor("s_rowP", [1, 64], F32))
            s_Drow = _e(nc.sbuf_tensor("s_Drow", [1, 64], F32))
            s_Dmat = _e(nc.sbuf_tensor("s_Dmat", [128, 64], F32))
            s_ybuf = _e(nc.sbuf_tensor("s_ybuf", [128, D_INNER], BF16))
            s_gbuf = _e(nc.sbuf_tensor("s_gbuf", [128, 1024], F32))
            s_ynb = _e(nc.sbuf_tensor("s_ynb", [128, D_INNER], BF16))
            psC_a = _e(nc.psum_tensor("psC_a", [128, 512], F32))
            psC_b = _e(nc.psum_tensor("psC_b", [128, 512], F32))
            psC_t0 = _e(nc.psum_tensor("psC_t0", [128, 1024], BF16))
            psC_t1 = _e(nc.psum_tensor("psC_t1", [128, 1024], BF16))
            psC_g0 = _e(nc.psum_tensor("psC_g0", [128, 512], F32))
            psC_g1 = _e(nc.psum_tensor("psC_g1", [128, 512], F32))
            psC_y0 = _e(nc.psum_tensor("psC_y0", [128, 512], F32))
            psC_y1 = _e(nc.psum_tensor("psC_y1", [128, 512], F32))

            Cb = {}
            for nm in ("drep normw bbev bbod cbev cbod r32f rnev rnod bneg xsj flat "
                       "sin cos phr ph2 kbuf rbB rbC rmsBT rmsCT i1 i2 t1 t2 Bh Ch "
                       "smB smC e1 W WG tyi xsd A8 Adec rowL rowP Drow Dmat ybuf "
                       "gbuf ynb cc").split():
                Cb[nm] = B("c_" + nm)
            pCa = B("psC_a"); pCb = B("psC_b")
            pCt = [B("psC_t0"), B("psC_t1")]
            pCg = [B("psC_g0"), B("psC_g1")]
            pCy = [B("psC_y0"), B("psC_y1")]
            stb = [B(f"c_state{i}") for i in range(8)]
            Ub = [B(f"c_U{i}") for i in range(4)]
            Bdb = [B(f"c_Bd{i}") for i in range(4)]
            Wb = [B(f"c_W{i}") for i in range(4)]
            WGb = [B(f"c_WG{i}") for i in range(4)]
            e1b = [B(f"c_e1{i}") for i in range(4)]

            dma(s_bneg[:], bnegp, writes=[Cb["bneg"]])
            dma(s_r32f[:], r32f, writes=[Cb["r32f"]])
            dma(s_rnev[:], rn_ev, writes=[Cb["rnev"]])
            dma(s_rnod[:], rn_od, writes=[Cb["rnod"]])
            crb = Cb["flat"]
            def bcast(dst, src_row, width, wbuf):
                for g0 in range(0, width, 512):
                    w = min(512, width - g0)
                    ps = psC_a if (g0 // 512) % 2 == 0 else psC_b
                    pbx = pCa if (g0 // 512) % 2 == 0 else pCb
                    dma(s_flat[0:1, 0:w], src_row[0:1, g0:g0 + w],
                        reads=[crb], writes=[crb])
                    mm(ps[:, 0:w], s_ones[:], s_flat[0:1, 0:w], True, True,
                       reads=[crb, cb], writes=[pbx])
                    p.op("dve", lambda e, d=dst, g=g0, ww=w, pp=ps: e.tensor_copy(
                        d[:, g:g + ww], pp[:, 0:ww]), reads=[pbx], writes=[wbuf])
            bcast(s_drep, d_rep_r, 4096, Cb["drep"])
            bcast(s_normw, normw_r, 4096, Cb["normw"])
            bcast(s_bbev, bb_ev, 2048, Cb["bbev"])
            bcast(s_bbod, bb_od, 2048, Cb["bbod"])
            bcast(s_cbev, cb_ev, 2048, Cb["cbev"])
            bcast(s_cbod, cb_od, 2048, Cb["cbod"])
            p.op("dve", lambda e: e.memset(s_state[:], 0.0), writes=stb)
            p.op("dve", lambda e: e.memset(s_rowP[:], 0.0), writes=[Cb["rowP"]])

            def view32(t, blk):
                return t[:].rearrange("p (h g) -> p h g", g=64)[:, :, blk * 32:(blk + 1) * 32]

            for j in range(NTH):
                halo = (j == 0)
                p.serial_now = ("C" in ser_phases) or ("Cpro" in ser_phases)
                dma(s_rowL[:], s_slamT[:, j * 128 + 127: j * 128 + 128],
                    reads=[slamTb, Cb["rowL"]], writes=[Cb["rowL"]])
                if not halo:
                    mm(psC_a[:, 0:64], s_ones[:], s_rowP[0:1, 0:64], True, True,
                       reads=[Cb["rowP"], cb], writes=[pCa])
                    p.op("dve", lambda e, jj=j: e.tensor_sub(
                        s_A8[:], psC_a[:, 0:64], s_slam[:, jj * 64:(jj + 1) * 64]),
                        reads=[slamb, pCa], writes=[Cb["A8"]])
                    p.op("act", lambda e: e.activation(s_A8[:], s_A8[:], AF.Exp),
                         reads=[Cb["A8"]], writes=[Cb["A8"]])
                    p.op("dve", lambda e: e.tensor_scalar(s_A8[:], s_A8[:], 0.125, None, ALU.mult),
                         reads=[Cb["A8"]], writes=[Cb["A8"]])
                p.op("dve", lambda e: e.tensor_sub(s_Drow[:], s_rowP[:], s_rowL[:]),
                     reads=[Cb["rowP"], Cb["rowL"]], writes=[Cb["Drow"]])
                p.op("act", lambda e: e.activation(s_Drow[:], s_Drow[:], AF.Exp),
                     reads=[Cb["Drow"]], writes=[Cb["Drow"]])
                p.op("dve", lambda e: e.tensor_copy(s_rowP[:], s_rowL[:]),
                     reads=[Cb["rowL"]], writes=[Cb["rowP"]])
                mm(psC_a[:, 0:64], s_ones[:], s_Drow[0:1, 0:64], True, True,
                   reads=[Cb["Drow"], cb], writes=[pCa])
                p.op("dve", lambda e: e.tensor_copy(s_Dmat[:], psC_a[:, 0:64]),
                     reads=[pCa], writes=[Cb["Dmat"]])
                mm(psC_b[:, 0:64], s_ones[:], s_rowL[0:1, 0:64], True, True,
                   reads=[Cb["rowL"], cb], writes=[pCb])
                p.op("dve", lambda e, jj=j: e.tensor_sub(
                    s_Adec[:], s_slam[:, jj * 64:(jj + 1) * 64], psC_b[:, 0:64]),
                    reads=[slamb, pCb], writes=[Cb["Adec"]])
                p.op("act", lambda e: e.activation(s_Adec[:], s_Adec[:], AF.Exp),
                     reads=[Cb["Adec"]], writes=[Cb["Adec"]])
                for (src_off, rb_t, rb_b, rT_t, rT_b) in (
                        ((0, s_rbB, Cb["rbB"], s_rmsBT, Cb["rmsBT"]),
                         (64, s_rbC, Cb["rbC"], s_rmsCT, Cb["rmsCT"]))
                        if not halo else
                        ((0, s_rbB, Cb["rbB"], s_rmsBT, Cb["rmsBT"]),)):
                    off = j * 192 + src_off
                    p.op("act", lambda e, o=off: e.activation(
                        s_gbuf[:, 0:64], s_raw[:, o:o + 64], AF.Square,
                        accum_out=s_sall[:, 20:21]),
                        reads=[rawb], writes=[Cb["gbuf"], Cb["cc"]])
                    p.op("act", lambda e: e.activation(s_sall[:, 21:22], s_sall[:, 20:21], AF.Ln,
                                                       scale=1.0 / 64.0, bias=s_actc[:, 2:3]),
                         reads=[Cb["cc"], cb], writes=[Cb["cc"]])
                    p.op("act", lambda e: e.activation(s_sall[:, 21:22], s_sall[:, 21:22], AF.Exp, scale=-0.5),
                         reads=[Cb["cc"]], writes=[Cb["cc"]])
                    p.op("dve", lambda e, o=off, rt=rb_t: e.tensor_scalar(
                        rt[:], s_raw[:, o:o + 64], s_sall[:, 21:22], None, ALU.mult),
                        reads=[rawb, Cb["cc"]], writes=[rb_b])
                    tr(psC_t0[0:64, 0:128], rb_t[:], s_idb[:], reads=[rb_b, cb], writes=[pCt[0]])
                    p.op("dve", lambda e, rt=rT_t: e.tensor_copy(rt[:], psC_t0[0:64, 0:128]),
                         reads=[pCt[0]], writes=[rT_b])
                    if t_rb is not None and t_rbt is not None and rb_t is s_rbB and not halo:
                        dma(t_rb.rearrange("(n p) f -> n p f", p=128)[j - 1], rb_t[:], reads=[rb_b])
                        dma(t_rbt[:, (j - 1) * 128:j * 128], rT_t[:], reads=[rT_b])

                for h2 in range(2):
                    p.serial_now = ("C" in ser_phases) or ("Cbc" in ser_phases)
                    hb = h2 * 1024
                    dma(s_xsj[:], xs_dram[j][:, h2 * 2048:(h2 + 1) * 2048],
                        reads=[Cb["xsj"], xsdb], writes=[Cb["xsj"]])
                    tensors_bc = (
                        ((s_rmsBT, Cb["rmsBT"], s_rnev, s_rnod, s_bbev, Cb["bbev"],
                          s_bbod, Cb["bbod"], s_Bh, Cb["Bh"], s_smB, Cb["smB"], t_bh),
                         (s_rmsCT, Cb["rmsCT"], s_rnev, s_rnod, s_cbev, Cb["cbev"],
                          s_cbod, Cb["cbod"], s_Ch, Cb["Ch"], s_smC, Cb["smC"], t_ch))
                        if not halo else
                        ((s_rmsBT, Cb["rmsBT"], s_rnev, s_rnod, s_bbev, Cb["bbev"],
                          s_bbod, Cb["bbod"], s_Bh, Cb["Bh"], s_smB, Cb["smB"], None),))
                    for g in range(2):
                        ps, pbx = (psC_a, pCa) if g == 0 else (psC_b, pCb)
                        mm(ps[:], s_sdtT[:, j * 128:(j + 1) * 128],
                           s_r32f[:, hb + g * 512: hb + (g + 1) * 512], True, True,
                           reads=[sdtTb, Cb["r32f"]], writes=[pbx])
                        p.op("dve", lambda e, pp=ps: e.tensor_scalar(
                            s_k[:], pp[:], INV_2PI, MAGIC, ALU.mult, ALU.add),
                            reads=[pbx], writes=[Cb["kbuf"]])
                        p.op("dve", lambda e: e.tensor_scalar(s_k[:], s_k[:], MAGIC, None, ALU.subtract),
                             reads=[Cb["kbuf"]], writes=[Cb["kbuf"]])
                        p.op("dve", lambda e, pp=ps: e.scalar_tensor_tensor(
                            s_phr[:], s_k[:], -CW1, pp[:], ALU.mult, ALU.add),
                            reads=[pbx, Cb["kbuf"]], writes=[Cb["phr"]])
                        p.op("dve", lambda e: e.scalar_tensor_tensor(
                            s_ph2[:], s_k[:], -CW2, s_phr[:], ALU.mult, ALU.add),
                            reads=[Cb["phr"], Cb["kbuf"]], writes=[Cb["ph2"]])
                        p.op("dve", lambda e: e.scalar_tensor_tensor(
                            s_phr[:], s_k[:], -CW3, s_ph2[:], ALU.mult, ALU.add),
                            reads=[Cb["ph2"], Cb["kbuf"]], writes=[Cb["phr"]])
                        p.op("act", lambda e, gg=g: e.activation(
                            s_sin[:, gg * 512:(gg + 1) * 512], s_phr[:], AF.Sin),
                            reads=[Cb["phr"]], writes=[Cb["sin"]])
                        p.op("act", lambda e: e.activation(s_ph2[:], s_phr[:], AF.Abs),
                             reads=[Cb["phr"]], writes=[Cb["ph2"]])
                        p.op("act", lambda e, gg=g: e.activation(
                            s_cos[:, gg * 512:(gg + 1) * 512], s_ph2[:], AF.Sin,
                            scale=-1.0, bias=s_actc[:, 3:4]),
                            reads=[Cb["ph2"], cb], writes=[Cb["cos"]])
                    if t_sc is not None and not halo:
                        dma(t_sc.rearrange("(n p) f -> n p f", p=128)[j - 1, :, h2 * 1024:(h2 + 1) * 1024],
                            s_sin[:], reads=[Cb["sin"]])
                    if t_cc2 is not None and not halo:
                        dma(t_cc2.rearrange("(n p) f -> n p f", p=128)[j - 1, :, h2 * 1024:(h2 + 1) * 1024],
                            s_cos[:], reads=[Cb["cos"]])
                    for (rT_t, rT_b, pev, pod, bev, bevb, bod, bodb, out_t, out_b,
                         sm_t, sm_b, tp) in tensors_bc:
                        for (pat, pat_b, bias_t, bias_b, dst, dst_b) in (
                                (pev, Cb["rnev"], bev, bevb, s_i1, Cb["i1"]),
                                (pod, Cb["rnod"], bod, bodb, s_i2, Cb["i2"])):
                            for g in range(2):
                                ps, pbx = (psC_a, pCa) if g == 0 else (psC_b, pCb)
                                mm(ps[:], rT_t[:], pat[:, hb + g * 512: hb + (g + 1) * 512],
                                   True, True, reads=[rT_b, pat_b], writes=[pbx])
                                p.op("dve", lambda e, pp=ps, gg=g, bt=bias_t, dd=dst, hb_=hb:
                                     e.tensor_add(dd[:, gg * 512:(gg + 1) * 512], pp[:],
                                                  bt[:, hb_ + gg * 512: hb_ + (gg + 1) * 512]),
                                     reads=[pbx, bias_b], writes=[dst_b])
                        i1v = s_i1[:].rearrange("p (h f) -> p h f", f=32)
                        i2v = s_i2[:].rearrange("p (h f) -> p h f", f=32)
                        sv = s_sin[:].rearrange("p (h f) -> p h f", f=32)
                        cv = s_cos[:].rearrange("p (h f) -> p h f", f=32)
                        t1v = s_t1[:].rearrange("p (h f) -> p h f", f=32)
                        t2v = s_t2[:].rearrange("p (h f) -> p h f", f=32)
                        o1 = view32(out_t, 0)
                        o2 = view32(out_t, 1)
                        p.op("dve", lambda e, o=t1v, a=i1v, c=cv: e.tensor_mul(o, a, c),
                             reads=[Cb["i1"], Cb["cos"]], writes=[Cb["t1"]])
                        p.op("dve", lambda e, o=t2v, a=i2v, c=sv: e.tensor_mul(o, a, c),
                             reads=[Cb["i2"], Cb["sin"]], writes=[Cb["t2"]])
                        p.op("dve", lambda e, o=o1, a=t1v, c=t2v: e.tensor_sub(o, a, c),
                             reads=[Cb["t1"], Cb["t2"]], writes=[out_b])
                        p.op("dve", lambda e, o=t1v, a=i1v, c=sv: e.tensor_mul(o, a, c),
                             reads=[Cb["i1"], Cb["sin"]], writes=[Cb["t1"]])
                        p.op("dve", lambda e, o=t2v, a=i2v, c=cv: e.tensor_mul(o, a, c),
                             reads=[Cb["i2"], Cb["cos"]], writes=[Cb["t2"]])
                        p.op("dve", lambda e, o=o2, a=t1v, c=t2v: e.tensor_add(o, a, c),
                             reads=[Cb["t1"], Cb["t2"]], writes=[out_b])
                        if tp is not None and not halo:
                            dma(tp.rearrange("(n p) f -> n p f", p=128)[j - 1, :, h2 * 2048:(h2 + 1) * 2048],
                                out_t[:], reads=[out_b])
                        if t_i1 is not None and t_i2 is not None and out_t is s_Bh and not halo:
                            dma(t_i1.rearrange("(n p) f -> n p f", p=128)[j - 1, :, h2 * 1024:(h2 + 1) * 1024],
                                s_i1[:], reads=[Cb["i1"]])
                            dma(t_i2.rearrange("(n p) f -> n p f", p=128)[j - 1, :, h2 * 1024:(h2 + 1) * 1024],
                                s_i2[:], reads=[Cb["i2"]])
                        for half16 in (() if halo else range(2)):
                            pst, pstb = (psC_t0, pCt[0]) if half16 == 0 else (psC_t1, pCt[1])
                            for g2 in range(8):
                                gg2 = half16 * 8 + g2
                                tr(pst[:, g2 * 128:(g2 + 1) * 128],
                                   out_t[:, gg2 * 128:(gg2 + 1) * 128], s_idb[:],
                                   reads=[out_b, cb], writes=[pstb])
                            p.op("dve", lambda e, st=sm_t, hh=half16, pp=pst: e.tensor_copy(
                                st[:, hh * 1024:(hh + 1) * 1024], pp[:]),
                                reads=[pstb], writes=[sm_b])
                    p.serial_now = ("C" in ser_phases) or ("Cscan" in ser_phases)
                    for bb2 in range(16):
                        psm, psmb_ = (psC_a, pCa) if bb2 % 2 == 0 else (psC_b, pCb)
                        psg, psgb = (psC_g0, pCg[0]) if bb2 % 2 == 0 else (psC_g1, pCg[1])
                        hh0 = bb2 * 2
                        if hh0 % 4 == 0 and not halo:
                            h0g = h2 * 32 + hh0
                            dma(s_flat[:], s_slamT[h0g:h0g + 4, j * 128:(j + 1) * 128],
                                reads=[slamTb, Cb["flat"]], writes=[Cb["flat"]])
                        for q in range(2):
                            hh = hh0 + q
                            h = h2 * 32 + hh
                            hg = j * 64 + h
                            row0 = q * 64
                            g2c = bb2 * 128
                            bank = h // 8
                            psy, psyb = (psC_y0, pCy[0]) if bank % 2 == 0 else (psC_y1, pCy[1])
                            sl = (h % 8) * 64
                            scol = (h // 2) * 64
                            if halo:
                                p.op("dve", lambda e, qq=q, hl=hh, c=hg: e.tensor_scalar(
                                    s_U[:, qq * 64:(qq + 1) * 64],
                                    s_xsj[:, hl * 64:(hl + 1) * 64],
                                    s_dt[:, c:c + 1], None, ALU.mult),
                                    reads=[Cb["xsj"], dtb_], writes=[Ub[q]])
                                p.op("dve", lambda e, qq=q, hh_=hh, c=h: e.tensor_scalar(
                                    s_Bdec[:, qq * 64:(qq + 1) * 64],
                                    s_Bh[:, hh_ * 64:(hh_ + 1) * 64],
                                    s_Adec[:, c:c + 1], None, ALU.mult),
                                    reads=[Cb["Bh"], Cb["Adec"]], writes=[Bdb[q]])
                                mm(psm[0:64, 256 + q * 64: 256 + (q + 1) * 64],
                                   s_Bdec[:, q * 64:(q + 1) * 64],
                                   s_U[:, q * 64:(q + 1) * 64], True, True,
                                   reads=[Bdb[q], Ub[q]], writes=[psmb_])
                                p.op("dve", lambda e, r0=row0, sc=scol, hh_=h: e.tensor_scalar(
                                    s_state[r0:r0 + 64, sc:sc + 64],
                                    s_state[r0:r0 + 64, sc:sc + 64],
                                    s_Dmat[r0:r0 + 64, hh_:hh_ + 1], None, ALU.mult),
                                    reads=[stb[h // 8], Cb["Dmat"]], writes=[stb[h // 8]])
                                p.op("dve", lambda e, r0=row0, sc=scol, qq=q, pp=psm: e.tensor_add(
                                    s_state[r0:r0 + 64, sc:sc + 64],
                                    s_state[r0:r0 + 64, sc:sc + 64],
                                    pp[0:64, 256 + qq * 64: 256 + (qq + 1) * 64]),
                                    reads=[stb[h // 8], psmb_], writes=[stb[h // 8]])
                                continue
                            mm(psm[:, q * 128:(q + 1) * 128], s_ones[:],
                               s_flat[0:1, (hh % 4) * 128:(hh % 4 + 1) * 128], True, True,
                               reads=[Cb["flat"], cb], writes=[psmb_])
                            p.op("dve", lambda e, qq=q, pp=psm: e.scalar_tensor_tensor(
                                s_e1[:, qq * 128:(qq + 1) * 128], pp[:, qq * 128:(qq + 1) * 128],
                                -1.0, s_bneg[:], ALU.mult, ALU.add),
                                reads=[psmb_, Cb["bneg"]], writes=[e1b[q]])
                            p.op("act", lambda e, qq=q, c=hg: e.activation(
                                s_W[:, qq * 128:(qq + 1) * 128], s_e1[:, qq * 128:(qq + 1) * 128],
                                AF.Exp, bias=s_slam[:, c:c + 1]),
                                reads=[e1b[q], slamb], writes=[Wb[q]])
                            mm(psg[:, q * 128:(q + 1) * 128],
                               s_smB[row0:row0 + 64, g2c:g2c + 128],
                               s_smC[row0:row0 + 64, g2c:g2c + 128], True, True,
                               reads=[Cb["smB"], Cb["smC"]], writes=[psgb])
                            p.op("dve", lambda e, qq=q, pp=psg: e.tensor_mul(
                                s_WG[:, qq * 128:(qq + 1) * 128],
                                s_W[:, qq * 128:(qq + 1) * 128],
                                pp[:, qq * 128:(qq + 1) * 128]),
                                reads=[Wb[q], psgb], writes=[WGb[q]])
                            p.op("dve", lambda e, qq=q, hl=hh, c=hg: e.tensor_scalar(
                                s_U[:, qq * 64:(qq + 1) * 64],
                                s_xsj[:, hl * 64:(hl + 1) * 64],
                                s_dt[:, c:c + 1], None, ALU.mult),
                                reads=[Cb["xsj"], dtb_], writes=[Ub[q]])
                            p.op("dve", lambda e, qq=q, hh_=hh, c=h: e.tensor_scalar(
                                s_Bdec[:, qq * 64:(qq + 1) * 64],
                                s_Bh[:, hh_ * 64:(hh_ + 1) * 64],
                                s_Adec[:, c:c + 1], None, ALU.mult),
                                reads=[Cb["Bh"], Cb["Adec"]], writes=[Bdb[q]])
                            mm(psy[:, sl:sl + 64], s_WG[:, q * 128:(q + 1) * 128],
                               s_U[:, q * 64:(q + 1) * 64], True, True,
                               reads=[WGb[q], Ub[q]], writes=[psyb])
                            mm(psg[:, 256 + q * 64: 256 + (q + 1) * 64],
                               s_smC[row0:row0 + 64, g2c:g2c + 128],
                               s_state[row0:row0 + 64, scol:scol + 64], True, True,
                               reads=[Cb["smC"], stb[h // 8]], writes=[psgb])
                            p.op("dve", lambda e, hh_=h, hl=hh: e.tensor_mul(
                                s_xsd[:, (hh_ % 8) * 64:((hh_ % 8) + 1) * 64],
                                s_xsj[:, hl * 64:(hl + 1) * 64],
                                s_drep[:, hh_ * 64:(hh_ + 1) * 64]),
                                reads=[Cb["xsj"], Cb["drep"]], writes=[Cb["xsd"]])
                            p.op("dve", lambda e, qq=q, hh_=h, pp=psg: e.scalar_tensor_tensor(
                                s_tyi[:, (hh_ % 8) * 64:((hh_ % 8) + 1) * 64],
                                pp[:, 256 + qq * 64: 256 + (qq + 1) * 64],
                                s_A8[:, hh_:hh_ + 1],
                                s_xsd[:, (hh_ % 8) * 64:((hh_ % 8) + 1) * 64],
                                ALU.mult, ALU.add),
                                reads=[psgb, Cb["A8"], Cb["xsd"]], writes=[Cb["tyi"]])
                            mm(psm[0:64, 256 + q * 64: 256 + (q + 1) * 64],
                               s_Bdec[:, q * 64:(q + 1) * 64],
                               s_U[:, q * 64:(q + 1) * 64], True, True,
                               reads=[Bdb[q], Ub[q]], writes=[psmb_])
                            p.op("dve", lambda e, r0=row0, sc=scol, hh_=h: e.tensor_scalar(
                                s_state[r0:r0 + 64, sc:sc + 64],
                                s_state[r0:r0 + 64, sc:sc + 64],
                                s_Dmat[r0:r0 + 64, hh_:hh_ + 1], None, ALU.mult),
                                reads=[stb[h // 8], Cb["Dmat"]], writes=[stb[h // 8]])
                            p.op("dve", lambda e, r0=row0, sc=scol, qq=q, pp=psm: e.tensor_add(
                                s_state[r0:r0 + 64, sc:sc + 64],
                                s_state[r0:r0 + 64, sc:sc + 64],
                                pp[0:64, 256 + qq * 64: 256 + (qq + 1) * 64]),
                                reads=[stb[h // 8], psmb_], writes=[stb[h // 8]])
                        if (hh0 + 2) % 8 == 0 and not halo:
                            bank = (h2 * 32 + hh0) // 8
                            psy, psyb = (psC_y0, pCy[0]) if bank % 2 == 0 else (psC_y1, pCy[1])
                            p.op("dve", lambda e, bk=bank, pp=psy: e.scalar_tensor_tensor(
                                s_ybuf[:, bk * 512:(bk + 1) * 512], pp[:], 0.125,
                                s_tyi[:], ALU.mult, ALU.add),
                                reads=[psyb, Cb["tyi"]], writes=[Cb["ybuf"]])
                p.serial_now = ("C" in ser_phases) or ("Cgate" in ser_phases)
                if halo:
                    continue
                if t_y is not None:
                    dma(t_y.rearrange("(n p) f -> n p f", p=128)[j - 1], s_ybuf[:],
                        reads=[Cb["ybuf"]])
                for gh in range(4):
                    p.op("dve", lambda e, jj=j - 1, g0=gh: e.tensor_mul(
                        s_gbuf[:], s_ybuf[:, g0 * 1024:(g0 + 1) * 1024],
                        s_pool[:, jj * 4096 + g0 * 1024: jj * 4096 + (g0 + 1) * 1024]),
                        reads=[Cb["ybuf"], poolb], writes=[Cb["gbuf"]])
                    p.op("act", lambda e, g0=gh: e.activation(
                        s_ynb[:, g0 * 1024:(g0 + 1) * 1024], s_gbuf[:], AF.Square,
                        accum_out=s_sall[:, 24 + g0:25 + g0]),
                        reads=[Cb["gbuf"]], writes=[Cb["ynb"], Cb["cc"]])
                p.op("dve", lambda e: e.tensor_add(s_sall[:, 24:25], s_sall[:, 24:25], s_sall[:, 25:26]),
                     reads=[Cb["cc"]], writes=[Cb["cc"]])
                p.op("dve", lambda e: e.tensor_add(s_sall[:, 26:27], s_sall[:, 26:27], s_sall[:, 27:28]),
                     reads=[Cb["cc"]], writes=[Cb["cc"]])
                p.op("dve", lambda e: e.tensor_add(s_sall[:, 22:23], s_sall[:, 24:25], s_sall[:, 26:27]),
                     reads=[Cb["cc"]], writes=[Cb["cc"]])
                p.op("act", lambda e: e.activation(s_sall[:, 23:24], s_sall[:, 22:23], AF.Ln,
                                                   scale=1.0 / float(D_INNER), bias=s_actc[:, 2:3]),
                     reads=[Cb["cc"], cb], writes=[Cb["cc"]])
                p.op("act", lambda e: e.activation(s_sall[:, 23:24], s_sall[:, 23:24], AF.Exp, scale=-0.5),
                     reads=[Cb["cc"]], writes=[Cb["cc"]])
                for gh in range(4):
                    p.op("dve", lambda e, jj=j - 1, g0=gh: e.tensor_mul(
                        s_gbuf[:], s_ybuf[:, g0 * 1024:(g0 + 1) * 1024],
                        s_pool[:, jj * 4096 + g0 * 1024: jj * 4096 + (g0 + 1) * 1024]),
                        reads=[Cb["ybuf"], poolb], writes=[Cb["gbuf"]])
                    p.op("dve", lambda e, g0=gh: e.scalar_tensor_tensor(
                        s_ynb[:, g0 * 1024:(g0 + 1) * 1024], s_gbuf[:], s_sall[:, 23:24],
                        s_normw[:, g0 * 1024:(g0 + 1) * 1024], ALU.mult, ALU.mult),
                        reads=[Cb["gbuf"], Cb["cc"], Cb["normw"]], writes=[Cb["ynb"]])
                if t_yn is not None:
                    for gh in range(4):
                        p.op("dve", lambda e, g0=gh: e.tensor_copy(
                            s_gbuf[:], s_ynb[:, g0 * 1024:(g0 + 1) * 1024]),
                            reads=[Cb["ynb"]], writes=[Cb["gbuf"]])
                        dma(t_yn.rearrange("(n p) f -> n p f", p=128)[j - 1, :, gh * 1024:(gh + 1) * 1024],
                            s_gbuf[:], reads=[Cb["gbuf"]])
                for q in range(4):
                    pst, pstb = (psC_t0, pCt[0]) if q % 2 == 0 else (psC_t1, pCt[1])
                    for c in range(8):
                        kc = q * 8 + c
                        tr(pst[:, c * 128:(c + 1) * 128], s_ynb[:, kc * 128:(kc + 1) * 128],
                           s_idb[:], reads=[Cb["ynb"], cb], writes=[pstb])
                    p.op("dve", lambda e, jj=j - 1, qq=q, pp=pst: e.tensor_copy(
                        s_pool[:, jj * 4096 + qq * 1024: jj * 4096 + (qq + 1) * 1024], pp[:]),
                        reads=[pstb], writes=[poolb])
    p.fence()

    p.serial_now = "E" in ser_phases
    with contextlib.ExitStack() as _esED:
        _e = _esED.enter_context
        s_attn = _e(nc.sbuf_tensor("s_attn", [128, NT * 2048], F32))
        attnb = B("attn")
        if n_phases >= 4:
            t_attn = tap("attn_out", [TOK, D_MODEL])
            with contextlib.ExitStack() as _esE:
                _e2 = _esE.enter_context
                s_web = [_e2(nc.sbuf_tensor(f"s_web{i}", [128, 512], F32)) for i in range(2)]
                s_webb = [_e2(nc.sbuf_tensor(f"s_webb{i}", [128, 512], BF16)) for i in range(2)]
                psE = _e2(nc.psum_tensor("psE", [128, 4096], F32))
                webb = [B("web0"), B("web1")]
                webbb = [B("webb0"), B("webb1")]
                pE = [B(f"psE{j}") for j in range(8)]
                w_out_t = w_out.rearrange("(kc p) o -> kc p o", p=128)
                for og in range(4):
                    for kc in range(32):
                        par = kc % 2
                        dma(s_web[par][:], w_out_t[kc, :, og * 512:(og + 1) * 512],
                            reads=[webb[par]], writes=[webb[par]])
                        p.op("pool", lambda e, pr=par: e.tensor_copy(s_webb[pr][:], s_web[pr][:]),
                             reads=[webb[par]], writes=[webbb[par]])
                        for j in range(NT):
                            mm(psE[:, j * 512:(j + 1) * 512],
                               s_pool[:, j * 4096 + kc * 128: j * 4096 + (kc + 1) * 128],
                               s_webb[par][:], kc == 0, kc == 31,
                               reads=[poolb, webbb[par]], writes=[pE[j]])
                    for j in range(NT):
                        p.op("dve", lambda e, jj=j, o=og: e.tensor_copy(
                            s_attn[:, jj * 2048 + o * 512: jj * 2048 + (o + 1) * 512],
                            psE[:, jj * 512:(jj + 1) * 512]),
                            reads=[pE[j]], writes=[attnb])
            if t_attn is not None:
                for j in range(NT):
                    dma(t_attn.rearrange("(n p) f -> n p f", p=128)[j],
                        s_attn[:, j * 2048:(j + 1) * 2048], reads=[attnb])
        p.fence()

        p.serial_now = "D1" in ser_phases
        if n_phases >= 5:
            t_gu = tap("gu", [TOK, D_FFN])
            t_gate = tap("gatev", [TOK, D_FFN])
            for kc in range(16):
                dma(s_pool[:, kc * 1024:(kc + 1) * 1024], xqT_dram[kc],
                    reads=[xqTdb, poolb], writes=[poolb])
            with contextlib.ExitStack() as _esD1:
                _e2 = _esD1.enter_context
                s_wfb = [_e2(nc.sbuf_tensor("s_wfb0", [128, 2048], F32))]
                s_wfq = _e2(nc.sbuf_tensor("s_wfq", [128, 2048], BF16))
                s_wqT = [_e2(nc.sbuf_tensor(f"s_wqT{i}", [128, 16 * 256], BF16)) for i in range(2)]
                s_gg = _e2(nc.sbuf_tensor("s_gg", [128, 4096], F32))
                s_sxw = _e2(nc.sbuf_tensor("s_sxw", [128, 16], F32))
                s_gu2 = _e2(nc.sbuf_tensor("s_gu2", [128, 256], F32))
                s_gstg = _e2(nc.sbuf_tensor("s_gstg", [128, 256], F32))
                s_wdb = _e2(nc.sbuf_tensor("s_wdb", [128, 1024], F32))
                s_wdq = _e2(nc.sbuf_tensor("s_wdq", [128, 1024], BF16))
                s_wdstg = _e2(nc.sbuf_tensor("s_wdstg", [128, 512], BF16))
                psD = _e2(nc.psum_tensor("psD", [128, 2048], F32))
                psDt = _e2(nc.psum_tensor("psDt", [128, 1024], BF16))
                psDt2 = _e2(nc.psum_tensor("psDt2", [128, 1024], BF16))
                wfbb = [B("wfb0")]
                wfqb = B("wfq")
                wqTb = [B("wqT0"), B("wqT1")]
                ggb = B("gg")
                sxwb = B("sxw")
                gstgb = B("gstg")
                wdbb = B("wdb"); wdqb = B("wdq"); wdstgb = B("wdstg")
                pD = [B(f"psD{j}") for j in range(4)]
                pDt = [B("psDt"), B("psDt2")]

                gu2b = B("gu2")
                for j in range(NT):
                    p.op("dve", lambda e, jj=j: e.tensor_mul(
                        s_sxw[:, jj:jj + 1], s_sall[:, jj:jj + 1], s_ws[:, 4:5]),
                        reads=[sallb, wsb2], writes=[sxwb])
                    p.op("dve", lambda e, jj=j: e.tensor_mul(
                        s_sxw[:, 8 + jj:9 + jj], s_sall[:, jj:jj + 1], s_ws[:, 5:6]),
                        reads=[sallb, wsb2], writes=[sxwb])

                def wd_prepass():
                    wd_v3 = wd.rearrange("(n p) k -> n p k", p=128)
                    for blk in range(128):
                        r, e8 = blk // 8, blk % 8
                        dma(s_wdb[:], wd_v3[r][:, e8 * 1024:(e8 + 1) * 1024],
                            reads=[wdbb], writes=[wdbb], q="act")
                        p.op("dve", lambda e: e.tensor_scalar(
                            s_wdb[:], s_wdb[:], s_ws[:, 2:3], MAGIC, ALU.mult, ALU.add),
                            reads=[wdbb, wsb2], writes=[wdbb])
                        p.op("dve", lambda e: e.tensor_scalar(
                            s_wdb[:], s_wdb[:], MAGIC, 1.0, ALU.subtract, ALU.min),
                            reads=[wdbb], writes=[wdbb])
                        p.op("dve", lambda e: e.tensor_scalar(
                            s_wdq[:], s_wdb[:], -1.0, None, ALU.max),
                            reads=[wdbb], writes=[wdqb])
                        pst, pstb = (psDt, pDt[0]) if blk % 2 == 0 else (psDt2, pDt[1])
                        for c in range(8):
                            tr(pst[:, c * 128:(c + 1) * 128],
                               s_wdq[:, c * 128:(c + 1) * 128], s_idb[:],
                               reads=[wdqb, cb], writes=[pstb])
                        for hf in range(2):
                            p.op("dve", lambda e, pp=pst, h0=hf: e.tensor_copy(
                                s_wdstg[:], pp[:, h0 * 512:(h0 + 1) * 512]),
                                reads=[pstb], writes=[wdstgb])
                            for c in range(4):
                                kcl = hf * 4 + c
                                kc2 = e8 * 8 + kcl
                                dma(wqdT_dram[kc2, :, r * 128:(r + 1) * 128],
                                    s_wdstg[:, c * 128:(c + 1) * 128],
                                    reads=[wdstgb], writes=[wqdb], q="act")
                        yield

                wdgen = wd_prepass()

                for g in range(16):
                    for mi, (wm, wscol) in enumerate(((wg, 4), (wu, 5))):
                        wm_t = wm.rearrange("(n p) k -> n p k", p=128)
                        for ho in range(2):
                            wp = (mi * 2 + ho) % 2
                            for c in range(2):
                                par = 0
                                dma(s_wfb[par][:], wm_t[g * 4 + ho * 2 + c],
                                    reads=[wfbb[par]], writes=[wfbb[par]])
                                wcol = 0 if wscol == 4 else 1
                                p.op("dve", lambda e, pr=par, wc=wcol: e.tensor_scalar(
                                    s_wfb[pr][:], s_wfb[pr][:], s_ws[:, wc:wc + 1], MAGIC,
                                    ALU.mult, ALU.add), reads=[wfbb[par], wsb2], writes=[wfbb[par]])
                                p.op("dve", lambda e, pr=par: e.tensor_scalar(
                                    s_wfb[pr][:], s_wfb[pr][:], MAGIC, 1.0, ALU.subtract, ALU.min),
                                    reads=[wfbb[par]], writes=[wfbb[par]])
                                p.op("dve", lambda e, pr=par: e.tensor_scalar(
                                    s_wfq[:], s_wfb[pr][:], -1.0, None, ALU.max),
                                    reads=[wfbb[par]], writes=[wfqb])
                                for q in range(2):
                                    pst, pstb = (psDt, pDt[0]) if q % 2 == 0 else (psDt2, pDt[1])
                                    for cc2 in range(8):
                                        kcl = q * 8 + cc2
                                        tr(pst[:, cc2 * 128:(cc2 + 1) * 128],
                                           s_wfq[:, kcl * 128:(kcl + 1) * 128], s_idb[:],
                                           reads=[wfqb, cb], writes=[pstb])
                                    p.op("dve", lambda e, qq=q, ccx=c, pp=pst, w=wp: e.tensor_copy(
                                        s_wqT[w][:].rearrange("p (kc o) -> p kc o", o=256)
                                        [:, qq * 8:(qq + 1) * 8, ccx * 128:(ccx + 1) * 128],
                                        pp[:].rearrange("p (kc t) -> p kc t", t=128)),
                                        reads=[pstb], writes=[wqTb[wp]])
                            for tg in range(2):
                                for kc in range(16):
                                    for ji in range(4):
                                        j = tg * 4 + ji
                                        mm(psD[:, ji * 512 + ho * 256: ji * 512 + ho * 256 + 256],
                                           s_pool[:, kc * 1024 + j * 128: kc * 1024 + j * 128 + 128],
                                           s_wqT[wp][:, kc * 256:(kc + 1) * 256], kc == 0, kc == 15,
                                           reads=[poolb, wqTb[wp]], writes=[pD[ji]])
                                for ji in range(4):
                                    j = tg * 4 + ji
                                    psl = psD[:, ji * 512 + ho * 256: ji * 512 + ho * 256 + 256]
                                    c0 = g * 512 + ho * 256
                                    if mi == 0:
                                        p.op("act", lambda e, jj=j, ps=psl, h0=ho: e.activation(
                                            s_gg[:, jj * 512 + h0 * 256: jj * 512 + h0 * 256 + 256],
                                            ps, AF.Silu, scale=s_sxw[:, jj:jj + 1]),
                                            reads=[pD[ji], sxwb], writes=[ggb])
                                        if t_gate is not None:
                                            dma(t_gate.rearrange("(n p) f -> n p f", p=128)[j, :, c0:c0 + 256],
                                                s_gg[:, j * 512 + ho * 256: j * 512 + ho * 256 + 256],
                                                reads=[ggb])
                                    else:
                                        p.op("dve", lambda e, jj=j, ps=psl: e.tensor_scalar(
                                            s_gstg[:, 0:256], ps,
                                            s_sxw[:, 8 + jj:9 + jj], None, ALU.mult),
                                            reads=[pD[ji], sxwb], writes=[gstgb])
                                        p.op("dve", lambda e, jj=j, h0=ho: e.tensor_mul(
                                            s_gu2[:, 0:256],
                                            s_gg[:, jj * 512 + h0 * 256: jj * 512 + h0 * 256 + 256],
                                            s_gstg[:, 0:256]),
                                            reads=[ggb, gstgb], writes=[gu2b])
                                        p.op("dve", lambda e: e.reduce_max(
                                            s_rmax[:, 8:9], s_gu2[:, 0:256], axis=AX,
                                            apply_absolute_value=True),
                                            reads=[gu2b], writes=[rmaxb])
                                        p.op("dve", lambda e, jj=j: e.tensor_max(
                                            s_rmax[:, jj:jj + 1], s_rmax[:, jj:jj + 1],
                                            s_rmax[:, 8:9]),
                                            reads=[rmaxb], writes=[rmaxb])
                                        dma(gu_dram[j, :, c0:c0 + 256], s_gu2[:, 0:256],
                                            reads=[gu2b], writes=[gudb])
                                        if t_gu is not None:
                                            dma(t_gu.rearrange("(n p) f -> n p f", p=128)[j, :, c0:c0 + 256],
                                                s_gu2[:, 0:256], reads=[gu2b])
                    for _ in range(8):
                        next(wdgen, None)
                for _ in wdgen:
                    pass
        p.fence()

        p.serial_now = "D2" in ser_phases
        if n_phases >= 6:
            with contextlib.ExitStack() as _esD2:
                _e2 = _esD2.enter_context
                s_gub = _e2(nc.sbuf_tensor("s_gub", [128, 4096], F32))
                s_xq2 = _e2(nc.sbuf_tensor("s_xq2", [128, 4096], BF16))
                s_wdT = [_e2(nc.sbuf_tensor(f"s_wdT{i}", [128, 512], BF16)) for i in range(2)]
                s_dcol = _e2(nc.sbuf_tensor("s_dcol", [128, 8], F32))
                s_xrl = _e2(nc.sbuf_tensor("s_xrl", [128, 512], F32))
                s_otmp = _e2(nc.sbuf_tensor("s_otmp", [128, 512], F32))
                psD2 = _e2(nc.psum_tensor("psD2", [128, 2048], F32))
                psDq = _e2(nc.psum_tensor("psDq", [128, 1024], BF16))
                psDq2 = _e2(nc.psum_tensor("psDq2", [128, 1024], BF16))
                gubb = B("gub"); xq2b = B("xq2")
                wdTb = [B("wdT0"), B("wdT1")]
                dcolb = B("dcol"); xrlb = B("xrl"); otmpb = B("otmp")
                pD2 = [B(f"psD2{j}") for j in range(4)]
                pDq = [B("psDq"), B("psDq2")]

                for half in range(2):
                    js = [half * 4 + i for i in range(4)]
                    for j in js:
                        p.op("dve", lambda e, jj=j: e.tensor_scalar(
                            s_sall[:, 8 + jj: 9 + jj], s_rmax[:, jj:jj + 1],
                            1.0 / 127.0, EPS_F, ALU.mult, ALU.add),
                            reads=[rmaxb], writes=[sallb])
                        p.op("dve", lambda e, jj=j: e.reciprocal(
                            s_sall[:, 16:17], s_sall[:, 8 + jj: 9 + jj]),
                            reads=[sallb], writes=[sallb])
                        for gq in range(2):
                            dma(s_gub[:], gu_dram[j, :, gq * 4096:(gq + 1) * 4096],
                                reads=[gudb, gubb], writes=[gubb])
                            p.op("dve", lambda e: e.tensor_scalar(
                                s_gub[:], s_gub[:], s_sall[:, 16:17], MAGIC, ALU.mult, ALU.add),
                                reads=[gubb, sallb], writes=[gubb])
                            p.op("dve", lambda e: e.tensor_scalar(
                                s_xq2[:], s_gub[:], MAGIC, None, ALU.subtract),
                                reads=[gubb], writes=[xq2b])
                            for q in range(4):
                                pst, pstb = (psDq, pDq[0]) if q % 2 == 0 else (psDq2, pDq[1])
                                for c in range(8):
                                    kcl = q * 8 + c
                                    tr(pst[:, c * 128:(c + 1) * 128],
                                       s_xq2[:, kcl * 128:(kcl + 1) * 128], s_idb[:],
                                       reads=[xq2b, cb], writes=[pstb])
                                p.op("dve", lambda e, jj=j, gg=gq, qq=q, pp=pst: e.tensor_copy(
                                    s_pool[:, (jj % 4) * 8192 + gg * 4096 + qq * 1024:
                                            (jj % 4) * 8192 + gg * 4096 + (qq + 1) * 1024], pp[:]),
                                    reads=[pstb], writes=[poolb])
                    for og in range(4):
                        for kc2 in range(64):
                            par = kc2 % 2
                            dma(s_wdT[par][:], wqdT_dram[kc2, :, og * 512:(og + 1) * 512],
                                reads=[wdTb[par], wqdb], writes=[wdTb[par]])
                            for ji, j in enumerate(js):
                                mm(psD2[:, ji * 512:(ji + 1) * 512],
                                   s_pool[:, (j % 4) * 8192 + kc2 * 128: (j % 4) * 8192 + kc2 * 128 + 128],
                                   s_wdT[par][:], kc2 == 0, kc2 == 63,
                                   reads=[poolb, wdTb[par]], writes=[pD2[ji]])
                        for ji, j in enumerate(js):
                            p.op("dve", lambda e, jj=j: e.tensor_mul(
                                s_dcol[:, jj % 4:jj % 4 + 1], s_sall[:, 8 + jj: 9 + jj],
                                s_ws[:, 6:7]), reads=[sallb, wsb2], writes=[dcolb])
                            dma(s_xrl[:], x_t[j + 1][:, og * 512:(og + 1) * 512],
                                reads=[xrlb], writes=[xrlb])
                            p.op("dve", lambda e, jj=j, ji_=ji, o=og: e.scalar_tensor_tensor(
                                s_otmp[:], psD2[:, ji_ * 512:(ji_ + 1) * 512],
                                s_dcol[:, jj % 4:jj % 4 + 1],
                                s_attn[:, jj * 2048 + o * 512: jj * 2048 + (o + 1) * 512],
                                ALU.mult, ALU.add),
                                reads=[pD2[ji], dcolb, attnb], writes=[otmpb])
                            p.op("dve", lambda e: e.tensor_add(s_otmp[:], s_otmp[:], s_xrl[:]),
                                 reads=[otmpb, xrlb], writes=[otmpb])
                            dma(y_t[j][:, og * 512:(og + 1) * 512], s_otmp[:],
                                reads=[otmpb], writes=[outb])

    if n_phases < 6:
        with nc.sbuf_tensor("s_dummy", [128, 2048], F32) as s_dummy:
            p.op("dve", lambda e: e.memset(s_dummy[:], 0.0), writes=[B("dummy")])
            for j in range(NT):
                dma(y_t[j], s_dummy[:], reads=[B("dummy")], writes=[outb])

    p.truncate = truncate
    p.finalize()
    return nc, tap_outs



def _consts(inputs):
    rope_freqs = np.asarray(inputs["L0_attn_rope_freqs"], np.float32)
    B_bias = np.asarray(inputs["L0_attn_B_bias"], np.float32)
    C_bias = np.asarray(inputs["L0_attn_C_bias"], np.float32)
    D_param = np.asarray(inputs["L0_attn_D_param"], np.float32)
    norm_w = np.asarray(inputs["L0_attn_norm_w"], np.float32)
    dt_bias = np.asarray(inputs["L0_attn_dt_bias"], np.float32)
    A_log = np.asarray(inputs["L0_attn_A_log"], np.float32)

    r32f = np.zeros((64, 2048), np.float32)
    for h in range(64):
        r32f[h, h * 32:(h + 1) * 32] = rope_freqs[h]
    rn_ev = np.zeros((64, 2048), ml_dtypes.bfloat16)
    rn_od = np.zeros((64, 2048), ml_dtypes.bfloat16)
    for h in range(64):
        for f in range(32):
            rn_ev[2 * f, h * 32 + f] = 1
            rn_od[2 * f + 1, h * 32 + f] = 1
    bneg = np.where(np.arange(128)[:, None] <= np.arange(128)[None, :], 0.0,
                    BIGNEG).astype(np.float32)
    return {
        "dt_bias_r": dt_bias.reshape(1, 64),
        "a_log_c": A_log.reshape(64, 1),
        "d_rep_r": np.repeat(D_param, 64).reshape(1, 4096),
        "bb_ev": np.ascontiguousarray(B_bias[:, 0::2]).reshape(1, 2048),
        "bb_od": np.ascontiguousarray(B_bias[:, 1::2]).reshape(1, 2048),
        "cb_ev": np.ascontiguousarray(C_bias[:, 0::2]).reshape(1, 2048),
        "cb_od": np.ascontiguousarray(C_bias[:, 1::2]).reshape(1, 2048),
        "normw_r": norm_w.reshape(1, 4096),
        "r32f": r32f,
        "rn_ev": rn_ev,
        "rn_od": rn_od,
        "idf": np.eye(128, dtype=np.float32),
        "idb": np.eye(128, dtype=ml_dtypes.bfloat16),
        "onesf": np.ones((1, 128), np.float32),
        "actc": np.broadcast_to(np.array([[1e-5, 1.0, 1e-6, 0.5 * math.pi, 0, 0, 0, 0]],
                                         np.float32), (128, 8)).copy(),
        "bnegp": bneg,
    }


def make_in_maps(inputs):
    x = np.asarray(inputs["L0_attn_x"], np.float32).reshape(8192, 2048)
    w_in = np.ascontiguousarray(np.asarray(inputs["L0_attn_W_in"], np.float32))
    w_out = np.ascontiguousarray(np.asarray(inputs["L0_attn_W_out"], np.float32))
    wgm = np.ascontiguousarray(np.asarray(inputs["L0_ffn_w_gate"], np.float32))
    wum = np.ascontiguousarray(np.asarray(inputs["L0_ffn_w_up"], np.float32))
    wdm = np.ascontiguousarray(np.asarray(inputs["L0_ffn_w_down"], np.float32))
    consts = _consts(inputs)
    in_maps = []
    zrow = np.zeros((128, 2048), np.float32)
    for k in range(8):
        m = dict(consts)
        halo = zrow if k == 0 else x[k * 1024 - 128: k * 1024]
        m["x"] = np.ascontiguousarray(np.concatenate([halo, x[k * 1024:(k + 1) * 1024]], axis=0))
        m["w_in"] = w_in
        m["w_out"] = w_out
        m["wg"] = wgm
        m["wu"] = wum
        m["wd"] = wdm
        in_maps.append(m)
    return in_maps


def kernel(**inputs):
    nc, _ = build_program()
    in_maps = make_in_maps(inputs)
    res = run_bass_kernel_spmd(nc, in_maps, core_ids=list(range(8)))
    out = np.concatenate([np.asarray(r["y"]) for r in res.results], axis=0)
    return out.reshape(1, 8192, 2048).astype(np.float32)


# version 08
# speedup vs baseline: 1.7761x; 1.0110x over previous
import contextlib
import math
import numpy as np
import ml_dtypes
import concourse.bass as bass
import concourse.mybir as mybir
from concourse.bass_utils import run_bass_kernel_spmd

F32 = mybir.dt.float32
BF16 = mybir.dt.bfloat16
AF = mybir.ActivationFunctionType
ALU = mybir.AluOpType
AX = mybir.AxisListType.X

MAGIC = float(np.float32(1.5 * 2 ** 23))
PI = float(np.float32(math.pi))
TWO_PI = 6.283185307179586
INV_2PI = float(np.float32(1.0 / TWO_PI))
CW1 = float(np.float32(6.28125))
_r1 = TWO_PI - 6.28125
CW2 = float(np.float32(_r1))
CW3 = float(np.float32(_r1 - float(np.float32(_r1))))
BIGNEG = -1e30

D_MODEL = 2048
N_HEADS = 64
D_INNER = 4096
D_IN_PROJ = 8384
D_FFN = 8192
EPS_A = 1e-6
EPS_F = 1e-5
EPS_LN = 1e-5

TOK = 1024
NT = 8
NTH = 9
TOKH = NTH * 128

NPH = 99


class Prog:

    def __init__(self, nc):
        self.nc = nc
        self.engines = {
            "pe": nc.tensor, "act": nc.scalar, "dve": nc.vector,
            "pool": nc.gpsimd, "sp": nc.sync,
        }
        self.ops = []
        self.deps = []
        self.dma_key = []
        self.cur_fence = set()
        self.last_on_eng = {}

    def fence(self):
        self.cur_fence = set(self.last_on_eng.values())

    class Buf:
        __slots__ = ("name", "writer", "readers", "psum")
        def __init__(self, name, psum=False):
            self.name = name
            self.writer = None
            self.readers = set()
            self.psum = psum

    def buf(self, name, psum=False):
        return Prog.Buf(name, psum)

    serialize_all = False
    serial_now = False

    def op(self, eng, emit_fn, reads=(), writes=(), dma=False):
        idx = len(self.ops)
        dep = set()
        if (self.serialize_all or self.serial_now) and idx > 0:
            dep.add(idx - 1)
        for bb in reads:
            if bb.writer is not None:
                dep.add(bb.writer)
            if bb.psum:
                for r in bb.readers:
                    if self.ops[r][0] != eng:
                        dep.add(r)
        for bb in writes:
            if bb.writer is not None:
                dep.add(bb.writer)
            dep.update(bb.readers)
        dep.update(self.cur_fence)
        dep.discard(idx)
        self.ops.append((eng, emit_fn, 16 if dma else 1))
        self.dma_key.append((writes[0].name if writes else "tapout") if dma else None)
        self.deps.append(dep)
        self.last_on_eng[eng] = idx
        for bb in reads:
            bb.readers.add(idx)
        for bb in writes:
            bb.writer = idx
            bb.readers = set()
        return idx

    truncate = None

    def finalize(self):
        nc = self.nc
        n = len(self.ops)
        if self.truncate is not None:
            n = min(n, self.truncate)
        op_eng = [o[0] for o in self.ops]
        needs_inc = [o[2] == 16 for o in self.ops]
        waits = [[] for _ in range(n)]
        import os
        pe_pe = os.environ.get("K_PEPE", "0") == "1"
        for i in range(n):
            e = op_eng[i]
            for j in self.deps[i]:
                je = op_eng[j]
                if je == e and e == "pe" and not pe_pe:
                    continue
                needs_inc[j] = True
                waits[i].append(j)
        op_key = [None] * n
        for j in range(n):
            if self.ops[j][2] == 16:
                op_key[j] = ("d", self.dma_key[j] or f"anon{j}")
            else:
                op_key[j] = ("e", op_eng[j])
        sem_val = [0] * n
        cnt = {}
        for j in range(n):
            k = op_key[j]
            if needs_inc[j]:
                cnt[k] = cnt.get(k, 0) + self.ops[j][2]
            sem_val[j] = cnt.get(k, 0)
        self.final_counts = dict(cnt)
        sems = {}
        for j in range(n):
            k = op_key[j]
            if k not in sems:
                sems[k] = nc.semaphore(f"ps_{k[0]}_{k[1]}"[:28]).__enter__()
        seen = {}
        for i in range(n):
            e, emit_fn, inc = self.ops[i]
            byk = {}
            for j in waits[i]:
                k = op_key[j]
                v = sem_val[j]
                if v > byk.get(k, 0):
                    byk[k] = v
            for k in sorted(byk):
                v = byk[k]
                if seen.get((e, k), 0) >= v:
                    continue
                self.engines[e].wait_ge(sems[k], v)
                seen[(e, k)] = v
            inst = emit_fn(self.engines[e])
            if needs_inc[i]:
                inst.then_inc(sems[op_key[i]], inc)
        for k, v in cnt.items():
            if v > 0:
                self.engines["sp" if k[0] == "d" else k[1]].wait_ge(sems[k], v)


def build_program(n_phases=NPH, taps=(), truncate=None, serialize_all=False,
                  ser_phases=()):
    taps = set(taps)
    ser_phases = set(ser_phases)
    nc = bass.Bass("TRN2", target_bir_lowering=False)
    p = Prog(nc)
    p.serialize_all = serialize_all

    def din(name, shape, dtype=F32):
        return nc.dram_tensor(name, shape, dtype, kind="ExternalInput").ap()

    x_hbm = din("x", [TOKH, D_MODEL])
    w_in = din("w_in", [D_MODEL, D_IN_PROJ])
    w_out = din("w_out", [D_INNER, D_MODEL])
    wg = din("wg", [D_FFN, D_MODEL])
    wu = din("wu", [D_FFN, D_MODEL])
    wd = din("wd", [D_MODEL, D_FFN])
    dt_bias_r = din("dt_bias_r", [1, N_HEADS])
    a_log_c = din("a_log_c", [N_HEADS, 1])
    d_rep_r = din("d_rep_r", [1, D_INNER])
    bb_ev = din("bb_ev", [1, 2048])
    bb_od = din("bb_od", [1, 2048])
    cb_ev = din("cb_ev", [1, 2048])
    cb_od = din("cb_od", [1, 2048])
    normw_r = din("normw_r", [1, D_INNER])
    r32f = din("r32f", [N_HEADS, 2048])
    rn_ev = din("rn_ev", [N_HEADS, 2048], BF16)
    rn_od = din("rn_od", [N_HEADS, 2048], BF16)
    idf = din("idf", [128, 128])
    idb = din("idb", [128, 128], BF16)
    onesf = din("onesf", [1, 128])
    actc = din("actc", [128, 8])
    bnegp = din("bnegp", [128, 128])

    y_hbm = nc.dram_tensor("y", [TOK, D_MODEL], F32, kind="ExternalOutput").ap()

    xqT_dram = nc.dram_tensor("xqT_scr", [16, 128, TOK], BF16, kind="Internal").ap()
    xs_dram = nc.dram_tensor("xs_scr", [NTH, 128, D_INNER], BF16, kind="Internal").ap()
    gu_dram = nc.dram_tensor("gu_scr", [NT, 128, D_FFN], F32, kind="Internal").ap()
    wqdT_dram = nc.dram_tensor("wqdT_scr", [64, 128, D_MODEL], BF16, kind="Internal").ap()

    tap_outs = {}
    def tap(name, shape, dtype=F32):
        if name in taps:
            tap_outs[name] = nc.dram_tensor("tap_" + name, shape, dtype,
                                            kind="ExternalOutput").ap()
            return tap_outs[name]
        return None

    A = nc.alloc_sbuf_tensor

    s_pool = A("s_pool", [128, NT * 4096], BF16)
    s_raw = A("s_raw", [128, NTH * 192], F32)
    s_dt = A("s_dt", [128, NTH * 64], F32)
    s_slam = A("s_slam", [128, NTH * 64], F32)
    s_sdtT = A("s_sdtT", [64, NTH * 128], F32)
    s_slamT = A("s_slamT", [64, NTH * 128], F32)
    s_sall = A("s_sall", [128, 28], F32)
    s_rmax = A("s_rmax", [128, 12], F32)
    s_carry = A("s_carry", [64, 4], F32)
    s_idf = A("s_idf", [128, 128], F32)
    s_idb = A("s_idb", [128, 128], BF16)
    s_ones = A("s_ones", [1, 128], F32)
    s_onesc = A("s_onesc", [128, 1], F32)
    s_lam = A("s_lam", [64, 1], F32)
    s_ws = A("s_ws", [128, 8], F32)
    s_actc = A("s_actc", [128, 8], F32)

    b = {}
    def B(name):
        if name not in b:
            b[name] = p.buf(name, psum=name.startswith("ps"))
        return b[name]

    def dma(dst, src, reads=(), writes=(), q="sp"):
        return p.op(q, lambda e, d=dst, s=src: e.dma_start(out=d, in_=s),
                    reads=reads, writes=writes, dma=True)

    def mm(ps, lhsT, rhs, start, stop, reads=(), writes=()):
        return p.op("pe", lambda e, o=ps, l=lhsT, r=rhs, st=start, sp=stop:
                    e.matmul(o, l, r, start=st, stop=sp), reads=reads, writes=writes)

    def tr(ps, in_, ident, reads=(), writes=()):
        return p.op("pe", lambda e, o=ps, i=in_, d=ident: e.transpose(o, i, d),
                    reads=reads, writes=writes)

    cb = B("consts")
    lamb = B("lam")
    carb = B("carry")
    sallb = B("sall")
    rmaxb = B("rmax")
    rawb = B("raw")
    dtb_ = B("dt")
    slamb = B("slam")
    sdtTb = B("sdtT")
    slamTb = B("slamT")
    wsb2 = B("ws")
    xqTdb = B("xqT_dram")
    xsdb = B("xs_dram")
    gudb = B("gu_dram")
    wqdb = B("wqd_dram")
    poolb = B("pool")
    outb = B("out")

    dma(s_idf[:], idf, writes=[cb])
    dma(s_idb[:], idb, writes=[cb])
    dma(s_ones[:], onesf, writes=[cb])
    dma(s_actc[:], actc, writes=[cb])
    p.op("dve", lambda e: e.memset(s_onesc[:], 1.0), writes=[cb])
    p.op("dve", lambda e: e.memset(s_carry[:], 0.0), writes=[carb])
    p.op("dve", lambda e: e.memset(s_rmax[:], 0.0), writes=[rmaxb])

    with nc.sbuf_tensor("s_alog", [64, 1], F32) as s_alog:
        dma(s_alog[:], a_log_c, writes=[lamb])
        p.op("act", lambda e: e.activation(s_lam[:], s_alog[:], AF.Exp),
             reads=[lamb], writes=[lamb])
    p.fence()

    x_t = x_hbm.rearrange("(n p) f -> n p f", p=128)
    y_t = y_hbm.rearrange("(n p) f -> n p f", p=128)

    p.serial_now = "AB" in ser_phases
    with contextlib.ExitStack() as _esAB:
        _e = _esAB.enter_context
        s_xnT = _e(nc.sbuf_tensor("s_xnT", [128, NTH * 2048], BF16))
        s_wsb = [_e(nc.sbuf_tensor(f"s_wsb{i}", [128, 2048], F32)) for i in range(2)]
        s_wsjunk = _e(nc.sbuf_tensor("s_wsjunk", [128, 2048], F32))
        s_psums = _e(nc.sbuf_tensor("s_psums", [128, 4], F32))
        s_scal = _e(nc.sbuf_tensor("s_scal", [1, 8], F32))
        xnTb = B("xnT")
        wsbb = [B("wsb0"), B("wsb1")]
        wsjb = B("wsjunk")
        psmb = B("psums")
        scalb = B("scal")

        p.op("dve", lambda e: e.memset(s_psums[:], 0.0), writes=[psmb])

        def scale_gen():
            wg_v = wg.rearrange("(n p) k -> n p k", p=128)
            wu_v = wu.rearrange("(n p) k -> n p k", p=128)
            wd_v = wd.rearrange("(n p) k -> n p k", p=128)
            views = [
                (0, lambda i: wg_v[i], 64),
                (1, lambda i: wu_v[i], 64),
                (2, lambda i: wd_v[i // 4][:, (i % 4) * 2048:(i % 4 + 1) * 2048], 64),
            ]
            for col, view, ntl in views:
                for i in range(ntl):
                    par = i % 2
                    dma(s_wsb[par][:], view(i), reads=[wsbb[par]], writes=[wsbb[par]],
                        q="act")
                    p.op("act", lambda e, pr=par, c=col: e.activation(
                        s_wsjunk[:], s_wsb[pr][:], AF.Abs,
                        accum_out=s_psums[:, 3:4]),
                        reads=[wsbb[par]], writes=[wsjb, psmb])
                    p.op("dve", lambda e, c=col: e.tensor_add(
                        s_psums[:, c:c + 1], s_psums[:, c:c + 1], s_psums[:, 3:4]),
                        reads=[psmb], writes=[psmb])
                    yield

        def scale_fin():
            with (nc.psum_tensor("ps_sc", [1, 8], F32) as ps_sc,
                  nc.psum_tensor("ps_ws", [128, 8], F32) as ps_ws):
                psc = B("ps_sc"); pws = B("ps_ws")
                mm(ps_sc[0:1, 0:3], s_onesc[:], s_psums[:, 0:3], True, True,
                   reads=[psmb, cb], writes=[psc])
                p.op("dve", lambda e: e.tensor_copy(s_scal[0:1, 0:3], ps_sc[0:1, 0:3]),
                     reads=[psc], writes=[scalb])
                nume = float(D_FFN * D_MODEL)
                p.op("dve", lambda e: e.tensor_scalar(
                    s_scal[0:1, 4:7], s_scal[0:1, 0:3], 1.0 / nume, EPS_F,
                    ALU.mult, ALU.add), reads=[scalb], writes=[scalb])
                p.op("dve", lambda e: e.reciprocal(s_scal[0:1, 0:3], s_scal[0:1, 4:7]),
                     reads=[scalb], writes=[scalb])
                mm(ps_ws[:, 0:8], s_ones[:], s_scal[0:1, 0:8], True, True,
                   reads=[scalb, cb], writes=[pws])
                p.op("dve", lambda e: e.tensor_copy(s_ws[:], ps_ws[:, 0:8]),
                     reads=[pws], writes=[wsb2])
            t_ws = tap("wscal", [128, 8])
            if t_ws is not None:
                dma(t_ws, s_ws[:], reads=[wsb2])

        sg = scale_gen()

        def pump(k):
            for _ in range(k):
                if next(sg, "END") == "END":
                    return

        t_xn = tap("xn", [TOK, D_MODEL])
        t_xq = tap("xq", [TOK, D_MODEL])
        t_raw = tap("raw", [TOK, 192])
        t_dt = tap("dt", [TOK, 64])
        t_sdt = tap("sdt", [64, TOK])
        t_slam = tap("slam", [64, TOK])

        with contextlib.ExitStack() as _esA:
            _e = _esA.enter_context
            s_wraw = _e(nc.sbuf_tensor("s_wraw", [128, 16 * 192], F32))
            s_dtb_bc = _e(nc.sbuf_tensor("s_dtb_bc", [128, 64], F32))
            s_xt = _e(nc.sbuf_tensor("s_xt", [128, 2048], F32))
            s_xn = _e(nc.sbuf_tensor("s_xn", [128, 2048], F32))
            s_xnTf = _e(nc.sbuf_tensor("s_xnTf", [128, 2048], F32))
            s_xq = _e(nc.sbuf_tensor("s_xq", [128, 2048], BF16))
            s_sq = _e(nc.sbuf_tensor("s_sq", [128, 2048], F32))
            s_c1 = _e(nc.sbuf_tensor("s_c1", [128, 8], F32))
            s_dtT = _e(nc.sbuf_tensor("s_dtT", [64, 128], F32))
            s_ldtT = _e(nc.sbuf_tensor("s_ldtT", [64, 128], F32))
            s_stg = _e(nc.sbuf_tensor("s_stg", [128, 512], BF16))
            psA0 = _e(nc.psum_tensor("psA0", [128, 512], F32))
            psA1 = _e(nc.psum_tensor("psA1", [128, 512], F32))
            psA2 = _e(nc.psum_tensor("psA2", [128, 512], BF16))
            psA3 = _e(nc.psum_tensor("psA3", [128, 192], F32))
            psA4 = _e(nc.psum_tensor("psA4", [64, 128], F32))
            psA5 = _e(nc.psum_tensor("psA5", [128, 64], F32))
            wrb = B("wraw"); dtbb = B("dtb_bc")
            xtb = B("xt"); xnb = B("xn"); xnTfb = B("xnTf"); xqb = B("xq")
            sqb = B("sq"); c1b = B("c1"); dtTb = B("dtT"); ldtTb = B("ldtT")
            stgb = B("stg")
            pA0 = B("psA0"); pA1 = B("psA1"); pA2 = B("psA2")
            pA3 = B("psA3"); pA4 = B("psA4"); pA5 = B("psA5")

            dma(s_wraw[:].rearrange("p (kc o) -> p kc o", kc=16),
                w_in[:, 8192:8384].rearrange("(kc p) o -> p kc o", p=128),
                writes=[wrb])
            with nc.sbuf_tensor("s_row64", [1, 64], F32) as s_row64:
                r64b = B("row64")
                dma(s_row64[:], dt_bias_r, writes=[r64b])
                mm(psA5[:], s_ones[:], s_row64[0:1, 0:64], True, True,
                   reads=[r64b, cb], writes=[pA5])
                p.op("dve", lambda e: e.tensor_copy(s_dtb_bc[:], psA5[:]),
                     reads=[pA5], writes=[dtbb])

            for j in range(NTH):
                jr = j - 1
                dma(s_xt[:], x_t[j], reads=[xtb], writes=[xtb])
                p.op("dve", lambda e: e.reduce_sum(s_c1[:, 0:1], s_xt[:], axis=AX),
                     reads=[xtb], writes=[c1b])
                p.op("act", lambda e: e.activation(s_sq[:], s_xt[:], AF.Square,
                                                   accum_out=s_c1[:, 1:2]),
                     reads=[xtb], writes=[sqb, c1b])
                p.op("dve", lambda e: e.tensor_scalar(s_c1[:, 2:3], s_c1[:, 0:1], -1.0 / 2048.0, None, ALU.mult),
                     reads=[c1b], writes=[c1b])
                p.op("dve", lambda e: e.tensor_mul(s_c1[:, 3:4], s_c1[:, 2:3], s_c1[:, 2:3]),
                     reads=[c1b], writes=[c1b])
                p.op("dve", lambda e: e.tensor_scalar(s_c1[:, 4:5], s_c1[:, 1:2], 1.0 / 2048.0, None, ALU.mult),
                     reads=[c1b], writes=[c1b])
                p.op("dve", lambda e: e.tensor_sub(s_c1[:, 4:5], s_c1[:, 4:5], s_c1[:, 3:4]),
                     reads=[c1b], writes=[c1b])
                p.op("act", lambda e: e.activation(s_c1[:, 5:6], s_c1[:, 4:5], AF.Ln, bias=s_actc[:, 0:1]),
                     reads=[c1b, cb], writes=[c1b])
                p.op("act", lambda e: e.activation(s_c1[:, 5:6], s_c1[:, 5:6], AF.Exp, scale=-0.5),
                     reads=[c1b], writes=[c1b])
                p.op("dve", lambda e: e.tensor_scalar(s_xn[:], s_xt[:], s_c1[:, 2:3], s_c1[:, 5:6], ALU.add, ALU.mult),
                     reads=[xtb, c1b], writes=[xnb])
                if t_xn is not None and j > 0:
                    dma(t_xn.rearrange("(n p) f -> n p f", p=128)[jr], s_xn[:], reads=[xnb])
                if j > 0:
                    p.op("dve", lambda e: e.reduce_max(s_c1[:, 6:7], s_xn[:], axis=AX, apply_absolute_value=True),
                         reads=[xnb], writes=[c1b])
                    p.op("dve", lambda e: e.tensor_scalar(s_c1[:, 6:7], s_c1[:, 6:7], 1.0 / 127.0, EPS_F, ALU.mult, ALU.add),
                         reads=[c1b], writes=[c1b])
                    p.op("dve", lambda e, jj=jr: e.tensor_copy(s_sall[:, jj:jj + 1], s_c1[:, 6:7]),
                         reads=[c1b], writes=[sallb])
                    p.op("dve", lambda e: e.reciprocal(s_c1[:, 7:8], s_c1[:, 6:7]),
                         reads=[c1b], writes=[c1b])
                    p.op("dve", lambda e: e.tensor_scalar(s_sq[:], s_xn[:], s_c1[:, 7:8], MAGIC, ALU.mult, ALU.add),
                         reads=[xnb, c1b], writes=[sqb])
                    p.op("dve", lambda e: e.tensor_scalar(s_xq[:], s_sq[:], MAGIC, None, ALU.subtract),
                         reads=[sqb], writes=[xqb])
                    if t_xq is not None:
                        p.op("dve", lambda e: e.tensor_copy(s_sq[:], s_xq[:]), reads=[xqb], writes=[sqb])
                        dma(t_xq.rearrange("(n p) f -> n p f", p=128)[jr], s_sq[:], reads=[sqb])
                for q in range(4):
                    ps_h, pb_h = (psA0, pA0) if q % 2 == 0 else (psA1, pA1)
                    for c in range(4):
                        kc = q * 4 + c
                        tr(ps_h[:, c * 128:(c + 1) * 128], s_xn[:, kc * 128:(kc + 1) * 128],
                           s_idf[:], reads=[xnb, cb], writes=[pb_h])
                    base = q * 512
                    p.op("act", lambda e, b0=base, ph=ps_h: e.activation(
                        s_xnTf[:, b0:b0 + 512], ph[:, 0:512], AF.Identity),
                        reads=[pb_h], writes=[xnTfb])
                    p.op("dve", lambda e, b0=base, jj=j, ph=ps_h: e.tensor_copy(
                        s_xnT[:, jj * 2048 + b0: jj * 2048 + b0 + 512], ph[:, 0:512]),
                        reads=[pb_h], writes=[xnTb])
                if j > 0:
                    for q in range(4):
                        for c in range(4):
                            kc = q * 4 + c
                            tr(psA2[:, c * 128:(c + 1) * 128], s_xq[:, kc * 128:(kc + 1) * 128],
                               s_idb[:], reads=[xqb, cb], writes=[pA2])
                        p.op("dve", lambda e: e.tensor_copy(s_stg[:], psA2[:, 0:512]),
                             reads=[pA2], writes=[stgb])
                        for c in range(4):
                            kc = q * 4 + c
                            dma(xqT_dram[kc, :, jr * 128:(jr + 1) * 128], s_stg[:, c * 128:(c + 1) * 128],
                                reads=[stgb], writes=[xqTdb])
                for kc in range(16):
                    mm(psA3[:], s_xnTf[:, kc * 128:(kc + 1) * 128],
                       s_wraw[:, kc * 192:(kc + 1) * 192],
                       kc == 0, kc == 15, reads=[xnTfb, wrb], writes=[pA3])
                p.op("dve", lambda e, jj=j: e.tensor_copy(s_raw[:, jj * 192:(jj + 1) * 192], psA3[:]),
                     reads=[pA3], writes=[rawb])
                if t_raw is not None and j > 0:
                    dma(t_raw.rearrange("(n p) f -> n p f", p=128)[jr],
                        s_raw[:, j * 192:(j + 1) * 192], reads=[rawb])
                o0 = j * 192 + 128
                p.op("dve", lambda e, o=o0: e.tensor_add(s_sq[:, 0:64], s_raw[:, o:o + 64], s_dtb_bc[:]),
                     reads=[rawb, dtbb], writes=[sqb])
                p.op("dve", lambda e: e.tensor_scalar(s_sq[:, 64:128], s_sq[:, 0:64], 0.0, None, ALU.max),
                     reads=[sqb], writes=[sqb])
                p.op("act", lambda e: e.activation(s_sq[:, 128:192], s_sq[:, 0:64], AF.Abs),
                     reads=[sqb], writes=[sqb])
                p.op("act", lambda e: e.activation(s_sq[:, 128:192], s_sq[:, 128:192], AF.Exp, scale=-1.0),
                     reads=[sqb], writes=[sqb])
                p.op("act", lambda e: e.activation(s_sq[:, 128:192], s_sq[:, 128:192], AF.Ln, bias=s_actc[:, 1:2]),
                     reads=[sqb, cb], writes=[sqb])
                p.op("dve", lambda e, jj=j: e.tensor_add(s_dt[:, jj * 64:(jj + 1) * 64],
                                                         s_sq[:, 64:128], s_sq[:, 128:192]),
                     reads=[sqb], writes=[dtb_])
                if t_dt is not None and j > 0:
                    dma(t_dt.rearrange("(n p) f -> n p f", p=128)[jr],
                        s_dt[:, j * 64:(j + 1) * 64], reads=[dtb_])
                tr(psA4[:], s_dt[:, j * 64:(j + 1) * 64], s_idf[:], reads=[dtb_, cb], writes=[pA4])
                p.op("dve", lambda e: e.tensor_copy(s_dtT[:], psA4[:]), reads=[pA4], writes=[dtTb])
                p.op("dve", lambda e: e.tensor_scalar(s_ldtT[:], s_dtT[:], s_lam[:], None, ALU.mult),
                     reads=[dtTb, lamb], writes=[ldtTb])
                p.op("dve", lambda e, jj=j: e.tensor_tensor_scan(
                    s_sdtT[:, jj * 128:(jj + 1) * 128], s_dtT[:], s_dtT[:], s_carry[:, 0:1],
                    ALU.add, ALU.bypass), reads=[dtTb, carb], writes=[sdtTb])
                p.op("dve", lambda e, jj=j: e.tensor_tensor_scan(
                    s_slamT[:, jj * 128:(jj + 1) * 128], s_ldtT[:], s_ldtT[:], s_carry[:, 1:2],
                    ALU.add, ALU.bypass), reads=[ldtTb, carb], writes=[slamTb])
                p.op("dve", lambda e, jj=j: e.tensor_copy(
                    s_carry[:, 0:1], s_sdtT[:, jj * 128 + 127: jj * 128 + 128]),
                    reads=[sdtTb], writes=[carb])
                p.op("dve", lambda e, jj=j: e.tensor_copy(
                    s_carry[:, 1:2], s_slamT[:, jj * 128 + 127: jj * 128 + 128]),
                    reads=[slamTb], writes=[carb])
                tr(psA5[:], s_slamT[:, j * 128:(j + 1) * 128], s_idf[0:64, 0:64],
                   reads=[slamTb, cb], writes=[pA5])
                p.op("dve", lambda e, jj=j: e.tensor_copy(s_slam[:, jj * 64:(jj + 1) * 64], psA5[:]),
                     reads=[pA5], writes=[slamb])
                pump(8)
            if t_sdt is not None:
                dma(t_sdt, s_sdtT[:], reads=[sdtTb])
            if t_slam is not None:
                dma(t_slam, s_slamT[:], reads=[slamTb])

        p.fence()
        if n_phases >= 2:
            t_z = tap("z", [TOK, D_INNER])
            t_xs = tap("xs", [TOK, D_INNER])
            with contextlib.ExitStack() as _esB:
                _e = _esB.enter_context
                s_wstg = _e(nc.sbuf_tensor("s_wstg", [128, 512], F32))
                s_wblk = _e(nc.sbuf_tensor("s_wblk", [128, 16 * 512], BF16))
                s_bstg = _e(nc.sbuf_tensor("s_bstg", [128, 512], BF16))
                s_bstgf = _e(nc.sbuf_tensor("s_bstgf", [128, 512], F32))
                psB = _e(nc.psum_tensor("psB", [128, 4096], F32))
                wstgb = B("wstg")
                wblkb = B("wblk")
                bstgb = B("bstg")
                bstgfb = B("bstgf")
                pB = [B(f"psB{j}") for j in range(8)]
                w_in_t = w_in.rearrange("(kc p) o -> kc p o", p=128)
                for g in range(16):
                    is_z = g < 8
                    for kc in range(16):
                        dma(s_wstg[:], w_in_t[kc, :, g * 512:(g + 1) * 512],
                            reads=[wstgb], writes=[wstgb])
                        p.op("pool", lambda e, k0=kc: e.tensor_copy(
                            s_wblk[:, k0 * 512:(k0 + 1) * 512], s_wstg[:]),
                            reads=[wstgb], writes=[wblkb])
                    for kc in range(16):
                        for jr in range(8):
                            jt = jr + 1
                            mm(psB[:, jr * 512:(jr + 1) * 512],
                               s_xnT[:, (jt * 16 + kc) * 128:(jt * 16 + kc + 1) * 128],
                               s_wblk[:, kc * 512:(kc + 1) * 512], kc == 0, kc == 15,
                               reads=[xnTb, wblkb], writes=[pB[jr]])
                    for jr in range(8):
                        pslice = psB[:, jr * 512:(jr + 1) * 512]
                        if is_z:
                            p.op("act", lambda e, jj=jr, gg=g, ps=pslice: e.activation(
                                s_pool[:, jj * 4096 + gg * 512: jj * 4096 + (gg + 1) * 512],
                                ps, AF.Silu), reads=[pB[jr]], writes=[poolb])
                            if t_z is not None:
                                p.op("dve", lambda e, ps=pslice: e.tensor_copy(s_bstgf[:], ps),
                                     reads=[pB[jr]], writes=[bstgfb])
                                dma(t_z.rearrange("(n p) f -> n p f", p=128)[jr, :, g * 512:(g + 1) * 512],
                                    s_bstgf[:], reads=[bstgfb])
                        else:
                            gg = g - 8
                            p.op("dve", lambda e, ps=pslice: e.tensor_copy(s_bstg[:], ps),
                                 reads=[pB[jr]], writes=[bstgb])
                            dma(xs_dram[jr + 1, :, gg * 512:(gg + 1) * 512], s_bstg[:],
                                reads=[bstgb], writes=[xsdb])
                            if t_xs is not None:
                                p.op("dve", lambda e, ps=pslice: e.tensor_copy(s_bstgf[:], ps),
                                     reads=[pB[jr]], writes=[bstgfb])
                                dma(t_xs.rearrange("(n p) f -> n p f", p=128)[jr, :, gg * 512:(gg + 1) * 512],
                                    s_bstgf[:], reads=[bstgfb])
                    if not is_z:
                        gg = g - 8
                        for kc in range(16):
                            mm(psB[:, 0:512],
                               s_xnT[:, (0 * 16 + kc) * 128:(0 * 16 + kc + 1) * 128],
                               s_wblk[:, kc * 512:(kc + 1) * 512], kc == 0, kc == 15,
                               reads=[xnTb, wblkb], writes=[pB[0]])
                        p.op("dve", lambda e: e.tensor_copy(s_bstg[:], psB[:, 0:512]),
                             reads=[pB[0]], writes=[bstgb])
                        dma(xs_dram[0, :, gg * 512:(gg + 1) * 512], s_bstg[:],
                            reads=[bstgb], writes=[xsdb])
                    pump(10)
        pump(10 ** 9)
        scale_fin()

    p.fence()

    p.serial_now = "C" in ser_phases
    if n_phases >= 3:
        t_rb = tap("rbv", [TOK, 64], BF16)
        t_rbt = tap("rbtv", [64, TOK], BF16)
        t_i1 = tap("i1v", [TOK, 2048], BF16)
        t_i2 = tap("i2v", [TOK, 2048], BF16)
        t_sc = tap("sinv", [TOK, 2048], BF16)
        t_cc2 = tap("cosv", [TOK, 2048], BF16)
        t_bh = tap("bh", [TOK, D_INNER], BF16)
        t_ch = tap("ch", [TOK, D_INNER], BF16)
        t_y = tap("y_scan", [TOK, D_INNER], BF16)
        t_yn = tap("yn", [TOK, D_INNER])
        with contextlib.ExitStack() as _esC:
            _e = _esC.enter_context
            s_drep = _e(nc.sbuf_tensor("s_drep", [128, D_INNER], BF16))
            s_normw = _e(nc.sbuf_tensor("s_normw", [128, D_INNER], BF16))
            s_bbev = _e(nc.sbuf_tensor("s_bbev", [128, 2048], BF16))
            s_bbod = _e(nc.sbuf_tensor("s_bbod", [128, 2048], BF16))
            s_cbev = _e(nc.sbuf_tensor("s_cbev", [128, 2048], BF16))
            s_cbod = _e(nc.sbuf_tensor("s_cbod", [128, 2048], BF16))
            s_r32f = _e(nc.sbuf_tensor("s_r32f", [64, 2048], F32))
            s_rnev = _e(nc.sbuf_tensor("s_rnev", [64, 2048], BF16))
            s_rnod = _e(nc.sbuf_tensor("s_rnod", [64, 2048], BF16))
            s_bneg = _e(nc.sbuf_tensor("s_bneg", [128, 128], F32))
            s_xsj = _e(nc.sbuf_tensor("s_xsj", [128, 2048], BF16))
            s_flat = _e(nc.sbuf_tensor("s_flat", [1, 512], F32))
            s_sin = _e(nc.sbuf_tensor("s_sin", [128, 1024], BF16))
            s_cos = _e(nc.sbuf_tensor("s_cos", [128, 1024], BF16))
            s_phr = _e(nc.sbuf_tensor("s_phr", [128, 512], F32))
            s_k = _e(nc.sbuf_tensor("s_k", [128, 512], F32))
            s_ph2 = _e(nc.sbuf_tensor("s_ph2", [128, 512], F32))
            s_rbB = _e(nc.sbuf_tensor("s_rbB", [128, 64], BF16))
            s_rbC = _e(nc.sbuf_tensor("s_rbC", [128, 64], BF16))
            s_rmsBT = _e(nc.sbuf_tensor("s_rmsBT", [64, 128], BF16))
            s_rmsCT = _e(nc.sbuf_tensor("s_rmsCT", [64, 128], BF16))
            s_i1 = _e(nc.sbuf_tensor("s_i1", [128, 1024], BF16))
            s_i2 = _e(nc.sbuf_tensor("s_i2", [128, 1024], BF16))
            s_t1 = _e(nc.sbuf_tensor("s_t1", [128, 1024], BF16))
            s_t2 = _e(nc.sbuf_tensor("s_t2", [128, 1024], BF16))
            s_Bh = _e(nc.sbuf_tensor("s_Bh", [128, 2048], BF16))
            s_Ch = _e(nc.sbuf_tensor("s_Ch", [128, 2048], BF16))
            s_smB = _e(nc.sbuf_tensor("s_smB", [128, 2048], BF16))
            s_smC = _e(nc.sbuf_tensor("s_smC", [128, 2048], BF16))
            s_state = _e(nc.sbuf_tensor("s_state", [128, 2048], BF16))
            s_e1 = _e(nc.sbuf_tensor("s_e1", [128, 256], F32))
            s_W = _e(nc.sbuf_tensor("s_W", [128, 256], BF16))
            s_WG = _e(nc.sbuf_tensor("s_WG", [128, 256], BF16))
            s_U = _e(nc.sbuf_tensor("s_U", [128, 128], BF16))
            s_Bdec = _e(nc.sbuf_tensor("s_Bdec", [128, 128], BF16))
            s_tyi = _e(nc.sbuf_tensor("s_tyi", [128, 512], F32))
            s_xsd = _e(nc.sbuf_tensor("s_xsd", [128, 512], BF16))
            s_A8 = _e(nc.sbuf_tensor("s_A8", [128, 64], F32))
            s_Adec = _e(nc.sbuf_tensor("s_Adec", [128, 64], F32))
            s_rowL = _e(nc.sbuf_tensor("s_rowL", [1, 64], F32))
            s_rowP = _e(nc.sbuf_tensor("s_rowP", [1, 64], F32))
            s_Drow = _e(nc.sbuf_tensor("s_Drow", [1, 64], F32))
            s_Dmat = _e(nc.sbuf_tensor("s_Dmat", [128, 64], F32))
            s_ybuf = _e(nc.sbuf_tensor("s_ybuf", [128, D_INNER], BF16))
            s_gbuf = _e(nc.sbuf_tensor("s_gbuf", [128, 1024], F32))
            s_ynb = _e(nc.sbuf_tensor("s_ynb", [128, D_INNER], BF16))
            psC_a = _e(nc.psum_tensor("psC_a", [128, 512], F32))
            psC_b = _e(nc.psum_tensor("psC_b", [128, 512], F32))
            psC_t0 = _e(nc.psum_tensor("psC_t0", [128, 1024], BF16))
            psC_t1 = _e(nc.psum_tensor("psC_t1", [128, 1024], BF16))
            psC_g0 = _e(nc.psum_tensor("psC_g0", [128, 512], F32))
            psC_g1 = _e(nc.psum_tensor("psC_g1", [128, 512], F32))
            psC_y0 = _e(nc.psum_tensor("psC_y0", [128, 512], F32))
            psC_y1 = _e(nc.psum_tensor("psC_y1", [128, 512], F32))

            Cb = {}
            for nm in ("drep normw bbev bbod cbev cbod r32f rnev rnod bneg xsj flat "
                       "sin cos phr ph2 kbuf rbB rbC rmsBT rmsCT i1 i2 t1 t2 Bh Ch "
                       "smB smC e1 W WG tyi xsd A8 Adec rowL rowP Drow Dmat ybuf "
                       "gbuf ynb cc").split():
                Cb[nm] = B("c_" + nm)
            pCa = B("psC_a"); pCb = B("psC_b")
            pCt = [B("psC_t0"), B("psC_t1")]
            pCg = [B("psC_g0"), B("psC_g1")]
            pCy = [B("psC_y0"), B("psC_y1")]
            stb = [B(f"c_state{i}") for i in range(8)]
            Ub = [B(f"c_U{i}") for i in range(4)]
            Bdb = [B(f"c_Bd{i}") for i in range(4)]
            Wb = [B(f"c_W{i}") for i in range(4)]
            WGb = [B(f"c_WG{i}") for i in range(4)]
            e1b = [B(f"c_e1{i}") for i in range(4)]

            dma(s_bneg[:], bnegp, writes=[Cb["bneg"]])
            dma(s_r32f[:], r32f, writes=[Cb["r32f"]])
            dma(s_rnev[:], rn_ev, writes=[Cb["rnev"]])
            dma(s_rnod[:], rn_od, writes=[Cb["rnod"]])
            crb = Cb["flat"]
            def bcast(dst, src_row, width, wbuf):
                for g0 in range(0, width, 512):
                    w = min(512, width - g0)
                    ps = psC_a if (g0 // 512) % 2 == 0 else psC_b
                    pbx = pCa if (g0 // 512) % 2 == 0 else pCb
                    dma(s_flat[0:1, 0:w], src_row[0:1, g0:g0 + w],
                        reads=[crb], writes=[crb])
                    mm(ps[:, 0:w], s_ones[:], s_flat[0:1, 0:w], True, True,
                       reads=[crb, cb], writes=[pbx])
                    p.op("dve", lambda e, d=dst, g=g0, ww=w, pp=ps: e.tensor_copy(
                        d[:, g:g + ww], pp[:, 0:ww]), reads=[pbx], writes=[wbuf])
            bcast(s_drep, d_rep_r, 4096, Cb["drep"])
            bcast(s_normw, normw_r, 4096, Cb["normw"])
            bcast(s_bbev, bb_ev, 2048, Cb["bbev"])
            bcast(s_bbod, bb_od, 2048, Cb["bbod"])
            bcast(s_cbev, cb_ev, 2048, Cb["cbev"])
            bcast(s_cbod, cb_od, 2048, Cb["cbod"])
            p.op("dve", lambda e: e.memset(s_state[:], 0.0), writes=stb)
            p.op("dve", lambda e: e.memset(s_rowP[:], 0.0), writes=[Cb["rowP"]])

            def view32(t, blk):
                return t[:].rearrange("p (h g) -> p h g", g=64)[:, :, blk * 32:(blk + 1) * 32]

            for j in range(NTH):
                halo = (j == 0)
                p.serial_now = ("C" in ser_phases) or ("Cpro" in ser_phases)
                dma(s_rowL[:], s_slamT[:, j * 128 + 127: j * 128 + 128],
                    reads=[slamTb, Cb["rowL"]], writes=[Cb["rowL"]])
                if not halo:
                    mm(psC_a[:, 0:64], s_ones[:], s_rowP[0:1, 0:64], True, True,
                       reads=[Cb["rowP"], cb], writes=[pCa])
                    p.op("dve", lambda e, jj=j: e.tensor_sub(
                        s_A8[:], psC_a[:, 0:64], s_slam[:, jj * 64:(jj + 1) * 64]),
                        reads=[slamb, pCa], writes=[Cb["A8"]])
                    p.op("act", lambda e: e.activation(s_A8[:], s_A8[:], AF.Exp),
                         reads=[Cb["A8"]], writes=[Cb["A8"]])
                    p.op("dve", lambda e: e.tensor_scalar(s_A8[:], s_A8[:], 0.125, None, ALU.mult),
                         reads=[Cb["A8"]], writes=[Cb["A8"]])
                p.op("dve", lambda e: e.tensor_sub(s_Drow[:], s_rowP[:], s_rowL[:]),
                     reads=[Cb["rowP"], Cb["rowL"]], writes=[Cb["Drow"]])
                p.op("act", lambda e: e.activation(s_Drow[:], s_Drow[:], AF.Exp),
                     reads=[Cb["Drow"]], writes=[Cb["Drow"]])
                p.op("dve", lambda e: e.tensor_copy(s_rowP[:], s_rowL[:]),
                     reads=[Cb["rowL"]], writes=[Cb["rowP"]])
                mm(psC_a[:, 0:64], s_ones[:], s_Drow[0:1, 0:64], True, True,
                   reads=[Cb["Drow"], cb], writes=[pCa])
                p.op("dve", lambda e: e.tensor_copy(s_Dmat[:], psC_a[:, 0:64]),
                     reads=[pCa], writes=[Cb["Dmat"]])
                mm(psC_b[:, 0:64], s_ones[:], s_rowL[0:1, 0:64], True, True,
                   reads=[Cb["rowL"], cb], writes=[pCb])
                p.op("dve", lambda e, jj=j: e.tensor_sub(
                    s_Adec[:], s_slam[:, jj * 64:(jj + 1) * 64], psC_b[:, 0:64]),
                    reads=[slamb, pCb], writes=[Cb["Adec"]])
                p.op("act", lambda e: e.activation(s_Adec[:], s_Adec[:], AF.Exp),
                     reads=[Cb["Adec"]], writes=[Cb["Adec"]])
                for (src_off, rb_t, rb_b, rT_t, rT_b) in (
                        ((0, s_rbB, Cb["rbB"], s_rmsBT, Cb["rmsBT"]),
                         (64, s_rbC, Cb["rbC"], s_rmsCT, Cb["rmsCT"]))
                        if not halo else
                        ((0, s_rbB, Cb["rbB"], s_rmsBT, Cb["rmsBT"]),)):
                    off = j * 192 + src_off
                    p.op("act", lambda e, o=off: e.activation(
                        s_gbuf[:, 0:64], s_raw[:, o:o + 64], AF.Square,
                        accum_out=s_sall[:, 20:21]),
                        reads=[rawb], writes=[Cb["gbuf"], Cb["cc"]])
                    p.op("act", lambda e: e.activation(s_sall[:, 21:22], s_sall[:, 20:21], AF.Ln,
                                                       scale=1.0 / 64.0, bias=s_actc[:, 2:3]),
                         reads=[Cb["cc"], cb], writes=[Cb["cc"]])
                    p.op("act", lambda e: e.activation(s_sall[:, 21:22], s_sall[:, 21:22], AF.Exp, scale=-0.5),
                         reads=[Cb["cc"]], writes=[Cb["cc"]])
                    p.op("dve", lambda e, o=off, rt=rb_t: e.tensor_scalar(
                        rt[:], s_raw[:, o:o + 64], s_sall[:, 21:22], None, ALU.mult),
                        reads=[rawb, Cb["cc"]], writes=[rb_b])
                    tr(psC_t0[0:64, 0:128], rb_t[:], s_idb[:], reads=[rb_b, cb], writes=[pCt[0]])
                    p.op("dve", lambda e, rt=rT_t: e.tensor_copy(rt[:], psC_t0[0:64, 0:128]),
                         reads=[pCt[0]], writes=[rT_b])
                    if t_rb is not None and t_rbt is not None and rb_t is s_rbB and not halo:
                        dma(t_rb.rearrange("(n p) f -> n p f", p=128)[j - 1], rb_t[:], reads=[rb_b])
                        dma(t_rbt[:, (j - 1) * 128:j * 128], rT_t[:], reads=[rT_b])

                for h2 in range(2):
                    p.serial_now = ("C" in ser_phases) or ("Cbc" in ser_phases)
                    hb = h2 * 1024
                    dma(s_xsj[:], xs_dram[j][:, h2 * 2048:(h2 + 1) * 2048],
                        reads=[Cb["xsj"], xsdb], writes=[Cb["xsj"]])
                    tensors_bc = (
                        ((s_rmsBT, Cb["rmsBT"], s_rnev, s_rnod, s_bbev, Cb["bbev"],
                          s_bbod, Cb["bbod"], s_Bh, Cb["Bh"], s_smB, Cb["smB"], t_bh),
                         (s_rmsCT, Cb["rmsCT"], s_rnev, s_rnod, s_cbev, Cb["cbev"],
                          s_cbod, Cb["cbod"], s_Ch, Cb["Ch"], s_smC, Cb["smC"], t_ch))
                        if not halo else
                        ((s_rmsBT, Cb["rmsBT"], s_rnev, s_rnod, s_bbev, Cb["bbev"],
                          s_bbod, Cb["bbod"], s_Bh, Cb["Bh"], s_smB, Cb["smB"], None),))
                    for g in range(2):
                        ps, pbx = (psC_a, pCa) if g == 0 else (psC_b, pCb)
                        mm(ps[:], s_sdtT[:, j * 128:(j + 1) * 128],
                           s_r32f[:, hb + g * 512: hb + (g + 1) * 512], True, True,
                           reads=[sdtTb, Cb["r32f"]], writes=[pbx])
                        p.op("dve", lambda e, pp=ps: e.tensor_scalar(
                            s_k[:], pp[:], INV_2PI, MAGIC, ALU.mult, ALU.add),
                            reads=[pbx], writes=[Cb["kbuf"]])
                        p.op("dve", lambda e: e.tensor_scalar(s_k[:], s_k[:], MAGIC, None, ALU.subtract),
                             reads=[Cb["kbuf"]], writes=[Cb["kbuf"]])
                        p.op("dve", lambda e, pp=ps: e.scalar_tensor_tensor(
                            s_phr[:], s_k[:], -CW1, pp[:], ALU.mult, ALU.add),
                            reads=[pbx, Cb["kbuf"]], writes=[Cb["phr"]])
                        p.op("dve", lambda e: e.scalar_tensor_tensor(
                            s_ph2[:], s_k[:], -CW2, s_phr[:], ALU.mult, ALU.add),
                            reads=[Cb["phr"], Cb["kbuf"]], writes=[Cb["ph2"]])
                        p.op("dve", lambda e: e.scalar_tensor_tensor(
                            s_phr[:], s_k[:], -CW3, s_ph2[:], ALU.mult, ALU.add),
                            reads=[Cb["ph2"], Cb["kbuf"]], writes=[Cb["phr"]])
                        p.op("act", lambda e, gg=g: e.activation(
                            s_sin[:, gg * 512:(gg + 1) * 512], s_phr[:], AF.Sin),
                            reads=[Cb["phr"]], writes=[Cb["sin"]])
                        p.op("act", lambda e: e.activation(s_ph2[:], s_phr[:], AF.Abs),
                             reads=[Cb["phr"]], writes=[Cb["ph2"]])
                        p.op("act", lambda e, gg=g: e.activation(
                            s_cos[:, gg * 512:(gg + 1) * 512], s_ph2[:], AF.Sin,
                            scale=-1.0, bias=s_actc[:, 3:4]),
                            reads=[Cb["ph2"], cb], writes=[Cb["cos"]])
                    if t_sc is not None and not halo:
                        dma(t_sc.rearrange("(n p) f -> n p f", p=128)[j - 1, :, h2 * 1024:(h2 + 1) * 1024],
                            s_sin[:], reads=[Cb["sin"]])
                    if t_cc2 is not None and not halo:
                        dma(t_cc2.rearrange("(n p) f -> n p f", p=128)[j - 1, :, h2 * 1024:(h2 + 1) * 1024],
                            s_cos[:], reads=[Cb["cos"]])
                    for (rT_t, rT_b, pev, pod, bev, bevb, bod, bodb, out_t, out_b,
                         sm_t, sm_b, tp) in tensors_bc:
                        for (pat, pat_b, bias_t, bias_b, dst, dst_b) in (
                                (pev, Cb["rnev"], bev, bevb, s_i1, Cb["i1"]),
                                (pod, Cb["rnod"], bod, bodb, s_i2, Cb["i2"])):
                            for g in range(2):
                                ps, pbx = (psC_a, pCa) if g == 0 else (psC_b, pCb)
                                mm(ps[:], rT_t[:], pat[:, hb + g * 512: hb + (g + 1) * 512],
                                   True, True, reads=[rT_b, pat_b], writes=[pbx])
                                p.op("dve", lambda e, pp=ps, gg=g, bt=bias_t, dd=dst, hb_=hb:
                                     e.tensor_add(dd[:, gg * 512:(gg + 1) * 512], pp[:],
                                                  bt[:, hb_ + gg * 512: hb_ + (gg + 1) * 512]),
                                     reads=[pbx, bias_b], writes=[dst_b])
                        i1v = s_i1[:].rearrange("p (h f) -> p h f", f=32)
                        i2v = s_i2[:].rearrange("p (h f) -> p h f", f=32)
                        sv = s_sin[:].rearrange("p (h f) -> p h f", f=32)
                        cv = s_cos[:].rearrange("p (h f) -> p h f", f=32)
                        t1v = s_t1[:].rearrange("p (h f) -> p h f", f=32)
                        t2v = s_t2[:].rearrange("p (h f) -> p h f", f=32)
                        o1 = view32(out_t, 0)
                        o2 = view32(out_t, 1)
                        p.op("dve", lambda e, o=t1v, a=i1v, c=cv: e.tensor_mul(o, a, c),
                             reads=[Cb["i1"], Cb["cos"]], writes=[Cb["t1"]])
                        p.op("dve", lambda e, o=t2v, a=i2v, c=sv: e.tensor_mul(o, a, c),
                             reads=[Cb["i2"], Cb["sin"]], writes=[Cb["t2"]])
                        p.op("dve", lambda e, o=o1, a=t1v, c=t2v: e.tensor_sub(o, a, c),
                             reads=[Cb["t1"], Cb["t2"]], writes=[out_b])
                        p.op("dve", lambda e, o=t1v, a=i1v, c=sv: e.tensor_mul(o, a, c),
                             reads=[Cb["i1"], Cb["sin"]], writes=[Cb["t1"]])
                        p.op("dve", lambda e, o=t2v, a=i2v, c=cv: e.tensor_mul(o, a, c),
                             reads=[Cb["i2"], Cb["cos"]], writes=[Cb["t2"]])
                        p.op("dve", lambda e, o=o2, a=t1v, c=t2v: e.tensor_add(o, a, c),
                             reads=[Cb["t1"], Cb["t2"]], writes=[out_b])
                        if tp is not None and not halo:
                            dma(tp.rearrange("(n p) f -> n p f", p=128)[j - 1, :, h2 * 2048:(h2 + 1) * 2048],
                                out_t[:], reads=[out_b])
                        if t_i1 is not None and t_i2 is not None and out_t is s_Bh and not halo:
                            dma(t_i1.rearrange("(n p) f -> n p f", p=128)[j - 1, :, h2 * 1024:(h2 + 1) * 1024],
                                s_i1[:], reads=[Cb["i1"]])
                            dma(t_i2.rearrange("(n p) f -> n p f", p=128)[j - 1, :, h2 * 1024:(h2 + 1) * 1024],
                                s_i2[:], reads=[Cb["i2"]])
                        for half16 in (() if halo else range(2)):
                            pst, pstb = (psC_t0, pCt[0]) if half16 == 0 else (psC_t1, pCt[1])
                            for g2 in range(8):
                                gg2 = half16 * 8 + g2
                                tr(pst[:, g2 * 128:(g2 + 1) * 128],
                                   out_t[:, gg2 * 128:(gg2 + 1) * 128], s_idb[:],
                                   reads=[out_b, cb], writes=[pstb])
                            p.op("dve", lambda e, st=sm_t, hh=half16, pp=pst: e.tensor_copy(
                                st[:, hh * 1024:(hh + 1) * 1024], pp[:]),
                                reads=[pstb], writes=[sm_b])
                    p.serial_now = ("C" in ser_phases) or ("Cscan" in ser_phases)
                    for bb2 in range(16):
                        psm, psmb_ = (psC_a, pCa) if bb2 % 2 == 0 else (psC_b, pCb)
                        psg, psgb = (psC_g0, pCg[0]) if bb2 % 2 == 0 else (psC_g1, pCg[1])
                        hh0 = bb2 * 2
                        if hh0 % 4 == 0 and not halo:
                            h0g = h2 * 32 + hh0
                            dma(s_flat[:], s_slamT[h0g:h0g + 4, j * 128:(j + 1) * 128],
                                reads=[slamTb, Cb["flat"]], writes=[Cb["flat"]])
                        for q in range(2):
                            hh = hh0 + q
                            h = h2 * 32 + hh
                            hg = j * 64 + h
                            row0 = q * 64
                            g2c = bb2 * 128
                            bank = h // 8
                            psy, psyb = (psC_y0, pCy[0]) if bank % 2 == 0 else (psC_y1, pCy[1])
                            sl = (h % 8) * 64
                            scol = (h // 2) * 64
                            if halo:
                                p.op("dve", lambda e, qq=q, hl=hh, c=hg: e.tensor_scalar(
                                    s_U[:, qq * 64:(qq + 1) * 64],
                                    s_xsj[:, hl * 64:(hl + 1) * 64],
                                    s_dt[:, c:c + 1], None, ALU.mult),
                                    reads=[Cb["xsj"], dtb_], writes=[Ub[q]])
                                p.op("dve", lambda e, qq=q, hh_=hh, c=h: e.tensor_scalar(
                                    s_Bdec[:, qq * 64:(qq + 1) * 64],
                                    s_Bh[:, hh_ * 64:(hh_ + 1) * 64],
                                    s_Adec[:, c:c + 1], None, ALU.mult),
                                    reads=[Cb["Bh"], Cb["Adec"]], writes=[Bdb[q]])
                                mm(psm[0:64, 256 + q * 64: 256 + (q + 1) * 64],
                                   s_Bdec[:, q * 64:(q + 1) * 64],
                                   s_U[:, q * 64:(q + 1) * 64], True, True,
                                   reads=[Bdb[q], Ub[q]], writes=[psmb_])
                                p.op("dve", lambda e, r0=row0, sc=scol, hh_=h: e.tensor_scalar(
                                    s_state[r0:r0 + 64, sc:sc + 64],
                                    s_state[r0:r0 + 64, sc:sc + 64],
                                    s_Dmat[r0:r0 + 64, hh_:hh_ + 1], None, ALU.mult),
                                    reads=[stb[h // 8], Cb["Dmat"]], writes=[stb[h // 8]])
                                p.op("dve", lambda e, r0=row0, sc=scol, qq=q, pp=psm: e.tensor_add(
                                    s_state[r0:r0 + 64, sc:sc + 64],
                                    s_state[r0:r0 + 64, sc:sc + 64],
                                    pp[0:64, 256 + qq * 64: 256 + (qq + 1) * 64]),
                                    reads=[stb[h // 8], psmb_], writes=[stb[h // 8]])
                                continue
                            mm(psm[:, q * 128:(q + 1) * 128], s_ones[:],
                               s_flat[0:1, (hh % 4) * 128:(hh % 4 + 1) * 128], True, True,
                               reads=[Cb["flat"], cb], writes=[psmb_])
                            mm(psg[:, q * 128:(q + 1) * 128],
                               s_smB[row0:row0 + 64, g2c:g2c + 128],
                               s_smC[row0:row0 + 64, g2c:g2c + 128], True, True,
                               reads=[Cb["smB"], Cb["smC"]], writes=[psgb])
                            p.op("dve", lambda e, qq=q, hl=hh, c=hg: e.tensor_scalar(
                                s_U[:, qq * 64:(qq + 1) * 64],
                                s_xsj[:, hl * 64:(hl + 1) * 64],
                                s_dt[:, c:c + 1], None, ALU.mult),
                                reads=[Cb["xsj"], dtb_], writes=[Ub[q]])
                            p.op("dve", lambda e, qq=q, hh_=hh, c=h: e.tensor_scalar(
                                s_Bdec[:, qq * 64:(qq + 1) * 64],
                                s_Bh[:, hh_ * 64:(hh_ + 1) * 64],
                                s_Adec[:, c:c + 1], None, ALU.mult),
                                reads=[Cb["Bh"], Cb["Adec"]], writes=[Bdb[q]])
                        for q in range(2):
                            if halo:
                                continue
                            hh = hh0 + q
                            h = h2 * 32 + hh
                            hg = j * 64 + h
                            row0 = q * 64
                            g2c = bb2 * 128
                            bank = h // 8
                            psy, psyb = (psC_y0, pCy[0]) if bank % 2 == 0 else (psC_y1, pCy[1])
                            sl = (h % 8) * 64
                            scol = (h // 2) * 64
                            p.op("dve", lambda e, qq=q, pp=psm: e.scalar_tensor_tensor(
                                s_e1[:, qq * 128:(qq + 1) * 128], pp[:, qq * 128:(qq + 1) * 128],
                                -1.0, s_bneg[:], ALU.mult, ALU.add),
                                reads=[psmb_, Cb["bneg"]], writes=[e1b[q]])
                            p.op("act", lambda e, qq=q, c=hg: e.activation(
                                s_W[:, qq * 128:(qq + 1) * 128], s_e1[:, qq * 128:(qq + 1) * 128],
                                AF.Exp, bias=s_slam[:, c:c + 1]),
                                reads=[e1b[q], slamb], writes=[Wb[q]])
                            p.op("dve", lambda e, qq=q, pp=psg: e.tensor_mul(
                                s_WG[:, qq * 128:(qq + 1) * 128],
                                s_W[:, qq * 128:(qq + 1) * 128],
                                pp[:, qq * 128:(qq + 1) * 128]),
                                reads=[Wb[q], psgb], writes=[WGb[q]])
                            mm(psy[:, sl:sl + 64], s_WG[:, q * 128:(q + 1) * 128],
                               s_U[:, q * 64:(q + 1) * 64], True, True,
                               reads=[WGb[q], Ub[q]], writes=[psyb])
                            mm(psg[:, 256 + q * 64: 256 + (q + 1) * 64],
                               s_smC[row0:row0 + 64, g2c:g2c + 128],
                               s_state[row0:row0 + 64, scol:scol + 64], True, True,
                               reads=[Cb["smC"], stb[h // 8]], writes=[psgb])
                            p.op("dve", lambda e, hh_=h, hl=hh: e.tensor_mul(
                                s_xsd[:, (hh_ % 8) * 64:((hh_ % 8) + 1) * 64],
                                s_xsj[:, hl * 64:(hl + 1) * 64],
                                s_drep[:, hh_ * 64:(hh_ + 1) * 64]),
                                reads=[Cb["xsj"], Cb["drep"]], writes=[Cb["xsd"]])
                            p.op("dve", lambda e, qq=q, hh_=h, pp=psg: e.scalar_tensor_tensor(
                                s_tyi[:, (hh_ % 8) * 64:((hh_ % 8) + 1) * 64],
                                pp[:, 256 + qq * 64: 256 + (qq + 1) * 64],
                                s_A8[:, hh_:hh_ + 1],
                                s_xsd[:, (hh_ % 8) * 64:((hh_ % 8) + 1) * 64],
                                ALU.mult, ALU.add),
                                reads=[psgb, Cb["A8"], Cb["xsd"]], writes=[Cb["tyi"]])
                            mm(psm[0:64, 256 + q * 64: 256 + (q + 1) * 64],
                               s_Bdec[:, q * 64:(q + 1) * 64],
                               s_U[:, q * 64:(q + 1) * 64], True, True,
                               reads=[Bdb[q], Ub[q]], writes=[psmb_])
                            p.op("dve", lambda e, r0=row0, sc=scol, hh_=h: e.tensor_scalar(
                                s_state[r0:r0 + 64, sc:sc + 64],
                                s_state[r0:r0 + 64, sc:sc + 64],
                                s_Dmat[r0:r0 + 64, hh_:hh_ + 1], None, ALU.mult),
                                reads=[stb[h // 8], Cb["Dmat"]], writes=[stb[h // 8]])
                            p.op("dve", lambda e, r0=row0, sc=scol, qq=q, pp=psm: e.tensor_add(
                                s_state[r0:r0 + 64, sc:sc + 64],
                                s_state[r0:r0 + 64, sc:sc + 64],
                                pp[0:64, 256 + qq * 64: 256 + (qq + 1) * 64]),
                                reads=[stb[h // 8], psmb_], writes=[stb[h // 8]])
                        if (hh0 + 2) % 8 == 0 and not halo:
                            bank = (h2 * 32 + hh0) // 8
                            psy, psyb = (psC_y0, pCy[0]) if bank % 2 == 0 else (psC_y1, pCy[1])
                            p.op("dve", lambda e, bk=bank, pp=psy: e.scalar_tensor_tensor(
                                s_ybuf[:, bk * 512:(bk + 1) * 512], pp[:], 0.125,
                                s_tyi[:], ALU.mult, ALU.add),
                                reads=[psyb, Cb["tyi"]], writes=[Cb["ybuf"]])
                p.serial_now = ("C" in ser_phases) or ("Cgate" in ser_phases)
                if halo:
                    continue
                if t_y is not None:
                    dma(t_y.rearrange("(n p) f -> n p f", p=128)[j - 1], s_ybuf[:],
                        reads=[Cb["ybuf"]])
                for gh in range(4):
                    p.op("dve", lambda e, jj=j - 1, g0=gh: e.tensor_mul(
                        s_gbuf[:], s_ybuf[:, g0 * 1024:(g0 + 1) * 1024],
                        s_pool[:, jj * 4096 + g0 * 1024: jj * 4096 + (g0 + 1) * 1024]),
                        reads=[Cb["ybuf"], poolb], writes=[Cb["gbuf"]])
                    p.op("act", lambda e, g0=gh: e.activation(
                        s_ynb[:, g0 * 1024:(g0 + 1) * 1024], s_gbuf[:], AF.Square,
                        accum_out=s_sall[:, 24 + g0:25 + g0]),
                        reads=[Cb["gbuf"]], writes=[Cb["ynb"], Cb["cc"]])
                p.op("dve", lambda e: e.tensor_add(s_sall[:, 24:25], s_sall[:, 24:25], s_sall[:, 25:26]),
                     reads=[Cb["cc"]], writes=[Cb["cc"]])
                p.op("dve", lambda e: e.tensor_add(s_sall[:, 26:27], s_sall[:, 26:27], s_sall[:, 27:28]),
                     reads=[Cb["cc"]], writes=[Cb["cc"]])
                p.op("dve", lambda e: e.tensor_add(s_sall[:, 22:23], s_sall[:, 24:25], s_sall[:, 26:27]),
                     reads=[Cb["cc"]], writes=[Cb["cc"]])
                p.op("act", lambda e: e.activation(s_sall[:, 23:24], s_sall[:, 22:23], AF.Ln,
                                                   scale=1.0 / float(D_INNER), bias=s_actc[:, 2:3]),
                     reads=[Cb["cc"], cb], writes=[Cb["cc"]])
                p.op("act", lambda e: e.activation(s_sall[:, 23:24], s_sall[:, 23:24], AF.Exp, scale=-0.5),
                     reads=[Cb["cc"]], writes=[Cb["cc"]])
                for gh in range(4):
                    p.op("dve", lambda e, jj=j - 1, g0=gh: e.tensor_mul(
                        s_gbuf[:], s_ybuf[:, g0 * 1024:(g0 + 1) * 1024],
                        s_pool[:, jj * 4096 + g0 * 1024: jj * 4096 + (g0 + 1) * 1024]),
                        reads=[Cb["ybuf"], poolb], writes=[Cb["gbuf"]])
                    p.op("dve", lambda e, g0=gh: e.scalar_tensor_tensor(
                        s_ynb[:, g0 * 1024:(g0 + 1) * 1024], s_gbuf[:], s_sall[:, 23:24],
                        s_normw[:, g0 * 1024:(g0 + 1) * 1024], ALU.mult, ALU.mult),
                        reads=[Cb["gbuf"], Cb["cc"], Cb["normw"]], writes=[Cb["ynb"]])
                if t_yn is not None:
                    for gh in range(4):
                        p.op("dve", lambda e, g0=gh: e.tensor_copy(
                            s_gbuf[:], s_ynb[:, g0 * 1024:(g0 + 1) * 1024]),
                            reads=[Cb["ynb"]], writes=[Cb["gbuf"]])
                        dma(t_yn.rearrange("(n p) f -> n p f", p=128)[j - 1, :, gh * 1024:(gh + 1) * 1024],
                            s_gbuf[:], reads=[Cb["gbuf"]])
                for q in range(4):
                    pst, pstb = (psC_t0, pCt[0]) if q % 2 == 0 else (psC_t1, pCt[1])
                    for c in range(8):
                        kc = q * 8 + c
                        tr(pst[:, c * 128:(c + 1) * 128], s_ynb[:, kc * 128:(kc + 1) * 128],
                           s_idb[:], reads=[Cb["ynb"], cb], writes=[pstb])
                    p.op("dve", lambda e, jj=j - 1, qq=q, pp=pst: e.tensor_copy(
                        s_pool[:, jj * 4096 + qq * 1024: jj * 4096 + (qq + 1) * 1024], pp[:]),
                        reads=[pstb], writes=[poolb])
    p.fence()

    p.serial_now = "E" in ser_phases
    with contextlib.ExitStack() as _esED:
        _e = _esED.enter_context
        s_attn = _e(nc.sbuf_tensor("s_attn", [128, NT * 2048], F32))
        attnb = B("attn")
        if n_phases >= 4:
            t_attn = tap("attn_out", [TOK, D_MODEL])
            with contextlib.ExitStack() as _esE:
                _e2 = _esE.enter_context
                s_web = [_e2(nc.sbuf_tensor(f"s_web{i}", [128, 512], F32)) for i in range(2)]
                s_webb = [_e2(nc.sbuf_tensor(f"s_webb{i}", [128, 512], BF16)) for i in range(2)]
                psE = _e2(nc.psum_tensor("psE", [128, 4096], F32))
                webb = [B("web0"), B("web1")]
                webbb = [B("webb0"), B("webb1")]
                pE = [B(f"psE{j}") for j in range(8)]
                w_out_t = w_out.rearrange("(kc p) o -> kc p o", p=128)
                for og in range(4):
                    for kc in range(32):
                        par = kc % 2
                        dma(s_web[par][:], w_out_t[kc, :, og * 512:(og + 1) * 512],
                            reads=[webb[par]], writes=[webb[par]])
                        p.op("pool", lambda e, pr=par: e.tensor_copy(s_webb[pr][:], s_web[pr][:]),
                             reads=[webb[par]], writes=[webbb[par]])
                        for j in range(NT):
                            mm(psE[:, j * 512:(j + 1) * 512],
                               s_pool[:, j * 4096 + kc * 128: j * 4096 + (kc + 1) * 128],
                               s_webb[par][:], kc == 0, kc == 31,
                               reads=[poolb, webbb[par]], writes=[pE[j]])
                    for j in range(NT):
                        p.op("dve", lambda e, jj=j, o=og: e.tensor_copy(
                            s_attn[:, jj * 2048 + o * 512: jj * 2048 + (o + 1) * 512],
                            psE[:, jj * 512:(jj + 1) * 512]),
                            reads=[pE[j]], writes=[attnb])
            if t_attn is not None:
                for j in range(NT):
                    dma(t_attn.rearrange("(n p) f -> n p f", p=128)[j],
                        s_attn[:, j * 2048:(j + 1) * 2048], reads=[attnb])
        p.fence()

        p.serial_now = "D1" in ser_phases
        if n_phases >= 5:
            t_gu = tap("gu", [TOK, D_FFN])
            t_gate = tap("gatev", [TOK, D_FFN])
            for kc in range(16):
                dma(s_pool[:, kc * 1024:(kc + 1) * 1024], xqT_dram[kc],
                    reads=[xqTdb, poolb], writes=[poolb])
            with contextlib.ExitStack() as _esD1:
                _e2 = _esD1.enter_context
                s_wfb = [_e2(nc.sbuf_tensor("s_wfb0", [128, 2048], F32))]
                s_wfq = _e2(nc.sbuf_tensor("s_wfq", [128, 2048], BF16))
                s_wqT = [_e2(nc.sbuf_tensor(f"s_wqT{i}", [128, 16 * 256], BF16)) for i in range(2)]
                s_gg = _e2(nc.sbuf_tensor("s_gg", [128, 4096], F32))
                s_sxw = _e2(nc.sbuf_tensor("s_sxw", [128, 16], F32))
                s_gu2 = _e2(nc.sbuf_tensor("s_gu2", [128, 256], F32))
                s_gstg = _e2(nc.sbuf_tensor("s_gstg", [128, 256], F32))
                s_wdb = _e2(nc.sbuf_tensor("s_wdb", [128, 1024], F32))
                s_wdq = _e2(nc.sbuf_tensor("s_wdq", [128, 1024], BF16))
                s_wdstg = _e2(nc.sbuf_tensor("s_wdstg", [128, 512], BF16))
                psD = _e2(nc.psum_tensor("psD", [128, 2048], F32))
                psDt = _e2(nc.psum_tensor("psDt", [128, 1024], BF16))
                psDt2 = _e2(nc.psum_tensor("psDt2", [128, 1024], BF16))
                wfbb = [B("wfb0")]
                wfqb = B("wfq")
                wqTb = [B("wqT0"), B("wqT1")]
                ggb = B("gg")
                sxwb = B("sxw")
                gstgb = B("gstg")
                wdbb = B("wdb"); wdqb = B("wdq"); wdstgb = B("wdstg")
                pD = [B(f"psD{j}") for j in range(4)]
                pDt = [B("psDt"), B("psDt2")]

                gu2b = B("gu2")
                for j in range(NT):
                    p.op("dve", lambda e, jj=j: e.tensor_mul(
                        s_sxw[:, jj:jj + 1], s_sall[:, jj:jj + 1], s_ws[:, 4:5]),
                        reads=[sallb, wsb2], writes=[sxwb])
                    p.op("dve", lambda e, jj=j: e.tensor_mul(
                        s_sxw[:, 8 + jj:9 + jj], s_sall[:, jj:jj + 1], s_ws[:, 5:6]),
                        reads=[sallb, wsb2], writes=[sxwb])

                def wd_prepass():
                    wd_v3 = wd.rearrange("(n p) k -> n p k", p=128)
                    for blk in range(128):
                        r, e8 = blk // 8, blk % 8
                        dma(s_wdb[:], wd_v3[r][:, e8 * 1024:(e8 + 1) * 1024],
                            reads=[wdbb], writes=[wdbb], q="act")
                        p.op("dve", lambda e: e.tensor_scalar(
                            s_wdb[:], s_wdb[:], s_ws[:, 2:3], MAGIC, ALU.mult, ALU.add),
                            reads=[wdbb, wsb2], writes=[wdbb])
                        p.op("dve", lambda e: e.tensor_scalar(
                            s_wdb[:], s_wdb[:], MAGIC, 1.0, ALU.subtract, ALU.min),
                            reads=[wdbb], writes=[wdbb])
                        p.op("dve", lambda e: e.tensor_scalar(
                            s_wdq[:], s_wdb[:], -1.0, None, ALU.max),
                            reads=[wdbb], writes=[wdqb])
                        pst, pstb = (psDt, pDt[0]) if blk % 2 == 0 else (psDt2, pDt[1])
                        for c in range(8):
                            tr(pst[:, c * 128:(c + 1) * 128],
                               s_wdq[:, c * 128:(c + 1) * 128], s_idb[:],
                               reads=[wdqb, cb], writes=[pstb])
                        for hf in range(2):
                            p.op("dve", lambda e, pp=pst, h0=hf: e.tensor_copy(
                                s_wdstg[:], pp[:, h0 * 512:(h0 + 1) * 512]),
                                reads=[pstb], writes=[wdstgb])
                            for c in range(4):
                                kcl = hf * 4 + c
                                kc2 = e8 * 8 + kcl
                                dma(wqdT_dram[kc2, :, r * 128:(r + 1) * 128],
                                    s_wdstg[:, c * 128:(c + 1) * 128],
                                    reads=[wdstgb], writes=[wqdb], q="act")
                        yield

                wdgen = wd_prepass()

                for g in range(16):
                    for mi, (wm, wscol) in enumerate(((wg, 4), (wu, 5))):
                        wm_t = wm.rearrange("(n p) k -> n p k", p=128)
                        for ho in range(2):
                            wp = (mi * 2 + ho) % 2
                            for c in range(2):
                                par = 0
                                dma(s_wfb[par][:], wm_t[g * 4 + ho * 2 + c],
                                    reads=[wfbb[par]], writes=[wfbb[par]])
                                wcol = 0 if wscol == 4 else 1
                                p.op("dve", lambda e, pr=par, wc=wcol: e.tensor_scalar(
                                    s_wfb[pr][:], s_wfb[pr][:], s_ws[:, wc:wc + 1], MAGIC,
                                    ALU.mult, ALU.add), reads=[wfbb[par], wsb2], writes=[wfbb[par]])
                                p.op("dve", lambda e, pr=par: e.tensor_scalar(
                                    s_wfb[pr][:], s_wfb[pr][:], MAGIC, 1.0, ALU.subtract, ALU.min),
                                    reads=[wfbb[par]], writes=[wfbb[par]])
                                p.op("dve", lambda e, pr=par: e.tensor_scalar(
                                    s_wfq[:], s_wfb[pr][:], -1.0, None, ALU.max),
                                    reads=[wfbb[par]], writes=[wfqb])
                                for q in range(2):
                                    pst, pstb = (psDt, pDt[0]) if q % 2 == 0 else (psDt2, pDt[1])
                                    for cc2 in range(8):
                                        kcl = q * 8 + cc2
                                        tr(pst[:, cc2 * 128:(cc2 + 1) * 128],
                                           s_wfq[:, kcl * 128:(kcl + 1) * 128], s_idb[:],
                                           reads=[wfqb, cb], writes=[pstb])
                                    p.op("dve", lambda e, qq=q, ccx=c, pp=pst, w=wp: e.tensor_copy(
                                        s_wqT[w][:].rearrange("p (kc o) -> p kc o", o=256)
                                        [:, qq * 8:(qq + 1) * 8, ccx * 128:(ccx + 1) * 128],
                                        pp[:].rearrange("p (kc t) -> p kc t", t=128)),
                                        reads=[pstb], writes=[wqTb[wp]])
                            for tg in range(2):
                                for kc in range(16):
                                    for ji in range(4):
                                        j = tg * 4 + ji
                                        mm(psD[:, ji * 512 + ho * 256: ji * 512 + ho * 256 + 256],
                                           s_pool[:, kc * 1024 + j * 128: kc * 1024 + j * 128 + 128],
                                           s_wqT[wp][:, kc * 256:(kc + 1) * 256], kc == 0, kc == 15,
                                           reads=[poolb, wqTb[wp]], writes=[pD[ji]])
                                for ji in range(4):
                                    j = tg * 4 + ji
                                    psl = psD[:, ji * 512 + ho * 256: ji * 512 + ho * 256 + 256]
                                    c0 = g * 512 + ho * 256
                                    if mi == 0:
                                        p.op("act", lambda e, jj=j, ps=psl, h0=ho: e.activation(
                                            s_gg[:, jj * 512 + h0 * 256: jj * 512 + h0 * 256 + 256],
                                            ps, AF.Silu, scale=s_sxw[:, jj:jj + 1]),
                                            reads=[pD[ji], sxwb], writes=[ggb])
                                        if t_gate is not None:
                                            dma(t_gate.rearrange("(n p) f -> n p f", p=128)[j, :, c0:c0 + 256],
                                                s_gg[:, j * 512 + ho * 256: j * 512 + ho * 256 + 256],
                                                reads=[ggb])
                                    else:
                                        p.op("dve", lambda e, jj=j, ps=psl: e.tensor_scalar(
                                            s_gstg[:, 0:256], ps,
                                            s_sxw[:, 8 + jj:9 + jj], None, ALU.mult),
                                            reads=[pD[ji], sxwb], writes=[gstgb])
                                        p.op("dve", lambda e, jj=j, h0=ho: e.tensor_mul(
                                            s_gu2[:, 0:256],
                                            s_gg[:, jj * 512 + h0 * 256: jj * 512 + h0 * 256 + 256],
                                            s_gstg[:, 0:256]),
                                            reads=[ggb, gstgb], writes=[gu2b])
                                        p.op("dve", lambda e: e.reduce_max(
                                            s_rmax[:, 8:9], s_gu2[:, 0:256], axis=AX,
                                            apply_absolute_value=True),
                                            reads=[gu2b], writes=[rmaxb])
                                        p.op("dve", lambda e, jj=j: e.tensor_max(
                                            s_rmax[:, jj:jj + 1], s_rmax[:, jj:jj + 1],
                                            s_rmax[:, 8:9]),
                                            reads=[rmaxb], writes=[rmaxb])
                                        dma(gu_dram[j, :, c0:c0 + 256], s_gu2[:, 0:256],
                                            reads=[gu2b], writes=[gudb])
                                        if t_gu is not None:
                                            dma(t_gu.rearrange("(n p) f -> n p f", p=128)[j, :, c0:c0 + 256],
                                                s_gu2[:, 0:256], reads=[gu2b])
                    for _ in range(8):
                        next(wdgen, None)
                for _ in wdgen:
                    pass
        p.fence()

        p.serial_now = "D2" in ser_phases
        if n_phases >= 6:
            with contextlib.ExitStack() as _esD2:
                _e2 = _esD2.enter_context
                s_gub = _e2(nc.sbuf_tensor("s_gub", [128, 4096], F32))
                s_xq2 = _e2(nc.sbuf_tensor("s_xq2", [128, 4096], BF16))
                s_wdT = [_e2(nc.sbuf_tensor(f"s_wdT{i}", [128, 512], BF16)) for i in range(2)]
                s_dcol = _e2(nc.sbuf_tensor("s_dcol", [128, 8], F32))
                s_xrl = _e2(nc.sbuf_tensor("s_xrl", [128, 512], F32))
                s_otmp = _e2(nc.sbuf_tensor("s_otmp", [128, 512], F32))
                psD2 = _e2(nc.psum_tensor("psD2", [128, 2048], F32))
                psDq = _e2(nc.psum_tensor("psDq", [128, 1024], BF16))
                psDq2 = _e2(nc.psum_tensor("psDq2", [128, 1024], BF16))
                gubb = B("gub"); xq2b = B("xq2")
                wdTb = [B("wdT0"), B("wdT1")]
                dcolb = B("dcol"); xrlb = B("xrl"); otmpb = B("otmp")
                pD2 = [B(f"psD2{j}") for j in range(4)]
                pDq = [B("psDq"), B("psDq2")]

                for half in range(2):
                    js = [half * 4 + i for i in range(4)]
                    for j in js:
                        p.op("dve", lambda e, jj=j: e.tensor_scalar(
                            s_sall[:, 8 + jj: 9 + jj], s_rmax[:, jj:jj + 1],
                            1.0 / 127.0, EPS_F, ALU.mult, ALU.add),
                            reads=[rmaxb], writes=[sallb])
                        p.op("dve", lambda e, jj=j: e.reciprocal(
                            s_sall[:, 16:17], s_sall[:, 8 + jj: 9 + jj]),
                            reads=[sallb], writes=[sallb])
                        for gq in range(2):
                            dma(s_gub[:], gu_dram[j, :, gq * 4096:(gq + 1) * 4096],
                                reads=[gudb, gubb], writes=[gubb])
                            p.op("dve", lambda e: e.tensor_scalar(
                                s_gub[:], s_gub[:], s_sall[:, 16:17], MAGIC, ALU.mult, ALU.add),
                                reads=[gubb, sallb], writes=[gubb])
                            p.op("dve", lambda e: e.tensor_scalar(
                                s_xq2[:], s_gub[:], MAGIC, None, ALU.subtract),
                                reads=[gubb], writes=[xq2b])
                            for q in range(4):
                                pst, pstb = (psDq, pDq[0]) if q % 2 == 0 else (psDq2, pDq[1])
                                for c in range(8):
                                    kcl = q * 8 + c
                                    tr(pst[:, c * 128:(c + 1) * 128],
                                       s_xq2[:, kcl * 128:(kcl + 1) * 128], s_idb[:],
                                       reads=[xq2b, cb], writes=[pstb])
                                p.op("dve", lambda e, jj=j, gg=gq, qq=q, pp=pst: e.tensor_copy(
                                    s_pool[:, (jj % 4) * 8192 + gg * 4096 + qq * 1024:
                                            (jj % 4) * 8192 + gg * 4096 + (qq + 1) * 1024], pp[:]),
                                    reads=[pstb], writes=[poolb])
                    for og in range(4):
                        for kc2 in range(64):
                            par = kc2 % 2
                            dma(s_wdT[par][:], wqdT_dram[kc2, :, og * 512:(og + 1) * 512],
                                reads=[wdTb[par], wqdb], writes=[wdTb[par]])
                            for ji, j in enumerate(js):
                                mm(psD2[:, ji * 512:(ji + 1) * 512],
                                   s_pool[:, (j % 4) * 8192 + kc2 * 128: (j % 4) * 8192 + kc2 * 128 + 128],
                                   s_wdT[par][:], kc2 == 0, kc2 == 63,
                                   reads=[poolb, wdTb[par]], writes=[pD2[ji]])
                        for ji, j in enumerate(js):
                            p.op("dve", lambda e, jj=j: e.tensor_mul(
                                s_dcol[:, jj % 4:jj % 4 + 1], s_sall[:, 8 + jj: 9 + jj],
                                s_ws[:, 6:7]), reads=[sallb, wsb2], writes=[dcolb])
                            dma(s_xrl[:], x_t[j + 1][:, og * 512:(og + 1) * 512],
                                reads=[xrlb], writes=[xrlb])
                            p.op("dve", lambda e, jj=j, ji_=ji, o=og: e.scalar_tensor_tensor(
                                s_otmp[:], psD2[:, ji_ * 512:(ji_ + 1) * 512],
                                s_dcol[:, jj % 4:jj % 4 + 1],
                                s_attn[:, jj * 2048 + o * 512: jj * 2048 + (o + 1) * 512],
                                ALU.mult, ALU.add),
                                reads=[pD2[ji], dcolb, attnb], writes=[otmpb])
                            p.op("dve", lambda e: e.tensor_add(s_otmp[:], s_otmp[:], s_xrl[:]),
                                 reads=[otmpb, xrlb], writes=[otmpb])
                            dma(y_t[j][:, og * 512:(og + 1) * 512], s_otmp[:],
                                reads=[otmpb], writes=[outb])

    if n_phases < 6:
        with nc.sbuf_tensor("s_dummy", [128, 2048], F32) as s_dummy:
            p.op("dve", lambda e: e.memset(s_dummy[:], 0.0), writes=[B("dummy")])
            for j in range(NT):
                dma(y_t[j], s_dummy[:], reads=[B("dummy")], writes=[outb])

    p.truncate = truncate
    p.finalize()
    return nc, tap_outs



def _consts(inputs):
    rope_freqs = np.asarray(inputs["L0_attn_rope_freqs"], np.float32)
    B_bias = np.asarray(inputs["L0_attn_B_bias"], np.float32)
    C_bias = np.asarray(inputs["L0_attn_C_bias"], np.float32)
    D_param = np.asarray(inputs["L0_attn_D_param"], np.float32)
    norm_w = np.asarray(inputs["L0_attn_norm_w"], np.float32)
    dt_bias = np.asarray(inputs["L0_attn_dt_bias"], np.float32)
    A_log = np.asarray(inputs["L0_attn_A_log"], np.float32)

    r32f = np.zeros((64, 2048), np.float32)
    for h in range(64):
        r32f[h, h * 32:(h + 1) * 32] = rope_freqs[h]
    rn_ev = np.zeros((64, 2048), ml_dtypes.bfloat16)
    rn_od = np.zeros((64, 2048), ml_dtypes.bfloat16)
    for h in range(64):
        for f in range(32):
            rn_ev[2 * f, h * 32 + f] = 1
            rn_od[2 * f + 1, h * 32 + f] = 1
    bneg = np.where(np.arange(128)[:, None] <= np.arange(128)[None, :], 0.0,
                    BIGNEG).astype(np.float32)
    return {
        "dt_bias_r": dt_bias.reshape(1, 64),
        "a_log_c": A_log.reshape(64, 1),
        "d_rep_r": np.repeat(D_param, 64).reshape(1, 4096),
        "bb_ev": np.ascontiguousarray(B_bias[:, 0::2]).reshape(1, 2048),
        "bb_od": np.ascontiguousarray(B_bias[:, 1::2]).reshape(1, 2048),
        "cb_ev": np.ascontiguousarray(C_bias[:, 0::2]).reshape(1, 2048),
        "cb_od": np.ascontiguousarray(C_bias[:, 1::2]).reshape(1, 2048),
        "normw_r": norm_w.reshape(1, 4096),
        "r32f": r32f,
        "rn_ev": rn_ev,
        "rn_od": rn_od,
        "idf": np.eye(128, dtype=np.float32),
        "idb": np.eye(128, dtype=ml_dtypes.bfloat16),
        "onesf": np.ones((1, 128), np.float32),
        "actc": np.broadcast_to(np.array([[1e-5, 1.0, 1e-6, 0.5 * math.pi, 0, 0, 0, 0]],
                                         np.float32), (128, 8)).copy(),
        "bnegp": bneg,
    }


def make_in_maps(inputs):
    x = np.asarray(inputs["L0_attn_x"], np.float32).reshape(8192, 2048)
    w_in = np.ascontiguousarray(np.asarray(inputs["L0_attn_W_in"], np.float32))
    w_out = np.ascontiguousarray(np.asarray(inputs["L0_attn_W_out"], np.float32))
    wgm = np.ascontiguousarray(np.asarray(inputs["L0_ffn_w_gate"], np.float32))
    wum = np.ascontiguousarray(np.asarray(inputs["L0_ffn_w_up"], np.float32))
    wdm = np.ascontiguousarray(np.asarray(inputs["L0_ffn_w_down"], np.float32))
    consts = _consts(inputs)
    in_maps = []
    zrow = np.zeros((128, 2048), np.float32)
    for k in range(8):
        m = dict(consts)
        halo = zrow if k == 0 else x[k * 1024 - 128: k * 1024]
        m["x"] = np.ascontiguousarray(np.concatenate([halo, x[k * 1024:(k + 1) * 1024]], axis=0))
        m["w_in"] = w_in
        m["w_out"] = w_out
        m["wg"] = wgm
        m["wu"] = wum
        m["wd"] = wdm
        in_maps.append(m)
    return in_maps


def kernel(**inputs):
    nc, _ = build_program()
    in_maps = make_in_maps(inputs)
    res = run_bass_kernel_spmd(nc, in_maps, core_ids=list(range(8)))
    out = np.concatenate([np.asarray(r["y"]) for r in res.results], axis=0)
    return out.reshape(1, 8192, 2048).astype(np.float32)
